# version 21
# baseline (speedup 1.0000x reference)
import numpy as np
import ml_dtypes
from contextlib import ExitStack

import concourse.bass as bass
import concourse.mybir as mybir
from concourse.bass_utils import run_bass_kernel_spmd
from concourse import dve_ops as _dve_ops
from concourse.dve_op import DveOp as _DveOp
from concourse.dve_spec import (Spec as _Spec, Src0 as _Src0, Src1 as _Src1,
                                C0 as _C0, C1 as _C1, relu as _relu,
                                lower as _dve_lower)


def _fp4_mag_sgn_ref(in0, in1, s0, s1, imm2):
    x = np.asarray(in0).astype(np.float32)
    g = np.asarray(in1).astype(np.float32)
    r1 = np.maximum(x - s0, 0.0)
    r2 = np.maximum(x + x - s1, 0.0)
    return (((x + r1) + r2) * g).astype(np.float32)


def _register_fp4_op():
    for o in _dve_ops.OPS:
        if o.name == "FP4_MAG_SGN":
            return o
    spec = _Spec(body=((_Src0 + _relu(_Src0 - _C0))
                       + _relu((_Src0 + _Src0) - _C1)) * _Src1)
    sha = {"v3": _dve_lower(spec, ver="v3", name="FP4_MAG_SGN").sha("v3")}
    op = _DveOp("FP4_MAG_SGN", spec, False, sha, reference=_fp4_mag_sgn_ref)
    _dve_ops.OPS.append(op)
    _dve_ops._SUB_OPCODE_FOR_NAME["FP4_MAG_SGN"] = (
        _dve_ops._CUSTOM_DVE_ROW_BASE + len(_dve_ops.OPS) - 1)
    _dve_ops.CUSTOM_DVE_SPECS["FP4_MAG_SGN"] = spec
    _dve_ops.CUSTOM_DVE_REFS["FP4_MAG_SGN"] = _fp4_mag_sgn_ref
    return op


FP4_OP = _register_fp4_op()

F32 = mybir.dt.float32
BF16 = mybir.dt.bfloat16
I32 = mybir.dt.int32
I16 = mybir.dt.int16
OP = mybir.AluOpType
AF = mybir.ActivationFunctionType
MLBF = ml_dtypes.bfloat16

P = 128
NTOK = 1024
D = 4096
NCH = 32
NC = 8
NEU = 10
N_IPC = 16
N_OSL = 4
OSL = 1024
NUNIT = N_IPC * N_OSL
H = 512
DEBUG = False
PHASE = 5
SERIAL_DEC = False


def build_program():
    nc = bass.Bass("TRN2", target_bir_lowering=False, detect_race_conditions=False)

    u_in = nc.dram_tensor("u_s", [D, NTOK], F32, kind="ExternalInput").ap()
    pk_in = nc.dram_tensor("packedT", [2048, D], I32, kind="ExternalInput").ap()
    bst_in = nc.dram_tensor("bsT3", [16, 16 * D], F32, kind="ExternalInput").ap()
    at_in = nc.dram_tensor("lA_T", [P, NCH * 16], F32, kind="ExternalInput").ap()
    bt_in = nc.dram_tensor("lB_T", [16, D], F32, kind="ExternalInput").ap()
    sm_in = nc.dram_tensor("sm", [64, 9], F32, kind="ExternalInput").ap()
    vcr_in = nc.dram_tensor("VcT_re", [64, 64], F32, kind="ExternalInput").ap()
    vci_in = nc.dram_tensor("VcT_im", [64, 64], F32, kind="ExternalInput").ap()
    scal_in = nc.dram_tensor("scal", [1, 3], F32, kind="ExternalInput").ap()
    i64_in = nc.dram_tensor("cI64", [64, 64], F32, kind="ExternalInput").ap()
    j128_in = nc.dram_tensor("cJ128", [P, P], BF16, kind="ExternalInput").ap()
    e8_in = nc.dram_tensor("cE8", [16, P], BF16, kind="ExternalInput").ap()
    ones_in = nc.dram_tensor("cOnes", [P, 4], F32, kind="ExternalInput").ap()
    onesb_in = nc.dram_tensor("cOnesB", [P, 4], BF16, kind="ExternalInput").ap()
    onesr_in = nc.dram_tensor("cOnesR", [1, P], F32, kind="ExternalInput").ap()
    onesrb_in = nc.dram_tensor("cOnesRB", [1, P], BF16, kind="ExternalInput").ap()

    out_t = nc.dram_tensor("out_s", [D, NTOK], F32, kind="ExternalOutput").ap()
    dbg = {}
    if DEBUG:
        for nm, shp in [("K", [1, P]), ("T0", [P, P]), ("BmT", [P, P]),
                        ("CBl", [P, P]), ("ACl", [P, P]), ("Ab", [64, P]),
                        ("x1", [P, NTOK]), ("W0", [P, P]), ("ln", [P, NTOK]),
                        ("smx", [64, 96]), ("smr", [1, 576]), ("g64", [64, 8])]:
            dbg[nm] = nc.dram_tensor("dbg_" + nm, shp, F32, kind="ExternalOutput").ap()

    wt_d = nc.dram_tensor("WT_d", [NCH, P, NCH, P], BF16, kind="Internal").ap()
    kpad_d = nc.dram_tensor("kpad_d", [1, 256], BF16, kind="Internal").ap()

    es = ExitStack()
    sb = lambda n, s, d: es.enter_context(nc.sbuf_tensor(n, s, d))
    ps_ = lambda n, s: es.enter_context(nc.psum_tensor(n, s, F32))
    sem = lambda n: es.enter_context(nc.semaphore(n))

    ux = sb("ux", [P, NCH * NTOK], BF16)
    ln1b = sb("ln1b", [P, 2 * NTOK], BF16)
    sst = [sb("sst0", [P, NTOK], BF16), sb("sst1", [P, NTOK], BF16)]
    m1b = sb("m1b", [P, NTOK], BF16)
    s1b = sb("s1b", [P, NTOK], BF16)
    s2b = sb("s2b", [P, NTOK], F32)
    xsq = sb("xsq", [P, 2 * NTOK], BF16)
    tevac = sb("tevac", [P, NTOK], BF16)
    outb = sb("outb", [P, 2 * NTOK], F32)
    T0b = sb("T0b", [P, P], BF16)
    BmT = sb("BmT", [P, P], BF16)
    CBl = sb("CBl", [P, P], BF16)
    ACl = sb("ACl", [P, P], BF16)
    wtb = sb("wtb", [P, 2 * D], BF16)
    colc = sb("colc", [P, 2], F32)
    midb = sb("midb", [16, NTOK], BF16)
    arowb = sb("arowb", [16, 1], BF16)
    atb = sb("atb", [P, NCH * 16], BF16)
    btb = sb("btb", [16, D], BF16)
    bstu = sb("bstu", [16, 2 * OSL], BF16)
    bstu16 = sb("bstu16", [16, 6 * OSL], BF16)
    e8g = sb("e8g", [16, P], BF16)
    slab = sb("slab", [P, 2 * OSL], I32)
    dc1 = sb("dc1", [P, 2 * OSL], I16)
    dc2 = sb("dc2", [P, 2 * OSL], I16)
    dc4 = sb("dc4", [P, 2 * OSL], I16)
    wlo = sb("wlo", [P, 2 * OSL], BF16)
    whi = sb("whi", [P, 2 * OSL], BF16)
    smt = sb("smt", [64, 9], F32)
    vctr = sb("vctr", [64, 64], F32)
    vcti = sb("vcti", [64, 64], F32)
    i64f = sb("i64f", [64, 64], F32)
    j128b = sb("j128b", [P, P], BF16)
    e8b = sb("e8b", [16, P], BF16)
    onesf = sb("onesf", [P, 4], F32)
    onesbf = sb("onesbf", [P, 4], BF16)
    onesrf = sb("onesrf", [1, P], F32)
    onesrbf = sb("onesrbf", [1, P], BF16)
    scalt = sb("scalt", [1, 4], F32)
    rowt = sb("rowt", [1, 2 * NTOK], F32)
    rowu = sb("rowu", [1, NTOK], F32)
    _pw_off = {}
    _pw_n = [0]
    for _j in range(13):
        _slots = (0, 1, 2, 3, 4, 5) if _j <= 6 else ((0, 1, 2, 3, 4) if _j <= 11 else (0, 1, 5))
        for _s in _slots:
            _pw_off[(_j, _s)] = _pw_n[0]
            _pw_n[0] += 64
    _pw_off[("scr", 0)] = _pw_n[0]
    _pw_n[0] += 64
    pw = sb("pw", [64, _pw_n[0]], F32)
    smx = sb("smx", [64, 24 * 4], F32)
    smr = sb("smr", [1, 576], F32)
    bkre = sb("bkre", [64, P], F32)
    bkim = sb("bkim", [64, P], F32)
    zre = sb("zre", [64, P], F32)
    zim = sb("zim", [64, P], F32)
    t128 = sb("t128", [P, 64], BF16)
    krow = sb("krow", [1, P], BF16)
    z128 = sb("z128", [1, P], BF16)
    g64 = sb("g64", [64, 8], F32)
    sc16 = sb("sc16", [16, 2], F32)

    pA = ps_("pA", [P, NTOK])
    pB = ps_("pB", [P, NTOK])
    pC = ps_("pC", [P, NTOK])
    pD = ps_("pD", [P, NTOK])

    s_sm = sem("s_sm")
    s_u = sem("s_u")
    KRC = {}

    def u_wait(e, c):
        e.wait_ge(s_u, 16 * (c + 1))
    s_pk = sem("s_pk")
    s_wo = sem("s_wo")
    s_wos = sem("s_wos")
    s_sl = sem("s_sl")
    s_sl2 = sem("s_sl2")
    s_bs = sem("s_bs")
    s_misc = sem("s_misc")
    s_wt = sem("s_wt")
    s_out = sem("s_out")
    s_tp = sem("s_tp")
    bV = sem("bV")
    bP = sem("bP")
    bS = sem("bS")
    c_ln = sem("c_ln")
    c_x1 = sem("c_x1")
    c_y = sem("c_y")
    c_s = sem("c_s")
    c_sev = sem("c_sev")
    c_sq = sem("c_sq")
    c_dec = sem("c_dec")
    c_e8 = sem("c_e8")
    c_ffn = sem("c_ffn")
    c_ev = sem("c_ev")
    c_cc = sem("c_cc")
    c_st = sem("c_st")
    c_sub = sem("c_sub")
    SC2 = {}
    SC3 = {}
    s_dbg = sem("s_dbg")

    ops = {"V": [], "P": [], "S": [], "G": [], "Y": []}

    def em(key, f):
        ops[key].append(f)

    seqcnt = {"V": 0, "P": 0, "S": 0}
    seqsem = {"V": bV, "P": bP, "S": bS}

    def step(key, *fns, raw=False):
        waits = [(k, seqcnt[k]) for k in "VPS" if seqcnt[k] > 0]
        base = seqcnt[key]
        n = 1 if raw or not fns else len(fns)
        seqcnt[key] += n

        def f(eng, waits=waits, fns=fns, base=base, raw=raw):
            for k, v in waits:
                eng.wait_ge(seqsem[k], v)
            if raw or not fns:
                for g in fns:
                    g(eng)
                eng.sem_inc(seqsem[key], 1)
                return
            for i, g in enumerate(fns):
                if i > 0:
                    eng.wait_ge(seqsem[key], base + i)
                r = g(eng)
                ok = False
                if r is not None and hasattr(r, "then_inc"):
                    tn = type(getattr(r, "ins", r)).__name__
                    if "EventSemaphore" not in tn and "DMACopy" not in tn:
                        r.then_inc(seqsem[key], 1)
                        ok = True
                if not ok:
                    eng.sem_inc(seqsem[key], 1)
        em(key, f)

    def barrier():
        for k in "VPS":
            step(k)

    _smx_n = [0]

    def acol():
        i = _smx_n[0]
        _smx_n[0] += 1
        assert i < 96
        return smx[:, i:i + 1]

    _smr_n = [0]

    def arow_(w=1):
        i = _smr_n[0]
        _smr_n[0] += w
        assert _smr_n[0] <= 576
        return smr[:, i:i + w]

    def pwt(j, which):
        off = _pw_off[(j, which)]
        return pw[:, off:off + 64]

    UNITS = [(k // N_IPC, k % N_IPC) for k in range(NUNIT)]
    PSC = ["C"] * 16 + ["D" if k % 2 == 0 else "C" for k in range(16, 64)]

    def _uwait(k):
        for j in range(k - 1, -1, -1):
            if PSC[j] == PSC[k]:
                return j + 1
        return 0

    def y_main(y):
        y.dma_start(out=smt[:], in_=sm_in).then_inc(s_sm, 16)
        y.dma_start(out=vctr[:], in_=vcr_in).then_inc(s_sm, 16)
        y.dma_start(out=vcti[:], in_=vci_in).then_inc(s_sm, 16)
        y.dma_start(out=scalt[:, 0:3], in_=scal_in).then_inc(s_sm, 16)
        y.dma_start(out=i64f[:], in_=i64_in).then_inc(s_sm, 16)
        y.dma_start(out=j128b[:], in_=j128_in).then_inc(s_sm, 16)
        y.dma_start(out=e8b[:], in_=e8_in).then_inc(s_sm, 16)
        y.dma_start(out=onesf[:], in_=ones_in).then_inc(s_sm, 16)
        y.dma_start(out=onesbf[:], in_=onesb_in).then_inc(s_sm, 16)
        y.dma_start(out=onesrf[:], in_=onesr_in).then_inc(s_sm, 16)
        y.dma_start(out=onesrbf[:], in_=onesrb_in).then_inc(s_sm, 16)
        if PHASE >= 2:
            slab_load(y, 0, s_sl)
            slab_load(y, 1, s_sl)
            for k in range(16):
                y.wait_ge(c_dec, k + 1)
                w_out(y, k, s_wos)
                if k + 2 <= 17:
                    slab_load(y, k + 2, s_sl)
        y.wait_ge(bV, KRC["v"])
        y.dma_start(out=kpad_d[:, 0:128], in_=z128[:]).then_inc(s_tp, 16)
        y.dma_start(out=kpad_d[:, 128:256], in_=krow[:]).then_inc(s_tp, 16)
        y.wait_ge(s_tp, 32)
        for p in range(64):
            y.dma_start(out=T0b[p:p + 1, :],
                        in_=kpad_d[0:1, 128 - p:256 - p]).then_inc(s_tp, 16)
        for oc in (range(NCH) if PHASE >= 4 else []):
            osl = (oc * P) // OSL
            y.wait_ge(s_wos, 64 * 16)
            if osl >= 1:
                y.wait_ge(s_wo, 64 * 16 * osl)
            if oc >= 2:
                y.wait_ge(c_ffn, oc - 1)
            y.dma_start(
                out=wtb[:, (oc % 2) * D:(oc % 2) * D + D],
                in_=wt_d[oc].rearrange("p ic q -> p (ic q)"),
            ).then_inc(s_wt, 16)
        y.wait_ge(s_out, 32 * 16)
    em("Y", y_main)

    def u_load(eng, c, sem_):
        eng.dma_start(
            out=ux[:, c * NTOK:(c + 1) * NTOK],
            in_=u_in[c * P:(c + 1) * P, :],
        ).then_inc(sem_, 16)

    wt4 = wt_d.rearrange("oc (p2 two) ic q -> ic two p2 oc q", two=2)

    def w_out(eng, k, sem_):
        osl, ipc = UNITS[k]
        for par, wsrc in ((0, wlo), (1, whi)):
            t = wsrc[:, (k % 2) * OSL:(k % 2 + 1) * OSL].rearrange(
                "p (ocs q) -> p ocs q", q=P)
            for h_ in range(2):
                eng.dma_start(
                    out=wt4[2 * ipc + h_, par, :, osl * 8:(osl + 1) * 8, :],
                    in_=t[h_ * 64:(h_ + 1) * 64],
                ).then_inc(sem_, 16)

    def slab_load(eng, k, sem_):
        osl, ipc = UNITS[k]
        eng.dma_start(
            out=slab[:, (k % 2) * OSL:(k % 2 + 1) * OSL],
            in_=pk_in[ipc * P:(ipc + 1) * P, osl * OSL:(osl + 1) * OSL],
        ).then_inc(sem_, 16)

    def bstu_load(g, k):
        osl, ipc = UNITS[k]
        out = bstu16[:, k * OSL:(k + 1) * OSL] if k < 6 else \
            bstu[:, (k % 2) * OSL:(k % 2 + 1) * OSL]
        g.dma_start(
            out=out,
            in_=bst_in[:, ipc * D + osl * OSL:ipc * D + (osl + 1) * OSL],
        ).then_inc(s_bs, 16)

    def g_main(g):
        g.dma_start(out=atb[:], in_=at_in).then_inc(s_misc, 16)
        g.dma_start(out=btb[:], in_=bt_in).then_inc(s_misc, 16)
        if PHASE >= 2:
            for k in range(8):
                bstu_load(g, k)
        for c in range(8):
            u_load(g, c, s_u)
        if PHASE >= 2:
            for k in range(8, 16):
                g.wait_ge(c_dec, k - 1)
                bstu_load(g, k)
                u_load(g, 2 * (k - 8) + 8, s_u)
                u_load(g, 2 * (k - 8) + 9, s_u)
        for c in range((24 if PHASE >= 2 else 8), NCH):
            u_load(g, c, s_u)
        if PHASE < 2:
            return
        for k in (16, 17):
            g.wait_ge(c_dec, k - 1)
            bstu_load(g, k)
        for k in range(16, NUNIT):
            g.wait_ge(c_dec, k + 1)
            w_out(g, k, s_wo)
            if k + 2 < NUNIT:
                bstu_load(g, k + 2)
                slab_load(g, k + 2, s_sl2)
    em("G", g_main)

    def decode_unit_v(k):
        osl, ipc = UNITS[k]
        sl = slab[:, (k % 2) * OSL:(k % 2 + 1) * OSL]
        pscale = pC if PSC[k] == "C" else pD
        lo = wlo[:, (k % 2) * OSL:(k % 2 + 1) * OSL]
        hi = whi[:, (k % 2) * OSL:(k % 2 + 1) * OSL]
        c16a = dc1[:, 0:2 * OSL]
        t1s = dc2[:, 0:2 * OSL]

        def f(e, k=k):
            if k <= 17:
                e.wait_ge(s_sl, 16 * (k + 1))
            else:
                e.wait_ge(s_sl2, 16 * (k - 17))
            if 2 <= k <= 17:
                e.wait_ge(s_wos, 64 * (k - 1))
            elif k >= 18:
                e.wait_ge(s_wo, 64 * (k - 17))
            e.wait_ge(c_e8, k + 1)
            dm = dc4[:].bitcast(I32)
            e.tensor_scalar(dm, sl, 4097, None, OP.mult)
            dv = dm.bitcast(I16)
            e.tensor_scalar(c16a, dv, 7, None, OP.bitwise_and)
            e.tensor_scalar(t1s, dv, 8, 2, OP.bitwise_and, OP.logical_shift_right)
            e.tensor_scalar(t1s, t1s, -1, 1, OP.mult, OP.add)
            e._custom_dve(FP4_OP, out=dc4[:, 0:2 * OSL].bitcast(BF16),
                          in0=c16a, in1=t1s, s0=4.0, s1=12.0)
            bv = dc4[:, 0:2 * OSL].bitcast(BF16)
            e.tensor_tensor(lo, bv[:, 0::2], pscale[:, 0:OSL], OP.mult)
            e.tensor_tensor(hi, bv[:, 1::2], pscale[:, 0:OSL], OP.mult).then_inc(c_dec, 1)
        return f

    def e8_unit_p(k):
        osl, ipc = UNITS[k]
        pscale = pC if PSC[k] == "C" else pD

        def f(e, k=k, osl=osl, ipc=ipc):
            e.wait_ge(s_bs, 16 * (k + 1))
            w = _uwait(k)
            if w > 0:
                e.wait_ge(c_dec, w)
            bs_ = bstu16[:, k * OSL:(k + 1) * OSL] if k < 6 else \
                bstu[:, (k % 2) * OSL:(k % 2 + 1) * OSL]
            e.matmul(pscale[:, 0:H], e8g[:], bs_[:, 0:H],
                     start=True, stop=True)
            e.matmul(pscale[:, H:OSL], e8g[:], bs_[:, H:OSL],
                     start=True, stop=True).then_inc(c_e8, 1)
        return f


    ALLSM = 11 * 16

    step("S",
         lambda e: e.wait_ge(s_sm, ALLSM),
         lambda e: e.activation(scalt[:, 3:4], scalt[:, 1:2], AF.Exp))
    step("V", lambda e: e.wait_ge(s_sm, ALLSM), lambda e: e.memset(smr[:], 0.0))
    step("P", lambda e: e.wait_ge(s_sm, ALLSM))

    step("P",
         lambda e: e.matmul(pD[0:64, 512:516], onesrf[:, 0:64], scalt[:, 0:4],
                            start=True, stop=True),
         lambda e: e.matmul(pD[0:64, 0:1], vctr[:], smt[:, 6:7],
                            start=True, stop=True),
         lambda e: e.matmul(pD[0:64, 1:2], vcti[:], smt[:, 6:7],
                            start=True, stop=True))
    alph = acol(); Dcol = acol(); Bre = acol(); Bim = acol(); stepc = acol()
    step("V",
         lambda e: e.tensor_scalar(alph, pD[0:64, 515:516], 0.5, None, OP.mult),
         lambda e: e.tensor_copy(stepc, pD[0:64, 515:516]),
         lambda e: e.tensor_copy(Dcol, pD[0:64, 512:513]),
         lambda e: e.tensor_copy(sc16[:, 0:1], pD[0:16, 514:515]),
         lambda e: e.tensor_copy(Bre, pD[0:64, 0:1]),
         lambda e: e.tensor_copy(Bim, pD[0:64, 1:2]))
    step("V", lambda e: e.tensor_scalar(e8g[:], e8b[:], sc16[:, 0:1], None, OP.mult))

    lam_re = smt[:, 0:1]; lam_im = smt[:, 1:2]
    p_re = smt[:, 2:3]; p_im = smt[:, 3:4]
    q_re = smt[:, 4:5]; q_im = smt[:, 5:6]
    ct_re = smt[:, 7:8]; ct_imr = smt[:, 8:9]
    qc_im = acol()
    dr = acol(); di = acol(); mag = acol(); rec = acol()
    d_re = acol(); d_im = acol()
    step("V",
         lambda e: e.tensor_scalar(qc_im, q_im, -1.0, None, OP.mult),
         lambda e: e.tensor_scalar(dr, lam_re, alph, -1.0, OP.mult, OP.mult),
         lambda e: e.tensor_scalar(dr, dr, 1.0, None, OP.add),
         lambda e: e.tensor_scalar(di, lam_im, alph, -1.0, OP.mult, OP.mult),
         lambda e: e.tensor_mul(mag, dr, dr),
         lambda e: e.tensor_mul(rec, di, di),
         lambda e: e.tensor_add(mag, mag, rec),
         lambda e: e.reciprocal(rec, mag),
         lambda e: e.tensor_mul(d_re, dr, rec),
         lambda e: e.tensor_mul(d_im, di, rec),
         lambda e: e.tensor_scalar(d_im, d_im, -1.0, None, OP.mult))

    def cmul_cols(o_re, o_im, a_re, a_im, b_re, b_im, t1, t2):
        return [
            lambda e: e.tensor_mul(t1, a_re, b_re),
            lambda e: e.tensor_mul(t2, a_im, b_im),
            lambda e: e.tensor_sub(o_re, t1, t2),
            lambda e: e.tensor_mul(t1, a_re, b_im),
            lambda e: e.tensor_mul(t2, a_im, b_re),
            lambda e: e.tensor_add(o_im, t1, t2),
        ]

    tc1 = acol(); tc2 = acol()
    u1_re = acol(); u1_im = acol()
    v1_re = acol(); v1_im = acol()
    e_re = acol(); e_im = acol()
    a0_re = acol(); a0_im = acol()
    se_re = acol(); se_im = acol()
    sB_re = acol(); sB_im = acol()
    v1e_re = acol(); v1e_im = acol()
    dB_re = acol(); dB_im = acol()
    step("V", *cmul_cols(u1_re, u1_im, d_re, d_im, p_re, p_im, tc1, tc2))
    step("V", *cmul_cols(v1_re, v1_im, q_re, qc_im, d_re, d_im, tc1, tc2))
    step("V",
         lambda e: e.tensor_scalar(e_re, lam_re, alph, None, OP.mult),
         lambda e: e.tensor_scalar(e_re, e_re, 1.0, None, OP.add),
         lambda e: e.tensor_scalar(e_im, lam_im, alph, None, OP.mult))
    step("V", *cmul_cols(a0_re, a0_im, d_re, d_im, e_re, e_im, tc1, tc2))
    step("V", *cmul_cols(v1e_re, v1e_im, v1_re, v1_im, e_re, e_im, tc1, tc2))
    step("V", *cmul_cols(dB_re, dB_im, d_re, d_im, Bre, Bim, tc1, tc2))
    step("V", *cmul_cols(se_re, se_im, v1_re, v1_im, p_re, p_im, tc1, tc2))
    step("V", *cmul_cols(sB_re, sB_im, v1_re, v1_im, Bre, Bim, tc1, tc2))

    sig = arow_(2)
    v1B = arow_(2)
    step("P",
         lambda e: e.matmul(pD[0:1, 512:513], onesf[0:64, 0:1], se_re, start=True, stop=True),
         lambda e: e.matmul(pD[0:1, 513:514], onesf[0:64, 0:1], se_im, start=True, stop=True),
         lambda e: e.matmul(pD[0:1, 514:515], onesf[0:64, 0:1], sB_re, start=True, stop=True),
         lambda e: e.matmul(pD[0:1, 515:516], onesf[0:64, 0:1], sB_im, start=True, stop=True))
    step("V",
         lambda e: e.tensor_copy(sig, pD[0:1, 512:514]),
         lambda e: e.tensor_copy(v1B, pD[0:1, 514:516]))

    arow2 = arow_(2)
    den = arow_(2)
    beta = arow_(2)
    gam = arow_(2)
    t1r = arow_(2)
    t0r = arow_(1)

    def cmul_row(o, a, b, t):
        return [
            lambda e: e.tensor_mul(t[:, 0:1], a[:, 0:1], b[:, 0:1]),
            lambda e: e.tensor_mul(o[:, 0:1], a[:, 1:2], b[:, 1:2]),
            lambda e: e.tensor_sub(o[:, 0:1], t[:, 0:1], o[:, 0:1]),
            lambda e: e.tensor_mul(t[:, 1:2], a[:, 0:1], b[:, 1:2]),
            lambda e: e.tensor_mul(o[:, 1:2], a[:, 1:2], b[:, 0:1]),
            lambda e: e.tensor_add(o[:, 1:2], t[:, 1:2], o[:, 1:2]),
        ]

    def cinv_scale_row(o, num_s, den_, t):
        return [
            lambda e: e.tensor_mul(t[:, 0:1], den_[:, 0:1], den_[:, 0:1]),
            lambda e: e.tensor_mul(t[:, 1:2], den_[:, 1:2], den_[:, 1:2]),
            lambda e: e.tensor_add(t[:, 0:1], t[:, 0:1], t[:, 1:2]),
            lambda e: e.reciprocal(t[:, 0:1], t[:, 0:1]),
            lambda e: e.tensor_mul(o[:, 0:1], den_[:, 0:1], t[:, 0:1]),
            lambda e: e.tensor_mul(o[:, 1:2], den_[:, 1:2], t[:, 0:1]),
            lambda e: e.tensor_scalar(o[:, 1:2], o[:, 1:2], -1.0, None, OP.mult),
            lambda e: e.tensor_scalar(o[:, 0:1], o[:, 0:1], num_s, None, OP.mult),
            lambda e: e.tensor_scalar(o[:, 1:2], o[:, 1:2], num_s, None, OP.mult),
        ]

    alphr = scalt[:, 1:2]
    alrow = arow_(1)
    step("V",
         lambda e: e.tensor_scalar(alrow, scalt[:, 3:4], 0.5, None, OP.mult))
    step("V",
         lambda e: e.tensor_scalar(den[:, 0:2], sig[:, 0:2], alrow, None, OP.mult),
         lambda e: e.tensor_scalar(den[:, 0:1], den[:, 0:1], 1.0, None, OP.add),
         *cinv_scale_row(beta, alrow, den, t1r))
    step("V",
         *cmul_row(arow2, beta, sig, t1r),
         lambda e: e.tensor_scalar(arow2[:, 0:2], arow2[:, 0:2], -1.0, None, OP.mult),
         lambda e: e.tensor_scalar(arow2[:, 0:1], arow2[:, 0:1], 1.0, None, OP.add),
         lambda e: e.tensor_scalar(gam[:, 0:2], arow2[:, 0:2], alrow, None, OP.mult))
    bv1B = arow_(2)
    step("V", *cmul_row(bv1B, beta, v1B, t1r))

    assert _smr_n[0] == 18
    step("P", lambda e: e.matmul(pD[0:64, 0:6], onesrf[:, 0:64],
                                 smr[:, 8:14], start=True, stop=True))
    gb_c = g64[:, 2:3]; gb_ci = g64[:, 3:4]
    bb_c = g64[:, 4:5]; bb_ci = g64[:, 5:6]
    bv_c = g64[:, 6:7]; bv_ci = g64[:, 7:8]
    step("P", lambda e: e.matmul(pD[0:64, 6:10], onesrf[:, 0:64],
                                 smr[:, 14:18], start=True, stop=True))
    step("V",
         lambda e: e.tensor_copy(g64[:, 4:6], pD[0:64, 0:2]),
         lambda e: e.tensor_copy(g64[:, 2:4], pD[0:64, 2:4]),
         lambda e: e.tensor_copy(g64[:, 6:8], pD[0:64, 8:10]))

    w1_re = acol(); w1_im = acol()
    Bb_re = acol(); Bb_im = acol()
    step("V",
         lambda e: e.tensor_scalar(tc1, q_re, gb_c, None, OP.mult),
         lambda e: e.tensor_scalar(tc2, qc_im, gb_ci, None, OP.mult),
         lambda e: e.tensor_sub(w1_re, tc1, tc2),
         lambda e: e.tensor_scalar(tc1, q_re, gb_ci, None, OP.mult),
         lambda e: e.tensor_scalar(tc2, qc_im, gb_c, None, OP.mult),
         lambda e: e.tensor_add(w1_im, tc1, tc2),
         lambda e: e.tensor_scalar(tc1, v1e_re, bb_c, None, OP.mult),
         lambda e: e.tensor_scalar(tc2, v1e_im, bb_ci, None, OP.mult),
         lambda e: e.tensor_sub(tc1, tc1, tc2),
         lambda e: e.tensor_add(w1_re, w1_re, tc1),
         lambda e: e.tensor_scalar(tc1, v1e_re, bb_ci, None, OP.mult),
         lambda e: e.tensor_scalar(tc2, v1e_im, bb_c, None, OP.mult),
         lambda e: e.tensor_add(tc1, tc1, tc2),
         lambda e: e.tensor_add(w1_im, w1_im, tc1))
    step("V",
         lambda e: e.tensor_scalar(tc1, u1_re, bv_c, None, OP.mult),
         lambda e: e.tensor_scalar(tc2, u1_im, bv_ci, None, OP.mult),
         lambda e: e.tensor_sub(tc1, tc1, tc2),
         lambda e: e.tensor_sub(Bb_re, dB_re, tc1),
         lambda e: e.tensor_scalar(tc1, u1_re, bv_ci, None, OP.mult),
         lambda e: e.tensor_scalar(tc2, u1_im, bv_c, None, OP.mult),
         lambda e: e.tensor_add(tc1, tc1, tc2),
         lambda e: e.tensor_sub(Bb_im, dB_im, tc1),
         lambda e: e.tensor_scalar(Bb_re, Bb_re, stepc, None, OP.mult),
         lambda e: e.tensor_scalar(Bb_im, Bb_im, stepc, None, OP.mult))

    u1r = arow_(64); u1i = arow_(64); w1r = arow_(64); w1i = arow_(64)
    nu1r = arow_(64); nu1i = arow_(64)
    step("P",
         lambda e: e.transpose(pD[0:1, 0:64], u1_re, i64f[:]),
         lambda e: e.transpose(pD[0:1, 64:128], u1_im, i64f[:]),
         lambda e: e.transpose(pD[0:1, 128:192], w1_re, i64f[:]),
         lambda e: e.transpose(pD[0:1, 192:256], w1_im, i64f[:]))
    step("V",
         lambda e: e.tensor_copy(u1r, pD[0:1, 0:64]),
         lambda e: e.tensor_copy(u1i, pD[0:1, 64:128]),
         lambda e: e.tensor_copy(w1r, pD[0:1, 128:192]),
         lambda e: e.tensor_copy(w1i, pD[0:1, 192:256]),
         lambda e: e.tensor_scalar(nu1r, u1r, -1.0, None, OP.mult),
         lambda e: e.tensor_scalar(nu1i, u1i, -1.0, None, OP.mult))
    step("P",
         lambda e: e.matmul(pD[0:64, 512:576], nu1r, w1r, start=True, stop=False),
         lambda e: e.matmul(pD[0:64, 512:576], u1i, w1i, start=False, stop=True),
         lambda e: e.matmul(pD[0:64, 576:640], nu1r, w1i, start=True, stop=False),
         lambda e: e.matmul(pD[0:64, 576:640], nu1i, w1r, start=False, stop=True))
    dgt = pw[:, _pw_off[("scr", 0)]:_pw_off[("scr", 0)] + 64]
    step("V",
         lambda e: e.tensor_scalar(dgt, i64f[:], a0_re, None, OP.mult),
         lambda e: e.tensor_add(pwt(0, 0), dgt, pD[0:64, 512:576]),
         lambda e: e.tensor_scalar(dgt, i64f[:], a0_im, None, OP.mult),
         lambda e: e.tensor_add(pwt(0, 1), dgt, pD[0:64, 576:640]),
         lambda e: e.tensor_scalar(pwt(0, 5), pwt(0, 1), -1.0, None, OP.mult))

    for j in range(12):
        step("P",
             lambda e, j=j: e.transpose(pD[0:64, 0:64], pwt(j, 0), i64f[:]),
             lambda e, j=j: e.transpose(pD[0:64, 64:128], pwt(j, 1), i64f[:]))
        step("V",
             lambda e, j=j: e.tensor_copy(pwt(j, 2), pD[0:64, 0:64]),
             lambda e, j=j: e.tensor_copy(pwt(j, 3), pD[0:64, 64:128]),
             lambda e, j=j: e.tensor_scalar(pwt(j, 4), pD[0:64, 64:128], -1.0, None, OP.mult))
        step("P",
             lambda e, j=j: e.matmul(pD[0:64, 512:576], pwt(j, 2), pwt(j, 0), start=True, stop=False),
             lambda e, j=j: e.matmul(pD[0:64, 512:576], pwt(j, 4), pwt(j, 1), start=False, stop=True),
             lambda e, j=j: e.matmul(pD[0:64, 576:640], pwt(j, 2), pwt(j, 1), start=True, stop=False),
             lambda e, j=j: e.matmul(pD[0:64, 576:640], pwt(j, 3), pwt(j, 0), start=False, stop=True))
        if PHASE >= 2:
            em("P", e8_unit_p(j))
        fns = [lambda e, j=j: e.tensor_copy(pwt(j + 1, 0), pD[0:64, 512:576]),
               lambda e, j=j: e.tensor_copy(pwt(j + 1, 1), pD[0:64, 576:640])]
        if j + 1 <= 6 or j + 1 == 12:
            fns.append(lambda e, j=j: e.tensor_scalar(pwt(j + 1, 5), pD[0:64, 576:640], -1.0, None, OP.mult))
        step("V", *fns)
        if PHASE >= 2:
            em("V", decode_unit_v(j))

    w_re = acol(); w_im = acol(); nw_im = acol(); ctn_im = acol()
    step("V",
         lambda e: e.tensor_copy(w_re, ct_re),
         lambda e: e.tensor_scalar(ctn_im, ct_imr, -1.0, None, OP.mult),
         lambda e: e.tensor_copy(w_im, ctn_im))
    for it in range(NEU):
        step("P",
             lambda e: e.matmul(pD[0:64, 0:1], pwt(12, 0), w_re, start=True, stop=False),
             lambda e: e.matmul(pD[0:64, 0:1], pwt(12, 5), w_im, start=False, stop=True),
             lambda e: e.matmul(pD[0:64, 1:2], pwt(12, 0), w_im, start=True, stop=False),
             lambda e: e.matmul(pD[0:64, 1:2], pwt(12, 1), w_re, start=False, stop=True))
        step("V",
             lambda e: e.tensor_add(w_re, pD[0:64, 0:1], ct_re),
             lambda e: e.tensor_add(w_im, pD[0:64, 1:2], ctn_im))
    step("V", lambda e: e.tensor_scalar(nw_im, w_im, -1.0, None, OP.mult))

    step("V",
         lambda e: e.tensor_copy(bkre[:, 0:1], Bb_re),
         lambda e: e.tensor_copy(bkim[:, 0:1], Bb_im))
    for j in range(7):
        w_ = 1 << j
        step("P",
             lambda e, j=j, w_=w_: e.matmul(pD[0:64, 0:w_], pwt(j, 2), bkre[:, 0:w_], start=True, stop=False),
             lambda e, j=j, w_=w_: e.matmul(pD[0:64, 0:w_], pwt(j, 4), bkim[:, 0:w_], start=False, stop=True),
             lambda e, j=j, w_=w_: e.matmul(pD[0:64, 64:64 + w_], pwt(j, 2), bkim[:, 0:w_], start=True, stop=False),
             lambda e, j=j, w_=w_: e.matmul(pD[0:64, 64:64 + w_], pwt(j, 3), bkre[:, 0:w_], start=False, stop=True))
        if PHASE >= 2 and j >= 3:
            em("P", e8_unit_p(9 + j))
        step("V",
             lambda e, w_=w_: e.tensor_copy(bkre[:, w_:2 * w_], pD[0:64, 0:w_]),
             lambda e, w_=w_: e.tensor_copy(bkim[:, w_:2 * w_], pD[0:64, 64:64 + w_]))
        if PHASE >= 2 and j >= 3:
            em("V", decode_unit_v(9 + j))

    step("P",
         lambda e: e.matmul(pD[0:64, 0:1], pwt(0, 0), w_re, start=True, stop=False),
         lambda e: e.matmul(pD[0:64, 0:1], pwt(0, 5), w_im, start=False, stop=True),
         lambda e: e.matmul(pD[0:64, 1:2], pwt(0, 0), w_im, start=True, stop=False),
         lambda e: e.matmul(pD[0:64, 1:2], pwt(0, 1), w_re, start=False, stop=True))
    step("V",
         lambda e: e.tensor_copy(zre[:, 0:1], pD[0:64, 0:1]),
         lambda e: e.tensor_copy(zim[:, 0:1], pD[0:64, 1:2]))
    for j in range(7):
        w_ = 1 << j
        step("P",
             lambda e, j=j, w_=w_: e.matmul(pD[0:64, 0:w_], pwt(j, 0), zre[:, 0:w_], start=True, stop=False),
             lambda e, j=j, w_=w_: e.matmul(pD[0:64, 0:w_], pwt(j, 5), zim[:, 0:w_], start=False, stop=True),
             lambda e, j=j, w_=w_: e.matmul(pD[0:64, 64:64 + w_], pwt(j, 0), zim[:, 0:w_], start=True, stop=False),
             lambda e, j=j, w_=w_: e.matmul(pD[0:64, 64:64 + w_], pwt(j, 1), zre[:, 0:w_], start=False, stop=True))
        step("V",
             lambda e, w_=w_: e.tensor_copy(zre[:, w_:2 * w_], pD[0:64, 0:w_]),
             lambda e, w_=w_: e.tensor_copy(zim[:, w_:2 * w_], pD[0:64, 64:64 + w_]))

    kfr = arow_(128)
    step("P",
         lambda e: e.matmul(pD[0:1, 0:128], w_re, bkre[:], start=True, stop=False),
         lambda e: e.matmul(pD[0:1, 0:128], nw_im, bkim[:], start=False, stop=True))
    step("V",
         lambda e: e.tensor_copy(kfr, pD[0:1, 0:128]),
         lambda e: e.tensor_scalar(kfr[:, 0:1], kfr[:, 0:1], scalt[:, 0:1], None, OP.add),
         lambda e: e.tensor_copy(krow[:], kfr),
         lambda e: e.memset(z128[:], 0.0))
    KRC["v"] = seqcnt["V"]

    def s_toep2(e):
        e.wait_ge(s_tp, 32)
        for p in range(64, P):
            e.dma_start(out=T0b[p:p + 1, :],
                        in_=kpad_d[0:1, 128 - p:256 - p]).then_inc(s_tp, 16)
    step("S", s_toep2, raw=True)

    step("P", lambda e: e.transpose(pD[0:128, 0:64], bkre[:], i64f[:]))
    step("V", lambda e: e.tensor_copy(t128[:], pD[0:128, 0:64]))
    step("P", lambda e: e.matmul(pD[0:128, 512:576], j128b[:], t128[:], start=True, stop=True))
    step("V", lambda e: e.tensor_copy(BmT[:, 0:64], pD[0:128, 512:576]))
    step("P", lambda e: e.transpose(pD[0:128, 0:64], bkim[:], i64f[:]))
    step("V", lambda e: e.tensor_copy(t128[:], pD[0:128, 0:64]))
    step("P", lambda e: e.matmul(pD[0:128, 512:576], j128b[:], t128[:], start=True, stop=True))
    step("V",
         lambda e: e.tensor_copy(BmT[:, 64:128], pD[0:128, 512:576]),
         lambda e: e.tensor_copy(CBl[0:64, :], zre[:]),
         lambda e: e.tensor_scalar(CBl[64:128, :], zim[:], -1.0, None, OP.mult),
         lambda e: e.tensor_copy(ACl[0:64, 0:64], pwt(7, 2)),
         lambda e: e.tensor_copy(ACl[0:64, 64:128], pwt(7, 3)),
         lambda e: e.tensor_scalar(ACl[64:128, 0:64], pwt(7, 3), -1.0, None, OP.mult),
         lambda e: e.tensor_copy(ACl[64:128, 64:128], pwt(7, 2)),
         lambda e: e.memset(sst[0][:], 0.0))

    def s_sq1(e):
        for c in range(NCH):
            u_wait(e, c)
            if c >= 2:
                e.wait_ge(c_st, c - 1)
            e.activation(xsq[:, (c % 2) * NTOK:(c % 2 + 1) * NTOK],
                         ux[:, c * NTOK:(c + 1) * NTOK], AF.Square).then_inc(c_sq, 1)
    em("S", s_sq1)

    def p_st1(e):
        for c in range(NCH):
            u_wait(e, c)
            e.wait_ge(c_sq, c + 1)
            for h_ in range(2):
                e.matmul(pA[0:1, h_ * H:(h_ + 1) * H], onesbf[:, 0:1],
                         ux[:, c * NTOK + h_ * H:c * NTOK + (h_ + 1) * H],
                         start=(c == 0), stop=(c == NCH - 1))
                ins = e.matmul(pB[0:1, h_ * H:(h_ + 1) * H], onesbf[:, 0:1],
                         xsq[:, (c % 2) * NTOK + h_ * H:(c % 2) * NTOK + (h_ + 1) * H],
                         start=(c == 0), stop=(c == NCH - 1))
            ins.then_inc(c_st, 1)
    em("P", p_st1)
    step("S")
    step("P")

    def rows_math(psum_sum, psum_sq, mrow, srow):
        return [
            lambda e: e.tensor_scalar(mrow, psum_sum, 1.0 / 4096.0, None, OP.mult),
            lambda e: e.tensor_scalar(rowu[:, 0:NTOK], psum_sq, 1.0 / 4096.0, None, OP.mult),
            lambda e: e.tensor_mul(srow, mrow, mrow),
            lambda e: e.tensor_sub(rowu[:, 0:NTOK], rowu[:, 0:NTOK], srow),
            lambda e: e.tensor_scalar(rowu[:, 0:NTOK], rowu[:, 0:NTOK], 1e-5, None, OP.add),
            lambda e: e.reciprocal(srow, rowu[:, 0:NTOK]),
        ]
    step("V",
         lambda e: e.wait_ge(c_st, NCH),
         *rows_math(pA[0:1, :], pB[0:1, :],
                    rowt[:, 0:NTOK], rowt[:, NTOK:2 * NTOK]))
    step("S", lambda e: e.activation(rowt[:, NTOK:2 * NTOK],
                                     rowt[:, NTOK:2 * NTOK], AF.Sqrt))
    step("V",
         lambda e: e.tensor_copy(xsq[0:1, 0:NTOK], rowt[:, 0:NTOK]),
         lambda e: e.tensor_copy(ln1b[0:1, 0:NTOK], rowt[:, NTOK:2 * NTOK]))
    step("P",
         lambda e: e.matmul(pA[:, 0:H], onesrbf[:], xsq[0:1, 0:H], start=True, stop=True),
         lambda e: e.matmul(pA[:, H:NTOK], onesrbf[:], xsq[0:1, H:NTOK], start=True, stop=True),
         lambda e: e.matmul(pB[:, 0:H], onesrbf[:], ln1b[0:1, 0:H], start=True, stop=True),
         lambda e: e.matmul(pB[:, H:NTOK], onesrbf[:], ln1b[0:1, H:NTOK], start=True, stop=True))
    step("S",
         lambda e: e.copy(m1b[:], pA[:, 0:NTOK]),
         lambda e: e.copy(s1b[:], pB[:, 0:NTOK]))
    barrier()

    def v_conv(e):
        for c in range(NCH):
            lt = ln1b[:, (c % 2) * NTOK:(c % 2 + 1) * NTOK]
            uc = ux[:, c * NTOK:(c + 1) * NTOK]
            if c >= 2:
                e.wait_ge(c_s, c - 1)
            e.tensor_sub(lt, uc, m1b[:])
            e.tensor_mul(lt, lt, s1b[:]).then_inc(c_ln, 1)
            e.wait_ge(c_y, c + 1)
            e.tensor_add(uc[:, 0:H], uc[:, 0:H], pA[:, 0:H])
            e.tensor_add(uc[:, H:NTOK], uc[:, H:NTOK], pA[:, H:NTOK]).then_inc(c_x1, 1)
            if PHASE >= 2 and c < 20:
                decode_unit_v(16 + c)(e)
    em("V", v_conv)

    def p_conv(e):
        for c in range(NCH):
            lt = ln1b[:, (c % 2) * NTOK:(c % 2 + 1) * NTOK]
            sprev = sst[c % 2]
            e.wait_ge(c_ln, c + 1)
            e.wait_ge(s_tp, 32 + 128 * 16)
            if c >= 1:
                e.wait_ge(c_x1, c)
                e.wait_ge(c_sev, c)
            for h_ in range(2):
                sl_ = slice(h_ * H, (h_ + 1) * H)
                e.matmul(pA[:, sl_], T0b[:], lt[:, sl_], start=True, stop=False)
                ins = e.matmul(pA[:, sl_], CBl[:], sprev[:, sl_], start=False, stop=True)
            ins.then_inc(c_y, 1)
            for h_ in range(2):
                sl_ = slice(h_ * H, (h_ + 1) * H)
                e.matmul(pB[:, sl_], ACl[:], sprev[:, sl_], start=True, stop=False)
                ins = e.matmul(pB[:, sl_], BmT[:], lt[:, sl_], start=False, stop=True)
            ins.then_inc(c_s, 1)
            if PHASE >= 2 and c < 20:
                e8_unit_p(16 + c)(e)
    em("P", p_conv)

    def s_conv(e):
        for c in range(NCH):
            e.wait_ge(c_s, c + 1)
            snew = sst[(c + 1) % 2]
            e.copy(snew[:, 0:H], pB[:, 0:H])
            e.copy(snew[:, H:NTOK], pB[:, H:NTOK]).then_inc(c_sev, 1)
    em("S", s_conv)

    barrier()

    def s_sq2(e):
        for c in range(NCH):
            e.wait_ge(c_x1, c + 1)
            if c >= 2:
                e.wait_ge(c_st, NCH + c - 1)
            e.activation(xsq[:, (c % 2) * NTOK:(c % 2 + 1) * NTOK],
                         ux[:, c * NTOK:(c + 1) * NTOK], AF.Square).then_inc(c_sq, 1)
    em("S", s_sq2)

    def p_st2(e):
        e.wait_ge(c_dec, 32)
        for c in range(NCH):
            e.wait_ge(c_sq, NCH + c + 1)
            for h_ in range(2):
                e.matmul(pC[0:1, h_ * H:(h_ + 1) * H], onesbf[:, 0:1],
                         ux[:, c * NTOK + h_ * H:c * NTOK + (h_ + 1) * H],
                         start=(c == 0), stop=(c == NCH - 1))
                ins = e.matmul(pD[0:1, h_ * H:(h_ + 1) * H], onesbf[:, 0:1],
                         xsq[:, (c % 2) * NTOK + h_ * H:(c % 2) * NTOK + (h_ + 1) * H],
                         start=(c == 0), stop=(c == NCH - 1))
            ins.then_inc(c_st, 1)
    em("P", p_st2)
    step("S")
    step("P")

    step("V",
         lambda e: e.wait_ge(c_st, 2 * NCH),
         *rows_math(pC[0:1, :], pD[0:1, :],
                    rowt[:, 0:NTOK], rowt[:, NTOK:2 * NTOK]))
    step("S", lambda e: e.activation(rowt[:, NTOK:2 * NTOK],
                                     rowt[:, NTOK:2 * NTOK], AF.Sqrt))
    step("S", lambda e: e.activation(rowu[:, 0:NTOK], rowu[:, 0:NTOK], AF.Sqrt))
    step("V", lambda e: e.tensor_mul(rowu[:, 0:NTOK], rowu[:, 0:NTOK], rowt[:, 0:NTOK]))
    step("S", lambda e: e.copy(xsq[0:1, 0:NTOK], rowu[:, 0:NTOK]))
    step("P",
         lambda e: e.matmul(pA[:, 0:H], onesrf[:], rowt[:, NTOK:NTOK + H], start=True, stop=True),
         lambda e: e.matmul(pA[:, H:NTOK], onesrf[:], rowt[:, NTOK + H:2 * NTOK], start=True, stop=True),
         lambda e: e.matmul(pB[:, 0:H], onesrf[:], rowt[:, 0:H], start=True, stop=True),
         lambda e: e.matmul(pB[:, H:NTOK], onesrf[:], rowt[:, H:NTOK], start=True, stop=True))
    m2b = ln1b[:, 0:NTOK]
    step("S",
         lambda e: e.copy(s2b[:], pA[:, 0:NTOK]),
         lambda e: e.copy(m2b, pB[:, 0:NTOK]))
    SC2["v"] = seqcnt["S"]

    def v_sub(e):
        e.wait_ge(bS, SC2["v"])
        for c in range(NCH):
            e.wait_ge(c_st, NCH + c + 1)
            e.tensor_sub(ux[:, c * NTOK:(c + 1) * NTOK],
                         ux[:, c * NTOK:(c + 1) * NTOK], m2b).then_inc(c_sub, 1)
    if PHASE >= 4:
        em("V", v_sub)

    if PHASE < 4:
        def s_store_x1(e):
            e.wait_ge(c_x1, NCH)
            for oc in range(NCH):
                e.activation(outb[:, (oc % 2) * NTOK:(oc % 2 + 1) * NTOK],
                             ux[:, oc * NTOK:(oc + 1) * NTOK], AF.Identity).then_inc(c_ev, 1)
                e.wait_ge(c_ev, oc + 1)
                e.dma_start(out=out_t[oc * P:(oc + 1) * P, :],
                            in_=outb[:, (oc % 2) * NTOK:(oc % 2 + 1) * NTOK]).then_inc(s_out, 16)
                if oc >= 1:
                    e.wait_ge(s_out, 16 * oc)
        em("S", s_store_x1)
        if PHASE in (2, 3):
            def v_dec_tail(e):
                for k in range(NCH, NUNIT):
                    decode_unit_v(k)(e)
            def p_dec_tail(e):
                for k in range(NCH, NUNIT):
                    e8_unit_p(k)(e)
            em("V", v_dec_tail)
            em("P", p_dec_tail)

    import os as _os
    if _os.environ.get("KNOPTEST"):
        def p_mid(e):
            for _ in range(110):
                e.nop()
    elif False:
        pass
    if _os.environ.get("KNOPTEST"):
        pass
    else:
     def p_mid(e):
        for c in range(NCH):
            if PHASE >= 4:
                e.wait_ge(c_sub, c + 1)
            for h_ in range(2):
                e.matmul(pA[0:16, h_ * H:(h_ + 1) * H],
                         atb[:, c * 16:(c + 1) * 16],
                         ux[:, c * NTOK + h_ * H:c * NTOK + (h_ + 1) * H],
                         start=(c == 0), stop=(c == NCH - 1))
            ins = e.matmul(pB[0:16, 0:1], atb[:, c * 16:(c + 1) * 16], onesbf[:, 0:1],
                           start=(c == 0), stop=(c == NCH - 1))
        return ins
    if PHASE == 4:
        def v_dec_tail2(e):
            for k in range(NCH, NUNIT):
                decode_unit_v(k)(e)
        def p_dec_tail2(e):
            for k in range(NCH, NUNIT):
                e8_unit_p(k)(e)
        em("V", v_dec_tail2)
        em("P", p_dec_tail2)
        barrier()
    if PHASE >= 3:
        step("P", p_mid)
        step("V",
             lambda e: e.tensor_copy(midb[:], pA[0:16, 0:NTOK]),
             lambda e: e.tensor_copy(arowb[:], pB[0:16, 0:1]))
        SC3["v"] = seqcnt["V"]
        barrier()

    def p_ffn(e):
        for oc in range(NCH):
            wt = wtb[:, (oc % 2) * D:(oc % 2 + 1) * D]
            if oc == 0:
                e.wait_ge(bV, SC3["v"])
            e.wait_ge(s_wt, 16 * (oc + 1))
            if oc >= 2:
                e.wait_ge(c_ev, oc - 1)
            for c in range(NCH):
                if oc == 0:
                    e.wait_ge(c_sub, c + 1)
                for h_ in range(2):
                    e.matmul((pA if oc % 2 == 0 else pB)[:, h_ * H:(h_ + 1) * H],
                             wt[:, c * P:(c + 1) * P],
                             ux[:, c * NTOK + h_ * H:c * NTOK + (h_ + 1) * H],
                             start=(c == 0), stop=False)
            ins = None
            for h_ in range(2):
                e.matmul((pA if oc % 2 == 0 else pB)[:, h_ * H:(h_ + 1) * H],
                         btb[:, oc * P:(oc + 1) * P],
                         midb[:, h_ * H:(h_ + 1) * H],
                         start=False, stop=False)
                ins = e.matmul((pA if oc % 2 == 0 else pB)[:, h_ * H:(h_ + 1) * H],
                         onesrbf[:, 0:P],
                         xsq[0:1, h_ * H:(h_ + 1) * H],
                         start=False, stop=True)
            ins.then_inc(c_ffn, 1)
            if oc < 14 and PHASE >= 5:
                e8_unit_p(36 + 2 * oc)(e)
                e8_unit_p(37 + 2 * oc)(e)
    if PHASE >= 4:
        em("P", p_ffn)

    def s_ffn(e):
        for oc in range(NCH):
            e.wait_ge(c_ev, oc + 1)
            e.dma_start(out=out_t[oc * P:(oc + 1) * P, :],
                        in_=outb[:, (oc % 2) * NTOK:(oc % 2 + 1) * NTOK]).then_inc(s_out, 16)
    if PHASE >= 4:
        em("S", s_ffn)

    def v_ffn(e):
        for oc in range(NCH):
            pp = pA if oc % 2 == 0 else pB
            e.wait_ge(c_ffn, oc + 1)
            if oc >= 2:
                e.wait_ge(s_out, 16 * (oc - 1))
            ob = outb[:, (oc % 2) * NTOK:(oc % 2 + 1) * NTOK]
            e.tensor_mul(tevac[:], pp[:, 0:NTOK], s2b[:])
            e.tensor_add(ob, tevac[:], ux[:, oc * NTOK:(oc + 1) * NTOK]).then_inc(c_ev, 1)
            if oc < 14 and PHASE >= 5:
                decode_unit_v(36 + 2 * oc)(e)
                decode_unit_v(37 + 2 * oc)(e)
    if PHASE >= 4:
        em("V", v_ffn)

    if DEBUG:
        def g_dbg(g):
            g.wait_ge(s_out, 32 * 16)
            dumps = [(dbg["K"], krow[:]), (dbg["T0"], T0b[:]), (dbg["BmT"], BmT[:]),
                     (dbg["CBl"], CBl[:]), (dbg["ACl"], ACl[:]),
                     (dbg["Ab"][:, 0:64], pw[:, 0:64]),
                     (dbg["Ab"][:, 64:128], pw[:, 64:128]),
                     (dbg["x1"], ux[:, 0:NTOK]),
                     (dbg["ln"], ln1b[:, NTOK:2 * NTOK]),
                     ] + (
                         [(dbg["W0"], wtb[:, D:D + P])] if PHASE >= 4 else [])
            for o_, i_ in dumps:
                g.dma_start(out=o_, in_=i_).then_inc(s_dbg, 16)
            g.wait_ge(s_dbg, 16 * len(dumps))
        em("G", g_dbg)

    with es, nc.Block() as block:
        @block.vector
        def _(vector):
            for f in ops["V"]:
                f(vector)

        @block.tensor
        def _(tensor):
            for f in ops["P"]:
                f(tensor)

        @block.scalar
        def _(scalar):
            for f in ops["S"]:
                f(scalar)

        @block.gpsimd
        def _(gpsimd):
            for f in ops["G"]:
                f(gpsimd)

        @block.sync
        def _(sync):
            for f in ops["Y"]:
                f(sync)

    return nc


_NC_CACHE = {}


def kernel(**inputs):
    u = np.asarray(inputs["L0_attn_u"], np.float32)
    lam = np.asarray(inputs["L0_attn_Lambda_ri"], np.float32)
    p_ri = np.asarray(inputs["L0_attn_p_ri"], np.float32)
    q_ri = np.asarray(inputs["L0_attn_q_ri"], np.float32)
    vc = np.asarray(inputs["L0_attn_Vc_ri"], np.float32)
    ct = np.asarray(inputs["L0_attn_Ct_ri"], np.float32)
    bp = np.asarray(inputs["L0_attn_Bp"], np.float32)
    Dv = np.asarray(inputs["L0_attn_D"], np.float32)
    ls = np.asarray(inputs["L0_attn_log_step"], np.float32)
    pk = np.asarray(inputs["L0_ffn_packed_weight"], np.int32)
    bs = np.asarray(inputs["L0_ffn_block_scales"], np.float32)
    gs = np.asarray(inputs["L0_ffn_global_scale"], np.float32)
    lA = np.asarray(inputs["L0_ffn_lora_A"], np.float32)
    lB = np.asarray(inputs["L0_ffn_lora_B"], np.float32)

    R = u.reshape(-1, D)
    packedT = np.ascontiguousarray(pk.T)
    bsT3 = np.ascontiguousarray(
        bs.T.reshape(N_IPC, 16, D).transpose(1, 0, 2).reshape(16, 16 * D))
    lA_T = np.ascontiguousarray(lA.T.reshape(NCH, P, 16).transpose(1, 0, 2).reshape(P, NCH * 16))
    lB_T = np.ascontiguousarray(lB.T)
    sm = np.ascontiguousarray(np.stack(
        [lam[:, 0], lam[:, 1], p_ri[:, 0], p_ri[:, 1], q_ri[:, 0], q_ri[:, 1],
         bp[:, 0], ct[0, :, 0], ct[0, :, 1]], axis=1))
    vctr = np.ascontiguousarray(vc[:, :, 0].T)
    vcti = np.ascontiguousarray(vc[:, :, 1].T)
    scal = np.ascontiguousarray(
        np.concatenate([Dv.reshape(1), ls.reshape(1), gs.reshape(1)]).reshape(1, 3))

    consts = {
        "cI64": np.eye(64, dtype=np.float32),
        "cJ128": np.eye(P, dtype=np.float32)[::-1].astype(MLBF),
        "cE8": np.repeat(0.5 * np.eye(16, dtype=np.float32), 8, axis=1).astype(MLBF),
        "cOnes": np.ones((P, 4), np.float32),
        "cOnesB": np.ones((P, 4), np.float32).astype(MLBF),
        "cOnesR": np.ones((1, P), np.float32),
        "cOnesRB": np.ones((1, P), np.float32).astype(MLBF),
    }

    key = ("prog", PHASE, DEBUG, SERIAL_DEC)
    if key not in _NC_CACHE:
        nc_new = build_program()
        mybir.codegen_inst_isa_subclasses(nc_new)
        _NC_CACHE[key] = nc_new
    nc = _NC_CACHE[key]

    in_maps = []
    for r in range(NC):
        m = {
            "u_s": np.ascontiguousarray(R[r * NTOK:(r + 1) * NTOK, :].T),
            "packedT": packedT, "bsT3": bsT3, "lA_T": lA_T, "lB_T": lB_T,
            "sm": sm, "VcT_re": vctr, "VcT_im": vcti, "scal": scal,
        }
        m.update(consts)
        in_maps.append(m)

    res = run_bass_kernel_spmd(nc, in_maps, core_ids=list(range(NC)))
    out = np.empty((NC * NTOK, D), np.float32)
    for r in range(NC):
        out[r * NTOK:(r + 1) * NTOK, :] = res.results[r]["out_s"].T
    if DEBUG:
        kernel._dbg = res.results
    return out.reshape(2, -1, D)


# version 22
# speedup vs baseline: 1.0175x; 1.0175x over previous
import numpy as np
import ml_dtypes
from contextlib import ExitStack

import concourse.bass as bass
import concourse.mybir as mybir
from concourse.bass_utils import run_bass_kernel_spmd
from concourse import dve_ops as _dve_ops
from concourse.dve_op import DveOp as _DveOp
from concourse.dve_spec import (Spec as _Spec, Src0 as _Src0, Src1 as _Src1,
                                C0 as _C0, C1 as _C1, relu as _relu,
                                lower as _dve_lower)


def _fp4_mag_sgn_ref(in0, in1, s0, s1, imm2):
    x = np.asarray(in0).astype(np.float32)
    g = np.asarray(in1).astype(np.float32)
    r1 = np.maximum(x - s0, 0.0)
    r2 = np.maximum(x + x - s1, 0.0)
    return (((x + r1) + r2) * g).astype(np.float32)


def _register_fp4_op():
    for o in _dve_ops.OPS:
        if o.name == "FP4_MAG_SGN":
            return o
    spec = _Spec(body=((_Src0 + _relu(_Src0 - _C0))
                       + _relu((_Src0 + _Src0) - _C1)) * _Src1)
    sha = {"v3": _dve_lower(spec, ver="v3", name="FP4_MAG_SGN").sha("v3")}
    op = _DveOp("FP4_MAG_SGN", spec, False, sha, reference=_fp4_mag_sgn_ref)
    _dve_ops.OPS.append(op)
    _dve_ops._SUB_OPCODE_FOR_NAME["FP4_MAG_SGN"] = (
        _dve_ops._CUSTOM_DVE_ROW_BASE + len(_dve_ops.OPS) - 1)
    _dve_ops.CUSTOM_DVE_SPECS["FP4_MAG_SGN"] = spec
    _dve_ops.CUSTOM_DVE_REFS["FP4_MAG_SGN"] = _fp4_mag_sgn_ref
    return op


FP4_OP = _register_fp4_op()

F32 = mybir.dt.float32
BF16 = mybir.dt.bfloat16
I32 = mybir.dt.int32
I16 = mybir.dt.int16
OP = mybir.AluOpType
AF = mybir.ActivationFunctionType
MLBF = ml_dtypes.bfloat16

P = 128
NTOK = 1024
D = 4096
NCH = 32
NC = 8
NEU = 10
N_IPC = 16
N_OSL = 4
OSL = 1024
NUNIT = N_IPC * N_OSL
H = 512
DEBUG = False
PHASE = 5
SERIAL_DEC = False


def build_program():
    nc = bass.Bass("TRN2", target_bir_lowering=False, detect_race_conditions=False)

    u_in = nc.dram_tensor("u_s", [D, NTOK], F32, kind="ExternalInput").ap()
    pk_in = nc.dram_tensor("packedT", [2048, D], I32, kind="ExternalInput").ap()
    bst_in = nc.dram_tensor("bsT3", [16, 16 * D], F32, kind="ExternalInput").ap()
    at_in = nc.dram_tensor("lA_T", [P, NCH * 16], F32, kind="ExternalInput").ap()
    bt_in = nc.dram_tensor("lB_T", [16, D], F32, kind="ExternalInput").ap()
    sm_in = nc.dram_tensor("sm", [64, 9], F32, kind="ExternalInput").ap()
    vcr_in = nc.dram_tensor("VcT_re", [64, 64], F32, kind="ExternalInput").ap()
    vci_in = nc.dram_tensor("VcT_im", [64, 64], F32, kind="ExternalInput").ap()
    scal_in = nc.dram_tensor("scal", [1, 3], F32, kind="ExternalInput").ap()
    i64_in = nc.dram_tensor("cI64", [64, 64], F32, kind="ExternalInput").ap()
    j128_in = nc.dram_tensor("cJ128", [P, P], BF16, kind="ExternalInput").ap()
    e8_in = nc.dram_tensor("cE8", [16, P], BF16, kind="ExternalInput").ap()
    ones_in = nc.dram_tensor("cOnes", [P, 4], F32, kind="ExternalInput").ap()
    onesb_in = nc.dram_tensor("cOnesB", [P, 4], BF16, kind="ExternalInput").ap()
    onesr_in = nc.dram_tensor("cOnesR", [1, P], F32, kind="ExternalInput").ap()
    onesrb_in = nc.dram_tensor("cOnesRB", [1, P], BF16, kind="ExternalInput").ap()

    out_t = nc.dram_tensor("out_s", [D, NTOK], F32, kind="ExternalOutput").ap()
    dbg = {}
    if DEBUG:
        for nm, shp in [("K", [1, P]), ("T0", [P, P]), ("BmT", [P, P]),
                        ("CBl", [P, P]), ("ACl", [P, P]), ("Ab", [64, P]),
                        ("x1", [P, NTOK]), ("W0", [P, P]), ("ln", [P, NTOK]),
                        ("smx", [64, 96]), ("smr", [1, 576]), ("g64", [64, 8])]:
            dbg[nm] = nc.dram_tensor("dbg_" + nm, shp, F32, kind="ExternalOutput").ap()

    wt_d = nc.dram_tensor("WT_d", [NCH, P, NCH, P], BF16, kind="Internal").ap()
    kpad_d = nc.dram_tensor("kpad_d", [1, 256], BF16, kind="Internal").ap()

    es = ExitStack()
    sb = lambda n, s, d: es.enter_context(nc.sbuf_tensor(n, s, d))
    ps_ = lambda n, s: es.enter_context(nc.psum_tensor(n, s, F32))
    sem = lambda n: es.enter_context(nc.semaphore(n))

    ux = sb("ux", [P, NCH * NTOK], BF16)
    ln1b = sb("ln1b", [P, 2 * NTOK], BF16)
    sst = [sb("sst0", [P, NTOK], BF16), sb("sst1", [P, NTOK], BF16)]
    m1b = sb("m1b", [P, NTOK], BF16)
    s1b = sb("s1b", [P, NTOK], BF16)
    s2b = sb("s2b", [P, NTOK], F32)
    xsq = sb("xsq", [P, 2 * NTOK], BF16)
    tevac = sb("tevac", [P, NTOK], BF16)
    outb = sb("outb", [P, 2 * NTOK], F32)
    T0b = sb("T0b", [P, P], BF16)
    BmT = sb("BmT", [P, P], BF16)
    CBl = sb("CBl", [P, P], BF16)
    ACl = sb("ACl", [P, P], BF16)
    wtb = sb("wtb", [P, 2 * D], BF16)
    colc = sb("colc", [P, 2], F32)
    midb = sb("midb", [16, NTOK], BF16)
    arowb = sb("arowb", [16, 1], BF16)
    atb = sb("atb", [P, NCH * 16], BF16)
    btb = sb("btb", [16, D], BF16)
    bstu = sb("bstu", [16, 2 * OSL], BF16)
    bstu16 = sb("bstu16", [16, 6 * OSL], BF16)
    e8g = sb("e8g", [16, P], BF16)
    slab = sb("slab", [P, 2 * OSL], I32)
    dc1 = sb("dc1", [P, 2 * OSL], I16)
    dc2 = sb("dc2", [P, 2 * OSL], I16)
    dc4 = sb("dc4", [P, 2 * OSL], I16)
    wlo = sb("wlo", [P, 2 * OSL], BF16)
    whi = sb("whi", [P, 2 * OSL], BF16)
    smt = sb("smt", [64, 9], F32)
    vctr = sb("vctr", [64, 64], F32)
    vcti = sb("vcti", [64, 64], F32)
    i64f = sb("i64f", [64, 64], F32)
    j128b = sb("j128b", [P, P], BF16)
    e8b = sb("e8b", [16, P], BF16)
    onesf = sb("onesf", [P, 4], F32)
    onesbf = sb("onesbf", [P, 4], BF16)
    onesrf = sb("onesrf", [1, P], F32)
    onesrbf = sb("onesrbf", [1, P], BF16)
    scalt = sb("scalt", [1, 4], F32)
    rowt = sb("rowt", [1, 2 * NTOK], F32)
    rowu = sb("rowu", [1, NTOK], F32)
    _pw_off = {}
    _pw_n = [0]
    for _j in range(13):
        _slots = (0, 1, 2, 3, 4, 5) if _j <= 6 else ((0, 1, 2, 3, 4) if _j <= 11 else (0, 1, 5))
        for _s in _slots:
            _pw_off[(_j, _s)] = _pw_n[0]
            _pw_n[0] += 64
    _pw_off[("scr", 0)] = _pw_n[0]
    _pw_n[0] += 64
    pw = sb("pw", [64, _pw_n[0]], F32)
    smx = sb("smx", [64, 24 * 4], F32)
    smr = sb("smr", [1, 576], F32)
    bkre = sb("bkre", [64, P], F32)
    bkim = sb("bkim", [64, P], F32)
    zre = sb("zre", [64, P], F32)
    zim = sb("zim", [64, P], F32)
    t128 = sb("t128", [P, 64], BF16)
    krow = sb("krow", [1, P], BF16)
    z128 = sb("z128", [1, P], BF16)
    g64 = sb("g64", [64, 8], F32)
    sc16 = sb("sc16", [16, 2], F32)

    pA = ps_("pA", [P, NTOK])
    pB = ps_("pB", [P, NTOK])
    pC = ps_("pC", [P, NTOK])
    pD = ps_("pD", [P, NTOK])

    s_sm = sem("s_sm")
    s_u = sem("s_u")
    KRC = {}

    def u_wait(e, c):
        e.wait_ge(s_u, 16 * (c + 1))
    s_pk = sem("s_pk")
    s_wo = sem("s_wo")
    s_wos = sem("s_wos")
    s_sl = sem("s_sl")
    s_sl2 = sem("s_sl2")
    s_bs = sem("s_bs")
    s_misc = sem("s_misc")
    s_wt = sem("s_wt")
    s_out = sem("s_out")
    s_tp = sem("s_tp")
    bV = sem("bV")
    bP = sem("bP")
    bS = sem("bS")
    c_ln = sem("c_ln")
    c_x1 = sem("c_x1")
    c_y = sem("c_y")
    c_s = sem("c_s")
    c_sev = sem("c_sev")
    c_sq = sem("c_sq")
    c_dec = sem("c_dec")
    c_e8 = sem("c_e8")
    c_ffn = sem("c_ffn")
    c_ev = sem("c_ev")
    c_cc = sem("c_cc")
    c_st = sem("c_st")
    c_sub = sem("c_sub")
    c_fh = sem("c_fh")
    SC2 = {}
    SC3 = {}
    s_dbg = sem("s_dbg")

    ops = {"V": [], "P": [], "S": [], "G": [], "Y": []}

    def em(key, f):
        ops[key].append(f)

    seqcnt = {"V": 0, "P": 0, "S": 0}
    seqsem = {"V": bV, "P": bP, "S": bS}

    def step(key, *fns, raw=False):
        waits = [(k, seqcnt[k]) for k in "VPS" if seqcnt[k] > 0]
        base = seqcnt[key]
        n = 1 if raw or not fns else len(fns)
        seqcnt[key] += n

        def f(eng, waits=waits, fns=fns, base=base, raw=raw):
            for k, v in waits:
                eng.wait_ge(seqsem[k], v)
            if raw or not fns:
                for g in fns:
                    g(eng)
                eng.sem_inc(seqsem[key], 1)
                return
            for i, g in enumerate(fns):
                if i > 0:
                    eng.wait_ge(seqsem[key], base + i)
                r = g(eng)
                ok = False
                if r is not None and hasattr(r, "then_inc"):
                    tn = type(getattr(r, "ins", r)).__name__
                    if "EventSemaphore" not in tn and "DMACopy" not in tn:
                        r.then_inc(seqsem[key], 1)
                        ok = True
                if not ok:
                    eng.sem_inc(seqsem[key], 1)
        em(key, f)

    def barrier():
        for k in "VPS":
            step(k)

    _smx_n = [0]

    def acol():
        i = _smx_n[0]
        _smx_n[0] += 1
        assert i < 96
        return smx[:, i:i + 1]

    _smr_n = [0]

    def arow_(w=1):
        i = _smr_n[0]
        _smr_n[0] += w
        assert _smr_n[0] <= 576
        return smr[:, i:i + w]

    def pwt(j, which):
        off = _pw_off[(j, which)]
        return pw[:, off:off + 64]

    UNITS = [(k // N_IPC, k % N_IPC) for k in range(NUNIT)]
    PSC = ["C"] * 16 + ["D" if k % 2 == 0 else "C" for k in range(16, 64)]

    def _uwait(k):
        for j in range(k - 1, -1, -1):
            if PSC[j] == PSC[k]:
                return j + 1
        return 0

    def y_main(y):
        y.dma_start(out=smt[:], in_=sm_in).then_inc(s_sm, 16)
        y.dma_start(out=vctr[:], in_=vcr_in).then_inc(s_sm, 16)
        y.dma_start(out=vcti[:], in_=vci_in).then_inc(s_sm, 16)
        y.dma_start(out=scalt[:, 0:3], in_=scal_in).then_inc(s_sm, 16)
        y.dma_start(out=i64f[:], in_=i64_in).then_inc(s_sm, 16)
        y.dma_start(out=j128b[:], in_=j128_in).then_inc(s_sm, 16)
        y.dma_start(out=e8b[:], in_=e8_in).then_inc(s_sm, 16)
        y.dma_start(out=onesf[:], in_=ones_in).then_inc(s_sm, 16)
        y.dma_start(out=onesbf[:], in_=onesb_in).then_inc(s_sm, 16)
        y.dma_start(out=onesrf[:], in_=onesr_in).then_inc(s_sm, 16)
        y.dma_start(out=onesrbf[:], in_=onesrb_in).then_inc(s_sm, 16)
        if PHASE >= 2:
            slab_load(y, 0, s_sl)
            slab_load(y, 1, s_sl)
            for k in range(16):
                y.wait_ge(c_dec, k + 1)
                w_out(y, k, s_wos)
                if k + 2 <= 17:
                    slab_load(y, k + 2, s_sl)
        y.wait_ge(bV, KRC["v"])
        y.dma_start(out=kpad_d[:, 0:128], in_=z128[:]).then_inc(s_tp, 16)
        y.dma_start(out=kpad_d[:, 128:256], in_=krow[:]).then_inc(s_tp, 16)
        y.wait_ge(s_tp, 32)
        for p in range(64):
            y.dma_start(out=T0b[p:p + 1, :],
                        in_=kpad_d[0:1, 128 - p:256 - p]).then_inc(s_tp, 16)
        for oc in (range(NCH) if PHASE >= 4 else []):
            osl = (oc * P) // OSL
            y.wait_ge(s_wos, 64 * 16)
            if osl >= 1:
                y.wait_ge(s_wo, 64 * 16 * osl)
            wd = wt_d[oc].rearrange("p ic q -> p (ic q)")
            if oc >= 2:
                y.wait_ge(c_fh, oc - 1)
            y.dma_start(
                out=wtb[:, (oc % 2) * D:(oc % 2) * D + D // 2],
                in_=wd[:, 0:D // 2],
            ).then_inc(s_wt, 16)
            if oc >= 2:
                y.wait_ge(c_ffn, oc - 1)
            y.dma_start(
                out=wtb[:, (oc % 2) * D + D // 2:(oc % 2) * D + D],
                in_=wd[:, D // 2:D],
            ).then_inc(s_wt, 16)
        y.wait_ge(s_out, 32 * 16)
    em("Y", y_main)

    def u_load(eng, c, sem_):
        eng.dma_start(
            out=ux[:, c * NTOK:(c + 1) * NTOK],
            in_=u_in[c * P:(c + 1) * P, :],
        ).then_inc(sem_, 16)

    wt4 = wt_d.rearrange("oc (p2 two) ic q -> ic two p2 oc q", two=2)

    def w_out(eng, k, sem_):
        osl, ipc = UNITS[k]
        for par, wsrc in ((0, wlo), (1, whi)):
            t = wsrc[:, (k % 2) * OSL:(k % 2 + 1) * OSL].rearrange(
                "p (ocs q) -> p ocs q", q=P)
            for h_ in range(2):
                eng.dma_start(
                    out=wt4[2 * ipc + h_, par, :, osl * 8:(osl + 1) * 8, :],
                    in_=t[h_ * 64:(h_ + 1) * 64],
                ).then_inc(sem_, 16)

    def slab_load(eng, k, sem_):
        osl, ipc = UNITS[k]
        eng.dma_start(
            out=slab[:, (k % 2) * OSL:(k % 2 + 1) * OSL],
            in_=pk_in[ipc * P:(ipc + 1) * P, osl * OSL:(osl + 1) * OSL],
        ).then_inc(sem_, 16)

    def bstu_load(g, k):
        osl, ipc = UNITS[k]
        out = bstu16[:, k * OSL:(k + 1) * OSL] if k < 6 else \
            bstu[:, (k % 2) * OSL:(k % 2 + 1) * OSL]
        g.dma_start(
            out=out,
            in_=bst_in[:, ipc * D + osl * OSL:ipc * D + (osl + 1) * OSL],
        ).then_inc(s_bs, 16)

    def g_main(g):
        g.dma_start(out=atb[:], in_=at_in).then_inc(s_misc, 16)
        g.dma_start(out=btb[:], in_=bt_in).then_inc(s_misc, 16)
        if PHASE >= 2:
            for k in range(8):
                bstu_load(g, k)
        for c in range(8):
            u_load(g, c, s_u)
        if PHASE >= 2:
            for k in range(8, 16):
                g.wait_ge(c_dec, k - 1)
                bstu_load(g, k)
                u_load(g, 2 * (k - 8) + 8, s_u)
                u_load(g, 2 * (k - 8) + 9, s_u)
        for c in range((24 if PHASE >= 2 else 8), NCH):
            u_load(g, c, s_u)
        if PHASE < 2:
            return
        for k in (16, 17):
            g.wait_ge(c_dec, k - 1)
            bstu_load(g, k)
        for k in range(16, NUNIT):
            g.wait_ge(c_dec, k + 1)
            w_out(g, k, s_wo)
            if k + 2 < NUNIT:
                bstu_load(g, k + 2)
                slab_load(g, k + 2, s_sl2)
    em("G", g_main)

    def decode_unit_v(k):
        osl, ipc = UNITS[k]
        sl = slab[:, (k % 2) * OSL:(k % 2 + 1) * OSL]
        pscale = pC if PSC[k] == "C" else pD
        lo = wlo[:, (k % 2) * OSL:(k % 2 + 1) * OSL]
        hi = whi[:, (k % 2) * OSL:(k % 2 + 1) * OSL]
        c16a = dc1[:, 0:2 * OSL]
        t1s = dc2[:, 0:2 * OSL]

        def f(e, k=k):
            if k <= 17:
                e.wait_ge(s_sl, 16 * (k + 1))
            else:
                e.wait_ge(s_sl2, 16 * (k - 17))
            if 2 <= k <= 17:
                e.wait_ge(s_wos, 64 * (k - 1))
            elif k >= 18:
                e.wait_ge(s_wo, 64 * (k - 17))
            e.wait_ge(c_e8, k + 1)
            dm = dc4[:].bitcast(I32)
            e.tensor_scalar(dm, sl, 4097, None, OP.mult)
            dv = dm.bitcast(I16)
            e.tensor_scalar(c16a, dv, 7, None, OP.bitwise_and)
            e.tensor_scalar(t1s, dv, 8, 2, OP.bitwise_and, OP.logical_shift_right)
            e.tensor_scalar(t1s, t1s, -1, 1, OP.mult, OP.add)
            e._custom_dve(FP4_OP, out=dc4[:, 0:2 * OSL].bitcast(BF16),
                          in0=c16a, in1=t1s, s0=4.0, s1=12.0)
            bv = dc4[:, 0:2 * OSL].bitcast(BF16)
            e.tensor_tensor(lo, bv[:, 0::2], pscale[:, 0:OSL], OP.mult)
            e.tensor_tensor(hi, bv[:, 1::2], pscale[:, 0:OSL], OP.mult).then_inc(c_dec, 1)
        return f

    def e8_unit_p(k):
        osl, ipc = UNITS[k]
        pscale = pC if PSC[k] == "C" else pD

        def f(e, k=k, osl=osl, ipc=ipc):
            e.wait_ge(s_bs, 16 * (k + 1))
            w = _uwait(k)
            if w > 0:
                e.wait_ge(c_dec, w)
            bs_ = bstu16[:, k * OSL:(k + 1) * OSL] if k < 6 else \
                bstu[:, (k % 2) * OSL:(k % 2 + 1) * OSL]
            e.matmul(pscale[:, 0:H], e8g[:], bs_[:, 0:H],
                     start=True, stop=True)
            e.matmul(pscale[:, H:OSL], e8g[:], bs_[:, H:OSL],
                     start=True, stop=True).then_inc(c_e8, 1)
        return f


    ALLSM = 11 * 16

    step("S",
         lambda e: e.wait_ge(s_sm, ALLSM),
         lambda e: e.activation(scalt[:, 3:4], scalt[:, 1:2], AF.Exp))
    step("V", lambda e: e.wait_ge(s_sm, ALLSM), lambda e: e.memset(smr[:], 0.0))
    step("P", lambda e: e.wait_ge(s_sm, ALLSM))

    step("P",
         lambda e: e.matmul(pD[0:64, 512:516], onesrf[:, 0:64], scalt[:, 0:4],
                            start=True, stop=True),
         lambda e: e.matmul(pD[0:64, 0:1], vctr[:], smt[:, 6:7],
                            start=True, stop=True),
         lambda e: e.matmul(pD[0:64, 1:2], vcti[:], smt[:, 6:7],
                            start=True, stop=True))
    alph = acol(); Dcol = acol(); Bre = acol(); Bim = acol(); stepc = acol()
    step("V",
         lambda e: e.tensor_scalar(alph, pD[0:64, 515:516], 0.5, None, OP.mult),
         lambda e: e.tensor_copy(stepc, pD[0:64, 515:516]),
         lambda e: e.tensor_copy(Dcol, pD[0:64, 512:513]),
         lambda e: e.tensor_copy(sc16[:, 0:1], pD[0:16, 514:515]),
         lambda e: e.tensor_copy(Bre, pD[0:64, 0:1]),
         lambda e: e.tensor_copy(Bim, pD[0:64, 1:2]))
    step("V", lambda e: e.tensor_scalar(e8g[:], e8b[:], sc16[:, 0:1], None, OP.mult))

    lam_re = smt[:, 0:1]; lam_im = smt[:, 1:2]
    p_re = smt[:, 2:3]; p_im = smt[:, 3:4]
    q_re = smt[:, 4:5]; q_im = smt[:, 5:6]
    ct_re = smt[:, 7:8]; ct_imr = smt[:, 8:9]
    qc_im = acol()
    dr = acol(); di = acol(); mag = acol(); rec = acol()
    d_re = acol(); d_im = acol()
    step("V",
         lambda e: e.tensor_scalar(qc_im, q_im, -1.0, None, OP.mult),
         lambda e: e.tensor_scalar(dr, lam_re, alph, -1.0, OP.mult, OP.mult),
         lambda e: e.tensor_scalar(dr, dr, 1.0, None, OP.add),
         lambda e: e.tensor_scalar(di, lam_im, alph, -1.0, OP.mult, OP.mult),
         lambda e: e.tensor_mul(mag, dr, dr),
         lambda e: e.tensor_mul(rec, di, di),
         lambda e: e.tensor_add(mag, mag, rec),
         lambda e: e.reciprocal(rec, mag),
         lambda e: e.tensor_mul(d_re, dr, rec),
         lambda e: e.tensor_mul(d_im, di, rec),
         lambda e: e.tensor_scalar(d_im, d_im, -1.0, None, OP.mult))

    def cmul_cols(o_re, o_im, a_re, a_im, b_re, b_im, t1, t2):
        return [
            lambda e: e.tensor_mul(t1, a_re, b_re),
            lambda e: e.tensor_mul(t2, a_im, b_im),
            lambda e: e.tensor_sub(o_re, t1, t2),
            lambda e: e.tensor_mul(t1, a_re, b_im),
            lambda e: e.tensor_mul(t2, a_im, b_re),
            lambda e: e.tensor_add(o_im, t1, t2),
        ]

    tc1 = acol(); tc2 = acol()
    u1_re = acol(); u1_im = acol()
    v1_re = acol(); v1_im = acol()
    e_re = acol(); e_im = acol()
    a0_re = acol(); a0_im = acol()
    se_re = acol(); se_im = acol()
    sB_re = acol(); sB_im = acol()
    v1e_re = acol(); v1e_im = acol()
    dB_re = acol(); dB_im = acol()
    step("V", *cmul_cols(u1_re, u1_im, d_re, d_im, p_re, p_im, tc1, tc2))
    step("V", *cmul_cols(v1_re, v1_im, q_re, qc_im, d_re, d_im, tc1, tc2))
    step("V",
         lambda e: e.tensor_scalar(e_re, lam_re, alph, None, OP.mult),
         lambda e: e.tensor_scalar(e_re, e_re, 1.0, None, OP.add),
         lambda e: e.tensor_scalar(e_im, lam_im, alph, None, OP.mult))
    step("V", *cmul_cols(a0_re, a0_im, d_re, d_im, e_re, e_im, tc1, tc2))
    step("V", *cmul_cols(v1e_re, v1e_im, v1_re, v1_im, e_re, e_im, tc1, tc2))
    step("V", *cmul_cols(dB_re, dB_im, d_re, d_im, Bre, Bim, tc1, tc2))
    step("V", *cmul_cols(se_re, se_im, v1_re, v1_im, p_re, p_im, tc1, tc2))
    step("V", *cmul_cols(sB_re, sB_im, v1_re, v1_im, Bre, Bim, tc1, tc2))

    sig = arow_(2)
    v1B = arow_(2)
    step("P",
         lambda e: e.matmul(pD[0:1, 512:513], onesf[0:64, 0:1], se_re, start=True, stop=True),
         lambda e: e.matmul(pD[0:1, 513:514], onesf[0:64, 0:1], se_im, start=True, stop=True),
         lambda e: e.matmul(pD[0:1, 514:515], onesf[0:64, 0:1], sB_re, start=True, stop=True),
         lambda e: e.matmul(pD[0:1, 515:516], onesf[0:64, 0:1], sB_im, start=True, stop=True))
    step("V",
         lambda e: e.tensor_copy(sig, pD[0:1, 512:514]),
         lambda e: e.tensor_copy(v1B, pD[0:1, 514:516]))

    arow2 = arow_(2)
    den = arow_(2)
    beta = arow_(2)
    gam = arow_(2)
    t1r = arow_(2)
    t0r = arow_(1)

    def cmul_row(o, a, b, t):
        return [
            lambda e: e.tensor_mul(t[:, 0:1], a[:, 0:1], b[:, 0:1]),
            lambda e: e.tensor_mul(o[:, 0:1], a[:, 1:2], b[:, 1:2]),
            lambda e: e.tensor_sub(o[:, 0:1], t[:, 0:1], o[:, 0:1]),
            lambda e: e.tensor_mul(t[:, 1:2], a[:, 0:1], b[:, 1:2]),
            lambda e: e.tensor_mul(o[:, 1:2], a[:, 1:2], b[:, 0:1]),
            lambda e: e.tensor_add(o[:, 1:2], t[:, 1:2], o[:, 1:2]),
        ]

    def cinv_scale_row(o, num_s, den_, t):
        return [
            lambda e: e.tensor_mul(t[:, 0:1], den_[:, 0:1], den_[:, 0:1]),
            lambda e: e.tensor_mul(t[:, 1:2], den_[:, 1:2], den_[:, 1:2]),
            lambda e: e.tensor_add(t[:, 0:1], t[:, 0:1], t[:, 1:2]),
            lambda e: e.reciprocal(t[:, 0:1], t[:, 0:1]),
            lambda e: e.tensor_mul(o[:, 0:1], den_[:, 0:1], t[:, 0:1]),
            lambda e: e.tensor_mul(o[:, 1:2], den_[:, 1:2], t[:, 0:1]),
            lambda e: e.tensor_scalar(o[:, 1:2], o[:, 1:2], -1.0, None, OP.mult),
            lambda e: e.tensor_scalar(o[:, 0:1], o[:, 0:1], num_s, None, OP.mult),
            lambda e: e.tensor_scalar(o[:, 1:2], o[:, 1:2], num_s, None, OP.mult),
        ]

    alphr = scalt[:, 1:2]
    alrow = arow_(1)
    step("V",
         lambda e: e.tensor_scalar(alrow, scalt[:, 3:4], 0.5, None, OP.mult))
    step("V",
         lambda e: e.tensor_scalar(den[:, 0:2], sig[:, 0:2], alrow, None, OP.mult),
         lambda e: e.tensor_scalar(den[:, 0:1], den[:, 0:1], 1.0, None, OP.add),
         *cinv_scale_row(beta, alrow, den, t1r))
    step("V",
         *cmul_row(arow2, beta, sig, t1r),
         lambda e: e.tensor_scalar(arow2[:, 0:2], arow2[:, 0:2], -1.0, None, OP.mult),
         lambda e: e.tensor_scalar(arow2[:, 0:1], arow2[:, 0:1], 1.0, None, OP.add),
         lambda e: e.tensor_scalar(gam[:, 0:2], arow2[:, 0:2], alrow, None, OP.mult))
    bv1B = arow_(2)
    step("V", *cmul_row(bv1B, beta, v1B, t1r))

    assert _smr_n[0] == 18
    step("P", lambda e: e.matmul(pD[0:64, 0:6], onesrf[:, 0:64],
                                 smr[:, 8:14], start=True, stop=True))
    gb_c = g64[:, 2:3]; gb_ci = g64[:, 3:4]
    bb_c = g64[:, 4:5]; bb_ci = g64[:, 5:6]
    bv_c = g64[:, 6:7]; bv_ci = g64[:, 7:8]
    step("P", lambda e: e.matmul(pD[0:64, 6:10], onesrf[:, 0:64],
                                 smr[:, 14:18], start=True, stop=True))
    step("V",
         lambda e: e.tensor_copy(g64[:, 4:6], pD[0:64, 0:2]),
         lambda e: e.tensor_copy(g64[:, 2:4], pD[0:64, 2:4]),
         lambda e: e.tensor_copy(g64[:, 6:8], pD[0:64, 8:10]))

    w1_re = acol(); w1_im = acol()
    Bb_re = acol(); Bb_im = acol()
    step("V",
         lambda e: e.tensor_scalar(tc1, q_re, gb_c, None, OP.mult),
         lambda e: e.tensor_scalar(tc2, qc_im, gb_ci, None, OP.mult),
         lambda e: e.tensor_sub(w1_re, tc1, tc2),
         lambda e: e.tensor_scalar(tc1, q_re, gb_ci, None, OP.mult),
         lambda e: e.tensor_scalar(tc2, qc_im, gb_c, None, OP.mult),
         lambda e: e.tensor_add(w1_im, tc1, tc2),
         lambda e: e.tensor_scalar(tc1, v1e_re, bb_c, None, OP.mult),
         lambda e: e.tensor_scalar(tc2, v1e_im, bb_ci, None, OP.mult),
         lambda e: e.tensor_sub(tc1, tc1, tc2),
         lambda e: e.tensor_add(w1_re, w1_re, tc1),
         lambda e: e.tensor_scalar(tc1, v1e_re, bb_ci, None, OP.mult),
         lambda e: e.tensor_scalar(tc2, v1e_im, bb_c, None, OP.mult),
         lambda e: e.tensor_add(tc1, tc1, tc2),
         lambda e: e.tensor_add(w1_im, w1_im, tc1))
    step("V",
         lambda e: e.tensor_scalar(tc1, u1_re, bv_c, None, OP.mult),
         lambda e: e.tensor_scalar(tc2, u1_im, bv_ci, None, OP.mult),
         lambda e: e.tensor_sub(tc1, tc1, tc2),
         lambda e: e.tensor_sub(Bb_re, dB_re, tc1),
         lambda e: e.tensor_scalar(tc1, u1_re, bv_ci, None, OP.mult),
         lambda e: e.tensor_scalar(tc2, u1_im, bv_c, None, OP.mult),
         lambda e: e.tensor_add(tc1, tc1, tc2),
         lambda e: e.tensor_sub(Bb_im, dB_im, tc1),
         lambda e: e.tensor_scalar(Bb_re, Bb_re, stepc, None, OP.mult),
         lambda e: e.tensor_scalar(Bb_im, Bb_im, stepc, None, OP.mult))

    u1r = arow_(64); u1i = arow_(64); w1r = arow_(64); w1i = arow_(64)
    nu1r = arow_(64); nu1i = arow_(64)
    step("P",
         lambda e: e.transpose(pD[0:1, 0:64], u1_re, i64f[:]),
         lambda e: e.transpose(pD[0:1, 64:128], u1_im, i64f[:]),
         lambda e: e.transpose(pD[0:1, 128:192], w1_re, i64f[:]),
         lambda e: e.transpose(pD[0:1, 192:256], w1_im, i64f[:]))
    step("V",
         lambda e: e.tensor_copy(u1r, pD[0:1, 0:64]),
         lambda e: e.tensor_copy(u1i, pD[0:1, 64:128]),
         lambda e: e.tensor_copy(w1r, pD[0:1, 128:192]),
         lambda e: e.tensor_copy(w1i, pD[0:1, 192:256]),
         lambda e: e.tensor_scalar(nu1r, u1r, -1.0, None, OP.mult),
         lambda e: e.tensor_scalar(nu1i, u1i, -1.0, None, OP.mult))
    step("P",
         lambda e: e.matmul(pD[0:64, 512:576], nu1r, w1r, start=True, stop=False),
         lambda e: e.matmul(pD[0:64, 512:576], u1i, w1i, start=False, stop=True),
         lambda e: e.matmul(pD[0:64, 576:640], nu1r, w1i, start=True, stop=False),
         lambda e: e.matmul(pD[0:64, 576:640], nu1i, w1r, start=False, stop=True))
    dgt = pw[:, _pw_off[("scr", 0)]:_pw_off[("scr", 0)] + 64]
    step("V",
         lambda e: e.tensor_scalar(dgt, i64f[:], a0_re, None, OP.mult),
         lambda e: e.tensor_add(pwt(0, 0), dgt, pD[0:64, 512:576]),
         lambda e: e.tensor_scalar(dgt, i64f[:], a0_im, None, OP.mult),
         lambda e: e.tensor_add(pwt(0, 1), dgt, pD[0:64, 576:640]),
         lambda e: e.tensor_scalar(pwt(0, 5), pwt(0, 1), -1.0, None, OP.mult))

    for j in range(12):
        step("P",
             lambda e, j=j: e.transpose(pD[0:64, 0:64], pwt(j, 0), i64f[:]),
             lambda e, j=j: e.transpose(pD[0:64, 64:128], pwt(j, 1), i64f[:]))
        step("V",
             lambda e, j=j: e.tensor_copy(pwt(j, 2), pD[0:64, 0:64]),
             lambda e, j=j: e.tensor_copy(pwt(j, 3), pD[0:64, 64:128]),
             lambda e, j=j: e.tensor_scalar(pwt(j, 4), pD[0:64, 64:128], -1.0, None, OP.mult))
        step("P",
             lambda e, j=j: e.matmul(pD[0:64, 512:576], pwt(j, 2), pwt(j, 0), start=True, stop=False),
             lambda e, j=j: e.matmul(pD[0:64, 512:576], pwt(j, 4), pwt(j, 1), start=False, stop=True),
             lambda e, j=j: e.matmul(pD[0:64, 576:640], pwt(j, 2), pwt(j, 1), start=True, stop=False),
             lambda e, j=j: e.matmul(pD[0:64, 576:640], pwt(j, 3), pwt(j, 0), start=False, stop=True))
        if PHASE >= 2:
            em("P", e8_unit_p(j))
        fns = [lambda e, j=j: e.tensor_copy(pwt(j + 1, 0), pD[0:64, 512:576]),
               lambda e, j=j: e.tensor_copy(pwt(j + 1, 1), pD[0:64, 576:640])]
        if j + 1 <= 6 or j + 1 == 12:
            fns.append(lambda e, j=j: e.tensor_scalar(pwt(j + 1, 5), pD[0:64, 576:640], -1.0, None, OP.mult))
        step("V", *fns)
        if PHASE >= 2:
            em("V", decode_unit_v(j))

    w_re = acol(); w_im = acol(); nw_im = acol(); ctn_im = acol()
    step("V",
         lambda e: e.tensor_copy(w_re, ct_re),
         lambda e: e.tensor_scalar(ctn_im, ct_imr, -1.0, None, OP.mult),
         lambda e: e.tensor_copy(w_im, ctn_im))
    for it in range(NEU):
        step("P",
             lambda e: e.matmul(pD[0:64, 0:1], pwt(12, 0), w_re, start=True, stop=False),
             lambda e: e.matmul(pD[0:64, 0:1], pwt(12, 5), w_im, start=False, stop=True),
             lambda e: e.matmul(pD[0:64, 1:2], pwt(12, 0), w_im, start=True, stop=False),
             lambda e: e.matmul(pD[0:64, 1:2], pwt(12, 1), w_re, start=False, stop=True))
        step("V",
             lambda e: e.tensor_add(w_re, pD[0:64, 0:1], ct_re),
             lambda e: e.tensor_add(w_im, pD[0:64, 1:2], ctn_im))
    step("V", lambda e: e.tensor_scalar(nw_im, w_im, -1.0, None, OP.mult))

    step("V",
         lambda e: e.tensor_copy(bkre[:, 0:1], Bb_re),
         lambda e: e.tensor_copy(bkim[:, 0:1], Bb_im))
    for j in range(7):
        w_ = 1 << j
        step("P",
             lambda e, j=j, w_=w_: e.matmul(pD[0:64, 0:w_], pwt(j, 2), bkre[:, 0:w_], start=True, stop=False),
             lambda e, j=j, w_=w_: e.matmul(pD[0:64, 0:w_], pwt(j, 4), bkim[:, 0:w_], start=False, stop=True),
             lambda e, j=j, w_=w_: e.matmul(pD[0:64, 64:64 + w_], pwt(j, 2), bkim[:, 0:w_], start=True, stop=False),
             lambda e, j=j, w_=w_: e.matmul(pD[0:64, 64:64 + w_], pwt(j, 3), bkre[:, 0:w_], start=False, stop=True))
        if PHASE >= 2 and j >= 3:
            em("P", e8_unit_p(9 + j))
        step("V",
             lambda e, w_=w_: e.tensor_copy(bkre[:, w_:2 * w_], pD[0:64, 0:w_]),
             lambda e, w_=w_: e.tensor_copy(bkim[:, w_:2 * w_], pD[0:64, 64:64 + w_]))
        if PHASE >= 2 and j >= 3:
            em("V", decode_unit_v(9 + j))

    step("P",
         lambda e: e.matmul(pD[0:64, 0:1], pwt(0, 0), w_re, start=True, stop=False),
         lambda e: e.matmul(pD[0:64, 0:1], pwt(0, 5), w_im, start=False, stop=True),
         lambda e: e.matmul(pD[0:64, 1:2], pwt(0, 0), w_im, start=True, stop=False),
         lambda e: e.matmul(pD[0:64, 1:2], pwt(0, 1), w_re, start=False, stop=True))
    step("V",
         lambda e: e.tensor_copy(zre[:, 0:1], pD[0:64, 0:1]),
         lambda e: e.tensor_copy(zim[:, 0:1], pD[0:64, 1:2]))
    for j in range(7):
        w_ = 1 << j
        step("P",
             lambda e, j=j, w_=w_: e.matmul(pD[0:64, 0:w_], pwt(j, 0), zre[:, 0:w_], start=True, stop=False),
             lambda e, j=j, w_=w_: e.matmul(pD[0:64, 0:w_], pwt(j, 5), zim[:, 0:w_], start=False, stop=True),
             lambda e, j=j, w_=w_: e.matmul(pD[0:64, 64:64 + w_], pwt(j, 0), zim[:, 0:w_], start=True, stop=False),
             lambda e, j=j, w_=w_: e.matmul(pD[0:64, 64:64 + w_], pwt(j, 1), zre[:, 0:w_], start=False, stop=True))
        step("V",
             lambda e, w_=w_: e.tensor_copy(zre[:, w_:2 * w_], pD[0:64, 0:w_]),
             lambda e, w_=w_: e.tensor_copy(zim[:, w_:2 * w_], pD[0:64, 64:64 + w_]))

    kfr = arow_(128)
    step("P",
         lambda e: e.matmul(pD[0:1, 0:128], w_re, bkre[:], start=True, stop=False),
         lambda e: e.matmul(pD[0:1, 0:128], nw_im, bkim[:], start=False, stop=True))
    step("V",
         lambda e: e.tensor_copy(kfr, pD[0:1, 0:128]),
         lambda e: e.tensor_scalar(kfr[:, 0:1], kfr[:, 0:1], scalt[:, 0:1], None, OP.add),
         lambda e: e.tensor_copy(krow[:], kfr),
         lambda e: e.memset(z128[:], 0.0))
    KRC["v"] = seqcnt["V"]

    def s_toep2(e):
        e.wait_ge(s_tp, 32)
        for p in range(64, P):
            e.dma_start(out=T0b[p:p + 1, :],
                        in_=kpad_d[0:1, 128 - p:256 - p]).then_inc(s_tp, 16)
    step("S", s_toep2, raw=True)

    step("P", lambda e: e.transpose(pD[0:128, 0:64], bkre[:], i64f[:]))
    step("V", lambda e: e.tensor_copy(t128[:], pD[0:128, 0:64]))
    step("P", lambda e: e.matmul(pD[0:128, 512:576], j128b[:], t128[:], start=True, stop=True))
    step("V", lambda e: e.tensor_copy(BmT[:, 0:64], pD[0:128, 512:576]))
    step("P", lambda e: e.transpose(pD[0:128, 0:64], bkim[:], i64f[:]))
    step("V", lambda e: e.tensor_copy(t128[:], pD[0:128, 0:64]))
    step("P", lambda e: e.matmul(pD[0:128, 512:576], j128b[:], t128[:], start=True, stop=True))
    step("V",
         lambda e: e.tensor_copy(BmT[:, 64:128], pD[0:128, 512:576]),
         lambda e: e.tensor_copy(CBl[0:64, :], zre[:]),
         lambda e: e.tensor_scalar(CBl[64:128, :], zim[:], -1.0, None, OP.mult),
         lambda e: e.tensor_copy(ACl[0:64, 0:64], pwt(7, 2)),
         lambda e: e.tensor_copy(ACl[0:64, 64:128], pwt(7, 3)),
         lambda e: e.tensor_scalar(ACl[64:128, 0:64], pwt(7, 3), -1.0, None, OP.mult),
         lambda e: e.tensor_copy(ACl[64:128, 64:128], pwt(7, 2)),
         lambda e: e.memset(sst[0][:], 0.0))

    def s_sq1(e):
        for c in range(NCH):
            u_wait(e, c)
            if c >= 2:
                e.wait_ge(c_st, c - 1)
            e.activation(xsq[:, (c % 2) * NTOK:(c % 2 + 1) * NTOK],
                         ux[:, c * NTOK:(c + 1) * NTOK], AF.Square).then_inc(c_sq, 1)
    em("S", s_sq1)

    def p_st1(e):
        for c in range(NCH):
            u_wait(e, c)
            e.wait_ge(c_sq, c + 1)
            for h_ in range(2):
                e.matmul(pA[0:1, h_ * H:(h_ + 1) * H], onesbf[:, 0:1],
                         ux[:, c * NTOK + h_ * H:c * NTOK + (h_ + 1) * H],
                         start=(c == 0), stop=(c == NCH - 1))
                ins = e.matmul(pB[0:1, h_ * H:(h_ + 1) * H], onesbf[:, 0:1],
                         xsq[:, (c % 2) * NTOK + h_ * H:(c % 2) * NTOK + (h_ + 1) * H],
                         start=(c == 0), stop=(c == NCH - 1))
            ins.then_inc(c_st, 1)
    em("P", p_st1)
    step("S")
    step("P")

    def rows_math(psum_sum, psum_sq, mrow, srow):
        return [
            lambda e: e.tensor_scalar(mrow, psum_sum, 1.0 / 4096.0, None, OP.mult),
            lambda e: e.tensor_scalar(rowu[:, 0:NTOK], psum_sq, 1.0 / 4096.0, None, OP.mult),
            lambda e: e.tensor_mul(srow, mrow, mrow),
            lambda e: e.tensor_sub(rowu[:, 0:NTOK], rowu[:, 0:NTOK], srow),
            lambda e: e.tensor_scalar(rowu[:, 0:NTOK], rowu[:, 0:NTOK], 1e-5, None, OP.add),
            lambda e: e.reciprocal(srow, rowu[:, 0:NTOK]),
        ]
    step("V",
         lambda e: e.wait_ge(c_st, NCH),
         *rows_math(pA[0:1, :], pB[0:1, :],
                    rowt[:, 0:NTOK], rowt[:, NTOK:2 * NTOK]))
    step("S", lambda e: e.activation(rowt[:, NTOK:2 * NTOK],
                                     rowt[:, NTOK:2 * NTOK], AF.Sqrt))
    step("V",
         lambda e: e.tensor_copy(xsq[0:1, 0:NTOK], rowt[:, 0:NTOK]),
         lambda e: e.tensor_copy(ln1b[0:1, 0:NTOK], rowt[:, NTOK:2 * NTOK]))
    step("P",
         lambda e: e.matmul(pA[:, 0:H], onesrbf[:], xsq[0:1, 0:H], start=True, stop=True),
         lambda e: e.matmul(pA[:, H:NTOK], onesrbf[:], xsq[0:1, H:NTOK], start=True, stop=True),
         lambda e: e.matmul(pB[:, 0:H], onesrbf[:], ln1b[0:1, 0:H], start=True, stop=True),
         lambda e: e.matmul(pB[:, H:NTOK], onesrbf[:], ln1b[0:1, H:NTOK], start=True, stop=True))
    step("S",
         lambda e: e.copy(m1b[:], pA[:, 0:NTOK]),
         lambda e: e.copy(s1b[:], pB[:, 0:NTOK]))
    barrier()

    def v_conv(e):
        for c in range(NCH):
            lt = ln1b[:, (c % 2) * NTOK:(c % 2 + 1) * NTOK]
            uc = ux[:, c * NTOK:(c + 1) * NTOK]
            if c >= 2:
                e.wait_ge(c_s, c - 1)
            e.tensor_sub(lt, uc, m1b[:])
            e.tensor_mul(lt, lt, s1b[:]).then_inc(c_ln, 1)
            e.wait_ge(c_y, c + 1)
            e.tensor_add(uc[:, 0:H], uc[:, 0:H], pA[:, 0:H])
            e.tensor_add(uc[:, H:NTOK], uc[:, H:NTOK], pA[:, H:NTOK]).then_inc(c_x1, 1)
            if PHASE >= 2 and c < 16:
                decode_unit_v(16 + c)(e)
    em("V", v_conv)

    def p_conv(e):
        for c in range(NCH):
            lt = ln1b[:, (c % 2) * NTOK:(c % 2 + 1) * NTOK]
            sprev = sst[c % 2]
            e.wait_ge(c_ln, c + 1)
            e.wait_ge(s_tp, 32 + 128 * 16)
            if c >= 1:
                e.wait_ge(c_x1, c)
                e.wait_ge(c_sev, c)
            for h_ in range(2):
                sl_ = slice(h_ * H, (h_ + 1) * H)
                e.matmul(pA[:, sl_], T0b[:], lt[:, sl_], start=True, stop=False)
                ins = e.matmul(pA[:, sl_], CBl[:], sprev[:, sl_], start=False, stop=True)
            ins.then_inc(c_y, 1)
            for h_ in range(2):
                sl_ = slice(h_ * H, (h_ + 1) * H)
                e.matmul(pB[:, sl_], ACl[:], sprev[:, sl_], start=True, stop=False)
                ins = e.matmul(pB[:, sl_], BmT[:], lt[:, sl_], start=False, stop=True)
            ins.then_inc(c_s, 1)
            if PHASE >= 2 and c < 16:
                e8_unit_p(16 + c)(e)
    em("P", p_conv)

    def s_conv(e):
        for c in range(NCH):
            e.wait_ge(c_s, c + 1)
            snew = sst[(c + 1) % 2]
            e.copy(snew[:, 0:H], pB[:, 0:H])
            e.copy(snew[:, H:NTOK], pB[:, H:NTOK]).then_inc(c_sev, 1)
    em("S", s_conv)

    barrier()

    def s_sq2(e):
        for c in range(NCH):
            e.wait_ge(c_x1, c + 1)
            if c >= 2:
                e.wait_ge(c_st, NCH + c - 1)
            e.activation(xsq[:, (c % 2) * NTOK:(c % 2 + 1) * NTOK],
                         ux[:, c * NTOK:(c + 1) * NTOK], AF.Square).then_inc(c_sq, 1)
    em("S", s_sq2)

    def p_st2(e):
        e.wait_ge(c_dec, 32)
        for c in range(NCH):
            e.wait_ge(c_sq, NCH + c + 1)
            for h_ in range(2):
                e.matmul(pC[0:1, h_ * H:(h_ + 1) * H], onesbf[:, 0:1],
                         ux[:, c * NTOK + h_ * H:c * NTOK + (h_ + 1) * H],
                         start=(c == 0), stop=(c == NCH - 1))
                ins = e.matmul(pD[0:1, h_ * H:(h_ + 1) * H], onesbf[:, 0:1],
                         xsq[:, (c % 2) * NTOK + h_ * H:(c % 2) * NTOK + (h_ + 1) * H],
                         start=(c == 0), stop=(c == NCH - 1))
            ins.then_inc(c_st, 1)
    em("P", p_st2)
    step("S")
    step("P")

    step("V",
         lambda e: e.wait_ge(c_st, 2 * NCH),
         *rows_math(pC[0:1, :], pD[0:1, :],
                    rowt[:, 0:NTOK], rowt[:, NTOK:2 * NTOK]))
    step("S", lambda e: e.activation(rowt[:, NTOK:2 * NTOK],
                                     rowt[:, NTOK:2 * NTOK], AF.Sqrt))
    step("S", lambda e: e.activation(rowu[:, 0:NTOK], rowu[:, 0:NTOK], AF.Sqrt))
    step("V", lambda e: e.tensor_mul(rowu[:, 0:NTOK], rowu[:, 0:NTOK], rowt[:, 0:NTOK]))
    step("S", lambda e: e.copy(xsq[0:1, 0:NTOK], rowu[:, 0:NTOK]))
    step("P",
         lambda e: e.matmul(pA[:, 0:H], onesrf[:], rowt[:, NTOK:NTOK + H], start=True, stop=True),
         lambda e: e.matmul(pA[:, H:NTOK], onesrf[:], rowt[:, NTOK + H:2 * NTOK], start=True, stop=True),
         lambda e: e.matmul(pB[:, 0:H], onesrf[:], rowt[:, 0:H], start=True, stop=True),
         lambda e: e.matmul(pB[:, H:NTOK], onesrf[:], rowt[:, H:NTOK], start=True, stop=True))
    m2b = ln1b[:, 0:NTOK]
    step("S",
         lambda e: e.copy(s2b[:], pA[:, 0:NTOK]),
         lambda e: e.copy(m2b, pB[:, 0:NTOK]))
    SC2["v"] = seqcnt["S"]

    def v_sub(e):
        e.wait_ge(bS, SC2["v"])
        for c in range(NCH):
            e.wait_ge(c_st, NCH + c + 1)
            e.tensor_sub(ux[:, c * NTOK:(c + 1) * NTOK],
                         ux[:, c * NTOK:(c + 1) * NTOK], m2b).then_inc(c_sub, 1)
    if PHASE >= 4:
        em("V", v_sub)

    if PHASE < 4:
        def s_store_x1(e):
            e.wait_ge(c_x1, NCH)
            for oc in range(NCH):
                e.activation(outb[:, (oc % 2) * NTOK:(oc % 2 + 1) * NTOK],
                             ux[:, oc * NTOK:(oc + 1) * NTOK], AF.Identity).then_inc(c_ev, 1)
                e.wait_ge(c_ev, oc + 1)
                e.dma_start(out=out_t[oc * P:(oc + 1) * P, :],
                            in_=outb[:, (oc % 2) * NTOK:(oc % 2 + 1) * NTOK]).then_inc(s_out, 16)
                if oc >= 1:
                    e.wait_ge(s_out, 16 * oc)
        em("S", s_store_x1)
        if PHASE in (2, 3):
            def v_dec_tail(e):
                for k in range(NCH, NUNIT):
                    decode_unit_v(k)(e)
            def p_dec_tail(e):
                for k in range(NCH, NUNIT):
                    e8_unit_p(k)(e)
            em("V", v_dec_tail)
            em("P", p_dec_tail)

    import os as _os
    if _os.environ.get("KNOPTEST"):
        def p_mid(e):
            for _ in range(110):
                e.nop()
    elif False:
        pass
    if _os.environ.get("KNOPTEST"):
        pass
    else:
     def p_mid(e):
        for c in range(NCH):
            if PHASE >= 4:
                e.wait_ge(c_sub, c + 1)
            for h_ in range(2):
                e.matmul(pA[0:16, h_ * H:(h_ + 1) * H],
                         atb[:, c * 16:(c + 1) * 16],
                         ux[:, c * NTOK + h_ * H:c * NTOK + (h_ + 1) * H],
                         start=(c == 0), stop=(c == NCH - 1))
            ins = e.matmul(pB[0:16, 0:1], atb[:, c * 16:(c + 1) * 16], onesbf[:, 0:1],
                           start=(c == 0), stop=(c == NCH - 1))
        return ins
    if PHASE == 4:
        def v_dec_tail2(e):
            for k in range(NCH, NUNIT):
                decode_unit_v(k)(e)
        def p_dec_tail2(e):
            for k in range(NCH, NUNIT):
                e8_unit_p(k)(e)
        em("V", v_dec_tail2)
        em("P", p_dec_tail2)
        barrier()
    if PHASE >= 3:
        step("P", p_mid)
        step("V",
             lambda e: e.tensor_copy(midb[:], pA[0:16, 0:NTOK]),
             lambda e: e.tensor_copy(arowb[:], pB[0:16, 0:1]))
        SC3["v"] = seqcnt["V"]
        barrier()

    def p_ffn(e):
        for oc in range(NCH):
            wt = wtb[:, (oc % 2) * D:(oc % 2 + 1) * D]
            if oc == 0:
                e.wait_ge(bV, SC3["v"])
            e.wait_ge(s_wt, 16 * (2 * oc + 1))
            if oc >= 2:
                e.wait_ge(c_ev, oc - 1)
            for c in range(NCH):
                if oc == 0:
                    e.wait_ge(c_sub, c + 1)
                if c == 16:
                    e.wait_ge(s_wt, 16 * (2 * oc + 2))
                ins_c = None
                for h_ in range(2):
                    ins_c = e.matmul((pA if oc % 2 == 0 else pB)[:, h_ * H:(h_ + 1) * H],
                             wt[:, c * P:(c + 1) * P],
                             ux[:, c * NTOK + h_ * H:c * NTOK + (h_ + 1) * H],
                             start=(c == 0), stop=False)
                if c == 15:
                    ins_c.then_inc(c_fh, 1)
            ins = None
            for h_ in range(2):
                e.matmul((pA if oc % 2 == 0 else pB)[:, h_ * H:(h_ + 1) * H],
                         btb[:, oc * P:(oc + 1) * P],
                         midb[:, h_ * H:(h_ + 1) * H],
                         start=False, stop=False)
                ins = e.matmul((pA if oc % 2 == 0 else pB)[:, h_ * H:(h_ + 1) * H],
                         onesrbf[:, 0:P],
                         xsq[0:1, h_ * H:(h_ + 1) * H],
                         start=False, stop=True)
            ins.then_inc(c_ffn, 1)
            if oc < 16 and PHASE >= 5:
                e8_unit_p(32 + 2 * oc)(e)
                e8_unit_p(33 + 2 * oc)(e)
    if PHASE >= 4:
        em("P", p_ffn)

    def s_ffn(e):
        for oc in range(NCH):
            e.wait_ge(c_ev, oc + 1)
            e.dma_start(out=out_t[oc * P:(oc + 1) * P, :],
                        in_=outb[:, (oc % 2) * NTOK:(oc % 2 + 1) * NTOK]).then_inc(s_out, 16)
    if PHASE >= 4:
        em("S", s_ffn)

    def v_ffn(e):
        for oc in range(NCH):
            pp = pA if oc % 2 == 0 else pB
            e.wait_ge(c_ffn, oc + 1)
            if oc >= 2:
                e.wait_ge(s_out, 16 * (oc - 1))
            ob = outb[:, (oc % 2) * NTOK:(oc % 2 + 1) * NTOK]
            e.tensor_mul(tevac[:], pp[:, 0:NTOK], s2b[:])
            e.tensor_add(ob, tevac[:], ux[:, oc * NTOK:(oc + 1) * NTOK]).then_inc(c_ev, 1)
            if oc < 16 and PHASE >= 5:
                decode_unit_v(32 + 2 * oc)(e)
                decode_unit_v(33 + 2 * oc)(e)
    if PHASE >= 4:
        em("V", v_ffn)

    if DEBUG:
        def g_dbg(g):
            g.wait_ge(s_out, 32 * 16)
            dumps = [(dbg["K"], krow[:]), (dbg["T0"], T0b[:]), (dbg["BmT"], BmT[:]),
                     (dbg["CBl"], CBl[:]), (dbg["ACl"], ACl[:]),
                     (dbg["Ab"][:, 0:64], pw[:, 0:64]),
                     (dbg["Ab"][:, 64:128], pw[:, 64:128]),
                     (dbg["x1"], ux[:, 0:NTOK]),
                     (dbg["ln"], ln1b[:, NTOK:2 * NTOK]),
                     ] + (
                         [(dbg["W0"], wtb[:, D:D + P])] if PHASE >= 4 else [])
            for o_, i_ in dumps:
                g.dma_start(out=o_, in_=i_).then_inc(s_dbg, 16)
            g.wait_ge(s_dbg, 16 * len(dumps))
        em("G", g_dbg)

    with es, nc.Block() as block:
        @block.vector
        def _(vector):
            for f in ops["V"]:
                f(vector)

        @block.tensor
        def _(tensor):
            for f in ops["P"]:
                f(tensor)

        @block.scalar
        def _(scalar):
            for f in ops["S"]:
                f(scalar)

        @block.gpsimd
        def _(gpsimd):
            for f in ops["G"]:
                f(gpsimd)

        @block.sync
        def _(sync):
            for f in ops["Y"]:
                f(sync)

    return nc


_NC_CACHE = {}


def kernel(**inputs):
    u = np.asarray(inputs["L0_attn_u"], np.float32)
    lam = np.asarray(inputs["L0_attn_Lambda_ri"], np.float32)
    p_ri = np.asarray(inputs["L0_attn_p_ri"], np.float32)
    q_ri = np.asarray(inputs["L0_attn_q_ri"], np.float32)
    vc = np.asarray(inputs["L0_attn_Vc_ri"], np.float32)
    ct = np.asarray(inputs["L0_attn_Ct_ri"], np.float32)
    bp = np.asarray(inputs["L0_attn_Bp"], np.float32)
    Dv = np.asarray(inputs["L0_attn_D"], np.float32)
    ls = np.asarray(inputs["L0_attn_log_step"], np.float32)
    pk = np.asarray(inputs["L0_ffn_packed_weight"], np.int32)
    bs = np.asarray(inputs["L0_ffn_block_scales"], np.float32)
    gs = np.asarray(inputs["L0_ffn_global_scale"], np.float32)
    lA = np.asarray(inputs["L0_ffn_lora_A"], np.float32)
    lB = np.asarray(inputs["L0_ffn_lora_B"], np.float32)

    R = u.reshape(-1, D)
    packedT = np.ascontiguousarray(pk.T)
    bsT3 = np.ascontiguousarray(
        bs.T.reshape(N_IPC, 16, D).transpose(1, 0, 2).reshape(16, 16 * D))
    lA_T = np.ascontiguousarray(lA.T.reshape(NCH, P, 16).transpose(1, 0, 2).reshape(P, NCH * 16))
    lB_T = np.ascontiguousarray(lB.T)
    sm = np.ascontiguousarray(np.stack(
        [lam[:, 0], lam[:, 1], p_ri[:, 0], p_ri[:, 1], q_ri[:, 0], q_ri[:, 1],
         bp[:, 0], ct[0, :, 0], ct[0, :, 1]], axis=1))
    vctr = np.ascontiguousarray(vc[:, :, 0].T)
    vcti = np.ascontiguousarray(vc[:, :, 1].T)
    scal = np.ascontiguousarray(
        np.concatenate([Dv.reshape(1), ls.reshape(1), gs.reshape(1)]).reshape(1, 3))

    consts = {
        "cI64": np.eye(64, dtype=np.float32),
        "cJ128": np.eye(P, dtype=np.float32)[::-1].astype(MLBF),
        "cE8": np.repeat(0.5 * np.eye(16, dtype=np.float32), 8, axis=1).astype(MLBF),
        "cOnes": np.ones((P, 4), np.float32),
        "cOnesB": np.ones((P, 4), np.float32).astype(MLBF),
        "cOnesR": np.ones((1, P), np.float32),
        "cOnesRB": np.ones((1, P), np.float32).astype(MLBF),
    }

    key = ("prog", PHASE, DEBUG, SERIAL_DEC)
    if key not in _NC_CACHE:
        nc_new = build_program()
        mybir.codegen_inst_isa_subclasses(nc_new)
        _NC_CACHE[key] = nc_new
    nc = _NC_CACHE[key]

    in_maps = []
    for r in range(NC):
        m = {
            "u_s": np.ascontiguousarray(R[r * NTOK:(r + 1) * NTOK, :].T),
            "packedT": packedT, "bsT3": bsT3, "lA_T": lA_T, "lB_T": lB_T,
            "sm": sm, "VcT_re": vctr, "VcT_im": vcti, "scal": scal,
        }
        m.update(consts)
        in_maps.append(m)

    res = run_bass_kernel_spmd(nc, in_maps, core_ids=list(range(NC)))
    out = np.empty((NC * NTOK, D), np.float32)
    for r in range(NC):
        out[r * NTOK:(r + 1) * NTOK, :] = res.results[r]["out_s"].T
    if DEBUG:
        kernel._dbg = res.results
    return out.reshape(2, -1, D)


# version 23
# speedup vs baseline: 1.0312x; 1.0134x over previous
import numpy as np
import ml_dtypes
from contextlib import ExitStack

import concourse.bass as bass
import concourse.mybir as mybir
from concourse.bass_utils import run_bass_kernel_spmd
from concourse import dve_ops as _dve_ops
from concourse.dve_op import DveOp as _DveOp
from concourse.dve_spec import (Spec as _Spec, Src0 as _Src0, Src1 as _Src1,
                                C0 as _C0, C1 as _C1, relu as _relu,
                                lower as _dve_lower)


def _fp4_mag_sgn_ref(in0, in1, s0, s1, imm2):
    x = np.asarray(in0).astype(np.float32)
    g = np.asarray(in1).astype(np.float32)
    r1 = np.maximum(x - s0, 0.0)
    r2 = np.maximum(x + x - s1, 0.0)
    return (((x + r1) + r2) * g).astype(np.float32)


def _register_fp4_op():
    for o in _dve_ops.OPS:
        if o.name == "FP4_MAG_SGN":
            return o
    spec = _Spec(body=((_Src0 + _relu(_Src0 - _C0))
                       + _relu((_Src0 + _Src0) - _C1)) * _Src1)
    sha = {"v3": _dve_lower(spec, ver="v3", name="FP4_MAG_SGN").sha("v3")}
    op = _DveOp("FP4_MAG_SGN", spec, False, sha, reference=_fp4_mag_sgn_ref)
    _dve_ops.OPS.append(op)
    _dve_ops._SUB_OPCODE_FOR_NAME["FP4_MAG_SGN"] = (
        _dve_ops._CUSTOM_DVE_ROW_BASE + len(_dve_ops.OPS) - 1)
    _dve_ops.CUSTOM_DVE_SPECS["FP4_MAG_SGN"] = spec
    _dve_ops.CUSTOM_DVE_REFS["FP4_MAG_SGN"] = _fp4_mag_sgn_ref
    return op


FP4_OP = _register_fp4_op()

F32 = mybir.dt.float32
BF16 = mybir.dt.bfloat16
I32 = mybir.dt.int32
I16 = mybir.dt.int16
OP = mybir.AluOpType
AF = mybir.ActivationFunctionType
MLBF = ml_dtypes.bfloat16

P = 128
NTOK = 1024
D = 4096
NCH = 32
NC = 8
NEU = 10
N_IPC = 16
N_OSL = 4
OSL = 1024
NUNIT = N_IPC * N_OSL
H = 512
DEBUG = False
PHASE = 5
SERIAL_DEC = False


def build_program():
    nc = bass.Bass("TRN2", target_bir_lowering=False, detect_race_conditions=False)

    u_in = nc.dram_tensor("u_s", [D, NTOK], F32, kind="ExternalInput").ap()
    pk_in = nc.dram_tensor("packedT", [2048, D], I32, kind="ExternalInput").ap()
    bst_in = nc.dram_tensor("bsT3", [16, 16 * D], F32, kind="ExternalInput").ap()
    at_in = nc.dram_tensor("lA_T", [P, NCH * 16], F32, kind="ExternalInput").ap()
    bt_in = nc.dram_tensor("lB_T", [16, D], F32, kind="ExternalInput").ap()
    sm_in = nc.dram_tensor("sm", [64, 9], F32, kind="ExternalInput").ap()
    vcr_in = nc.dram_tensor("VcT_re", [64, 64], F32, kind="ExternalInput").ap()
    vci_in = nc.dram_tensor("VcT_im", [64, 64], F32, kind="ExternalInput").ap()
    scal_in = nc.dram_tensor("scal", [1, 3], F32, kind="ExternalInput").ap()
    i64_in = nc.dram_tensor("cI64", [64, 64], F32, kind="ExternalInput").ap()
    j128_in = nc.dram_tensor("cJ128", [P, P], BF16, kind="ExternalInput").ap()
    e8_in = nc.dram_tensor("cE8", [16, P], BF16, kind="ExternalInput").ap()
    ones_in = nc.dram_tensor("cOnes", [P, 4], F32, kind="ExternalInput").ap()
    onesb_in = nc.dram_tensor("cOnesB", [P, 4], BF16, kind="ExternalInput").ap()
    onesr_in = nc.dram_tensor("cOnesR", [1, P], F32, kind="ExternalInput").ap()
    onesrb_in = nc.dram_tensor("cOnesRB", [1, P], BF16, kind="ExternalInput").ap()

    out_t = nc.dram_tensor("out_s", [D, NTOK], F32, kind="ExternalOutput").ap()
    dbg = {}
    if DEBUG:
        for nm, shp in [("K", [1, P]), ("T0", [P, P]), ("BmT", [P, P]),
                        ("CBl", [P, P]), ("ACl", [P, P]), ("Ab", [64, P]),
                        ("x1", [P, NTOK]), ("W0", [P, P]), ("ln", [P, NTOK]),
                        ("smx", [64, 96]), ("smr", [1, 576]), ("g64", [64, 8])]:
            dbg[nm] = nc.dram_tensor("dbg_" + nm, shp, F32, kind="ExternalOutput").ap()

    wt_d = nc.dram_tensor("WT_d", [NCH, P, NCH, P], BF16, kind="Internal").ap()
    kpad_d = nc.dram_tensor("kpad_d", [1, 256], BF16, kind="Internal").ap()

    es = ExitStack()
    sb = lambda n, s, d: es.enter_context(nc.sbuf_tensor(n, s, d))
    ps_ = lambda n, s: es.enter_context(nc.psum_tensor(n, s, F32))
    sem = lambda n: es.enter_context(nc.semaphore(n))

    ux = sb("ux", [P, NCH * NTOK], BF16)
    ln1b = sb("ln1b", [P, 2 * NTOK], BF16)
    sst = [sb("sst0", [P, NTOK], BF16), sb("sst1", [P, NTOK], BF16)]
    m1b = sb("m1b", [P, NTOK], BF16)
    s1b = sb("s1b", [P, NTOK], BF16)
    s2b = sb("s2b", [P, NTOK], F32)
    xsq = sb("xsq", [P, 2 * NTOK], BF16)
    tevac = sb("tevac", [P, NTOK], BF16)
    outb = sb("outb", [P, 2 * NTOK], F32)
    T0b = sb("T0b", [P, P], BF16)
    BmT = sb("BmT", [P, P], BF16)
    CBl = sb("CBl", [P, P], BF16)
    ACl = sb("ACl", [P, P], BF16)
    wtb = sb("wtb", [P, 2 * D], BF16)
    colc = sb("colc", [P, 2], F32)
    midb = sb("midb", [16, NTOK], BF16)
    arowb = sb("arowb", [16, 1], BF16)
    atb = sb("atb", [P, NCH * 16], BF16)
    btb = sb("btb", [16, D], BF16)
    bstu = sb("bstu", [16, 2 * OSL], BF16)
    bstu16 = sb("bstu16", [16, 6 * OSL], BF16)
    e8g = sb("e8g", [16, P], BF16)
    slab = sb("slab", [P, 2 * OSL], I32)
    dc1 = sb("dc1", [P, 2 * OSL], I16)
    dc2 = sb("dc2", [P, 2 * OSL], I16)
    dc4 = sb("dc4", [P, 2 * OSL], I16)
    wlo = sb("wlo", [P, 2 * OSL], BF16)
    whi = sb("whi", [P, 2 * OSL], BF16)
    smt = sb("smt", [64, 9], F32)
    vctr = sb("vctr", [64, 64], F32)
    vcti = sb("vcti", [64, 64], F32)
    i64f = sb("i64f", [64, 64], F32)
    j128b = sb("j128b", [P, P], BF16)
    e8b = sb("e8b", [16, P], BF16)
    onesf = sb("onesf", [P, 4], F32)
    onesbf = sb("onesbf", [P, 4], BF16)
    onesrf = sb("onesrf", [1, P], F32)
    onesrbf = sb("onesrbf", [1, P], BF16)
    scalt = sb("scalt", [1, 4], F32)
    rowt = sb("rowt", [1, 2 * NTOK], F32)
    rowu = sb("rowu", [1, NTOK], F32)
    _pw_off = {}
    _pw_n = [0]
    for _j in range(13):
        _slots = (0, 1, 2, 3, 4, 5) if _j <= 6 else ((0, 1, 2, 3, 4) if _j <= 11 else (0, 1, 5))
        for _s in _slots:
            _pw_off[(_j, _s)] = _pw_n[0]
            _pw_n[0] += 64
    _pw_off[("scr", 0)] = _pw_n[0]
    _pw_n[0] += 64
    pw = sb("pw", [64, _pw_n[0]], F32)
    smx = sb("smx", [64, 24 * 4], F32)
    smr = sb("smr", [1, 576], F32)
    bkre = sb("bkre", [64, P], F32)
    bkim = sb("bkim", [64, P], F32)
    zre = sb("zre", [64, P], F32)
    zim = sb("zim", [64, P], F32)
    t128 = sb("t128", [P, 64], BF16)
    krow = sb("krow", [1, P], BF16)
    z128 = sb("z128", [1, P], BF16)
    g64 = sb("g64", [64, 8], F32)
    sc16 = sb("sc16", [16, 2], F32)

    pA = ps_("pA", [P, NTOK])
    pB = ps_("pB", [P, NTOK])
    pC = ps_("pC", [P, NTOK])
    pD = ps_("pD", [P, NTOK])

    s_sm = sem("s_sm")
    s_u = sem("s_u")
    KRC = {}

    def u_wait(e, c):
        e.wait_ge(s_u, 16 * (c + 1))
    s_pk = sem("s_pk")
    s_wo = sem("s_wo")
    s_wos = sem("s_wos")
    s_sl = sem("s_sl")
    s_sl2 = sem("s_sl2")
    s_bs = sem("s_bs")
    s_misc = sem("s_misc")
    s_wt = sem("s_wt")
    s_out = sem("s_out")
    s_tp = sem("s_tp")
    bV = sem("bV")
    bP = sem("bP")
    bS = sem("bS")
    c_ln = sem("c_ln")
    c_x1 = sem("c_x1")
    c_y = sem("c_y")
    c_s = sem("c_s")
    c_sev = sem("c_sev")
    c_sq = sem("c_sq")
    c_dec = sem("c_dec")
    c_e8 = sem("c_e8")
    c_ffn = sem("c_ffn")
    c_ev = sem("c_ev")
    c_cc = sem("c_cc")
    c_st = sem("c_st")
    c_sub = sem("c_sub")
    c_fh = sem("c_fh")
    SC2 = {}
    SC3 = {}
    s_dbg = sem("s_dbg")

    ops = {"V": [], "P": [], "S": [], "G": [], "Y": []}

    def em(key, f):
        ops[key].append(f)

    seqcnt = {"V": 0, "P": 0, "S": 0}
    seqsem = {"V": bV, "P": bP, "S": bS}

    def step(key, *fns, raw=False):
        waits = [(k, seqcnt[k]) for k in "VPS" if seqcnt[k] > 0]
        base = seqcnt[key]
        n = 1 if raw or not fns else len(fns)
        seqcnt[key] += n

        def f(eng, waits=waits, fns=fns, base=base, raw=raw):
            for k, v in waits:
                eng.wait_ge(seqsem[k], v)
            if raw or not fns:
                for g in fns:
                    g(eng)
                eng.sem_inc(seqsem[key], 1)
                return
            for i, g in enumerate(fns):
                if i > 0:
                    eng.wait_ge(seqsem[key], base + i)
                r = g(eng)
                ok = False
                if r is not None and hasattr(r, "then_inc"):
                    tn = type(getattr(r, "ins", r)).__name__
                    if "EventSemaphore" not in tn and "DMACopy" not in tn:
                        r.then_inc(seqsem[key], 1)
                        ok = True
                if not ok:
                    eng.sem_inc(seqsem[key], 1)
        em(key, f)

    def barrier():
        for k in "VPS":
            step(k)

    _smx_n = [0]

    def acol():
        i = _smx_n[0]
        _smx_n[0] += 1
        assert i < 96
        return smx[:, i:i + 1]

    _smr_n = [0]

    def arow_(w=1):
        i = _smr_n[0]
        _smr_n[0] += w
        assert _smr_n[0] <= 576
        return smr[:, i:i + w]

    def pwt(j, which):
        off = _pw_off[(j, which)]
        return pw[:, off:off + 64]

    UNITS = [(k // N_IPC, k % N_IPC) for k in range(NUNIT)]
    PSC = ["C"] * 16 + ["D" if k % 2 == 0 else "C" for k in range(16, 64)]

    def _uwait(k):
        for j in range(k - 1, -1, -1):
            if PSC[j] == PSC[k]:
                return j + 1
        return 0

    def y_main(y):
        y.dma_start(out=smt[:], in_=sm_in).then_inc(s_sm, 16)
        y.dma_start(out=vctr[:], in_=vcr_in).then_inc(s_sm, 16)
        y.dma_start(out=vcti[:], in_=vci_in).then_inc(s_sm, 16)
        y.dma_start(out=scalt[:, 0:3], in_=scal_in).then_inc(s_sm, 16)
        y.dma_start(out=i64f[:], in_=i64_in).then_inc(s_sm, 16)
        y.dma_start(out=j128b[:], in_=j128_in).then_inc(s_sm, 16)
        y.dma_start(out=e8b[:], in_=e8_in).then_inc(s_sm, 16)
        y.dma_start(out=onesf[:], in_=ones_in).then_inc(s_sm, 16)
        y.dma_start(out=onesbf[:], in_=onesb_in).then_inc(s_sm, 16)
        y.dma_start(out=onesrf[:], in_=onesr_in).then_inc(s_sm, 16)
        y.dma_start(out=onesrbf[:], in_=onesrb_in).then_inc(s_sm, 16)
        if PHASE >= 2:
            slab_load(y, 0, s_sl)
            slab_load(y, 1, s_sl)
            for k in range(16):
                y.wait_ge(c_dec, k + 1)
                w_out(y, k, s_wos)
                if k + 2 <= 17:
                    slab_load(y, k + 2, s_sl)
        y.wait_ge(bV, KRC["v"])
        y.dma_start(out=kpad_d[:, 0:128], in_=z128[:]).then_inc(s_tp, 16)
        y.dma_start(out=kpad_d[:, 128:256], in_=krow[:]).then_inc(s_tp, 16)
        y.wait_ge(s_tp, 32)
        for p in range(64):
            y.dma_start(out=T0b[p:p + 1, :],
                        in_=kpad_d[0:1, 128 - p:256 - p]).then_inc(s_tp, 16)
        for oc in (range(NCH) if PHASE >= 4 else []):
            osl = (oc * P) // OSL
            y.wait_ge(s_wos, 64 * 16)
            if osl >= 1:
                y.wait_ge(s_wo, 64 * 16 * osl)
            wd = wt_d[oc].rearrange("p ic q -> p (ic q)")
            if oc >= 2:
                y.wait_ge(c_fh, oc - 1)
            y.dma_start(
                out=wtb[:, (oc % 2) * D:(oc % 2) * D + D // 2],
                in_=wd[:, 0:D // 2],
            ).then_inc(s_wt, 16)
            if oc >= 2:
                y.wait_ge(c_ffn, oc - 1)
            y.dma_start(
                out=wtb[:, (oc % 2) * D + D // 2:(oc % 2) * D + D],
                in_=wd[:, D // 2:D],
            ).then_inc(s_wt, 16)
        y.wait_ge(s_out, 32 * 16)
    em("Y", y_main)

    def u_load(eng, c, sem_):
        eng.dma_start(
            out=ux[:, c * NTOK:(c + 1) * NTOK],
            in_=u_in[c * P:(c + 1) * P, :],
        ).then_inc(sem_, 16)

    wt4 = wt_d.rearrange("oc (p2 two) ic q -> ic two p2 oc q", two=2)

    def w_out(eng, k, sem_):
        osl, ipc = UNITS[k]
        for par, wsrc in ((0, wlo), (1, whi)):
            t = wsrc[:, (k % 2) * OSL:(k % 2 + 1) * OSL].rearrange(
                "p (ocs q) -> p ocs q", q=P)
            for h_ in range(2):
                eng.dma_start(
                    out=wt4[2 * ipc + h_, par, :, osl * 8:(osl + 1) * 8, :],
                    in_=t[h_ * 64:(h_ + 1) * 64],
                ).then_inc(sem_, 16)

    def slab_load(eng, k, sem_):
        osl, ipc = UNITS[k]
        eng.dma_start(
            out=slab[:, (k % 2) * OSL:(k % 2 + 1) * OSL],
            in_=pk_in[ipc * P:(ipc + 1) * P, osl * OSL:(osl + 1) * OSL],
        ).then_inc(sem_, 16)

    def bstu_load(g, k):
        osl, ipc = UNITS[k]
        out = bstu16[:, k * OSL:(k + 1) * OSL] if k < 6 else \
            bstu[:, (k % 2) * OSL:(k % 2 + 1) * OSL]
        g.dma_start(
            out=out,
            in_=bst_in[:, ipc * D + osl * OSL:ipc * D + (osl + 1) * OSL],
        ).then_inc(s_bs, 16)

    def g_main(g):
        g.dma_start(out=atb[:], in_=at_in).then_inc(s_misc, 16)
        g.dma_start(out=btb[:], in_=bt_in).then_inc(s_misc, 16)
        if PHASE >= 2:
            for k in range(8):
                bstu_load(g, k)
        for c in range(NCH):
            u_load(g, c, s_u)
        if PHASE >= 2:
            for k in range(8, 16):
                g.wait_ge(c_dec, k - 1)
                bstu_load(g, k)
        if PHASE < 2:
            return
        for k in (16, 17):
            g.wait_ge(c_dec, k - 1)
            bstu_load(g, k)
        for k in range(16, NUNIT):
            g.wait_ge(c_dec, k + 1)
            w_out(g, k, s_wo)
            if k + 2 < NUNIT:
                bstu_load(g, k + 2)
                slab_load(g, k + 2, s_sl2)
    em("G", g_main)

    def decode_unit_v(k):
        osl, ipc = UNITS[k]
        sl = slab[:, (k % 2) * OSL:(k % 2 + 1) * OSL]
        pscale = pC if PSC[k] == "C" else pD
        lo = wlo[:, (k % 2) * OSL:(k % 2 + 1) * OSL]
        hi = whi[:, (k % 2) * OSL:(k % 2 + 1) * OSL]
        c16a = dc1[:, 0:2 * OSL]
        t1s = dc2[:, 0:2 * OSL]

        def f(e, k=k):
            if k <= 17:
                e.wait_ge(s_sl, 16 * (k + 1))
            else:
                e.wait_ge(s_sl2, 16 * (k - 17))
            if 2 <= k <= 17:
                e.wait_ge(s_wos, 64 * (k - 1))
            elif k >= 18:
                e.wait_ge(s_wo, 64 * (k - 17))
            e.wait_ge(c_e8, k + 1)
            dm = dc4[:].bitcast(I32)
            e.tensor_scalar(dm, sl, 4097, None, OP.mult)
            dv = dm.bitcast(I16)
            e.tensor_scalar(c16a, dv, 7, None, OP.bitwise_and)
            e.tensor_scalar(t1s, dv, 8, 2, OP.bitwise_and, OP.logical_shift_right)
            e.tensor_scalar(t1s, t1s, -1, 1, OP.mult, OP.add)
            e._custom_dve(FP4_OP, out=dc4[:, 0:2 * OSL].bitcast(BF16),
                          in0=c16a, in1=t1s, s0=4.0, s1=12.0)
            bv = dc4[:, 0:2 * OSL].bitcast(BF16)
            e.tensor_tensor(lo, bv[:, 0::2], pscale[:, 0:OSL], OP.mult)
            e.tensor_tensor(hi, bv[:, 1::2], pscale[:, 0:OSL], OP.mult).then_inc(c_dec, 1)
        return f

    def st1_sq_v(c):
        def f(e, c=c):
            u_wait(e, c)
            if c >= 2:
                e.wait_ge(c_st, c - 1)
            e.tensor_mul(xsq[:, (c % 2) * NTOK:(c % 2 + 1) * NTOK],
                         ux[:, c * NTOK:(c + 1) * NTOK],
                         ux[:, c * NTOK:(c + 1) * NTOK]).then_inc(c_sq, 1)
        return f

    def st1_mm_p(c):
        def f(e, c=c):
            u_wait(e, c)
            e.wait_ge(c_sq, c + 1)
            for h_ in range(2):
                e.matmul(pA[0:1, h_ * H:(h_ + 1) * H], onesbf[:, 0:1],
                         ux[:, c * NTOK + h_ * H:c * NTOK + (h_ + 1) * H],
                         start=(c == 0), stop=(c == NCH - 1))
                ins = e.matmul(pB[0:1, h_ * H:(h_ + 1) * H], onesbf[:, 0:1],
                         xsq[:, (c % 2) * NTOK + h_ * H:(c % 2) * NTOK + (h_ + 1) * H],
                         start=(c == 0), stop=(c == NCH - 1))
            ins.then_inc(c_st, 1)
        return f

    def e8_unit_p(k):
        osl, ipc = UNITS[k]
        pscale = pC if PSC[k] == "C" else pD

        def f(e, k=k, osl=osl, ipc=ipc):
            e.wait_ge(s_bs, 16 * (k + 1))
            w = _uwait(k)
            if w > 0:
                e.wait_ge(c_dec, w)
            bs_ = bstu16[:, k * OSL:(k + 1) * OSL] if k < 6 else \
                bstu[:, (k % 2) * OSL:(k % 2 + 1) * OSL]
            e.matmul(pscale[:, 0:H], e8g[:], bs_[:, 0:H],
                     start=True, stop=True)
            e.matmul(pscale[:, H:OSL], e8g[:], bs_[:, H:OSL],
                     start=True, stop=True).then_inc(c_e8, 1)
        return f


    ALLSM = 11 * 16

    step("S",
         lambda e: e.wait_ge(s_sm, ALLSM),
         lambda e: e.activation(scalt[:, 3:4], scalt[:, 1:2], AF.Exp))
    step("V", lambda e: e.wait_ge(s_sm, ALLSM), lambda e: e.memset(smr[:], 0.0))
    step("P", lambda e: e.wait_ge(s_sm, ALLSM))

    step("P",
         lambda e: e.matmul(pD[0:64, 512:516], onesrf[:, 0:64], scalt[:, 0:4],
                            start=True, stop=True),
         lambda e: e.matmul(pD[0:64, 0:1], vctr[:], smt[:, 6:7],
                            start=True, stop=True),
         lambda e: e.matmul(pD[0:64, 1:2], vcti[:], smt[:, 6:7],
                            start=True, stop=True))
    alph = acol(); Dcol = acol(); Bre = acol(); Bim = acol(); stepc = acol()
    step("V",
         lambda e: e.tensor_scalar(alph, pD[0:64, 515:516], 0.5, None, OP.mult),
         lambda e: e.tensor_copy(stepc, pD[0:64, 515:516]),
         lambda e: e.tensor_copy(Dcol, pD[0:64, 512:513]),
         lambda e: e.tensor_copy(sc16[:, 0:1], pD[0:16, 514:515]),
         lambda e: e.tensor_copy(Bre, pD[0:64, 0:1]),
         lambda e: e.tensor_copy(Bim, pD[0:64, 1:2]))
    step("V", lambda e: e.tensor_scalar(e8g[:], e8b[:], sc16[:, 0:1], None, OP.mult))

    lam_re = smt[:, 0:1]; lam_im = smt[:, 1:2]
    p_re = smt[:, 2:3]; p_im = smt[:, 3:4]
    q_re = smt[:, 4:5]; q_im = smt[:, 5:6]
    ct_re = smt[:, 7:8]; ct_imr = smt[:, 8:9]
    qc_im = acol()
    dr = acol(); di = acol(); mag = acol(); rec = acol()
    d_re = acol(); d_im = acol()
    step("V",
         lambda e: e.tensor_scalar(qc_im, q_im, -1.0, None, OP.mult),
         lambda e: e.tensor_scalar(dr, lam_re, alph, -1.0, OP.mult, OP.mult),
         lambda e: e.tensor_scalar(dr, dr, 1.0, None, OP.add),
         lambda e: e.tensor_scalar(di, lam_im, alph, -1.0, OP.mult, OP.mult),
         lambda e: e.tensor_mul(mag, dr, dr),
         lambda e: e.tensor_mul(rec, di, di),
         lambda e: e.tensor_add(mag, mag, rec),
         lambda e: e.reciprocal(rec, mag),
         lambda e: e.tensor_mul(d_re, dr, rec),
         lambda e: e.tensor_mul(d_im, di, rec),
         lambda e: e.tensor_scalar(d_im, d_im, -1.0, None, OP.mult))

    def cmul_cols(o_re, o_im, a_re, a_im, b_re, b_im, t1, t2):
        return [
            lambda e: e.tensor_mul(t1, a_re, b_re),
            lambda e: e.tensor_mul(t2, a_im, b_im),
            lambda e: e.tensor_sub(o_re, t1, t2),
            lambda e: e.tensor_mul(t1, a_re, b_im),
            lambda e: e.tensor_mul(t2, a_im, b_re),
            lambda e: e.tensor_add(o_im, t1, t2),
        ]

    tc1 = acol(); tc2 = acol()
    u1_re = acol(); u1_im = acol()
    v1_re = acol(); v1_im = acol()
    e_re = acol(); e_im = acol()
    a0_re = acol(); a0_im = acol()
    se_re = acol(); se_im = acol()
    sB_re = acol(); sB_im = acol()
    v1e_re = acol(); v1e_im = acol()
    dB_re = acol(); dB_im = acol()
    step("V", *cmul_cols(u1_re, u1_im, d_re, d_im, p_re, p_im, tc1, tc2))
    step("V", *cmul_cols(v1_re, v1_im, q_re, qc_im, d_re, d_im, tc1, tc2))
    step("V",
         lambda e: e.tensor_scalar(e_re, lam_re, alph, None, OP.mult),
         lambda e: e.tensor_scalar(e_re, e_re, 1.0, None, OP.add),
         lambda e: e.tensor_scalar(e_im, lam_im, alph, None, OP.mult))
    step("V", *cmul_cols(a0_re, a0_im, d_re, d_im, e_re, e_im, tc1, tc2))
    step("V", *cmul_cols(v1e_re, v1e_im, v1_re, v1_im, e_re, e_im, tc1, tc2))
    step("V", *cmul_cols(dB_re, dB_im, d_re, d_im, Bre, Bim, tc1, tc2))
    step("V", *cmul_cols(se_re, se_im, v1_re, v1_im, p_re, p_im, tc1, tc2))
    step("V", *cmul_cols(sB_re, sB_im, v1_re, v1_im, Bre, Bim, tc1, tc2))

    sig = arow_(2)
    v1B = arow_(2)
    step("P",
         lambda e: e.matmul(pD[0:1, 512:513], onesf[0:64, 0:1], se_re, start=True, stop=True),
         lambda e: e.matmul(pD[0:1, 513:514], onesf[0:64, 0:1], se_im, start=True, stop=True),
         lambda e: e.matmul(pD[0:1, 514:515], onesf[0:64, 0:1], sB_re, start=True, stop=True),
         lambda e: e.matmul(pD[0:1, 515:516], onesf[0:64, 0:1], sB_im, start=True, stop=True))
    step("V",
         lambda e: e.tensor_copy(sig, pD[0:1, 512:514]),
         lambda e: e.tensor_copy(v1B, pD[0:1, 514:516]))

    arow2 = arow_(2)
    den = arow_(2)
    beta = arow_(2)
    gam = arow_(2)
    t1r = arow_(2)
    t0r = arow_(1)

    def cmul_row(o, a, b, t):
        return [
            lambda e: e.tensor_mul(t[:, 0:1], a[:, 0:1], b[:, 0:1]),
            lambda e: e.tensor_mul(o[:, 0:1], a[:, 1:2], b[:, 1:2]),
            lambda e: e.tensor_sub(o[:, 0:1], t[:, 0:1], o[:, 0:1]),
            lambda e: e.tensor_mul(t[:, 1:2], a[:, 0:1], b[:, 1:2]),
            lambda e: e.tensor_mul(o[:, 1:2], a[:, 1:2], b[:, 0:1]),
            lambda e: e.tensor_add(o[:, 1:2], t[:, 1:2], o[:, 1:2]),
        ]

    def cinv_scale_row(o, num_s, den_, t):
        return [
            lambda e: e.tensor_mul(t[:, 0:1], den_[:, 0:1], den_[:, 0:1]),
            lambda e: e.tensor_mul(t[:, 1:2], den_[:, 1:2], den_[:, 1:2]),
            lambda e: e.tensor_add(t[:, 0:1], t[:, 0:1], t[:, 1:2]),
            lambda e: e.reciprocal(t[:, 0:1], t[:, 0:1]),
            lambda e: e.tensor_mul(o[:, 0:1], den_[:, 0:1], t[:, 0:1]),
            lambda e: e.tensor_mul(o[:, 1:2], den_[:, 1:2], t[:, 0:1]),
            lambda e: e.tensor_scalar(o[:, 1:2], o[:, 1:2], -1.0, None, OP.mult),
            lambda e: e.tensor_scalar(o[:, 0:1], o[:, 0:1], num_s, None, OP.mult),
            lambda e: e.tensor_scalar(o[:, 1:2], o[:, 1:2], num_s, None, OP.mult),
        ]

    alphr = scalt[:, 1:2]
    alrow = arow_(1)
    step("V",
         lambda e: e.tensor_scalar(alrow, scalt[:, 3:4], 0.5, None, OP.mult))
    step("V",
         lambda e: e.tensor_scalar(den[:, 0:2], sig[:, 0:2], alrow, None, OP.mult),
         lambda e: e.tensor_scalar(den[:, 0:1], den[:, 0:1], 1.0, None, OP.add),
         *cinv_scale_row(beta, alrow, den, t1r))
    step("V",
         *cmul_row(arow2, beta, sig, t1r),
         lambda e: e.tensor_scalar(arow2[:, 0:2], arow2[:, 0:2], -1.0, None, OP.mult),
         lambda e: e.tensor_scalar(arow2[:, 0:1], arow2[:, 0:1], 1.0, None, OP.add),
         lambda e: e.tensor_scalar(gam[:, 0:2], arow2[:, 0:2], alrow, None, OP.mult))
    bv1B = arow_(2)
    step("V", *cmul_row(bv1B, beta, v1B, t1r))

    assert _smr_n[0] == 18
    step("P", lambda e: e.matmul(pD[0:64, 0:6], onesrf[:, 0:64],
                                 smr[:, 8:14], start=True, stop=True))
    gb_c = g64[:, 2:3]; gb_ci = g64[:, 3:4]
    bb_c = g64[:, 4:5]; bb_ci = g64[:, 5:6]
    bv_c = g64[:, 6:7]; bv_ci = g64[:, 7:8]
    step("P", lambda e: e.matmul(pD[0:64, 6:10], onesrf[:, 0:64],
                                 smr[:, 14:18], start=True, stop=True))
    step("V",
         lambda e: e.tensor_copy(g64[:, 4:6], pD[0:64, 0:2]),
         lambda e: e.tensor_copy(g64[:, 2:4], pD[0:64, 2:4]),
         lambda e: e.tensor_copy(g64[:, 6:8], pD[0:64, 8:10]))

    w1_re = acol(); w1_im = acol()
    Bb_re = acol(); Bb_im = acol()
    step("V",
         lambda e: e.tensor_scalar(tc1, q_re, gb_c, None, OP.mult),
         lambda e: e.tensor_scalar(tc2, qc_im, gb_ci, None, OP.mult),
         lambda e: e.tensor_sub(w1_re, tc1, tc2),
         lambda e: e.tensor_scalar(tc1, q_re, gb_ci, None, OP.mult),
         lambda e: e.tensor_scalar(tc2, qc_im, gb_c, None, OP.mult),
         lambda e: e.tensor_add(w1_im, tc1, tc2),
         lambda e: e.tensor_scalar(tc1, v1e_re, bb_c, None, OP.mult),
         lambda e: e.tensor_scalar(tc2, v1e_im, bb_ci, None, OP.mult),
         lambda e: e.tensor_sub(tc1, tc1, tc2),
         lambda e: e.tensor_add(w1_re, w1_re, tc1),
         lambda e: e.tensor_scalar(tc1, v1e_re, bb_ci, None, OP.mult),
         lambda e: e.tensor_scalar(tc2, v1e_im, bb_c, None, OP.mult),
         lambda e: e.tensor_add(tc1, tc1, tc2),
         lambda e: e.tensor_add(w1_im, w1_im, tc1))
    step("V",
         lambda e: e.tensor_scalar(tc1, u1_re, bv_c, None, OP.mult),
         lambda e: e.tensor_scalar(tc2, u1_im, bv_ci, None, OP.mult),
         lambda e: e.tensor_sub(tc1, tc1, tc2),
         lambda e: e.tensor_sub(Bb_re, dB_re, tc1),
         lambda e: e.tensor_scalar(tc1, u1_re, bv_ci, None, OP.mult),
         lambda e: e.tensor_scalar(tc2, u1_im, bv_c, None, OP.mult),
         lambda e: e.tensor_add(tc1, tc1, tc2),
         lambda e: e.tensor_sub(Bb_im, dB_im, tc1),
         lambda e: e.tensor_scalar(Bb_re, Bb_re, stepc, None, OP.mult),
         lambda e: e.tensor_scalar(Bb_im, Bb_im, stepc, None, OP.mult))

    u1r = arow_(64); u1i = arow_(64); w1r = arow_(64); w1i = arow_(64)
    nu1r = arow_(64); nu1i = arow_(64)
    step("P",
         lambda e: e.transpose(pD[0:1, 0:64], u1_re, i64f[:]),
         lambda e: e.transpose(pD[0:1, 64:128], u1_im, i64f[:]),
         lambda e: e.transpose(pD[0:1, 128:192], w1_re, i64f[:]),
         lambda e: e.transpose(pD[0:1, 192:256], w1_im, i64f[:]))
    step("V",
         lambda e: e.tensor_copy(u1r, pD[0:1, 0:64]),
         lambda e: e.tensor_copy(u1i, pD[0:1, 64:128]),
         lambda e: e.tensor_copy(w1r, pD[0:1, 128:192]),
         lambda e: e.tensor_copy(w1i, pD[0:1, 192:256]),
         lambda e: e.tensor_scalar(nu1r, u1r, -1.0, None, OP.mult),
         lambda e: e.tensor_scalar(nu1i, u1i, -1.0, None, OP.mult))
    step("P",
         lambda e: e.matmul(pD[0:64, 512:576], nu1r, w1r, start=True, stop=False),
         lambda e: e.matmul(pD[0:64, 512:576], u1i, w1i, start=False, stop=True),
         lambda e: e.matmul(pD[0:64, 576:640], nu1r, w1i, start=True, stop=False),
         lambda e: e.matmul(pD[0:64, 576:640], nu1i, w1r, start=False, stop=True))
    dgt = pw[:, _pw_off[("scr", 0)]:_pw_off[("scr", 0)] + 64]
    step("V",
         lambda e: e.tensor_scalar(dgt, i64f[:], a0_re, None, OP.mult),
         lambda e: e.tensor_add(pwt(0, 0), dgt, pD[0:64, 512:576]),
         lambda e: e.tensor_scalar(dgt, i64f[:], a0_im, None, OP.mult),
         lambda e: e.tensor_add(pwt(0, 1), dgt, pD[0:64, 576:640]),
         lambda e: e.tensor_scalar(pwt(0, 5), pwt(0, 1), -1.0, None, OP.mult))

    for j in range(12):
        step("P",
             lambda e, j=j: e.transpose(pD[0:64, 0:64], pwt(j, 0), i64f[:]),
             lambda e, j=j: e.transpose(pD[0:64, 64:128], pwt(j, 1), i64f[:]))
        step("V",
             lambda e, j=j: e.tensor_copy(pwt(j, 2), pD[0:64, 0:64]),
             lambda e, j=j: e.tensor_copy(pwt(j, 3), pD[0:64, 64:128]),
             lambda e, j=j: e.tensor_scalar(pwt(j, 4), pD[0:64, 64:128], -1.0, None, OP.mult))
        step("P",
             lambda e, j=j: e.matmul(pD[0:64, 512:576], pwt(j, 2), pwt(j, 0), start=True, stop=False),
             lambda e, j=j: e.matmul(pD[0:64, 512:576], pwt(j, 4), pwt(j, 1), start=False, stop=True),
             lambda e, j=j: e.matmul(pD[0:64, 576:640], pwt(j, 2), pwt(j, 1), start=True, stop=False),
             lambda e, j=j: e.matmul(pD[0:64, 576:640], pwt(j, 3), pwt(j, 0), start=False, stop=True))
        if PHASE >= 2:
            em("P", e8_unit_p(j))
        if j >= 1:
            em("P", st1_mm_p(2 * j - 2))
            em("P", st1_mm_p(2 * j - 1))
        fns = [lambda e, j=j: e.tensor_copy(pwt(j + 1, 0), pD[0:64, 512:576]),
               lambda e, j=j: e.tensor_copy(pwt(j + 1, 1), pD[0:64, 576:640])]
        if j + 1 <= 6 or j + 1 == 12:
            fns.append(lambda e, j=j: e.tensor_scalar(pwt(j + 1, 5), pD[0:64, 576:640], -1.0, None, OP.mult))
        step("V", *fns)
        if PHASE >= 2:
            em("V", decode_unit_v(j))
        em("V", st1_sq_v(2 * j))
        em("V", st1_sq_v(2 * j + 1))

    w_re = acol(); w_im = acol(); nw_im = acol(); ctn_im = acol()
    step("V",
         lambda e: e.tensor_copy(w_re, ct_re),
         lambda e: e.tensor_scalar(ctn_im, ct_imr, -1.0, None, OP.mult),
         lambda e: e.tensor_copy(w_im, ctn_im))
    for it in range(NEU):
        step("P",
             lambda e: e.matmul(pD[0:64, 0:1], pwt(12, 0), w_re, start=True, stop=False),
             lambda e: e.matmul(pD[0:64, 0:1], pwt(12, 5), w_im, start=False, stop=True),
             lambda e: e.matmul(pD[0:64, 1:2], pwt(12, 0), w_im, start=True, stop=False),
             lambda e: e.matmul(pD[0:64, 1:2], pwt(12, 1), w_re, start=False, stop=True))
        step("V",
             lambda e: e.tensor_add(w_re, pD[0:64, 0:1], ct_re),
             lambda e: e.tensor_add(w_im, pD[0:64, 1:2], ctn_im))
    step("V", lambda e: e.tensor_scalar(nw_im, w_im, -1.0, None, OP.mult))

    step("V",
         lambda e: e.tensor_copy(bkre[:, 0:1], Bb_re),
         lambda e: e.tensor_copy(bkim[:, 0:1], Bb_im))
    for j in range(7):
        w_ = 1 << j
        step("P",
             lambda e, j=j, w_=w_: e.matmul(pD[0:64, 0:w_], pwt(j, 2), bkre[:, 0:w_], start=True, stop=False),
             lambda e, j=j, w_=w_: e.matmul(pD[0:64, 0:w_], pwt(j, 4), bkim[:, 0:w_], start=False, stop=True),
             lambda e, j=j, w_=w_: e.matmul(pD[0:64, 64:64 + w_], pwt(j, 2), bkim[:, 0:w_], start=True, stop=False),
             lambda e, j=j, w_=w_: e.matmul(pD[0:64, 64:64 + w_], pwt(j, 3), bkre[:, 0:w_], start=False, stop=True))
        if PHASE >= 2 and j >= 3:
            em("P", e8_unit_p(9 + j))
        if j == 0:
            em("P", st1_mm_p(22))
            em("P", st1_mm_p(23))
        else:
            em("P", st1_mm_p(23 + j))
        step("V",
             lambda e, w_=w_: e.tensor_copy(bkre[:, w_:2 * w_], pD[0:64, 0:w_]),
             lambda e, w_=w_: e.tensor_copy(bkim[:, w_:2 * w_], pD[0:64, 64:64 + w_]))
        if PHASE >= 2 and j >= 3:
            em("V", decode_unit_v(9 + j))
        em("V", st1_sq_v(24 + j))

    em("V", st1_sq_v(31))
    em("P", st1_mm_p(30))
    em("P", st1_mm_p(31))

    step("P",
         lambda e: e.matmul(pD[0:64, 0:1], pwt(0, 0), w_re, start=True, stop=False),
         lambda e: e.matmul(pD[0:64, 0:1], pwt(0, 5), w_im, start=False, stop=True),
         lambda e: e.matmul(pD[0:64, 1:2], pwt(0, 0), w_im, start=True, stop=False),
         lambda e: e.matmul(pD[0:64, 1:2], pwt(0, 1), w_re, start=False, stop=True))
    step("V",
         lambda e: e.tensor_copy(zre[:, 0:1], pD[0:64, 0:1]),
         lambda e: e.tensor_copy(zim[:, 0:1], pD[0:64, 1:2]))
    for j in range(7):
        w_ = 1 << j
        step("P",
             lambda e, j=j, w_=w_: e.matmul(pD[0:64, 0:w_], pwt(j, 0), zre[:, 0:w_], start=True, stop=False),
             lambda e, j=j, w_=w_: e.matmul(pD[0:64, 0:w_], pwt(j, 5), zim[:, 0:w_], start=False, stop=True),
             lambda e, j=j, w_=w_: e.matmul(pD[0:64, 64:64 + w_], pwt(j, 0), zim[:, 0:w_], start=True, stop=False),
             lambda e, j=j, w_=w_: e.matmul(pD[0:64, 64:64 + w_], pwt(j, 1), zre[:, 0:w_], start=False, stop=True))
        step("V",
             lambda e, w_=w_: e.tensor_copy(zre[:, w_:2 * w_], pD[0:64, 0:w_]),
             lambda e, w_=w_: e.tensor_copy(zim[:, w_:2 * w_], pD[0:64, 64:64 + w_]))

    kfr = arow_(128)
    step("P",
         lambda e: e.matmul(pD[0:1, 0:128], w_re, bkre[:], start=True, stop=False),
         lambda e: e.matmul(pD[0:1, 0:128], nw_im, bkim[:], start=False, stop=True))
    step("V",
         lambda e: e.tensor_copy(kfr, pD[0:1, 0:128]),
         lambda e: e.tensor_scalar(kfr[:, 0:1], kfr[:, 0:1], scalt[:, 0:1], None, OP.add),
         lambda e: e.tensor_copy(krow[:], kfr),
         lambda e: e.memset(z128[:], 0.0))
    KRC["v"] = seqcnt["V"]

    def s_toep2(e):
        e.wait_ge(s_tp, 32)
        for p in range(64, P):
            e.dma_start(out=T0b[p:p + 1, :],
                        in_=kpad_d[0:1, 128 - p:256 - p]).then_inc(s_tp, 16)
    step("S", s_toep2, raw=True)

    step("P", lambda e: e.transpose(pD[0:128, 0:64], bkre[:], i64f[:]))
    step("V", lambda e: e.tensor_copy(t128[:], pD[0:128, 0:64]))
    step("P", lambda e: e.matmul(pD[0:128, 512:576], j128b[:], t128[:], start=True, stop=True))
    step("V", lambda e: e.tensor_copy(BmT[:, 0:64], pD[0:128, 512:576]))
    step("P", lambda e: e.transpose(pD[0:128, 0:64], bkim[:], i64f[:]))
    step("V", lambda e: e.tensor_copy(t128[:], pD[0:128, 0:64]))
    step("P", lambda e: e.matmul(pD[0:128, 512:576], j128b[:], t128[:], start=True, stop=True))
    step("V",
         lambda e: e.tensor_copy(BmT[:, 64:128], pD[0:128, 512:576]),
         lambda e: e.tensor_copy(CBl[0:64, :], zre[:]),
         lambda e: e.tensor_scalar(CBl[64:128, :], zim[:], -1.0, None, OP.mult),
         lambda e: e.tensor_copy(ACl[0:64, 0:64], pwt(7, 2)),
         lambda e: e.tensor_copy(ACl[0:64, 64:128], pwt(7, 3)),
         lambda e: e.tensor_scalar(ACl[64:128, 0:64], pwt(7, 3), -1.0, None, OP.mult),
         lambda e: e.tensor_copy(ACl[64:128, 64:128], pwt(7, 2)),
         lambda e: e.memset(sst[0][:], 0.0))

    step("S")
    step("P")

    def rows_math(psum_sum, psum_sq, mrow, srow):
        return [
            lambda e: e.tensor_scalar(mrow, psum_sum, 1.0 / 4096.0, None, OP.mult),
            lambda e: e.tensor_scalar(rowu[:, 0:NTOK], psum_sq, 1.0 / 4096.0, None, OP.mult),
            lambda e: e.tensor_mul(srow, mrow, mrow),
            lambda e: e.tensor_sub(rowu[:, 0:NTOK], rowu[:, 0:NTOK], srow),
            lambda e: e.tensor_scalar(rowu[:, 0:NTOK], rowu[:, 0:NTOK], 1e-5, None, OP.add),
            lambda e: e.reciprocal(srow, rowu[:, 0:NTOK]),
        ]
    step("V",
         lambda e: e.wait_ge(c_st, NCH),
         *rows_math(pA[0:1, :], pB[0:1, :],
                    rowt[:, 0:NTOK], rowt[:, NTOK:2 * NTOK]))
    step("S", lambda e: e.activation(rowt[:, NTOK:2 * NTOK],
                                     rowt[:, NTOK:2 * NTOK], AF.Sqrt))
    step("V",
         lambda e: e.tensor_copy(xsq[0:1, 0:NTOK], rowt[:, 0:NTOK]),
         lambda e: e.tensor_copy(ln1b[0:1, 0:NTOK], rowt[:, NTOK:2 * NTOK]))
    step("P",
         lambda e: e.matmul(pA[:, 0:H], onesrbf[:], xsq[0:1, 0:H], start=True, stop=True),
         lambda e: e.matmul(pA[:, H:NTOK], onesrbf[:], xsq[0:1, H:NTOK], start=True, stop=True),
         lambda e: e.matmul(pB[:, 0:H], onesrbf[:], ln1b[0:1, 0:H], start=True, stop=True),
         lambda e: e.matmul(pB[:, H:NTOK], onesrbf[:], ln1b[0:1, H:NTOK], start=True, stop=True))
    step("S",
         lambda e: e.copy(m1b[:], pA[:, 0:NTOK]),
         lambda e: e.copy(s1b[:], pB[:, 0:NTOK]))
    barrier()

    def v_conv(e):
        for c in range(NCH):
            lt = ln1b[:, (c % 2) * NTOK:(c % 2 + 1) * NTOK]
            uc = ux[:, c * NTOK:(c + 1) * NTOK]
            if c >= 2:
                e.wait_ge(c_s, c - 1)
            e.tensor_sub(lt, uc, m1b[:])
            e.tensor_mul(lt, lt, s1b[:]).then_inc(c_ln, 1)
            e.wait_ge(c_y, c + 1)
            e.tensor_add(uc[:, 0:H], uc[:, 0:H], pA[:, 0:H])
            e.tensor_add(uc[:, H:NTOK], uc[:, H:NTOK], pA[:, H:NTOK]).then_inc(c_x1, 1)
            if PHASE >= 2 and c < 16:
                decode_unit_v(16 + c)(e)
    em("V", v_conv)

    def p_conv(e):
        for c in range(NCH):
            lt = ln1b[:, (c % 2) * NTOK:(c % 2 + 1) * NTOK]
            sprev = sst[c % 2]
            e.wait_ge(c_ln, c + 1)
            e.wait_ge(s_tp, 32 + 128 * 16)
            if c >= 1:
                e.wait_ge(c_x1, c)
                e.wait_ge(c_sev, c)
            for h_ in range(2):
                sl_ = slice(h_ * H, (h_ + 1) * H)
                e.matmul(pA[:, sl_], T0b[:], lt[:, sl_], start=True, stop=False)
                ins = e.matmul(pA[:, sl_], CBl[:], sprev[:, sl_], start=False, stop=True)
            ins.then_inc(c_y, 1)
            for h_ in range(2):
                sl_ = slice(h_ * H, (h_ + 1) * H)
                e.matmul(pB[:, sl_], ACl[:], sprev[:, sl_], start=True, stop=False)
                ins = e.matmul(pB[:, sl_], BmT[:], lt[:, sl_], start=False, stop=True)
            ins.then_inc(c_s, 1)
            if PHASE >= 2 and c < 16:
                e8_unit_p(16 + c)(e)
    em("P", p_conv)

    def s_conv(e):
        for c in range(NCH):
            e.wait_ge(c_s, c + 1)
            snew = sst[(c + 1) % 2]
            e.copy(snew[:, 0:H], pB[:, 0:H])
            e.copy(snew[:, H:NTOK], pB[:, H:NTOK]).then_inc(c_sev, 1)
    em("S", s_conv)

    barrier()

    def s_sq2(e):
        for c in range(NCH):
            e.wait_ge(c_x1, c + 1)
            if c >= 2:
                e.wait_ge(c_st, NCH + c - 1)
            e.activation(xsq[:, (c % 2) * NTOK:(c % 2 + 1) * NTOK],
                         ux[:, c * NTOK:(c + 1) * NTOK], AF.Square).then_inc(c_sq, 1)
    em("S", s_sq2)

    def p_st2(e):
        e.wait_ge(c_dec, 32)
        for c in range(NCH):
            e.wait_ge(c_sq, NCH + c + 1)
            for h_ in range(2):
                e.matmul(pC[0:1, h_ * H:(h_ + 1) * H], onesbf[:, 0:1],
                         ux[:, c * NTOK + h_ * H:c * NTOK + (h_ + 1) * H],
                         start=(c == 0), stop=(c == NCH - 1))
                ins = e.matmul(pD[0:1, h_ * H:(h_ + 1) * H], onesbf[:, 0:1],
                         xsq[:, (c % 2) * NTOK + h_ * H:(c % 2) * NTOK + (h_ + 1) * H],
                         start=(c == 0), stop=(c == NCH - 1))
            ins.then_inc(c_st, 1)
    em("P", p_st2)
    step("S")
    step("P")

    step("V",
         lambda e: e.wait_ge(c_st, 2 * NCH),
         *rows_math(pC[0:1, :], pD[0:1, :],
                    rowt[:, 0:NTOK], rowt[:, NTOK:2 * NTOK]))
    step("S", lambda e: e.activation(rowt[:, NTOK:2 * NTOK],
                                     rowt[:, NTOK:2 * NTOK], AF.Sqrt))
    step("S", lambda e: e.activation(rowu[:, 0:NTOK], rowu[:, 0:NTOK], AF.Sqrt))
    step("V", lambda e: e.tensor_mul(rowu[:, 0:NTOK], rowu[:, 0:NTOK], rowt[:, 0:NTOK]))
    step("S", lambda e: e.copy(xsq[0:1, 0:NTOK], rowu[:, 0:NTOK]))
    step("P",
         lambda e: e.matmul(pA[:, 0:H], onesrf[:], rowt[:, NTOK:NTOK + H], start=True, stop=True),
         lambda e: e.matmul(pA[:, H:NTOK], onesrf[:], rowt[:, NTOK + H:2 * NTOK], start=True, stop=True),
         lambda e: e.matmul(pB[:, 0:H], onesrf[:], rowt[:, 0:H], start=True, stop=True),
         lambda e: e.matmul(pB[:, H:NTOK], onesrf[:], rowt[:, H:NTOK], start=True, stop=True))
    m2b = ln1b[:, 0:NTOK]
    step("S",
         lambda e: e.copy(s2b[:], pA[:, 0:NTOK]),
         lambda e: e.copy(m2b, pB[:, 0:NTOK]))
    SC2["v"] = seqcnt["S"]

    def v_sub(e):
        e.wait_ge(bS, SC2["v"])
        for c in range(NCH):
            e.wait_ge(c_st, NCH + c + 1)
            e.tensor_sub(ux[:, c * NTOK:(c + 1) * NTOK],
                         ux[:, c * NTOK:(c + 1) * NTOK], m2b).then_inc(c_sub, 1)
    if PHASE >= 4:
        em("V", v_sub)

    if PHASE < 4:
        def s_store_x1(e):
            e.wait_ge(c_x1, NCH)
            for oc in range(NCH):
                e.activation(outb[:, (oc % 2) * NTOK:(oc % 2 + 1) * NTOK],
                             ux[:, oc * NTOK:(oc + 1) * NTOK], AF.Identity).then_inc(c_ev, 1)
                e.wait_ge(c_ev, oc + 1)
                e.dma_start(out=out_t[oc * P:(oc + 1) * P, :],
                            in_=outb[:, (oc % 2) * NTOK:(oc % 2 + 1) * NTOK]).then_inc(s_out, 16)
                if oc >= 1:
                    e.wait_ge(s_out, 16 * oc)
        em("S", s_store_x1)
        if PHASE in (2, 3):
            def v_dec_tail(e):
                for k in range(NCH, NUNIT):
                    decode_unit_v(k)(e)
            def p_dec_tail(e):
                for k in range(NCH, NUNIT):
                    e8_unit_p(k)(e)
            em("V", v_dec_tail)
            em("P", p_dec_tail)

    import os as _os
    if _os.environ.get("KNOPTEST"):
        def p_mid(e):
            for _ in range(110):
                e.nop()
    elif False:
        pass
    if _os.environ.get("KNOPTEST"):
        pass
    else:
     def p_mid(e):
        for c in range(NCH):
            if PHASE >= 4:
                e.wait_ge(c_sub, c + 1)
            for h_ in range(2):
                e.matmul(pA[0:16, h_ * H:(h_ + 1) * H],
                         atb[:, c * 16:(c + 1) * 16],
                         ux[:, c * NTOK + h_ * H:c * NTOK + (h_ + 1) * H],
                         start=(c == 0), stop=(c == NCH - 1))
            ins = e.matmul(pB[0:16, 0:1], atb[:, c * 16:(c + 1) * 16], onesbf[:, 0:1],
                           start=(c == 0), stop=(c == NCH - 1))
        return ins
    if PHASE == 4:
        def v_dec_tail2(e):
            for k in range(NCH, NUNIT):
                decode_unit_v(k)(e)
        def p_dec_tail2(e):
            for k in range(NCH, NUNIT):
                e8_unit_p(k)(e)
        em("V", v_dec_tail2)
        em("P", p_dec_tail2)
        barrier()
    if PHASE >= 3:
        step("P", p_mid)
        step("V",
             lambda e: e.tensor_copy(midb[:], pA[0:16, 0:NTOK]),
             lambda e: e.tensor_copy(arowb[:], pB[0:16, 0:1]))
        SC3["v"] = seqcnt["V"]
        barrier()

    def p_ffn(e):
        for oc in range(NCH):
            wt = wtb[:, (oc % 2) * D:(oc % 2 + 1) * D]
            if oc == 0:
                e.wait_ge(bV, SC3["v"])
            e.wait_ge(s_wt, 16 * (2 * oc + 1))
            if oc >= 2:
                e.wait_ge(c_ev, oc - 1)
            for c in range(NCH):
                if oc == 0:
                    e.wait_ge(c_sub, c + 1)
                if c == 16:
                    e.wait_ge(s_wt, 16 * (2 * oc + 2))
                ins_c = None
                for h_ in range(2):
                    ins_c = e.matmul((pA if oc % 2 == 0 else pB)[:, h_ * H:(h_ + 1) * H],
                             wt[:, c * P:(c + 1) * P],
                             ux[:, c * NTOK + h_ * H:c * NTOK + (h_ + 1) * H],
                             start=(c == 0), stop=False)
                if c == 15:
                    ins_c.then_inc(c_fh, 1)
            ins = None
            for h_ in range(2):
                e.matmul((pA if oc % 2 == 0 else pB)[:, h_ * H:(h_ + 1) * H],
                         btb[:, oc * P:(oc + 1) * P],
                         midb[:, h_ * H:(h_ + 1) * H],
                         start=False, stop=False)
                ins = e.matmul((pA if oc % 2 == 0 else pB)[:, h_ * H:(h_ + 1) * H],
                         onesrbf[:, 0:P],
                         xsq[0:1, h_ * H:(h_ + 1) * H],
                         start=False, stop=True)
            ins.then_inc(c_ffn, 1)
            if oc < 16 and PHASE >= 5:
                e8_unit_p(32 + 2 * oc)(e)
                e8_unit_p(33 + 2 * oc)(e)
    if PHASE >= 4:
        em("P", p_ffn)

    def s_ffn(e):
        for oc in range(NCH):
            e.wait_ge(c_ev, oc + 1)
            e.dma_start(out=out_t[oc * P:(oc + 1) * P, :],
                        in_=outb[:, (oc % 2) * NTOK:(oc % 2 + 1) * NTOK]).then_inc(s_out, 16)
    if PHASE >= 4:
        em("S", s_ffn)

    def v_ffn(e):
        for oc in range(NCH):
            pp = pA if oc % 2 == 0 else pB
            e.wait_ge(c_ffn, oc + 1)
            if oc >= 2:
                e.wait_ge(s_out, 16 * (oc - 1))
            ob = outb[:, (oc % 2) * NTOK:(oc % 2 + 1) * NTOK]
            e.tensor_mul(tevac[:], pp[:, 0:NTOK], s2b[:])
            e.tensor_add(ob, tevac[:], ux[:, oc * NTOK:(oc + 1) * NTOK]).then_inc(c_ev, 1)
            if oc < 16 and PHASE >= 5:
                decode_unit_v(32 + 2 * oc)(e)
                decode_unit_v(33 + 2 * oc)(e)
    if PHASE >= 4:
        em("V", v_ffn)

    if DEBUG:
        def g_dbg(g):
            g.wait_ge(s_out, 32 * 16)
            dumps = [(dbg["K"], krow[:]), (dbg["T0"], T0b[:]), (dbg["BmT"], BmT[:]),
                     (dbg["CBl"], CBl[:]), (dbg["ACl"], ACl[:]),
                     (dbg["Ab"][:, 0:64], pw[:, 0:64]),
                     (dbg["Ab"][:, 64:128], pw[:, 64:128]),
                     (dbg["x1"], ux[:, 0:NTOK]),
                     (dbg["ln"], ln1b[:, NTOK:2 * NTOK]),
                     ] + (
                         [(dbg["W0"], wtb[:, D:D + P])] if PHASE >= 4 else [])
            for o_, i_ in dumps:
                g.dma_start(out=o_, in_=i_).then_inc(s_dbg, 16)
            g.wait_ge(s_dbg, 16 * len(dumps))
        em("G", g_dbg)

    with es, nc.Block() as block:
        @block.vector
        def _(vector):
            for f in ops["V"]:
                f(vector)

        @block.tensor
        def _(tensor):
            for f in ops["P"]:
                f(tensor)

        @block.scalar
        def _(scalar):
            for f in ops["S"]:
                f(scalar)

        @block.gpsimd
        def _(gpsimd):
            for f in ops["G"]:
                f(gpsimd)

        @block.sync
        def _(sync):
            for f in ops["Y"]:
                f(sync)

    return nc


_NC_CACHE = {}


def kernel(**inputs):
    u = np.asarray(inputs["L0_attn_u"], np.float32)
    lam = np.asarray(inputs["L0_attn_Lambda_ri"], np.float32)
    p_ri = np.asarray(inputs["L0_attn_p_ri"], np.float32)
    q_ri = np.asarray(inputs["L0_attn_q_ri"], np.float32)
    vc = np.asarray(inputs["L0_attn_Vc_ri"], np.float32)
    ct = np.asarray(inputs["L0_attn_Ct_ri"], np.float32)
    bp = np.asarray(inputs["L0_attn_Bp"], np.float32)
    Dv = np.asarray(inputs["L0_attn_D"], np.float32)
    ls = np.asarray(inputs["L0_attn_log_step"], np.float32)
    pk = np.asarray(inputs["L0_ffn_packed_weight"], np.int32)
    bs = np.asarray(inputs["L0_ffn_block_scales"], np.float32)
    gs = np.asarray(inputs["L0_ffn_global_scale"], np.float32)
    lA = np.asarray(inputs["L0_ffn_lora_A"], np.float32)
    lB = np.asarray(inputs["L0_ffn_lora_B"], np.float32)

    R = u.reshape(-1, D)
    packedT = np.ascontiguousarray(pk.T)
    bsT3 = np.ascontiguousarray(
        bs.T.reshape(N_IPC, 16, D).transpose(1, 0, 2).reshape(16, 16 * D))
    lA_T = np.ascontiguousarray(lA.T.reshape(NCH, P, 16).transpose(1, 0, 2).reshape(P, NCH * 16))
    lB_T = np.ascontiguousarray(lB.T)
    sm = np.ascontiguousarray(np.stack(
        [lam[:, 0], lam[:, 1], p_ri[:, 0], p_ri[:, 1], q_ri[:, 0], q_ri[:, 1],
         bp[:, 0], ct[0, :, 0], ct[0, :, 1]], axis=1))
    vctr = np.ascontiguousarray(vc[:, :, 0].T)
    vcti = np.ascontiguousarray(vc[:, :, 1].T)
    scal = np.ascontiguousarray(
        np.concatenate([Dv.reshape(1), ls.reshape(1), gs.reshape(1)]).reshape(1, 3))

    consts = {
        "cI64": np.eye(64, dtype=np.float32),
        "cJ128": np.eye(P, dtype=np.float32)[::-1].astype(MLBF),
        "cE8": np.repeat(0.5 * np.eye(16, dtype=np.float32), 8, axis=1).astype(MLBF),
        "cOnes": np.ones((P, 4), np.float32),
        "cOnesB": np.ones((P, 4), np.float32).astype(MLBF),
        "cOnesR": np.ones((1, P), np.float32),
        "cOnesRB": np.ones((1, P), np.float32).astype(MLBF),
    }

    key = ("prog", PHASE, DEBUG, SERIAL_DEC)
    if key not in _NC_CACHE:
        nc_new = build_program()
        mybir.codegen_inst_isa_subclasses(nc_new)
        _NC_CACHE[key] = nc_new
    nc = _NC_CACHE[key]

    in_maps = []
    for r in range(NC):
        m = {
            "u_s": np.ascontiguousarray(R[r * NTOK:(r + 1) * NTOK, :].T),
            "packedT": packedT, "bsT3": bsT3, "lA_T": lA_T, "lB_T": lB_T,
            "sm": sm, "VcT_re": vctr, "VcT_im": vcti, "scal": scal,
        }
        m.update(consts)
        in_maps.append(m)

    res = run_bass_kernel_spmd(nc, in_maps, core_ids=list(range(NC)))
    out = np.empty((NC * NTOK, D), np.float32)
    for r in range(NC):
        out[r * NTOK:(r + 1) * NTOK, :] = res.results[r]["out_s"].T
    if DEBUG:
        kernel._dbg = res.results
    return out.reshape(2, -1, D)


# version 24
# speedup vs baseline: 1.0483x; 1.0166x over previous
import numpy as np
import ml_dtypes
from contextlib import ExitStack

import concourse.bass as bass
import concourse.mybir as mybir
from concourse.bass_utils import run_bass_kernel_spmd
from concourse import dve_ops as _dve_ops
from concourse.dve_op import DveOp as _DveOp
from concourse.dve_spec import (Spec as _Spec, Src0 as _Src0, Src1 as _Src1,
                                C0 as _C0, C1 as _C1, relu as _relu,
                                lower as _dve_lower)


def _fp4_mag_sgn_ref(in0, in1, s0, s1, imm2):
    x = np.asarray(in0).astype(np.float32)
    g = np.asarray(in1).astype(np.float32)
    r1 = np.maximum(x - s0, 0.0)
    r2 = np.maximum(x + x - s1, 0.0)
    return (((x + r1) + r2) * g).astype(np.float32)


def _register_fp4_op():
    for o in _dve_ops.OPS:
        if o.name == "FP4_MAG_SGN":
            return o
    spec = _Spec(body=((_Src0 + _relu(_Src0 - _C0))
                       + _relu((_Src0 + _Src0) - _C1)) * _Src1)
    sha = {"v3": _dve_lower(spec, ver="v3", name="FP4_MAG_SGN").sha("v3")}
    op = _DveOp("FP4_MAG_SGN", spec, False, sha, reference=_fp4_mag_sgn_ref)
    _dve_ops.OPS.append(op)
    _dve_ops._SUB_OPCODE_FOR_NAME["FP4_MAG_SGN"] = (
        _dve_ops._CUSTOM_DVE_ROW_BASE + len(_dve_ops.OPS) - 1)
    _dve_ops.CUSTOM_DVE_SPECS["FP4_MAG_SGN"] = spec
    _dve_ops.CUSTOM_DVE_REFS["FP4_MAG_SGN"] = _fp4_mag_sgn_ref
    return op


FP4_OP = _register_fp4_op()

F32 = mybir.dt.float32
BF16 = mybir.dt.bfloat16
I32 = mybir.dt.int32
I16 = mybir.dt.int16
OP = mybir.AluOpType
AF = mybir.ActivationFunctionType
MLBF = ml_dtypes.bfloat16

P = 128
NTOK = 1024
D = 4096
NCH = 32
NC = 8
NEU = 10
N_IPC = 16
N_OSL = 4
OSL = 1024
NUNIT = N_IPC * N_OSL
H = 512
DEBUG = False
PHASE = 5
SERIAL_DEC = False


def build_program():
    nc = bass.Bass("TRN2", target_bir_lowering=False, detect_race_conditions=False)

    u_in = nc.dram_tensor("u_s", [D, NTOK], F32, kind="ExternalInput").ap()
    pk_in = nc.dram_tensor("packedT", [2048, D], I32, kind="ExternalInput").ap()
    bst_in = nc.dram_tensor("bsT3", [16, 16 * D], F32, kind="ExternalInput").ap()
    at_in = nc.dram_tensor("lA_T", [P, NCH * 16], F32, kind="ExternalInput").ap()
    bt_in = nc.dram_tensor("lB_T", [16, D], F32, kind="ExternalInput").ap()
    sm_in = nc.dram_tensor("sm", [64, 9], F32, kind="ExternalInput").ap()
    vcr_in = nc.dram_tensor("VcT_re", [64, 64], F32, kind="ExternalInput").ap()
    vci_in = nc.dram_tensor("VcT_im", [64, 64], F32, kind="ExternalInput").ap()
    scal_in = nc.dram_tensor("scal", [1, 3], F32, kind="ExternalInput").ap()
    i64_in = nc.dram_tensor("cI64", [64, 64], F32, kind="ExternalInput").ap()
    j128_in = nc.dram_tensor("cJ128", [P, P], BF16, kind="ExternalInput").ap()
    e8_in = nc.dram_tensor("cE8", [16, P], BF16, kind="ExternalInput").ap()
    ones_in = nc.dram_tensor("cOnes", [P, 4], F32, kind="ExternalInput").ap()
    onesb_in = nc.dram_tensor("cOnesB", [P, 4], BF16, kind="ExternalInput").ap()
    onesr_in = nc.dram_tensor("cOnesR", [1, P], F32, kind="ExternalInput").ap()
    onesrb_in = nc.dram_tensor("cOnesRB", [1, P], BF16, kind="ExternalInput").ap()

    out_t = nc.dram_tensor("out_s", [D, NTOK], F32, kind="ExternalOutput").ap()
    dbg = {}
    if DEBUG:
        for nm, shp in [("K", [1, P]), ("T0", [P, P]), ("BmT", [P, P]),
                        ("CBl", [P, P]), ("ACl", [P, P]), ("Ab", [64, P]),
                        ("x1", [P, NTOK]), ("W0", [P, P]), ("ln", [P, NTOK]),
                        ("smx", [64, 96]), ("smr", [1, 576]), ("g64", [64, 8])]:
            dbg[nm] = nc.dram_tensor("dbg_" + nm, shp, F32, kind="ExternalOutput").ap()

    wt_d = nc.dram_tensor("WT_d", [NCH, P, NCH, P], BF16, kind="Internal").ap()
    kpad_d = nc.dram_tensor("kpad_d", [1, 256], BF16, kind="Internal").ap()

    es = ExitStack()
    sb = lambda n, s, d: es.enter_context(nc.sbuf_tensor(n, s, d))
    ps_ = lambda n, s: es.enter_context(nc.psum_tensor(n, s, F32))
    sem = lambda n: es.enter_context(nc.semaphore(n))

    ux = sb("ux", [P, NCH * NTOK], BF16)
    ln1b = sb("ln1b", [P, 2 * NTOK], BF16)
    sst = [sb("sst0", [P, NTOK], BF16), sb("sst1", [P, NTOK], BF16)]
    m1b = sb("m1b", [P, NTOK], BF16)
    s1b = sb("s1b", [P, NTOK], BF16)
    s2b = sb("s2b", [P, NTOK], F32)
    xsq = sb("xsq", [P, 2 * NTOK], BF16)
    tevac = sb("tevac", [P, NTOK], BF16)
    outb = sb("outb", [P, 2 * NTOK], F32)
    T0b = sb("T0b", [P, P], BF16)
    BmT = sb("BmT", [P, P], BF16)
    CBl = sb("CBl", [P, P], BF16)
    ACl = sb("ACl", [P, P], BF16)
    wtb = sb("wtb", [P, 2 * D], BF16)
    colc = sb("colc", [P, 2], F32)
    midb = sb("midb", [16, NTOK], BF16)
    arowb = sb("arowb", [16, 1], BF16)
    atb = sb("atb", [P, NCH * 16], BF16)
    btb = sb("btb", [16, D], BF16)
    bstu = sb("bstu", [16, 2 * OSL], BF16)
    bstu16 = sb("bstu16", [16, 6 * OSL], BF16)
    e8g = sb("e8g", [16, P], BF16)
    slab = sb("slab", [P, 2 * OSL], I32)
    dc1 = sb("dc1", [P, 2 * OSL], I16)
    dc2 = sb("dc2", [P, 2 * OSL], I16)
    dc4 = sb("dc4", [P, 2 * OSL], I16)
    wlo = sb("wlo", [P, 2 * OSL], BF16)
    whi = sb("whi", [P, 2 * OSL], BF16)
    smt = sb("smt", [64, 9], F32)
    vctr = sb("vctr", [64, 64], F32)
    vcti = sb("vcti", [64, 64], F32)
    i64f = sb("i64f", [64, 64], F32)
    j128b = sb("j128b", [P, P], BF16)
    e8b = sb("e8b", [16, P], BF16)
    onesf = sb("onesf", [P, 4], F32)
    onesbf = sb("onesbf", [P, 4], BF16)
    onesrf = sb("onesrf", [1, P], F32)
    onesrbf = sb("onesrbf", [1, P], BF16)
    scalt = sb("scalt", [1, 4], F32)
    rowt = sb("rowt", [1, 2 * NTOK], F32)
    rowu = sb("rowu", [1, NTOK], F32)
    _pw_off = {}
    _pw_n = [0]
    for _j in range(13):
        _slots = (0, 1, 2, 3, 4, 5) if _j <= 6 else ((0, 1, 2, 3, 4) if _j <= 11 else (0, 1, 5))
        for _s in _slots:
            _pw_off[(_j, _s)] = _pw_n[0]
            _pw_n[0] += 64
    _pw_off[("scr", 0)] = _pw_n[0]
    _pw_n[0] += 64
    pw = sb("pw", [64, _pw_n[0]], F32)
    smx = sb("smx", [64, 24 * 4], F32)
    smr = sb("smr", [1, 576], F32)
    bkre = sb("bkre", [64, P], F32)
    bkim = sb("bkim", [64, P], F32)
    zre = sb("zre", [64, P], F32)
    zim = sb("zim", [64, P], F32)
    t128 = sb("t128", [P, 64], BF16)
    krow = sb("krow", [1, P], BF16)
    z128 = sb("z128", [1, P], BF16)
    g64 = sb("g64", [64, 8], F32)
    sc16 = sb("sc16", [16, 2], F32)

    pA = ps_("pA", [P, NTOK])
    pB = ps_("pB", [P, NTOK])
    pC = ps_("pC", [P, NTOK])
    pD = ps_("pD", [P, NTOK])

    s_sm = sem("s_sm")
    s_u = sem("s_u")
    KRC = {}

    def u_wait(e, c):
        e.wait_ge(s_u, 16 * (c + 1))
    s_pk = sem("s_pk")
    s_wo = sem("s_wo")
    s_wos = sem("s_wos")
    s_sl = sem("s_sl")
    s_sl2 = sem("s_sl2")
    s_bs = sem("s_bs")
    s_misc = sem("s_misc")
    s_wt = sem("s_wt")
    s_out = sem("s_out")
    s_tp = sem("s_tp")
    bV = sem("bV")
    bP = sem("bP")
    bS = sem("bS")
    c_ln = sem("c_ln")
    c_x1 = sem("c_x1")
    c_y = sem("c_y")
    c_s = sem("c_s")
    c_sev = sem("c_sev")
    c_sq = sem("c_sq")
    c_dec = sem("c_dec")
    c_e8 = sem("c_e8")
    c_ffn = sem("c_ffn")
    c_ev = sem("c_ev")
    c_cc = sem("c_cc")
    c_st = sem("c_st")
    c_sub = sem("c_sub")
    c_fh = sem("c_fh")
    SC2 = {}
    SC3 = {}
    s_dbg = sem("s_dbg")

    ops = {"V": [], "P": [], "S": [], "G": [], "Y": []}

    def em(key, f):
        ops[key].append(f)

    seqcnt = {"V": 0, "P": 0, "S": 0}
    seqsem = {"V": bV, "P": bP, "S": bS}

    def step(key, *fns, raw=False):
        waits = [(k, seqcnt[k]) for k in "VPS" if seqcnt[k] > 0]
        base = seqcnt[key]
        n = 1 if raw or not fns else len(fns)
        seqcnt[key] += n

        def f(eng, waits=waits, fns=fns, base=base, raw=raw):
            for k, v in waits:
                eng.wait_ge(seqsem[k], v)
            if raw or not fns:
                for g in fns:
                    g(eng)
                eng.sem_inc(seqsem[key], 1)
                return
            for i, g in enumerate(fns):
                if i > 0:
                    eng.wait_ge(seqsem[key], base + i)
                r = g(eng)
                ok = False
                if r is not None and hasattr(r, "then_inc"):
                    tn = type(getattr(r, "ins", r)).__name__
                    if "EventSemaphore" not in tn and "DMACopy" not in tn:
                        r.then_inc(seqsem[key], 1)
                        ok = True
                if not ok:
                    eng.sem_inc(seqsem[key], 1)
        em(key, f)

    def barrier():
        for k in "VPS":
            step(k)

    _smx_n = [0]

    def acol():
        i = _smx_n[0]
        _smx_n[0] += 1
        assert i < 96
        return smx[:, i:i + 1]

    _smr_n = [0]

    def arow_(w=1):
        i = _smr_n[0]
        _smr_n[0] += w
        assert _smr_n[0] <= 576
        return smr[:, i:i + w]

    def pwt(j, which):
        off = _pw_off[(j, which)]
        return pw[:, off:off + 64]

    UNITS = [(k // N_IPC, k % N_IPC) for k in range(NUNIT)]
    PSC = ["C"] * 16 + ["D" if k % 2 == 0 else "C" for k in range(16, 64)]

    def _uwait(k):
        for j in range(k - 1, -1, -1):
            if PSC[j] == PSC[k]:
                return j + 1
        return 0

    def y_main(y):
        y.dma_start(out=smt[:], in_=sm_in).then_inc(s_sm, 16)
        y.dma_start(out=vctr[:], in_=vcr_in).then_inc(s_sm, 16)
        y.dma_start(out=vcti[:], in_=vci_in).then_inc(s_sm, 16)
        y.dma_start(out=scalt[:, 0:3], in_=scal_in).then_inc(s_sm, 16)
        y.dma_start(out=i64f[:], in_=i64_in).then_inc(s_sm, 16)
        y.dma_start(out=j128b[:], in_=j128_in).then_inc(s_sm, 16)
        y.dma_start(out=e8b[:], in_=e8_in).then_inc(s_sm, 16)
        y.dma_start(out=onesf[:], in_=ones_in).then_inc(s_sm, 16)
        y.dma_start(out=onesbf[:], in_=onesb_in).then_inc(s_sm, 16)
        y.dma_start(out=onesrf[:], in_=onesr_in).then_inc(s_sm, 16)
        y.dma_start(out=onesrbf[:], in_=onesrb_in).then_inc(s_sm, 16)
        if PHASE >= 2:
            slab_load(y, 0, s_sl)
            slab_load(y, 1, s_sl)
            for k in range(16):
                y.wait_ge(c_dec, k + 1)
                w_out(y, k, s_wos)
                if k + 2 <= 17:
                    slab_load(y, k + 2, s_sl)
        y.wait_ge(bV, KRC["v"])
        y.dma_start(out=kpad_d[:, 0:128], in_=z128[:]).then_inc(s_tp, 16)
        y.dma_start(out=kpad_d[:, 128:256], in_=krow[:]).then_inc(s_tp, 16)
        y.wait_ge(s_tp, 32)
        for p in range(64):
            y.dma_start(out=T0b[p:p + 1, :],
                        in_=kpad_d[0:1, 128 - p:256 - p]).then_inc(s_tp, 16)
        for oc in (range(NCH) if PHASE >= 4 else []):
            osl = (oc * P) // OSL
            y.wait_ge(s_wos, 64 * 16)
            if osl >= 1:
                y.wait_ge(s_wo, 64 * 16 * osl)
            wd = wt_d[oc].rearrange("p ic q -> p (ic q)")
            if oc >= 2:
                y.wait_ge(c_fh, oc - 1)
            y.dma_start(
                out=wtb[:, (oc % 2) * D:(oc % 2) * D + D // 2],
                in_=wd[:, 0:D // 2],
            ).then_inc(s_wt, 16)
            if oc >= 2:
                y.wait_ge(c_ffn, oc - 1)
            y.dma_start(
                out=wtb[:, (oc % 2) * D + D // 2:(oc % 2) * D + D],
                in_=wd[:, D // 2:D],
            ).then_inc(s_wt, 16)
        y.wait_ge(s_out, 32 * 16)
    em("Y", y_main)

    def u_load(eng, c, sem_):
        eng.dma_start(
            out=ux[:, c * NTOK:(c + 1) * NTOK],
            in_=u_in[c * P:(c + 1) * P, :],
        ).then_inc(sem_, 16)

    wt4 = wt_d.rearrange("oc (p2 two) ic q -> ic two p2 oc q", two=2)

    def w_out(eng, k, sem_):
        osl, ipc = UNITS[k]
        for par, wsrc in ((0, wlo), (1, whi)):
            t = wsrc[:, (k % 2) * OSL:(k % 2 + 1) * OSL].rearrange(
                "p (ocs q) -> p ocs q", q=P)
            for h_ in range(2):
                eng.dma_start(
                    out=wt4[2 * ipc + h_, par, :, osl * 8:(osl + 1) * 8, :],
                    in_=t[h_ * 64:(h_ + 1) * 64],
                ).then_inc(sem_, 16)

    def slab_load(eng, k, sem_):
        osl, ipc = UNITS[k]
        eng.dma_start(
            out=slab[:, (k % 2) * OSL:(k % 2 + 1) * OSL],
            in_=pk_in[ipc * P:(ipc + 1) * P, osl * OSL:(osl + 1) * OSL],
        ).then_inc(sem_, 16)

    def bstu_load(g, k):
        osl, ipc = UNITS[k]
        out = bstu16[:, k * OSL:(k + 1) * OSL] if k < 6 else \
            bstu[:, (k % 2) * OSL:(k % 2 + 1) * OSL]
        g.dma_start(
            out=out,
            in_=bst_in[:, ipc * D + osl * OSL:ipc * D + (osl + 1) * OSL],
        ).then_inc(s_bs, 16)

    def g_main(g):
        g.dma_start(out=atb[:], in_=at_in).then_inc(s_misc, 16)
        g.dma_start(out=btb[:], in_=bt_in).then_inc(s_misc, 16)
        if PHASE >= 2:
            for k in range(8):
                bstu_load(g, k)
        for c in range(NCH):
            u_load(g, c, s_u)
        if PHASE >= 2:
            for k in range(8, 16):
                g.wait_ge(c_dec, k - 1)
                bstu_load(g, k)
        if PHASE < 2:
            return
        for k in (16, 17):
            g.wait_ge(c_dec, k - 1)
            bstu_load(g, k)
        for k in range(16, NUNIT):
            g.wait_ge(c_dec, k + 1)
            w_out(g, k, s_wo)
            if k + 2 < NUNIT:
                bstu_load(g, k + 2)
                slab_load(g, k + 2, s_sl2)
    em("G", g_main)

    def decode_unit_v(k):
        osl, ipc = UNITS[k]
        sl = slab[:, (k % 2) * OSL:(k % 2 + 1) * OSL]
        pscale = pC if PSC[k] == "C" else pD
        lo = wlo[:, (k % 2) * OSL:(k % 2 + 1) * OSL]
        hi = whi[:, (k % 2) * OSL:(k % 2 + 1) * OSL]
        c16a = dc1[:, 0:2 * OSL]
        t1s = dc2[:, 0:2 * OSL]

        def f(e, k=k):
            if k <= 17:
                e.wait_ge(s_sl, 16 * (k + 1))
            else:
                e.wait_ge(s_sl2, 16 * (k - 17))
            if 2 <= k <= 17:
                e.wait_ge(s_wos, 64 * (k - 1))
            elif k >= 18:
                e.wait_ge(s_wo, 64 * (k - 17))
            e.wait_ge(c_e8, k + 1)
            dm = dc4[:].bitcast(I32)
            e.tensor_scalar(dm, sl, 4097, None, OP.mult)
            dv = dm.bitcast(I16)
            e.tensor_scalar(c16a, dv, 7, None, OP.bitwise_and)
            e.tensor_scalar(t1s, dv, 8, 2, OP.bitwise_and, OP.logical_shift_right)
            e.tensor_scalar(t1s, t1s, -1, 1, OP.mult, OP.add)
            e._custom_dve(FP4_OP, out=dc4[:, 0:2 * OSL].bitcast(BF16),
                          in0=c16a, in1=t1s, s0=4.0, s1=12.0)
            bv = dc4[:, 0:2 * OSL].bitcast(BF16)
            e.tensor_tensor(lo, bv[:, 0::2], pscale[:, 0:OSL], OP.mult)
            e.tensor_tensor(hi, bv[:, 1::2], pscale[:, 0:OSL], OP.mult).then_inc(c_dec, 1)
        return f

    def st1_sq_v(c):
        def f(e, c=c):
            u_wait(e, c)
            if c >= 2:
                e.wait_ge(c_st, c - 1)
            e.tensor_mul(xsq[:, (c % 2) * NTOK:(c % 2 + 1) * NTOK],
                         ux[:, c * NTOK:(c + 1) * NTOK],
                         ux[:, c * NTOK:(c + 1) * NTOK]).then_inc(c_sq, 1)
        return f

    def st1_mm_p(c):
        def f(e, c=c):
            u_wait(e, c)
            e.wait_ge(c_sq, c + 1)
            for h_ in range(2):
                e.matmul(pA[0:1, h_ * H:(h_ + 1) * H], onesbf[:, 0:1],
                         ux[:, c * NTOK + h_ * H:c * NTOK + (h_ + 1) * H],
                         start=(c == 0), stop=(c == NCH - 1))
                ins = e.matmul(pB[0:1, h_ * H:(h_ + 1) * H], onesbf[:, 0:1],
                         xsq[:, (c % 2) * NTOK + h_ * H:(c % 2) * NTOK + (h_ + 1) * H],
                         start=(c == 0), stop=(c == NCH - 1))
            ins.then_inc(c_st, 1)
        return f

    def e8_unit_p(k):
        osl, ipc = UNITS[k]
        pscale = pC if PSC[k] == "C" else pD

        def f(e, k=k, osl=osl, ipc=ipc):
            e.wait_ge(s_bs, 16 * (k + 1))
            w = _uwait(k)
            if w > 0:
                e.wait_ge(c_dec, w)
            bs_ = bstu16[:, k * OSL:(k + 1) * OSL] if k < 6 else \
                bstu[:, (k % 2) * OSL:(k % 2 + 1) * OSL]
            e.matmul(pscale[:, 0:H], e8g[:], bs_[:, 0:H],
                     start=True, stop=True)
            e.matmul(pscale[:, H:OSL], e8g[:], bs_[:, H:OSL],
                     start=True, stop=True).then_inc(c_e8, 1)
        return f


    ALLSM = 11 * 16

    step("S",
         lambda e: e.wait_ge(s_sm, ALLSM),
         lambda e: e.activation(scalt[:, 3:4], scalt[:, 1:2], AF.Exp))
    step("V", lambda e: e.wait_ge(s_sm, ALLSM), lambda e: e.memset(smr[:], 0.0))
    step("P", lambda e: e.wait_ge(s_sm, ALLSM))

    step("P",
         lambda e: e.matmul(pD[0:64, 512:516], onesrf[:, 0:64], scalt[:, 0:4],
                            start=True, stop=True),
         lambda e: e.matmul(pD[0:64, 0:1], vctr[:], smt[:, 6:7],
                            start=True, stop=True),
         lambda e: e.matmul(pD[0:64, 1:2], vcti[:], smt[:, 6:7],
                            start=True, stop=True))
    alph = acol(); Dcol = acol(); Bre = acol(); Bim = acol(); stepc = acol()
    step("V",
         lambda e: e.tensor_scalar(alph, pD[0:64, 515:516], 0.5, None, OP.mult),
         lambda e: e.tensor_copy(stepc, pD[0:64, 515:516]),
         lambda e: e.tensor_copy(Dcol, pD[0:64, 512:513]),
         lambda e: e.tensor_copy(sc16[:, 0:1], pD[0:16, 514:515]),
         lambda e: e.tensor_copy(Bre, pD[0:64, 0:1]),
         lambda e: e.tensor_copy(Bim, pD[0:64, 1:2]))
    step("V", lambda e: e.tensor_scalar(e8g[:], e8b[:], sc16[:, 0:1], None, OP.mult))

    lam_re = smt[:, 0:1]; lam_im = smt[:, 1:2]
    p_re = smt[:, 2:3]; p_im = smt[:, 3:4]
    q_re = smt[:, 4:5]; q_im = smt[:, 5:6]
    ct_re = smt[:, 7:8]; ct_imr = smt[:, 8:9]
    qc_im = acol()
    dr = acol(); di = acol(); mag = acol(); rec = acol()
    d_re = acol(); d_im = acol()
    step("V",
         lambda e: e.tensor_scalar(qc_im, q_im, -1.0, None, OP.mult),
         lambda e: e.tensor_scalar(dr, lam_re, alph, -1.0, OP.mult, OP.mult),
         lambda e: e.tensor_scalar(dr, dr, 1.0, None, OP.add),
         lambda e: e.tensor_scalar(di, lam_im, alph, -1.0, OP.mult, OP.mult),
         lambda e: e.tensor_mul(mag, dr, dr),
         lambda e: e.tensor_mul(rec, di, di),
         lambda e: e.tensor_add(mag, mag, rec),
         lambda e: e.reciprocal(rec, mag),
         lambda e: e.tensor_mul(d_re, dr, rec),
         lambda e: e.tensor_mul(d_im, di, rec),
         lambda e: e.tensor_scalar(d_im, d_im, -1.0, None, OP.mult))

    def cmul_cols(o_re, o_im, a_re, a_im, b_re, b_im, t1, t2):
        return [
            lambda e: e.tensor_mul(t1, a_re, b_re),
            lambda e: e.tensor_mul(t2, a_im, b_im),
            lambda e: e.tensor_sub(o_re, t1, t2),
            lambda e: e.tensor_mul(t1, a_re, b_im),
            lambda e: e.tensor_mul(t2, a_im, b_re),
            lambda e: e.tensor_add(o_im, t1, t2),
        ]

    tc1 = acol(); tc2 = acol()
    u1_re = acol(); u1_im = acol()
    v1_re = acol(); v1_im = acol()
    e_re = acol(); e_im = acol()
    a0_re = acol(); a0_im = acol()
    se_re = acol(); se_im = acol()
    sB_re = acol(); sB_im = acol()
    v1e_re = acol(); v1e_im = acol()
    dB_re = acol(); dB_im = acol()
    step("V", *cmul_cols(u1_re, u1_im, d_re, d_im, p_re, p_im, tc1, tc2))
    step("V", *cmul_cols(v1_re, v1_im, q_re, qc_im, d_re, d_im, tc1, tc2))
    step("V",
         lambda e: e.tensor_scalar(e_re, lam_re, alph, None, OP.mult),
         lambda e: e.tensor_scalar(e_re, e_re, 1.0, None, OP.add),
         lambda e: e.tensor_scalar(e_im, lam_im, alph, None, OP.mult))
    step("V", *cmul_cols(a0_re, a0_im, d_re, d_im, e_re, e_im, tc1, tc2))
    step("V", *cmul_cols(v1e_re, v1e_im, v1_re, v1_im, e_re, e_im, tc1, tc2))
    step("V", *cmul_cols(dB_re, dB_im, d_re, d_im, Bre, Bim, tc1, tc2))
    step("V", *cmul_cols(se_re, se_im, v1_re, v1_im, p_re, p_im, tc1, tc2))
    step("V", *cmul_cols(sB_re, sB_im, v1_re, v1_im, Bre, Bim, tc1, tc2))

    sig = arow_(2)
    v1B = arow_(2)
    step("P",
         lambda e: e.matmul(pD[0:1, 512:513], onesf[0:64, 0:1], se_re, start=True, stop=True),
         lambda e: e.matmul(pD[0:1, 513:514], onesf[0:64, 0:1], se_im, start=True, stop=True),
         lambda e: e.matmul(pD[0:1, 514:515], onesf[0:64, 0:1], sB_re, start=True, stop=True),
         lambda e: e.matmul(pD[0:1, 515:516], onesf[0:64, 0:1], sB_im, start=True, stop=True))
    step("V",
         lambda e: e.tensor_copy(sig, pD[0:1, 512:514]),
         lambda e: e.tensor_copy(v1B, pD[0:1, 514:516]))

    arow2 = arow_(2)
    den = arow_(2)
    beta = arow_(2)
    gam = arow_(2)
    t1r = arow_(2)
    t0r = arow_(1)

    def cmul_row(o, a, b, t):
        return [
            lambda e: e.tensor_mul(t[:, 0:1], a[:, 0:1], b[:, 0:1]),
            lambda e: e.tensor_mul(o[:, 0:1], a[:, 1:2], b[:, 1:2]),
            lambda e: e.tensor_sub(o[:, 0:1], t[:, 0:1], o[:, 0:1]),
            lambda e: e.tensor_mul(t[:, 1:2], a[:, 0:1], b[:, 1:2]),
            lambda e: e.tensor_mul(o[:, 1:2], a[:, 1:2], b[:, 0:1]),
            lambda e: e.tensor_add(o[:, 1:2], t[:, 1:2], o[:, 1:2]),
        ]

    def cinv_scale_row(o, num_s, den_, t):
        return [
            lambda e: e.tensor_mul(t[:, 0:1], den_[:, 0:1], den_[:, 0:1]),
            lambda e: e.tensor_mul(t[:, 1:2], den_[:, 1:2], den_[:, 1:2]),
            lambda e: e.tensor_add(t[:, 0:1], t[:, 0:1], t[:, 1:2]),
            lambda e: e.reciprocal(t[:, 0:1], t[:, 0:1]),
            lambda e: e.tensor_mul(o[:, 0:1], den_[:, 0:1], t[:, 0:1]),
            lambda e: e.tensor_mul(o[:, 1:2], den_[:, 1:2], t[:, 0:1]),
            lambda e: e.tensor_scalar(o[:, 1:2], o[:, 1:2], -1.0, None, OP.mult),
            lambda e: e.tensor_scalar(o[:, 0:1], o[:, 0:1], num_s, None, OP.mult),
            lambda e: e.tensor_scalar(o[:, 1:2], o[:, 1:2], num_s, None, OP.mult),
        ]

    alphr = scalt[:, 1:2]
    alrow = arow_(1)
    step("V",
         lambda e: e.tensor_scalar(alrow, scalt[:, 3:4], 0.5, None, OP.mult))
    step("V",
         lambda e: e.tensor_scalar(den[:, 0:2], sig[:, 0:2], alrow, None, OP.mult),
         lambda e: e.tensor_scalar(den[:, 0:1], den[:, 0:1], 1.0, None, OP.add),
         *cinv_scale_row(beta, alrow, den, t1r))
    step("V",
         *cmul_row(arow2, beta, sig, t1r),
         lambda e: e.tensor_scalar(arow2[:, 0:2], arow2[:, 0:2], -1.0, None, OP.mult),
         lambda e: e.tensor_scalar(arow2[:, 0:1], arow2[:, 0:1], 1.0, None, OP.add),
         lambda e: e.tensor_scalar(gam[:, 0:2], arow2[:, 0:2], alrow, None, OP.mult))
    bv1B = arow_(2)
    step("V", *cmul_row(bv1B, beta, v1B, t1r))

    assert _smr_n[0] == 18
    step("P", lambda e: e.matmul(pD[0:64, 0:6], onesrf[:, 0:64],
                                 smr[:, 8:14], start=True, stop=True))
    gb_c = g64[:, 2:3]; gb_ci = g64[:, 3:4]
    bb_c = g64[:, 4:5]; bb_ci = g64[:, 5:6]
    bv_c = g64[:, 6:7]; bv_ci = g64[:, 7:8]
    step("P", lambda e: e.matmul(pD[0:64, 6:10], onesrf[:, 0:64],
                                 smr[:, 14:18], start=True, stop=True))
    step("V",
         lambda e: e.tensor_copy(g64[:, 4:6], pD[0:64, 0:2]),
         lambda e: e.tensor_copy(g64[:, 2:4], pD[0:64, 2:4]),
         lambda e: e.tensor_copy(g64[:, 6:8], pD[0:64, 8:10]))

    w1_re = acol(); w1_im = acol()
    Bb_re = acol(); Bb_im = acol()
    step("V",
         lambda e: e.tensor_scalar(tc1, q_re, gb_c, None, OP.mult),
         lambda e: e.tensor_scalar(tc2, qc_im, gb_ci, None, OP.mult),
         lambda e: e.tensor_sub(w1_re, tc1, tc2),
         lambda e: e.tensor_scalar(tc1, q_re, gb_ci, None, OP.mult),
         lambda e: e.tensor_scalar(tc2, qc_im, gb_c, None, OP.mult),
         lambda e: e.tensor_add(w1_im, tc1, tc2),
         lambda e: e.tensor_scalar(tc1, v1e_re, bb_c, None, OP.mult),
         lambda e: e.tensor_scalar(tc2, v1e_im, bb_ci, None, OP.mult),
         lambda e: e.tensor_sub(tc1, tc1, tc2),
         lambda e: e.tensor_add(w1_re, w1_re, tc1),
         lambda e: e.tensor_scalar(tc1, v1e_re, bb_ci, None, OP.mult),
         lambda e: e.tensor_scalar(tc2, v1e_im, bb_c, None, OP.mult),
         lambda e: e.tensor_add(tc1, tc1, tc2),
         lambda e: e.tensor_add(w1_im, w1_im, tc1))
    step("V",
         lambda e: e.tensor_scalar(tc1, u1_re, bv_c, None, OP.mult),
         lambda e: e.tensor_scalar(tc2, u1_im, bv_ci, None, OP.mult),
         lambda e: e.tensor_sub(tc1, tc1, tc2),
         lambda e: e.tensor_sub(Bb_re, dB_re, tc1),
         lambda e: e.tensor_scalar(tc1, u1_re, bv_ci, None, OP.mult),
         lambda e: e.tensor_scalar(tc2, u1_im, bv_c, None, OP.mult),
         lambda e: e.tensor_add(tc1, tc1, tc2),
         lambda e: e.tensor_sub(Bb_im, dB_im, tc1),
         lambda e: e.tensor_scalar(Bb_re, Bb_re, stepc, None, OP.mult),
         lambda e: e.tensor_scalar(Bb_im, Bb_im, stepc, None, OP.mult))

    u1r = arow_(64); u1i = arow_(64); w1r = arow_(64); w1i = arow_(64)
    nu1r = arow_(64); nu1i = arow_(64)
    step("P",
         lambda e: e.transpose(pD[0:1, 0:64], u1_re, i64f[:]),
         lambda e: e.transpose(pD[0:1, 64:128], u1_im, i64f[:]),
         lambda e: e.transpose(pD[0:1, 128:192], w1_re, i64f[:]),
         lambda e: e.transpose(pD[0:1, 192:256], w1_im, i64f[:]))
    step("V",
         lambda e: e.tensor_copy(u1r, pD[0:1, 0:64]),
         lambda e: e.tensor_copy(u1i, pD[0:1, 64:128]),
         lambda e: e.tensor_copy(w1r, pD[0:1, 128:192]),
         lambda e: e.tensor_copy(w1i, pD[0:1, 192:256]),
         lambda e: e.tensor_scalar(nu1r, u1r, -1.0, None, OP.mult),
         lambda e: e.tensor_scalar(nu1i, u1i, -1.0, None, OP.mult))
    step("P",
         lambda e: e.matmul(pD[0:64, 512:576], nu1r, w1r, start=True, stop=False),
         lambda e: e.matmul(pD[0:64, 512:576], u1i, w1i, start=False, stop=True),
         lambda e: e.matmul(pD[0:64, 576:640], nu1r, w1i, start=True, stop=False),
         lambda e: e.matmul(pD[0:64, 576:640], nu1i, w1r, start=False, stop=True))
    dgt = pw[:, _pw_off[("scr", 0)]:_pw_off[("scr", 0)] + 64]
    step("V",
         lambda e: e.tensor_scalar(dgt, i64f[:], a0_re, None, OP.mult),
         lambda e: e.tensor_add(pwt(0, 0), dgt, pD[0:64, 512:576]),
         lambda e: e.tensor_scalar(dgt, i64f[:], a0_im, None, OP.mult),
         lambda e: e.tensor_add(pwt(0, 1), dgt, pD[0:64, 576:640]),
         lambda e: e.tensor_scalar(pwt(0, 5), pwt(0, 1), -1.0, None, OP.mult))

    for j in range(12):
        step("P",
             lambda e, j=j: e.transpose(pD[0:64, 0:64], pwt(j, 0), i64f[:]),
             lambda e, j=j: e.transpose(pD[0:64, 64:128], pwt(j, 1), i64f[:]))
        step("V",
             lambda e, j=j: e.tensor_copy(pwt(j, 2), pD[0:64, 0:64]),
             lambda e, j=j: e.tensor_copy(pwt(j, 3), pD[0:64, 64:128]),
             lambda e, j=j: e.tensor_scalar(pwt(j, 4), pD[0:64, 64:128], -1.0, None, OP.mult))
        step("P",
             lambda e, j=j: e.matmul(pD[0:64, 512:576], pwt(j, 2), pwt(j, 0), start=True, stop=False),
             lambda e, j=j: e.matmul(pD[0:64, 512:576], pwt(j, 4), pwt(j, 1), start=False, stop=True),
             lambda e, j=j: e.matmul(pD[0:64, 576:640], pwt(j, 2), pwt(j, 1), start=True, stop=False),
             lambda e, j=j: e.matmul(pD[0:64, 576:640], pwt(j, 3), pwt(j, 0), start=False, stop=True))
        if PHASE >= 2:
            em("P", e8_unit_p(j))
        if j >= 1:
            em("P", st1_mm_p(2 * j - 2))
            em("P", st1_mm_p(2 * j - 1))
        fns = [lambda e, j=j: e.tensor_copy(pwt(j + 1, 0), pD[0:64, 512:576]),
               lambda e, j=j: e.tensor_copy(pwt(j + 1, 1), pD[0:64, 576:640])]
        if j + 1 <= 6 or j + 1 == 12:
            fns.append(lambda e, j=j: e.tensor_scalar(pwt(j + 1, 5), pD[0:64, 576:640], -1.0, None, OP.mult))
        step("V", *fns)
        if PHASE >= 2:
            em("V", decode_unit_v(j))
        em("V", st1_sq_v(2 * j))
        em("V", st1_sq_v(2 * j + 1))

    w_re = acol(); w_im = acol(); nw_im = acol(); ctn_im = acol()
    step("V",
         lambda e: e.tensor_copy(w_re, ct_re),
         lambda e: e.tensor_scalar(ctn_im, ct_imr, -1.0, None, OP.mult),
         lambda e: e.tensor_copy(w_im, ctn_im))
    for it in range(NEU):
        step("P",
             lambda e: e.matmul(pD[0:64, 0:1], pwt(12, 0), w_re, start=True, stop=False),
             lambda e: e.matmul(pD[0:64, 0:1], pwt(12, 5), w_im, start=False, stop=True),
             lambda e: e.matmul(pD[0:64, 1:2], pwt(12, 0), w_im, start=True, stop=False),
             lambda e: e.matmul(pD[0:64, 1:2], pwt(12, 1), w_re, start=False, stop=True))
        step("V",
             lambda e: e.tensor_add(w_re, pD[0:64, 0:1], ct_re),
             lambda e: e.tensor_add(w_im, pD[0:64, 1:2], ctn_im))
    step("V", lambda e: e.tensor_scalar(nw_im, w_im, -1.0, None, OP.mult))

    step("V",
         lambda e: e.tensor_copy(bkre[:, 0:1], Bb_re),
         lambda e: e.tensor_copy(bkim[:, 0:1], Bb_im))
    for j in range(7):
        w_ = 1 << j
        step("P",
             lambda e, j=j, w_=w_: e.matmul(pD[0:64, 0:w_], pwt(j, 2), bkre[:, 0:w_], start=True, stop=False),
             lambda e, j=j, w_=w_: e.matmul(pD[0:64, 0:w_], pwt(j, 4), bkim[:, 0:w_], start=False, stop=True),
             lambda e, j=j, w_=w_: e.matmul(pD[0:64, 64:64 + w_], pwt(j, 2), bkim[:, 0:w_], start=True, stop=False),
             lambda e, j=j, w_=w_: e.matmul(pD[0:64, 64:64 + w_], pwt(j, 3), bkre[:, 0:w_], start=False, stop=True))
        if PHASE >= 2 and j >= 3:
            em("P", e8_unit_p(9 + j))
        if j == 0:
            em("P", st1_mm_p(22))
            em("P", st1_mm_p(23))
        else:
            em("P", st1_mm_p(23 + j))
        step("V",
             lambda e, w_=w_: e.tensor_copy(bkre[:, w_:2 * w_], pD[0:64, 0:w_]),
             lambda e, w_=w_: e.tensor_copy(bkim[:, w_:2 * w_], pD[0:64, 64:64 + w_]))
        if PHASE >= 2 and j >= 3:
            em("V", decode_unit_v(9 + j))
        em("V", st1_sq_v(24 + j))

    em("V", st1_sq_v(31))
    em("P", st1_mm_p(30))
    em("P", st1_mm_p(31))

    step("P",
         lambda e: e.matmul(pD[0:64, 0:1], pwt(0, 0), w_re, start=True, stop=False),
         lambda e: e.matmul(pD[0:64, 0:1], pwt(0, 5), w_im, start=False, stop=True),
         lambda e: e.matmul(pD[0:64, 1:2], pwt(0, 0), w_im, start=True, stop=False),
         lambda e: e.matmul(pD[0:64, 1:2], pwt(0, 1), w_re, start=False, stop=True))
    step("V",
         lambda e: e.tensor_copy(zre[:, 0:1], pD[0:64, 0:1]),
         lambda e: e.tensor_copy(zim[:, 0:1], pD[0:64, 1:2]))
    for j in range(7):
        w_ = 1 << j
        step("P",
             lambda e, j=j, w_=w_: e.matmul(pD[0:64, 0:w_], pwt(j, 0), zre[:, 0:w_], start=True, stop=False),
             lambda e, j=j, w_=w_: e.matmul(pD[0:64, 0:w_], pwt(j, 5), zim[:, 0:w_], start=False, stop=True),
             lambda e, j=j, w_=w_: e.matmul(pD[0:64, 64:64 + w_], pwt(j, 0), zim[:, 0:w_], start=True, stop=False),
             lambda e, j=j, w_=w_: e.matmul(pD[0:64, 64:64 + w_], pwt(j, 1), zre[:, 0:w_], start=False, stop=True))
        step("V",
             lambda e, w_=w_: e.tensor_copy(zre[:, w_:2 * w_], pD[0:64, 0:w_]),
             lambda e, w_=w_: e.tensor_copy(zim[:, w_:2 * w_], pD[0:64, 64:64 + w_]))

    kfr = arow_(128)
    step("P",
         lambda e: e.matmul(pD[0:1, 0:128], w_re, bkre[:], start=True, stop=False),
         lambda e: e.matmul(pD[0:1, 0:128], nw_im, bkim[:], start=False, stop=True))
    step("V",
         lambda e: e.tensor_copy(kfr, pD[0:1, 0:128]),
         lambda e: e.tensor_scalar(kfr[:, 0:1], kfr[:, 0:1], scalt[:, 0:1], None, OP.add),
         lambda e: e.tensor_copy(krow[:], kfr),
         lambda e: e.memset(z128[:], 0.0))
    KRC["v"] = seqcnt["V"]

    def s_toep2(e):
        e.wait_ge(s_tp, 32)
        for p in range(64, P):
            e.dma_start(out=T0b[p:p + 1, :],
                        in_=kpad_d[0:1, 128 - p:256 - p]).then_inc(s_tp, 16)
    step("S", s_toep2, raw=True)

    step("P", lambda e: e.transpose(pD[0:128, 0:64], bkre[:], i64f[:]))
    step("V", lambda e: e.tensor_copy(t128[:], pD[0:128, 0:64]))
    step("P", lambda e: e.matmul(pD[0:128, 512:576], j128b[:], t128[:], start=True, stop=True))
    step("V", lambda e: e.tensor_copy(BmT[:, 0:64], pD[0:128, 512:576]))
    step("P", lambda e: e.transpose(pD[0:128, 0:64], bkim[:], i64f[:]))
    step("V", lambda e: e.tensor_copy(t128[:], pD[0:128, 0:64]))
    step("P", lambda e: e.matmul(pD[0:128, 512:576], j128b[:], t128[:], start=True, stop=True))
    step("V",
         lambda e: e.tensor_copy(BmT[:, 64:128], pD[0:128, 512:576]),
         lambda e: e.tensor_copy(CBl[0:64, :], zre[:]),
         lambda e: e.tensor_scalar(CBl[64:128, :], zim[:], -1.0, None, OP.mult),
         lambda e: e.tensor_copy(ACl[0:64, 0:64], pwt(7, 2)),
         lambda e: e.tensor_copy(ACl[0:64, 64:128], pwt(7, 3)),
         lambda e: e.tensor_scalar(ACl[64:128, 0:64], pwt(7, 3), -1.0, None, OP.mult),
         lambda e: e.tensor_copy(ACl[64:128, 64:128], pwt(7, 2)),
         lambda e: e.memset(sst[0][:], 0.0))

    step("S")
    step("P")

    def rows_math(psum_sum, psum_sq, mrow, srow):
        return [
            lambda e: e.tensor_scalar(mrow, psum_sum, 1.0 / 4096.0, None, OP.mult),
            lambda e: e.tensor_scalar(rowu[:, 0:NTOK], psum_sq, 1.0 / 4096.0, None, OP.mult),
            lambda e: e.tensor_mul(srow, mrow, mrow),
            lambda e: e.tensor_sub(rowu[:, 0:NTOK], rowu[:, 0:NTOK], srow),
            lambda e: e.tensor_scalar(rowu[:, 0:NTOK], rowu[:, 0:NTOK], 1e-5, None, OP.add),
            lambda e: e.reciprocal(srow, rowu[:, 0:NTOK]),
        ]
    step("V",
         lambda e: e.wait_ge(c_st, NCH),
         *rows_math(pA[0:1, :], pB[0:1, :],
                    rowt[:, 0:NTOK], rowt[:, NTOK:2 * NTOK]))
    step("S", lambda e: e.activation(rowt[:, NTOK:2 * NTOK],
                                     rowt[:, NTOK:2 * NTOK], AF.Sqrt))
    step("V",
         lambda e: e.tensor_copy(xsq[0:1, 0:NTOK], rowt[:, 0:NTOK]),
         lambda e: e.tensor_copy(ln1b[0:1, 0:NTOK], rowt[:, NTOK:2 * NTOK]))
    step("P",
         lambda e: e.matmul(pA[:, 0:H], onesrbf[:], xsq[0:1, 0:H], start=True, stop=True),
         lambda e: e.matmul(pA[:, H:NTOK], onesrbf[:], xsq[0:1, H:NTOK], start=True, stop=True),
         lambda e: e.matmul(pB[:, 0:H], onesrbf[:], ln1b[0:1, 0:H], start=True, stop=True),
         lambda e: e.matmul(pB[:, H:NTOK], onesrbf[:], ln1b[0:1, H:NTOK], start=True, stop=True))
    step("S",
         lambda e: e.copy(m1b[:], pA[:, 0:NTOK]),
         lambda e: e.copy(s1b[:], pB[:, 0:NTOK]))
    barrier()

    def st2_sq(e, c):
        e.wait_ge(c_x1, c + 1)
        if c >= 2:
            e.wait_ge(c_st, NCH + c - 1)
        e.activation(xsq[:, (c % 2) * NTOK:(c % 2 + 1) * NTOK],
                     ux[:, c * NTOK:(c + 1) * NTOK], AF.Square).then_inc(c_sq, 1)

    def st2_mm(e, c):
        if c == 0:
            e.wait_ge(c_dec, 32)
        e.wait_ge(c_sq, NCH + c + 1)
        for h_ in range(2):
            e.matmul(pC[0:1, h_ * H:(h_ + 1) * H], onesbf[:, 0:1],
                     ux[:, c * NTOK + h_ * H:c * NTOK + (h_ + 1) * H],
                     start=(c == 0), stop=(c == NCH - 1))
            ins = e.matmul(pD[0:1, h_ * H:(h_ + 1) * H], onesbf[:, 0:1],
                     xsq[:, (c % 2) * NTOK + h_ * H:(c % 2) * NTOK + (h_ + 1) * H],
                     start=(c == 0), stop=(c == NCH - 1))
        ins.then_inc(c_st, 1)

    def v_conv(e):
        for c in range(NCH):
            lt = ln1b[:, (c % 2) * NTOK:(c % 2 + 1) * NTOK]
            uc = ux[:, c * NTOK:(c + 1) * NTOK]
            if c >= 2:
                e.wait_ge(c_s, c - 1)
            e.tensor_sub(lt, uc, m1b[:])
            e.tensor_mul(lt, lt, s1b[:]).then_inc(c_ln, 1)
            e.wait_ge(c_y, c + 1)
            e.tensor_add(uc[:, 0:H], uc[:, 0:H], pA[:, 0:H])
            e.tensor_add(uc[:, H:NTOK], uc[:, H:NTOK], pA[:, H:NTOK]).then_inc(c_x1, 1)
            if PHASE >= 2 and c < 16:
                decode_unit_v(16 + c)(e)
    em("V", v_conv)

    def p_conv(e):
        for c in range(NCH):
            lt = ln1b[:, (c % 2) * NTOK:(c % 2 + 1) * NTOK]
            sprev = sst[c % 2]
            e.wait_ge(c_ln, c + 1)
            e.wait_ge(s_tp, 32 + 128 * 16)
            if c >= 1:
                e.wait_ge(c_x1, c)
                e.wait_ge(c_sev, c)
            for h_ in range(2):
                sl_ = slice(h_ * H, (h_ + 1) * H)
                e.matmul(pA[:, sl_], T0b[:], lt[:, sl_], start=True, stop=False)
                ins = e.matmul(pA[:, sl_], CBl[:], sprev[:, sl_], start=False, stop=True)
            ins.then_inc(c_y, 1)
            for h_ in range(2):
                sl_ = slice(h_ * H, (h_ + 1) * H)
                e.matmul(pB[:, sl_], ACl[:], sprev[:, sl_], start=True, stop=False)
                ins = e.matmul(pB[:, sl_], BmT[:], lt[:, sl_], start=False, stop=True)
            ins.then_inc(c_s, 1)
            if PHASE >= 2 and c < 16:
                e8_unit_p(16 + c)(e)
            if PHASE >= 3 and 18 <= c:
                st2_mm(e, c - 18)
    em("P", p_conv)

    def s_conv(e):
        for c in range(NCH):
            e.wait_ge(c_s, c + 1)
            snew = sst[(c + 1) % 2]
            e.copy(snew[:, 0:H], pB[:, 0:H])
            e.copy(snew[:, H:NTOK], pB[:, H:NTOK]).then_inc(c_sev, 1)
            if PHASE >= 3 and 17 <= c:
                st2_sq(e, c - 17)
    em("S", s_conv)

    barrier()

    def s_sq2(e):
        for c in range(15, NCH):
            st2_sq(e, c)
    em("S", s_sq2)

    def p_st2(e):
        for c in range(14, NCH):
            st2_mm(e, c)
    em("P", p_st2)
    step("S")
    step("P")

    step("V",
         lambda e: e.wait_ge(c_st, 2 * NCH),
         *rows_math(pC[0:1, :], pD[0:1, :],
                    rowt[:, 0:NTOK], rowt[:, NTOK:2 * NTOK]))
    step("S", lambda e: e.activation(rowt[:, NTOK:2 * NTOK],
                                     rowt[:, NTOK:2 * NTOK], AF.Sqrt))
    step("S", lambda e: e.activation(rowu[:, 0:NTOK], rowu[:, 0:NTOK], AF.Sqrt))
    step("V", lambda e: e.tensor_mul(rowu[:, 0:NTOK], rowu[:, 0:NTOK], rowt[:, 0:NTOK]))
    step("S", lambda e: e.copy(xsq[0:1, 0:NTOK], rowu[:, 0:NTOK]))
    step("P",
         lambda e: e.matmul(pA[:, 0:H], onesrf[:], rowt[:, NTOK:NTOK + H], start=True, stop=True),
         lambda e: e.matmul(pA[:, H:NTOK], onesrf[:], rowt[:, NTOK + H:2 * NTOK], start=True, stop=True),
         lambda e: e.matmul(pB[:, 0:H], onesrf[:], rowt[:, 0:H], start=True, stop=True),
         lambda e: e.matmul(pB[:, H:NTOK], onesrf[:], rowt[:, H:NTOK], start=True, stop=True))
    m2b = ln1b[:, 0:NTOK]
    step("S",
         lambda e: e.copy(s2b[:], pA[:, 0:NTOK]),
         lambda e: e.copy(m2b, pB[:, 0:NTOK]))
    SC2["v"] = seqcnt["S"]

    def v_sub(e):
        e.wait_ge(bS, SC2["v"])
        for c in range(NCH):
            e.wait_ge(c_st, NCH + c + 1)
            e.tensor_sub(ux[:, c * NTOK:(c + 1) * NTOK],
                         ux[:, c * NTOK:(c + 1) * NTOK], m2b).then_inc(c_sub, 1)
    if PHASE >= 4:
        em("V", v_sub)

    if PHASE < 4:
        def s_store_x1(e):
            e.wait_ge(c_x1, NCH)
            for oc in range(NCH):
                e.activation(outb[:, (oc % 2) * NTOK:(oc % 2 + 1) * NTOK],
                             ux[:, oc * NTOK:(oc + 1) * NTOK], AF.Identity).then_inc(c_ev, 1)
                e.wait_ge(c_ev, oc + 1)
                e.dma_start(out=out_t[oc * P:(oc + 1) * P, :],
                            in_=outb[:, (oc % 2) * NTOK:(oc % 2 + 1) * NTOK]).then_inc(s_out, 16)
                if oc >= 1:
                    e.wait_ge(s_out, 16 * oc)
        em("S", s_store_x1)
        if PHASE in (2, 3):
            def v_dec_tail(e):
                for k in range(NCH, NUNIT):
                    decode_unit_v(k)(e)
            def p_dec_tail(e):
                for k in range(NCH, NUNIT):
                    e8_unit_p(k)(e)
            em("V", v_dec_tail)
            em("P", p_dec_tail)

    import os as _os
    if _os.environ.get("KNOPTEST"):
        def p_mid(e):
            for _ in range(110):
                e.nop()
    elif False:
        pass
    if _os.environ.get("KNOPTEST"):
        pass
    else:
     def p_mid(e):
        for c in range(NCH):
            if PHASE >= 4:
                e.wait_ge(c_sub, c + 1)
            for h_ in range(2):
                e.matmul(pA[0:16, h_ * H:(h_ + 1) * H],
                         atb[:, c * 16:(c + 1) * 16],
                         ux[:, c * NTOK + h_ * H:c * NTOK + (h_ + 1) * H],
                         start=(c == 0), stop=(c == NCH - 1))
            ins = e.matmul(pB[0:16, 0:1], atb[:, c * 16:(c + 1) * 16], onesbf[:, 0:1],
                           start=(c == 0), stop=(c == NCH - 1))
        return ins
    if PHASE == 4:
        def v_dec_tail2(e):
            for k in range(NCH, NUNIT):
                decode_unit_v(k)(e)
        def p_dec_tail2(e):
            for k in range(NCH, NUNIT):
                e8_unit_p(k)(e)
        em("V", v_dec_tail2)
        em("P", p_dec_tail2)
        barrier()
    if PHASE >= 3:
        step("P", p_mid)
        step("V",
             lambda e: e.tensor_copy(midb[:], pA[0:16, 0:NTOK]),
             lambda e: e.tensor_copy(arowb[:], pB[0:16, 0:1]))
        SC3["v"] = seqcnt["V"]
        barrier()

    def p_ffn(e):
        for oc in range(NCH):
            wt = wtb[:, (oc % 2) * D:(oc % 2 + 1) * D]
            if oc == 0:
                e.wait_ge(bV, SC3["v"])
            e.wait_ge(s_wt, 16 * (2 * oc + 1))
            if oc >= 2:
                e.wait_ge(c_ev, oc - 1)
            for c in range(NCH):
                if oc == 0:
                    e.wait_ge(c_sub, c + 1)
                if c == 16:
                    e.wait_ge(s_wt, 16 * (2 * oc + 2))
                ins_c = None
                for h_ in range(2):
                    ins_c = e.matmul((pA if oc % 2 == 0 else pB)[:, h_ * H:(h_ + 1) * H],
                             wt[:, c * P:(c + 1) * P],
                             ux[:, c * NTOK + h_ * H:c * NTOK + (h_ + 1) * H],
                             start=(c == 0), stop=False)
                if c == 15:
                    ins_c.then_inc(c_fh, 1)
            ins = None
            for h_ in range(2):
                e.matmul((pA if oc % 2 == 0 else pB)[:, h_ * H:(h_ + 1) * H],
                         btb[:, oc * P:(oc + 1) * P],
                         midb[:, h_ * H:(h_ + 1) * H],
                         start=False, stop=False)
                ins = e.matmul((pA if oc % 2 == 0 else pB)[:, h_ * H:(h_ + 1) * H],
                         onesrbf[:, 0:P],
                         xsq[0:1, h_ * H:(h_ + 1) * H],
                         start=False, stop=True)
            ins.then_inc(c_ffn, 1)
            if oc < 16 and PHASE >= 5:
                e8_unit_p(32 + 2 * oc)(e)
                e8_unit_p(33 + 2 * oc)(e)
    if PHASE >= 4:
        em("P", p_ffn)

    def s_ffn(e):
        for oc in range(NCH):
            e.wait_ge(c_ev, oc + 1)
            e.dma_start(out=out_t[oc * P:(oc + 1) * P, :],
                        in_=outb[:, (oc % 2) * NTOK:(oc % 2 + 1) * NTOK]).then_inc(s_out, 16)
    if PHASE >= 4:
        em("S", s_ffn)

    def v_ffn(e):
        for oc in range(NCH):
            pp = pA if oc % 2 == 0 else pB
            e.wait_ge(c_ffn, oc + 1)
            if oc >= 2:
                e.wait_ge(s_out, 16 * (oc - 1))
            ob = outb[:, (oc % 2) * NTOK:(oc % 2 + 1) * NTOK]
            e.tensor_mul(tevac[:], pp[:, 0:NTOK], s2b[:])
            e.tensor_add(ob, tevac[:], ux[:, oc * NTOK:(oc + 1) * NTOK]).then_inc(c_ev, 1)
            if oc < 16 and PHASE >= 5:
                decode_unit_v(32 + 2 * oc)(e)
                decode_unit_v(33 + 2 * oc)(e)
    if PHASE >= 4:
        em("V", v_ffn)

    if DEBUG:
        def g_dbg(g):
            g.wait_ge(s_out, 32 * 16)
            dumps = [(dbg["K"], krow[:]), (dbg["T0"], T0b[:]), (dbg["BmT"], BmT[:]),
                     (dbg["CBl"], CBl[:]), (dbg["ACl"], ACl[:]),
                     (dbg["Ab"][:, 0:64], pw[:, 0:64]),
                     (dbg["Ab"][:, 64:128], pw[:, 64:128]),
                     (dbg["x1"], ux[:, 0:NTOK]),
                     (dbg["ln"], ln1b[:, NTOK:2 * NTOK]),
                     ] + (
                         [(dbg["W0"], wtb[:, D:D + P])] if PHASE >= 4 else [])
            for o_, i_ in dumps:
                g.dma_start(out=o_, in_=i_).then_inc(s_dbg, 16)
            g.wait_ge(s_dbg, 16 * len(dumps))
        em("G", g_dbg)

    with es, nc.Block() as block:
        @block.vector
        def _(vector):
            for f in ops["V"]:
                f(vector)

        @block.tensor
        def _(tensor):
            for f in ops["P"]:
                f(tensor)

        @block.scalar
        def _(scalar):
            for f in ops["S"]:
                f(scalar)

        @block.gpsimd
        def _(gpsimd):
            for f in ops["G"]:
                f(gpsimd)

        @block.sync
        def _(sync):
            for f in ops["Y"]:
                f(sync)

    return nc


_NC_CACHE = {}


def kernel(**inputs):
    u = np.asarray(inputs["L0_attn_u"], np.float32)
    lam = np.asarray(inputs["L0_attn_Lambda_ri"], np.float32)
    p_ri = np.asarray(inputs["L0_attn_p_ri"], np.float32)
    q_ri = np.asarray(inputs["L0_attn_q_ri"], np.float32)
    vc = np.asarray(inputs["L0_attn_Vc_ri"], np.float32)
    ct = np.asarray(inputs["L0_attn_Ct_ri"], np.float32)
    bp = np.asarray(inputs["L0_attn_Bp"], np.float32)
    Dv = np.asarray(inputs["L0_attn_D"], np.float32)
    ls = np.asarray(inputs["L0_attn_log_step"], np.float32)
    pk = np.asarray(inputs["L0_ffn_packed_weight"], np.int32)
    bs = np.asarray(inputs["L0_ffn_block_scales"], np.float32)
    gs = np.asarray(inputs["L0_ffn_global_scale"], np.float32)
    lA = np.asarray(inputs["L0_ffn_lora_A"], np.float32)
    lB = np.asarray(inputs["L0_ffn_lora_B"], np.float32)

    R = u.reshape(-1, D)
    packedT = np.ascontiguousarray(pk.T)
    bsT3 = np.ascontiguousarray(
        bs.T.reshape(N_IPC, 16, D).transpose(1, 0, 2).reshape(16, 16 * D))
    lA_T = np.ascontiguousarray(lA.T.reshape(NCH, P, 16).transpose(1, 0, 2).reshape(P, NCH * 16))
    lB_T = np.ascontiguousarray(lB.T)
    sm = np.ascontiguousarray(np.stack(
        [lam[:, 0], lam[:, 1], p_ri[:, 0], p_ri[:, 1], q_ri[:, 0], q_ri[:, 1],
         bp[:, 0], ct[0, :, 0], ct[0, :, 1]], axis=1))
    vctr = np.ascontiguousarray(vc[:, :, 0].T)
    vcti = np.ascontiguousarray(vc[:, :, 1].T)
    scal = np.ascontiguousarray(
        np.concatenate([Dv.reshape(1), ls.reshape(1), gs.reshape(1)]).reshape(1, 3))

    consts = {
        "cI64": np.eye(64, dtype=np.float32),
        "cJ128": np.eye(P, dtype=np.float32)[::-1].astype(MLBF),
        "cE8": np.repeat(0.5 * np.eye(16, dtype=np.float32), 8, axis=1).astype(MLBF),
        "cOnes": np.ones((P, 4), np.float32),
        "cOnesB": np.ones((P, 4), np.float32).astype(MLBF),
        "cOnesR": np.ones((1, P), np.float32),
        "cOnesRB": np.ones((1, P), np.float32).astype(MLBF),
    }

    key = ("prog", PHASE, DEBUG, SERIAL_DEC)
    if key not in _NC_CACHE:
        nc_new = build_program()
        mybir.codegen_inst_isa_subclasses(nc_new)
        _NC_CACHE[key] = nc_new
    nc = _NC_CACHE[key]

    in_maps = []
    for r in range(NC):
        m = {
            "u_s": np.ascontiguousarray(R[r * NTOK:(r + 1) * NTOK, :].T),
            "packedT": packedT, "bsT3": bsT3, "lA_T": lA_T, "lB_T": lB_T,
            "sm": sm, "VcT_re": vctr, "VcT_im": vcti, "scal": scal,
        }
        m.update(consts)
        in_maps.append(m)

    res = run_bass_kernel_spmd(nc, in_maps, core_ids=list(range(NC)))
    out = np.empty((NC * NTOK, D), np.float32)
    for r in range(NC):
        out[r * NTOK:(r + 1) * NTOK, :] = res.results[r]["out_s"].T
    if DEBUG:
        kernel._dbg = res.results
    return out.reshape(2, -1, D)


# version 25
# speedup vs baseline: 1.0659x; 1.0168x over previous
import numpy as np
import ml_dtypes
from contextlib import ExitStack

import concourse.bass as bass
import concourse.mybir as mybir
from concourse.bass_utils import run_bass_kernel_spmd
from concourse import dve_ops as _dve_ops
from concourse.dve_op import DveOp as _DveOp
from concourse.dve_spec import (Spec as _Spec, Src0 as _Src0, Src1 as _Src1,
                                C0 as _C0, C1 as _C1, relu as _relu,
                                lower as _dve_lower)


def _fp4_mag_sgn_ref(in0, in1, s0, s1, imm2):
    x = np.asarray(in0).astype(np.float32)
    g = np.asarray(in1).astype(np.float32)
    r1 = np.maximum(x - s0, 0.0)
    r2 = np.maximum(x + x - s1, 0.0)
    return (((x + r1) + r2) * g).astype(np.float32)


def _register_fp4_op():
    for o in _dve_ops.OPS:
        if o.name == "FP4_MAG_SGN":
            return o
    spec = _Spec(body=((_Src0 + _relu(_Src0 - _C0))
                       + _relu((_Src0 + _Src0) - _C1)) * _Src1)
    sha = {"v3": _dve_lower(spec, ver="v3", name="FP4_MAG_SGN").sha("v3")}
    op = _DveOp("FP4_MAG_SGN", spec, False, sha, reference=_fp4_mag_sgn_ref)
    _dve_ops.OPS.append(op)
    _dve_ops._SUB_OPCODE_FOR_NAME["FP4_MAG_SGN"] = (
        _dve_ops._CUSTOM_DVE_ROW_BASE + len(_dve_ops.OPS) - 1)
    _dve_ops.CUSTOM_DVE_SPECS["FP4_MAG_SGN"] = spec
    _dve_ops.CUSTOM_DVE_REFS["FP4_MAG_SGN"] = _fp4_mag_sgn_ref
    return op


FP4_OP = _register_fp4_op()

F32 = mybir.dt.float32
BF16 = mybir.dt.bfloat16
I32 = mybir.dt.int32
I16 = mybir.dt.int16
OP = mybir.AluOpType
AF = mybir.ActivationFunctionType
MLBF = ml_dtypes.bfloat16

P = 128
NTOK = 1024
D = 4096
NCH = 32
NC = 8
NEU = 10
N_IPC = 16
N_OSL = 4
OSL = 1024
NUNIT = N_IPC * N_OSL
H = 512
DEBUG = False
PHASE = 5
SERIAL_DEC = False


def build_program():
    nc = bass.Bass("TRN2", target_bir_lowering=False, detect_race_conditions=False)

    u_in = nc.dram_tensor("u_s", [D, NTOK], F32, kind="ExternalInput").ap()
    pk_in = nc.dram_tensor("packedT", [2048, D], I32, kind="ExternalInput").ap()
    bst_in = nc.dram_tensor("bsT3", [16, 16 * D], F32, kind="ExternalInput").ap()
    at_in = nc.dram_tensor("lA_T", [P, NCH * 16], F32, kind="ExternalInput").ap()
    bt_in = nc.dram_tensor("lB_T", [16, D], F32, kind="ExternalInput").ap()
    sm_in = nc.dram_tensor("sm", [64, 9], F32, kind="ExternalInput").ap()
    vcr_in = nc.dram_tensor("VcT_re", [64, 64], F32, kind="ExternalInput").ap()
    vci_in = nc.dram_tensor("VcT_im", [64, 64], F32, kind="ExternalInput").ap()
    scal_in = nc.dram_tensor("scal", [1, 3], F32, kind="ExternalInput").ap()
    i64_in = nc.dram_tensor("cI64", [64, 64], F32, kind="ExternalInput").ap()
    j128_in = nc.dram_tensor("cJ128", [P, P], BF16, kind="ExternalInput").ap()
    e8_in = nc.dram_tensor("cE8", [16, P], BF16, kind="ExternalInput").ap()
    ones_in = nc.dram_tensor("cOnes", [P, 4], F32, kind="ExternalInput").ap()
    onesb_in = nc.dram_tensor("cOnesB", [P, 4], BF16, kind="ExternalInput").ap()
    onesr_in = nc.dram_tensor("cOnesR", [1, P], F32, kind="ExternalInput").ap()
    onesrb_in = nc.dram_tensor("cOnesRB", [1, P], BF16, kind="ExternalInput").ap()

    out_t = nc.dram_tensor("out_s", [D, NTOK], F32, kind="ExternalOutput").ap()
    dbg = {}
    if DEBUG:
        for nm, shp in [("K", [1, P]), ("T0", [P, P]), ("BmT", [P, P]),
                        ("CBl", [P, P]), ("ACl", [P, P]), ("Ab", [64, P]),
                        ("x1", [P, NTOK]), ("W0", [P, P]), ("ln", [P, NTOK]),
                        ("smx", [64, 96]), ("smr", [1, 576]), ("g64", [64, 8])]:
            dbg[nm] = nc.dram_tensor("dbg_" + nm, shp, F32, kind="ExternalOutput").ap()

    wt_d = nc.dram_tensor("WT_d", [NCH, P, NCH, P], BF16, kind="Internal").ap()
    kpad_d = nc.dram_tensor("kpad_d", [1, 256], BF16, kind="Internal").ap()

    es = ExitStack()
    sb = lambda n, s, d: es.enter_context(nc.sbuf_tensor(n, s, d))
    ps_ = lambda n, s: es.enter_context(nc.psum_tensor(n, s, F32))
    sem = lambda n: es.enter_context(nc.semaphore(n))

    ux = sb("ux", [P, NCH * NTOK], BF16)
    ln1b = sb("ln1b", [P, 2 * NTOK], BF16)
    sst = [sb("sst0", [P, NTOK], BF16), sb("sst1", [P, NTOK], BF16)]
    m1b = sb("m1b", [P, NTOK], BF16)
    s1b = sb("s1b", [P, NTOK], BF16)
    s2b = sb("s2b", [P, NTOK], F32)
    xsq = sb("xsq", [P, 2 * NTOK], BF16)
    tevac = sb("tevac", [P, NTOK], BF16)
    outb = sb("outb", [P, 2 * NTOK], F32)
    T0b = sb("T0b", [P, P], BF16)
    BmT = sb("BmT", [P, P], BF16)
    CBl = sb("CBl", [P, P], BF16)
    ACl = sb("ACl", [P, P], BF16)
    wtb = sb("wtb", [P, 2 * D], BF16)
    colc = sb("colc", [P, 2], F32)
    midb = sb("midb", [16, NTOK], BF16)
    arowb = sb("arowb", [16, 1], BF16)
    atb = sb("atb", [P, NCH * 16], BF16)
    btb = sb("btb", [16, D], BF16)
    bstu = sb("bstu", [16, 2 * OSL], BF16)
    bstu16 = sb("bstu16", [16, 6 * OSL], BF16)
    e8g = sb("e8g", [16, P], BF16)
    slab = sb("slab", [P, 2 * OSL], I32)
    dc1 = sb("dc1", [P, 2 * OSL], I16)
    dc2 = sb("dc2", [P, 2 * OSL], I16)
    dc4 = sb("dc4", [P, 2 * OSL], I16)
    wlo = sb("wlo", [P, 2 * OSL], BF16)
    whi = sb("whi", [P, 2 * OSL], BF16)
    smt = sb("smt", [64, 9], F32)
    vctr = sb("vctr", [64, 64], F32)
    vcti = sb("vcti", [64, 64], F32)
    i64f = sb("i64f", [64, 64], F32)
    j128b = sb("j128b", [P, P], BF16)
    e8b = sb("e8b", [16, P], BF16)
    onesf = sb("onesf", [P, 4], F32)
    onesbf = sb("onesbf", [P, 4], BF16)
    onesrf = sb("onesrf", [1, P], F32)
    onesrbf = sb("onesrbf", [1, P], BF16)
    scalt = sb("scalt", [1, 4], F32)
    rowt = sb("rowt", [1, 2 * NTOK], F32)
    rowu = sb("rowu", [1, NTOK], F32)
    _pw_off = {}
    _pw_n = [0]
    for _j in range(13):
        _slots = (0, 1, 2, 3, 4, 5) if _j <= 6 else ((0, 1, 2, 3, 4) if _j <= 11 else (0, 1, 5))
        for _s in _slots:
            _pw_off[(_j, _s)] = _pw_n[0]
            _pw_n[0] += 64
    _pw_off[("scr", 0)] = _pw_n[0]
    _pw_n[0] += 64
    pw = sb("pw", [64, _pw_n[0]], F32)
    smx = sb("smx", [64, 24 * 4], F32)
    smr = sb("smr", [1, 576], F32)
    bkre = sb("bkre", [64, P], F32)
    bkim = sb("bkim", [64, P], F32)
    zre = sb("zre", [64, P], F32)
    zim = sb("zim", [64, P], F32)
    t128 = sb("t128", [P, 64], BF16)
    krow = sb("krow", [1, P], BF16)
    z128 = sb("z128", [1, P], BF16)
    g64 = sb("g64", [64, 8], F32)
    sc16 = sb("sc16", [16, 2], F32)

    pA = ps_("pA", [P, NTOK])
    pB = ps_("pB", [P, NTOK])
    pC = ps_("pC", [P, NTOK])
    pD = ps_("pD", [P, NTOK])

    s_sm = sem("s_sm")
    s_u = sem("s_u")
    KRC = {}

    def u_wait(e, c):
        e.wait_ge(s_u, 16 * (c + 1))
    s_pk = sem("s_pk")
    s_wo = sem("s_wo")
    s_wos = sem("s_wos")
    s_sl = sem("s_sl")
    s_sl2 = sem("s_sl2")
    s_bs = sem("s_bs")
    s_misc = sem("s_misc")
    s_wt = sem("s_wt")
    s_out = sem("s_out")
    s_tp = sem("s_tp")
    bV = sem("bV")
    bP = sem("bP")
    bS = sem("bS")
    c_ln = sem("c_ln")
    c_x1 = sem("c_x1")
    c_y = sem("c_y")
    c_s = sem("c_s")
    c_sev = sem("c_sev")
    c_sq = sem("c_sq")
    c_dec = sem("c_dec")
    c_e8 = sem("c_e8")
    c_ffn = sem("c_ffn")
    c_ev = sem("c_ev")
    c_cc = sem("c_cc")
    c_st = sem("c_st")
    c_sub = sem("c_sub")
    c_fh = sem("c_fh")
    SC2 = {}
    SC3 = {}
    s_dbg = sem("s_dbg")

    ops = {"V": [], "P": [], "S": [], "G": [], "Y": []}

    def em(key, f):
        ops[key].append(f)

    seqcnt = {"V": 0, "P": 0, "S": 0}
    seqsem = {"V": bV, "P": bP, "S": bS}

    def step(key, *fns, raw=False):
        waits = [(k, seqcnt[k]) for k in "VPS" if seqcnt[k] > 0]
        base = seqcnt[key]
        n = 1 if raw or not fns else len(fns)
        seqcnt[key] += n

        def f(eng, waits=waits, fns=fns, base=base, raw=raw):
            for k, v in waits:
                eng.wait_ge(seqsem[k], v)
            if raw or not fns:
                for g in fns:
                    g(eng)
                eng.sem_inc(seqsem[key], 1)
                return
            for i, g in enumerate(fns):
                if i > 0:
                    eng.wait_ge(seqsem[key], base + i)
                r = g(eng)
                ok = False
                if r is not None and hasattr(r, "then_inc"):
                    tn = type(getattr(r, "ins", r)).__name__
                    if "EventSemaphore" not in tn and "DMACopy" not in tn:
                        r.then_inc(seqsem[key], 1)
                        ok = True
                if not ok:
                    eng.sem_inc(seqsem[key], 1)
        em(key, f)

    def barrier():
        for k in "VPS":
            step(k)

    _smx_n = [0]

    def acol():
        i = _smx_n[0]
        _smx_n[0] += 1
        assert i < 96
        return smx[:, i:i + 1]

    _smr_n = [0]

    def arow_(w=1):
        i = _smr_n[0]
        _smr_n[0] += w
        assert _smr_n[0] <= 576
        return smr[:, i:i + w]

    def pwt(j, which):
        off = _pw_off[(j, which)]
        return pw[:, off:off + 64]

    UNITS = [(k // N_IPC, k % N_IPC) for k in range(NUNIT)]
    PSC = ["C"] * 16 + ["D" if k % 2 == 0 else "C" for k in range(16, 64)]

    def _uwait(k):
        for j in range(k - 1, -1, -1):
            if PSC[j] == PSC[k]:
                return j + 1
        return 0

    def y_main(y):
        y.dma_start(out=smt[:], in_=sm_in).then_inc(s_sm, 16)
        y.dma_start(out=vctr[:], in_=vcr_in).then_inc(s_sm, 16)
        y.dma_start(out=vcti[:], in_=vci_in).then_inc(s_sm, 16)
        y.dma_start(out=scalt[:, 0:3], in_=scal_in).then_inc(s_sm, 16)
        y.dma_start(out=i64f[:], in_=i64_in).then_inc(s_sm, 16)
        y.dma_start(out=j128b[:], in_=j128_in).then_inc(s_sm, 16)
        y.dma_start(out=e8b[:], in_=e8_in).then_inc(s_sm, 16)
        y.dma_start(out=onesf[:], in_=ones_in).then_inc(s_sm, 16)
        y.dma_start(out=onesbf[:], in_=onesb_in).then_inc(s_sm, 16)
        y.dma_start(out=onesrf[:], in_=onesr_in).then_inc(s_sm, 16)
        y.dma_start(out=onesrbf[:], in_=onesrb_in).then_inc(s_sm, 16)
        if PHASE >= 2:
            slab_load(y, 0, s_sl)
            slab_load(y, 1, s_sl)
            for k in range(16):
                y.wait_ge(c_dec, k + 1)
                w_out(y, k, s_wos)
                if k + 2 <= 17:
                    slab_load(y, k + 2, s_sl)
        y.wait_ge(bV, KRC["v"])
        y.dma_start(out=kpad_d[:, 0:128], in_=z128[:]).then_inc(s_tp, 16)
        y.dma_start(out=kpad_d[:, 128:256], in_=krow[:]).then_inc(s_tp, 16)
        y.wait_ge(s_tp, 32)
        for p in range(64):
            y.dma_start(out=T0b[p:p + 1, :],
                        in_=kpad_d[0:1, 128 - p:256 - p]).then_inc(s_tp, 16)
        for oc in (range(NCH) if PHASE >= 4 else []):
            osl = (oc * P) // OSL
            y.wait_ge(s_wos, 64 * 16)
            if osl >= 1:
                y.wait_ge(s_wo, 64 * 16 * osl)
            wd = wt_d[oc].rearrange("p ic q -> p (ic q)")
            if oc >= 2:
                y.wait_ge(c_fh, oc - 1)
            y.dma_start(
                out=wtb[:, (oc % 2) * D:(oc % 2) * D + D // 2],
                in_=wd[:, 0:D // 2],
            ).then_inc(s_wt, 16)
            if oc >= 2:
                y.wait_ge(c_ffn, oc - 1)
            y.dma_start(
                out=wtb[:, (oc % 2) * D + D // 2:(oc % 2) * D + D],
                in_=wd[:, D // 2:D],
            ).then_inc(s_wt, 16)
        y.wait_ge(s_out, 32 * 16)
    em("Y", y_main)

    def u_load(eng, c, sem_):
        eng.dma_start(
            out=ux[:, c * NTOK:(c + 1) * NTOK],
            in_=u_in[c * P:(c + 1) * P, :],
        ).then_inc(sem_, 16)

    wt4 = wt_d.rearrange("oc (p2 two) ic q -> ic two p2 oc q", two=2)

    def w_out(eng, k, sem_):
        osl, ipc = UNITS[k]
        for par, wsrc in ((0, wlo), (1, whi)):
            t = wsrc[:, (k % 2) * OSL:(k % 2 + 1) * OSL].rearrange(
                "p (ocs q) -> p ocs q", q=P)
            for h_ in range(2):
                eng.dma_start(
                    out=wt4[2 * ipc + h_, par, :, osl * 8:(osl + 1) * 8, :],
                    in_=t[h_ * 64:(h_ + 1) * 64],
                ).then_inc(sem_, 16)

    def slab_load(eng, k, sem_):
        osl, ipc = UNITS[k]
        eng.dma_start(
            out=slab[:, (k % 2) * OSL:(k % 2 + 1) * OSL],
            in_=pk_in[ipc * P:(ipc + 1) * P, osl * OSL:(osl + 1) * OSL],
        ).then_inc(sem_, 16)

    def bstu_load(g, k):
        osl, ipc = UNITS[k]
        out = bstu16[:, k * OSL:(k + 1) * OSL] if k < 6 else \
            bstu[:, (k % 2) * OSL:(k % 2 + 1) * OSL]
        g.dma_start(
            out=out,
            in_=bst_in[:, ipc * D + osl * OSL:ipc * D + (osl + 1) * OSL],
        ).then_inc(s_bs, 16)

    def g_main(g):
        g.dma_start(out=atb[:], in_=at_in).then_inc(s_misc, 16)
        g.dma_start(out=btb[:], in_=bt_in).then_inc(s_misc, 16)
        if PHASE >= 2:
            for k in range(8):
                bstu_load(g, k)
        for c in range(NCH):
            u_load(g, c, s_u)
        if PHASE >= 2:
            for k in range(8, 16):
                g.wait_ge(c_dec, k - 1)
                bstu_load(g, k)
        if PHASE < 2:
            return
        for k in (16, 17):
            g.wait_ge(c_dec, k - 1)
            bstu_load(g, k)
        for k in range(16, NUNIT):
            g.wait_ge(c_dec, k + 1)
            w_out(g, k, s_wo)
            if k + 2 < NUNIT:
                bstu_load(g, k + 2)
                slab_load(g, k + 2, s_sl2)
    em("G", g_main)

    def decode_unit_v(k):
        osl, ipc = UNITS[k]
        sl = slab[:, (k % 2) * OSL:(k % 2 + 1) * OSL]
        pscale = pC if PSC[k] == "C" else pD
        lo = wlo[:, (k % 2) * OSL:(k % 2 + 1) * OSL]
        hi = whi[:, (k % 2) * OSL:(k % 2 + 1) * OSL]
        c16a = dc1[:, 0:2 * OSL]
        t1s = dc2[:, 0:2 * OSL]

        def f(e, k=k):
            if k <= 17:
                e.wait_ge(s_sl, 16 * (k + 1))
            else:
                e.wait_ge(s_sl2, 16 * (k - 17))
            if 2 <= k <= 17:
                e.wait_ge(s_wos, 64 * (k - 1))
            elif k >= 18:
                e.wait_ge(s_wo, 64 * (k - 17))
            e.wait_ge(c_e8, k + 1)
            dm = dc4[:].bitcast(I32)
            e.tensor_scalar(dm, sl, 4097, None, OP.mult)
            dv = dm.bitcast(I16)
            e.tensor_scalar(c16a, dv, 7, None, OP.bitwise_and)
            e.tensor_scalar(t1s, dv, 8, 2, OP.bitwise_and, OP.logical_shift_right)
            e.tensor_scalar(t1s, t1s, -1, 1, OP.mult, OP.add)
            e._custom_dve(FP4_OP, out=dc4[:, 0:2 * OSL].bitcast(BF16),
                          in0=c16a, in1=t1s, s0=4.0, s1=12.0)
            bv = dc4[:, 0:2 * OSL].bitcast(BF16)
            e.tensor_tensor(lo, bv[:, 0::2], pscale[:, 0:OSL], OP.mult)
            e.tensor_tensor(hi, bv[:, 1::2], pscale[:, 0:OSL], OP.mult).then_inc(c_dec, 1)
        return f

    def st1_sq_v(c):
        def f(e, c=c):
            u_wait(e, c)
            if c >= 2:
                e.wait_ge(c_st, c - 1)
            e.tensor_mul(xsq[:, (c % 2) * NTOK:(c % 2 + 1) * NTOK],
                         ux[:, c * NTOK:(c + 1) * NTOK],
                         ux[:, c * NTOK:(c + 1) * NTOK]).then_inc(c_sq, 1)
        return f

    def st1_mm_p(c):
        def f(e, c=c):
            u_wait(e, c)
            e.wait_ge(c_sq, c + 1)
            for h_ in range(2):
                e.matmul(pA[0:1, h_ * H:(h_ + 1) * H], onesbf[:, 0:1],
                         ux[:, c * NTOK + h_ * H:c * NTOK + (h_ + 1) * H],
                         start=(c == 0), stop=(c == NCH - 1))
                ins = e.matmul(pB[0:1, h_ * H:(h_ + 1) * H], onesbf[:, 0:1],
                         xsq[:, (c % 2) * NTOK + h_ * H:(c % 2) * NTOK + (h_ + 1) * H],
                         start=(c == 0), stop=(c == NCH - 1))
            ins.then_inc(c_st, 1)
        return f

    def e8_unit_p(k):
        osl, ipc = UNITS[k]
        pscale = pC if PSC[k] == "C" else pD

        def f(e, k=k, osl=osl, ipc=ipc):
            e.wait_ge(s_bs, 16 * (k + 1))
            w = _uwait(k)
            if w > 0:
                e.wait_ge(c_dec, w)
            bs_ = bstu16[:, k * OSL:(k + 1) * OSL] if k < 6 else \
                bstu[:, (k % 2) * OSL:(k % 2 + 1) * OSL]
            e.matmul(pscale[:, 0:H], e8g[:], bs_[:, 0:H],
                     start=True, stop=True)
            e.matmul(pscale[:, H:OSL], e8g[:], bs_[:, H:OSL],
                     start=True, stop=True).then_inc(c_e8, 1)
        return f


    ALLSM = 11 * 16

    step("S",
         lambda e: e.wait_ge(s_sm, ALLSM),
         lambda e: e.activation(scalt[:, 3:4], scalt[:, 1:2], AF.Exp))
    step("V", lambda e: e.wait_ge(s_sm, ALLSM), lambda e: e.memset(smr[:], 0.0))
    step("P", lambda e: e.wait_ge(s_sm, ALLSM))

    step("P",
         lambda e: e.matmul(pD[0:64, 512:516], onesrf[:, 0:64], scalt[:, 0:4],
                            start=True, stop=True),
         lambda e: e.matmul(pD[0:64, 0:1], vctr[:], smt[:, 6:7],
                            start=True, stop=True),
         lambda e: e.matmul(pD[0:64, 1:2], vcti[:], smt[:, 6:7],
                            start=True, stop=True))
    alph = acol(); Dcol = acol(); Bre = acol(); Bim = acol(); stepc = acol()
    step("V",
         lambda e: e.tensor_scalar(alph, pD[0:64, 515:516], 0.5, None, OP.mult),
         lambda e: e.tensor_copy(stepc, pD[0:64, 515:516]),
         lambda e: e.tensor_copy(Dcol, pD[0:64, 512:513]),
         lambda e: e.tensor_copy(sc16[:, 0:1], pD[0:16, 514:515]),
         lambda e: e.tensor_copy(Bre, pD[0:64, 0:1]),
         lambda e: e.tensor_copy(Bim, pD[0:64, 1:2]))
    step("V", lambda e: e.tensor_scalar(e8g[:], e8b[:], sc16[:, 0:1], None, OP.mult))

    lam_re = smt[:, 0:1]; lam_im = smt[:, 1:2]
    p_re = smt[:, 2:3]; p_im = smt[:, 3:4]
    q_re = smt[:, 4:5]; q_im = smt[:, 5:6]
    ct_re = smt[:, 7:8]; ct_imr = smt[:, 8:9]
    qc_im = acol()
    dr = acol(); di = acol(); mag = acol(); rec = acol()
    d_re = acol(); d_im = acol()
    step("V",
         lambda e: e.tensor_scalar(qc_im, q_im, -1.0, None, OP.mult),
         lambda e: e.tensor_scalar(dr, lam_re, alph, -1.0, OP.mult, OP.mult),
         lambda e: e.tensor_scalar(dr, dr, 1.0, None, OP.add),
         lambda e: e.tensor_scalar(di, lam_im, alph, -1.0, OP.mult, OP.mult),
         lambda e: e.tensor_mul(mag, dr, dr),
         lambda e: e.tensor_mul(rec, di, di),
         lambda e: e.tensor_add(mag, mag, rec),
         lambda e: e.reciprocal(rec, mag),
         lambda e: e.tensor_mul(d_re, dr, rec),
         lambda e: e.tensor_mul(d_im, di, rec),
         lambda e: e.tensor_scalar(d_im, d_im, -1.0, None, OP.mult))

    def cmul_cols(o_re, o_im, a_re, a_im, b_re, b_im, t1, t2):
        return [
            lambda e: e.tensor_mul(t1, a_re, b_re),
            lambda e: e.tensor_mul(t2, a_im, b_im),
            lambda e: e.tensor_sub(o_re, t1, t2),
            lambda e: e.tensor_mul(t1, a_re, b_im),
            lambda e: e.tensor_mul(t2, a_im, b_re),
            lambda e: e.tensor_add(o_im, t1, t2),
        ]

    tc1 = acol(); tc2 = acol()
    u1_re = acol(); u1_im = acol()
    v1_re = acol(); v1_im = acol()
    e_re = acol(); e_im = acol()
    a0_re = acol(); a0_im = acol()
    se_re = acol(); se_im = acol()
    sB_re = acol(); sB_im = acol()
    v1e_re = acol(); v1e_im = acol()
    dB_re = acol(); dB_im = acol()
    step("V", *cmul_cols(u1_re, u1_im, d_re, d_im, p_re, p_im, tc1, tc2))
    step("V", *cmul_cols(v1_re, v1_im, q_re, qc_im, d_re, d_im, tc1, tc2))
    step("V",
         lambda e: e.tensor_scalar(e_re, lam_re, alph, None, OP.mult),
         lambda e: e.tensor_scalar(e_re, e_re, 1.0, None, OP.add),
         lambda e: e.tensor_scalar(e_im, lam_im, alph, None, OP.mult))
    step("V", *cmul_cols(a0_re, a0_im, d_re, d_im, e_re, e_im, tc1, tc2))
    step("V", *cmul_cols(v1e_re, v1e_im, v1_re, v1_im, e_re, e_im, tc1, tc2))
    step("V", *cmul_cols(dB_re, dB_im, d_re, d_im, Bre, Bim, tc1, tc2))
    step("V", *cmul_cols(se_re, se_im, v1_re, v1_im, p_re, p_im, tc1, tc2))
    step("V", *cmul_cols(sB_re, sB_im, v1_re, v1_im, Bre, Bim, tc1, tc2))

    sig = arow_(2)
    v1B = arow_(2)
    step("P",
         lambda e: e.matmul(pD[0:1, 512:513], onesf[0:64, 0:1], se_re, start=True, stop=True),
         lambda e: e.matmul(pD[0:1, 513:514], onesf[0:64, 0:1], se_im, start=True, stop=True),
         lambda e: e.matmul(pD[0:1, 514:515], onesf[0:64, 0:1], sB_re, start=True, stop=True),
         lambda e: e.matmul(pD[0:1, 515:516], onesf[0:64, 0:1], sB_im, start=True, stop=True))
    step("V",
         lambda e: e.tensor_copy(sig, pD[0:1, 512:514]),
         lambda e: e.tensor_copy(v1B, pD[0:1, 514:516]))

    arow2 = arow_(2)
    den = arow_(2)
    beta = arow_(2)
    gam = arow_(2)
    t1r = arow_(2)
    t0r = arow_(1)

    def cmul_row(o, a, b, t):
        return [
            lambda e: e.tensor_mul(t[:, 0:1], a[:, 0:1], b[:, 0:1]),
            lambda e: e.tensor_mul(o[:, 0:1], a[:, 1:2], b[:, 1:2]),
            lambda e: e.tensor_sub(o[:, 0:1], t[:, 0:1], o[:, 0:1]),
            lambda e: e.tensor_mul(t[:, 1:2], a[:, 0:1], b[:, 1:2]),
            lambda e: e.tensor_mul(o[:, 1:2], a[:, 1:2], b[:, 0:1]),
            lambda e: e.tensor_add(o[:, 1:2], t[:, 1:2], o[:, 1:2]),
        ]

    def cinv_scale_row(o, num_s, den_, t):
        return [
            lambda e: e.tensor_mul(t[:, 0:1], den_[:, 0:1], den_[:, 0:1]),
            lambda e: e.tensor_mul(t[:, 1:2], den_[:, 1:2], den_[:, 1:2]),
            lambda e: e.tensor_add(t[:, 0:1], t[:, 0:1], t[:, 1:2]),
            lambda e: e.reciprocal(t[:, 0:1], t[:, 0:1]),
            lambda e: e.tensor_mul(o[:, 0:1], den_[:, 0:1], t[:, 0:1]),
            lambda e: e.tensor_mul(o[:, 1:2], den_[:, 1:2], t[:, 0:1]),
            lambda e: e.tensor_scalar(o[:, 1:2], o[:, 1:2], -1.0, None, OP.mult),
            lambda e: e.tensor_scalar(o[:, 0:1], o[:, 0:1], num_s, None, OP.mult),
            lambda e: e.tensor_scalar(o[:, 1:2], o[:, 1:2], num_s, None, OP.mult),
        ]

    alphr = scalt[:, 1:2]
    alrow = arow_(1)
    step("V",
         lambda e: e.tensor_scalar(alrow, scalt[:, 3:4], 0.5, None, OP.mult))
    step("V",
         lambda e: e.tensor_scalar(den[:, 0:2], sig[:, 0:2], alrow, None, OP.mult),
         lambda e: e.tensor_scalar(den[:, 0:1], den[:, 0:1], 1.0, None, OP.add),
         *cinv_scale_row(beta, alrow, den, t1r))
    step("V",
         *cmul_row(arow2, beta, sig, t1r),
         lambda e: e.tensor_scalar(arow2[:, 0:2], arow2[:, 0:2], -1.0, None, OP.mult),
         lambda e: e.tensor_scalar(arow2[:, 0:1], arow2[:, 0:1], 1.0, None, OP.add),
         lambda e: e.tensor_scalar(gam[:, 0:2], arow2[:, 0:2], alrow, None, OP.mult))
    bv1B = arow_(2)
    step("V", *cmul_row(bv1B, beta, v1B, t1r))

    assert _smr_n[0] == 18
    step("P", lambda e: e.matmul(pD[0:64, 0:6], onesrf[:, 0:64],
                                 smr[:, 8:14], start=True, stop=True))
    gb_c = g64[:, 2:3]; gb_ci = g64[:, 3:4]
    bb_c = g64[:, 4:5]; bb_ci = g64[:, 5:6]
    bv_c = g64[:, 6:7]; bv_ci = g64[:, 7:8]
    step("P", lambda e: e.matmul(pD[0:64, 6:10], onesrf[:, 0:64],
                                 smr[:, 14:18], start=True, stop=True))
    step("V",
         lambda e: e.tensor_copy(g64[:, 4:6], pD[0:64, 0:2]),
         lambda e: e.tensor_copy(g64[:, 2:4], pD[0:64, 2:4]),
         lambda e: e.tensor_copy(g64[:, 6:8], pD[0:64, 8:10]))

    w1_re = acol(); w1_im = acol()
    Bb_re = acol(); Bb_im = acol()
    step("V",
         lambda e: e.tensor_scalar(tc1, q_re, gb_c, None, OP.mult),
         lambda e: e.tensor_scalar(tc2, qc_im, gb_ci, None, OP.mult),
         lambda e: e.tensor_sub(w1_re, tc1, tc2),
         lambda e: e.tensor_scalar(tc1, q_re, gb_ci, None, OP.mult),
         lambda e: e.tensor_scalar(tc2, qc_im, gb_c, None, OP.mult),
         lambda e: e.tensor_add(w1_im, tc1, tc2),
         lambda e: e.tensor_scalar(tc1, v1e_re, bb_c, None, OP.mult),
         lambda e: e.tensor_scalar(tc2, v1e_im, bb_ci, None, OP.mult),
         lambda e: e.tensor_sub(tc1, tc1, tc2),
         lambda e: e.tensor_add(w1_re, w1_re, tc1),
         lambda e: e.tensor_scalar(tc1, v1e_re, bb_ci, None, OP.mult),
         lambda e: e.tensor_scalar(tc2, v1e_im, bb_c, None, OP.mult),
         lambda e: e.tensor_add(tc1, tc1, tc2),
         lambda e: e.tensor_add(w1_im, w1_im, tc1))
    step("V",
         lambda e: e.tensor_scalar(tc1, u1_re, bv_c, None, OP.mult),
         lambda e: e.tensor_scalar(tc2, u1_im, bv_ci, None, OP.mult),
         lambda e: e.tensor_sub(tc1, tc1, tc2),
         lambda e: e.tensor_sub(Bb_re, dB_re, tc1),
         lambda e: e.tensor_scalar(tc1, u1_re, bv_ci, None, OP.mult),
         lambda e: e.tensor_scalar(tc2, u1_im, bv_c, None, OP.mult),
         lambda e: e.tensor_add(tc1, tc1, tc2),
         lambda e: e.tensor_sub(Bb_im, dB_im, tc1),
         lambda e: e.tensor_scalar(Bb_re, Bb_re, stepc, None, OP.mult),
         lambda e: e.tensor_scalar(Bb_im, Bb_im, stepc, None, OP.mult))

    u1r = arow_(64); u1i = arow_(64); w1r = arow_(64); w1i = arow_(64)
    nu1r = arow_(64); nu1i = arow_(64)
    step("P",
         lambda e: e.transpose(pD[0:1, 0:64], u1_re, i64f[:]),
         lambda e: e.transpose(pD[0:1, 64:128], u1_im, i64f[:]),
         lambda e: e.transpose(pD[0:1, 128:192], w1_re, i64f[:]),
         lambda e: e.transpose(pD[0:1, 192:256], w1_im, i64f[:]))
    step("V",
         lambda e: e.tensor_copy(u1r, pD[0:1, 0:64]),
         lambda e: e.tensor_copy(u1i, pD[0:1, 64:128]),
         lambda e: e.tensor_copy(w1r, pD[0:1, 128:192]),
         lambda e: e.tensor_copy(w1i, pD[0:1, 192:256]),
         lambda e: e.tensor_scalar(nu1r, u1r, -1.0, None, OP.mult),
         lambda e: e.tensor_scalar(nu1i, u1i, -1.0, None, OP.mult))
    step("P",
         lambda e: e.matmul(pD[0:64, 512:576], nu1r, w1r, start=True, stop=False),
         lambda e: e.matmul(pD[0:64, 512:576], u1i, w1i, start=False, stop=True),
         lambda e: e.matmul(pD[0:64, 576:640], nu1r, w1i, start=True, stop=False),
         lambda e: e.matmul(pD[0:64, 576:640], nu1i, w1r, start=False, stop=True))
    dgt = pw[:, _pw_off[("scr", 0)]:_pw_off[("scr", 0)] + 64]
    step("V",
         lambda e: e.tensor_scalar(dgt, i64f[:], a0_re, None, OP.mult),
         lambda e: e.tensor_add(pwt(0, 0), dgt, pD[0:64, 512:576]),
         lambda e: e.tensor_scalar(dgt, i64f[:], a0_im, None, OP.mult),
         lambda e: e.tensor_add(pwt(0, 1), dgt, pD[0:64, 576:640]),
         lambda e: e.tensor_scalar(pwt(0, 5), pwt(0, 1), -1.0, None, OP.mult))

    for j in range(12):
        step("P",
             lambda e, j=j: e.transpose(pD[0:64, 0:64], pwt(j, 0), i64f[:]),
             lambda e, j=j: e.transpose(pD[0:64, 64:128], pwt(j, 1), i64f[:]))
        step("V",
             lambda e, j=j: e.tensor_copy(pwt(j, 2), pD[0:64, 0:64]),
             lambda e, j=j: e.tensor_copy(pwt(j, 3), pD[0:64, 64:128]),
             lambda e, j=j: e.tensor_scalar(pwt(j, 4), pD[0:64, 64:128], -1.0, None, OP.mult))
        step("P",
             lambda e, j=j: e.matmul(pD[0:64, 512:576], pwt(j, 2), pwt(j, 0), start=True, stop=False),
             lambda e, j=j: e.matmul(pD[0:64, 512:576], pwt(j, 4), pwt(j, 1), start=False, stop=True),
             lambda e, j=j: e.matmul(pD[0:64, 576:640], pwt(j, 2), pwt(j, 1), start=True, stop=False),
             lambda e, j=j: e.matmul(pD[0:64, 576:640], pwt(j, 3), pwt(j, 0), start=False, stop=True))
        if PHASE >= 2:
            em("P", e8_unit_p(j))
        if j >= 1:
            em("P", st1_mm_p(2 * j - 2))
            em("P", st1_mm_p(2 * j - 1))
        fns = [lambda e, j=j: e.tensor_copy(pwt(j + 1, 0), pD[0:64, 512:576]),
               lambda e, j=j: e.tensor_copy(pwt(j + 1, 1), pD[0:64, 576:640])]
        if j + 1 <= 6 or j + 1 == 12:
            fns.append(lambda e, j=j: e.tensor_scalar(pwt(j + 1, 5), pD[0:64, 576:640], -1.0, None, OP.mult))
        step("V", *fns)
        if PHASE >= 2:
            em("V", decode_unit_v(j))
        em("V", st1_sq_v(2 * j))
        em("V", st1_sq_v(2 * j + 1))

    w_re = acol(); w_im = acol(); nw_im = acol(); ctn_im = acol()
    step("V",
         lambda e: e.tensor_copy(w_re, ct_re),
         lambda e: e.tensor_scalar(ctn_im, ct_imr, -1.0, None, OP.mult),
         lambda e: e.tensor_copy(w_im, ctn_im))
    for it in range(NEU):
        step("P",
             lambda e: e.matmul(pD[0:64, 0:1], pwt(12, 0), w_re, start=True, stop=False),
             lambda e: e.matmul(pD[0:64, 0:1], pwt(12, 5), w_im, start=False, stop=True),
             lambda e: e.matmul(pD[0:64, 1:2], pwt(12, 0), w_im, start=True, stop=False),
             lambda e: e.matmul(pD[0:64, 1:2], pwt(12, 1), w_re, start=False, stop=True))
        step("V",
             lambda e: e.tensor_add(w_re, pD[0:64, 0:1], ct_re),
             lambda e: e.tensor_add(w_im, pD[0:64, 1:2], ctn_im))
    step("V", lambda e: e.tensor_scalar(nw_im, w_im, -1.0, None, OP.mult))

    step("V",
         lambda e: e.tensor_copy(bkre[:, 0:1], Bb_re),
         lambda e: e.tensor_copy(bkim[:, 0:1], Bb_im))
    for j in range(7):
        w_ = 1 << j
        step("P",
             lambda e, j=j, w_=w_: e.matmul(pD[0:64, 0:w_], pwt(j, 2), bkre[:, 0:w_], start=True, stop=False),
             lambda e, j=j, w_=w_: e.matmul(pD[0:64, 0:w_], pwt(j, 4), bkim[:, 0:w_], start=False, stop=True),
             lambda e, j=j, w_=w_: e.matmul(pD[0:64, 64:64 + w_], pwt(j, 2), bkim[:, 0:w_], start=True, stop=False),
             lambda e, j=j, w_=w_: e.matmul(pD[0:64, 64:64 + w_], pwt(j, 3), bkre[:, 0:w_], start=False, stop=True))
        if PHASE >= 2 and j >= 3:
            em("P", e8_unit_p(9 + j))
        if j == 0:
            em("P", st1_mm_p(22))
            em("P", st1_mm_p(23))
        else:
            em("P", st1_mm_p(23 + j))
        step("V",
             lambda e, w_=w_: e.tensor_copy(bkre[:, w_:2 * w_], pD[0:64, 0:w_]),
             lambda e, w_=w_: e.tensor_copy(bkim[:, w_:2 * w_], pD[0:64, 64:64 + w_]))
        if PHASE >= 2 and j >= 3:
            em("V", decode_unit_v(9 + j))
        em("V", st1_sq_v(24 + j))

    em("V", st1_sq_v(31))
    em("P", st1_mm_p(30))
    em("P", st1_mm_p(31))

    step("P",
         lambda e: e.matmul(pD[0:64, 0:1], pwt(0, 0), w_re, start=True, stop=False),
         lambda e: e.matmul(pD[0:64, 0:1], pwt(0, 5), w_im, start=False, stop=True),
         lambda e: e.matmul(pD[0:64, 1:2], pwt(0, 0), w_im, start=True, stop=False),
         lambda e: e.matmul(pD[0:64, 1:2], pwt(0, 1), w_re, start=False, stop=True))
    step("V",
         lambda e: e.tensor_copy(zre[:, 0:1], pD[0:64, 0:1]),
         lambda e: e.tensor_copy(zim[:, 0:1], pD[0:64, 1:2]))
    for j in range(7):
        w_ = 1 << j
        step("P",
             lambda e, j=j, w_=w_: e.matmul(pD[0:64, 0:w_], pwt(j, 0), zre[:, 0:w_], start=True, stop=False),
             lambda e, j=j, w_=w_: e.matmul(pD[0:64, 0:w_], pwt(j, 5), zim[:, 0:w_], start=False, stop=True),
             lambda e, j=j, w_=w_: e.matmul(pD[0:64, 64:64 + w_], pwt(j, 0), zim[:, 0:w_], start=True, stop=False),
             lambda e, j=j, w_=w_: e.matmul(pD[0:64, 64:64 + w_], pwt(j, 1), zre[:, 0:w_], start=False, stop=True))
        step("V",
             lambda e, w_=w_: e.tensor_copy(zre[:, w_:2 * w_], pD[0:64, 0:w_]),
             lambda e, w_=w_: e.tensor_copy(zim[:, w_:2 * w_], pD[0:64, 64:64 + w_]))

    kfr = arow_(128)
    step("P",
         lambda e: e.matmul(pD[0:1, 0:128], w_re, bkre[:], start=True, stop=False),
         lambda e: e.matmul(pD[0:1, 0:128], nw_im, bkim[:], start=False, stop=True))
    step("V",
         lambda e: e.tensor_copy(kfr, pD[0:1, 0:128]),
         lambda e: e.tensor_scalar(kfr[:, 0:1], kfr[:, 0:1], scalt[:, 0:1], None, OP.add),
         lambda e: e.tensor_copy(krow[:], kfr),
         lambda e: e.memset(z128[:], 0.0))
    KRC["v"] = seqcnt["V"]

    def s_toep2(e):
        e.wait_ge(s_tp, 32)
        for p in range(64, P):
            e.dma_start(out=T0b[p:p + 1, :],
                        in_=kpad_d[0:1, 128 - p:256 - p]).then_inc(s_tp, 16)
    step("S", s_toep2, raw=True)

    step("P", lambda e: e.transpose(pD[0:128, 0:64], bkre[:], i64f[:]))
    step("V", lambda e: e.tensor_copy(t128[:], pD[0:128, 0:64]))
    step("P", lambda e: e.matmul(pD[0:128, 512:576], j128b[:], t128[:], start=True, stop=True))
    step("V", lambda e: e.tensor_copy(BmT[:, 0:64], pD[0:128, 512:576]))
    step("P", lambda e: e.transpose(pD[0:128, 0:64], bkim[:], i64f[:]))
    step("V", lambda e: e.tensor_copy(t128[:], pD[0:128, 0:64]))
    step("P", lambda e: e.matmul(pD[0:128, 512:576], j128b[:], t128[:], start=True, stop=True))
    step("V",
         lambda e: e.tensor_copy(BmT[:, 64:128], pD[0:128, 512:576]),
         lambda e: e.tensor_copy(CBl[0:64, :], zre[:]),
         lambda e: e.tensor_scalar(CBl[64:128, :], zim[:], -1.0, None, OP.mult),
         lambda e: e.tensor_copy(ACl[0:64, 0:64], pwt(7, 2)),
         lambda e: e.tensor_copy(ACl[0:64, 64:128], pwt(7, 3)),
         lambda e: e.tensor_scalar(ACl[64:128, 0:64], pwt(7, 3), -1.0, None, OP.mult),
         lambda e: e.tensor_copy(ACl[64:128, 64:128], pwt(7, 2)),
         lambda e: e.memset(sst[0][:], 0.0))

    step("S")
    step("P")

    def rows_math(psum_sum, psum_sq, mrow, srow):
        return [
            lambda e: e.tensor_scalar(mrow, psum_sum, 1.0 / 4096.0, None, OP.mult),
            lambda e: e.tensor_scalar(rowu[:, 0:NTOK], psum_sq, 1.0 / 4096.0, None, OP.mult),
            lambda e: e.tensor_mul(srow, mrow, mrow),
            lambda e: e.tensor_sub(rowu[:, 0:NTOK], rowu[:, 0:NTOK], srow),
            lambda e: e.tensor_scalar(rowu[:, 0:NTOK], rowu[:, 0:NTOK], 1e-5, None, OP.add),
            lambda e: e.reciprocal(srow, rowu[:, 0:NTOK]),
        ]
    step("V",
         lambda e: e.wait_ge(c_st, NCH),
         *rows_math(pA[0:1, :], pB[0:1, :],
                    rowt[:, 0:NTOK], rowt[:, NTOK:2 * NTOK]))
    step("S", lambda e: e.activation(rowt[:, NTOK:2 * NTOK],
                                     rowt[:, NTOK:2 * NTOK], AF.Sqrt))
    step("V",
         lambda e: e.tensor_copy(xsq[0:1, 0:NTOK], rowt[:, 0:NTOK]),
         lambda e: e.tensor_copy(ln1b[0:1, 0:NTOK], rowt[:, NTOK:2 * NTOK]))
    step("P",
         lambda e: e.matmul(pA[:, 0:H], onesrbf[:], xsq[0:1, 0:H], start=True, stop=True),
         lambda e: e.matmul(pA[:, H:NTOK], onesrbf[:], xsq[0:1, H:NTOK], start=True, stop=True),
         lambda e: e.matmul(pB[:, 0:H], onesrbf[:], ln1b[0:1, 0:H], start=True, stop=True),
         lambda e: e.matmul(pB[:, H:NTOK], onesrbf[:], ln1b[0:1, H:NTOK], start=True, stop=True))
    step("S",
         lambda e: e.copy(m1b[:], pA[:, 0:NTOK]),
         lambda e: e.copy(s1b[:], pB[:, 0:NTOK]))
    barrier()

    def st2_sq(e, c):
        e.wait_ge(c_x1, c + 1)
        if c >= 2:
            e.wait_ge(c_st, NCH + c - 1)
        e.activation(xsq[:, (c % 2) * NTOK:(c % 2 + 1) * NTOK],
                     ux[:, c * NTOK:(c + 1) * NTOK], AF.Square).then_inc(c_sq, 1)

    def st2_mm(e, c):
        if c == 0:
            e.wait_ge(c_dec, 32)
        e.wait_ge(c_sq, NCH + c + 1)
        for h_ in range(2):
            e.matmul(pC[0:1, h_ * H:(h_ + 1) * H], onesbf[:, 0:1],
                     ux[:, c * NTOK + h_ * H:c * NTOK + (h_ + 1) * H],
                     start=(c == 0), stop=(c == NCH - 1))
            ins = e.matmul(pD[0:1, h_ * H:(h_ + 1) * H], onesbf[:, 0:1],
                     xsq[:, (c % 2) * NTOK + h_ * H:(c % 2) * NTOK + (h_ + 1) * H],
                     start=(c == 0), stop=(c == NCH - 1))
        ins.then_inc(c_st, 1)

    def v_conv(e):
        for c in range(NCH):
            lt = ln1b[:, (c % 2) * NTOK:(c % 2 + 1) * NTOK]
            uc = ux[:, c * NTOK:(c + 1) * NTOK]
            if c >= 2:
                e.wait_ge(c_s, c - 1)
            e.tensor_sub(lt, uc, m1b[:])
            e.tensor_mul(lt, lt, s1b[:]).then_inc(c_ln, 1)
            e.wait_ge(c_y, c + 1)
            e.tensor_add(uc[:, 0:H], uc[:, 0:H], pA[:, 0:H])
            e.tensor_add(uc[:, H:NTOK], uc[:, H:NTOK], pA[:, H:NTOK]).then_inc(c_x1, 1)
            if PHASE >= 2 and c < 16:
                decode_unit_v(16 + c)(e)
    em("V", v_conv)

    def p_conv(e):
        for c in range(NCH):
            lt = ln1b[:, (c % 2) * NTOK:(c % 2 + 1) * NTOK]
            sprev = sst[c % 2]
            e.wait_ge(c_ln, c + 1)
            e.wait_ge(s_tp, 32 + 128 * 16)
            if c >= 1:
                e.wait_ge(c_x1, c)
                e.wait_ge(c_sev, c)
            for h_ in range(2):
                sl_ = slice(h_ * H, (h_ + 1) * H)
                e.matmul(pA[:, sl_], T0b[:], lt[:, sl_], start=True, stop=False)
                ins = e.matmul(pA[:, sl_], CBl[:], sprev[:, sl_], start=False, stop=True)
            ins.then_inc(c_y, 1)
            for h_ in range(2):
                sl_ = slice(h_ * H, (h_ + 1) * H)
                e.matmul(pB[:, sl_], ACl[:], sprev[:, sl_], start=True, stop=False)
                ins = e.matmul(pB[:, sl_], BmT[:], lt[:, sl_], start=False, stop=True)
            ins.then_inc(c_s, 1)
            if PHASE >= 2 and c < 16:
                e8_unit_p(16 + c)(e)
            if PHASE >= 3 and 18 <= c:
                st2_mm(e, 2 * (c - 18))
                st2_mm(e, 2 * (c - 18) + 1)
    em("P", p_conv)

    def s_conv(e):
        for c in range(NCH):
            e.wait_ge(c_s, c + 1)
            snew = sst[(c + 1) % 2]
            e.copy(snew[:, 0:H], pB[:, 0:H])
            e.copy(snew[:, H:NTOK], pB[:, H:NTOK]).then_inc(c_sev, 1)
            if PHASE >= 3 and 17 <= c:
                st2_sq(e, 2 * (c - 17))
                if 2 * (c - 17) + 1 < 30:
                    st2_sq(e, 2 * (c - 17) + 1)
    em("S", s_conv)

    barrier()

    def s_sq2(e):
        for c in range(30, NCH):
            st2_sq(e, c)
    em("S", s_sq2)

    def p_st2(e):
        for c in range(28, NCH):
            st2_mm(e, c)
    em("P", p_st2)
    step("S")
    step("P")

    step("V",
         lambda e: e.wait_ge(c_st, 2 * NCH),
         *rows_math(pC[0:1, :], pD[0:1, :],
                    rowt[:, 0:NTOK], rowt[:, NTOK:2 * NTOK]))
    step("S", lambda e: e.activation(rowt[:, NTOK:2 * NTOK],
                                     rowt[:, NTOK:2 * NTOK], AF.Sqrt))
    step("S", lambda e: e.activation(rowu[:, 0:NTOK], rowu[:, 0:NTOK], AF.Sqrt))
    step("V", lambda e: e.tensor_mul(rowu[:, 0:NTOK], rowu[:, 0:NTOK], rowt[:, 0:NTOK]))
    step("S", lambda e: e.copy(xsq[0:1, 0:NTOK], rowu[:, 0:NTOK]))
    step("P",
         lambda e: e.matmul(pA[:, 0:H], onesrf[:], rowt[:, NTOK:NTOK + H], start=True, stop=True),
         lambda e: e.matmul(pA[:, H:NTOK], onesrf[:], rowt[:, NTOK + H:2 * NTOK], start=True, stop=True),
         lambda e: e.matmul(pB[:, 0:H], onesrf[:], rowt[:, 0:H], start=True, stop=True),
         lambda e: e.matmul(pB[:, H:NTOK], onesrf[:], rowt[:, H:NTOK], start=True, stop=True))
    m2b = ln1b[:, 0:NTOK]
    step("S",
         lambda e: e.copy(s2b[:], pA[:, 0:NTOK]),
         lambda e: e.copy(m2b, pB[:, 0:NTOK]))
    SC2["v"] = seqcnt["S"]

    def v_sub(e):
        e.wait_ge(bS, SC2["v"])
        for c in range(NCH):
            e.wait_ge(c_st, NCH + c + 1)
            e.tensor_sub(ux[:, c * NTOK:(c + 1) * NTOK],
                         ux[:, c * NTOK:(c + 1) * NTOK], m2b).then_inc(c_sub, 1)
    if PHASE >= 4:
        em("V", v_sub)

    if PHASE < 4:
        def s_store_x1(e):
            e.wait_ge(c_x1, NCH)
            for oc in range(NCH):
                e.activation(outb[:, (oc % 2) * NTOK:(oc % 2 + 1) * NTOK],
                             ux[:, oc * NTOK:(oc + 1) * NTOK], AF.Identity).then_inc(c_ev, 1)
                e.wait_ge(c_ev, oc + 1)
                e.dma_start(out=out_t[oc * P:(oc + 1) * P, :],
                            in_=outb[:, (oc % 2) * NTOK:(oc % 2 + 1) * NTOK]).then_inc(s_out, 16)
                if oc >= 1:
                    e.wait_ge(s_out, 16 * oc)
        em("S", s_store_x1)
        if PHASE in (2, 3):
            def v_dec_tail(e):
                for k in range(NCH, NUNIT):
                    decode_unit_v(k)(e)
            def p_dec_tail(e):
                for k in range(NCH, NUNIT):
                    e8_unit_p(k)(e)
            em("V", v_dec_tail)
            em("P", p_dec_tail)

    import os as _os
    if _os.environ.get("KNOPTEST"):
        def p_mid(e):
            for _ in range(110):
                e.nop()
    elif False:
        pass
    if _os.environ.get("KNOPTEST"):
        pass
    else:
     def p_mid(e):
        for c in range(NCH):
            if PHASE >= 4:
                e.wait_ge(c_sub, c + 1)
            for h_ in range(2):
                e.matmul(pA[0:16, h_ * H:(h_ + 1) * H],
                         atb[:, c * 16:(c + 1) * 16],
                         ux[:, c * NTOK + h_ * H:c * NTOK + (h_ + 1) * H],
                         start=(c == 0), stop=(c == NCH - 1))
            ins = e.matmul(pB[0:16, 0:1], atb[:, c * 16:(c + 1) * 16], onesbf[:, 0:1],
                           start=(c == 0), stop=(c == NCH - 1))
        return ins
    if PHASE == 4:
        def v_dec_tail2(e):
            for k in range(NCH, NUNIT):
                decode_unit_v(k)(e)
        def p_dec_tail2(e):
            for k in range(NCH, NUNIT):
                e8_unit_p(k)(e)
        em("V", v_dec_tail2)
        em("P", p_dec_tail2)
        barrier()
    if PHASE >= 3:
        step("P", p_mid)
        step("V",
             lambda e: e.tensor_copy(midb[:], pA[0:16, 0:NTOK]),
             lambda e: e.tensor_copy(arowb[:], pB[0:16, 0:1]))
        SC3["v"] = seqcnt["V"]
        barrier()

    def p_ffn(e):
        for oc in range(NCH):
            wt = wtb[:, (oc % 2) * D:(oc % 2 + 1) * D]
            if oc == 0:
                e.wait_ge(bV, SC3["v"])
            e.wait_ge(s_wt, 16 * (2 * oc + 1))
            if oc >= 2:
                e.wait_ge(c_ev, oc - 1)
            for c in range(NCH):
                if oc == 0:
                    e.wait_ge(c_sub, c + 1)
                if c == 16:
                    e.wait_ge(s_wt, 16 * (2 * oc + 2))
                ins_c = None
                for h_ in range(2):
                    ins_c = e.matmul((pA if oc % 2 == 0 else pB)[:, h_ * H:(h_ + 1) * H],
                             wt[:, c * P:(c + 1) * P],
                             ux[:, c * NTOK + h_ * H:c * NTOK + (h_ + 1) * H],
                             start=(c == 0), stop=False)
                if c == 15:
                    ins_c.then_inc(c_fh, 1)
            ins = None
            for h_ in range(2):
                e.matmul((pA if oc % 2 == 0 else pB)[:, h_ * H:(h_ + 1) * H],
                         btb[:, oc * P:(oc + 1) * P],
                         midb[:, h_ * H:(h_ + 1) * H],
                         start=False, stop=False)
                ins = e.matmul((pA if oc % 2 == 0 else pB)[:, h_ * H:(h_ + 1) * H],
                         onesrbf[:, 0:P],
                         xsq[0:1, h_ * H:(h_ + 1) * H],
                         start=False, stop=True)
            ins.then_inc(c_ffn, 1)
            if oc < 16 and PHASE >= 5:
                e8_unit_p(32 + 2 * oc)(e)
                e8_unit_p(33 + 2 * oc)(e)
    if PHASE >= 4:
        em("P", p_ffn)

    def s_ffn(e):
        for oc in range(NCH):
            e.wait_ge(c_ev, oc + 1)
            e.dma_start(out=out_t[oc * P:(oc + 1) * P, :],
                        in_=outb[:, (oc % 2) * NTOK:(oc % 2 + 1) * NTOK]).then_inc(s_out, 16)
    if PHASE >= 4:
        em("S", s_ffn)

    def v_ffn(e):
        for oc in range(NCH):
            pp = pA if oc % 2 == 0 else pB
            e.wait_ge(c_ffn, oc + 1)
            if oc >= 2:
                e.wait_ge(s_out, 16 * (oc - 1))
            ob = outb[:, (oc % 2) * NTOK:(oc % 2 + 1) * NTOK]
            e.tensor_mul(tevac[:], pp[:, 0:NTOK], s2b[:])
            e.tensor_add(ob, tevac[:], ux[:, oc * NTOK:(oc + 1) * NTOK]).then_inc(c_ev, 1)
            if oc < 16 and PHASE >= 5:
                decode_unit_v(32 + 2 * oc)(e)
                decode_unit_v(33 + 2 * oc)(e)
    if PHASE >= 4:
        em("V", v_ffn)

    if DEBUG:
        def g_dbg(g):
            g.wait_ge(s_out, 32 * 16)
            dumps = [(dbg["K"], krow[:]), (dbg["T0"], T0b[:]), (dbg["BmT"], BmT[:]),
                     (dbg["CBl"], CBl[:]), (dbg["ACl"], ACl[:]),
                     (dbg["Ab"][:, 0:64], pw[:, 0:64]),
                     (dbg["Ab"][:, 64:128], pw[:, 64:128]),
                     (dbg["x1"], ux[:, 0:NTOK]),
                     (dbg["ln"], ln1b[:, NTOK:2 * NTOK]),
                     ] + (
                         [(dbg["W0"], wtb[:, D:D + P])] if PHASE >= 4 else [])
            for o_, i_ in dumps:
                g.dma_start(out=o_, in_=i_).then_inc(s_dbg, 16)
            g.wait_ge(s_dbg, 16 * len(dumps))
        em("G", g_dbg)

    with es, nc.Block() as block:
        @block.vector
        def _(vector):
            for f in ops["V"]:
                f(vector)

        @block.tensor
        def _(tensor):
            for f in ops["P"]:
                f(tensor)

        @block.scalar
        def _(scalar):
            for f in ops["S"]:
                f(scalar)

        @block.gpsimd
        def _(gpsimd):
            for f in ops["G"]:
                f(gpsimd)

        @block.sync
        def _(sync):
            for f in ops["Y"]:
                f(sync)

    return nc


_NC_CACHE = {}


def kernel(**inputs):
    u = np.asarray(inputs["L0_attn_u"], np.float32)
    lam = np.asarray(inputs["L0_attn_Lambda_ri"], np.float32)
    p_ri = np.asarray(inputs["L0_attn_p_ri"], np.float32)
    q_ri = np.asarray(inputs["L0_attn_q_ri"], np.float32)
    vc = np.asarray(inputs["L0_attn_Vc_ri"], np.float32)
    ct = np.asarray(inputs["L0_attn_Ct_ri"], np.float32)
    bp = np.asarray(inputs["L0_attn_Bp"], np.float32)
    Dv = np.asarray(inputs["L0_attn_D"], np.float32)
    ls = np.asarray(inputs["L0_attn_log_step"], np.float32)
    pk = np.asarray(inputs["L0_ffn_packed_weight"], np.int32)
    bs = np.asarray(inputs["L0_ffn_block_scales"], np.float32)
    gs = np.asarray(inputs["L0_ffn_global_scale"], np.float32)
    lA = np.asarray(inputs["L0_ffn_lora_A"], np.float32)
    lB = np.asarray(inputs["L0_ffn_lora_B"], np.float32)

    R = u.reshape(-1, D)
    packedT = np.ascontiguousarray(pk.T)
    bsT3 = np.ascontiguousarray(
        bs.T.reshape(N_IPC, 16, D).transpose(1, 0, 2).reshape(16, 16 * D))
    lA_T = np.ascontiguousarray(lA.T.reshape(NCH, P, 16).transpose(1, 0, 2).reshape(P, NCH * 16))
    lB_T = np.ascontiguousarray(lB.T)
    sm = np.ascontiguousarray(np.stack(
        [lam[:, 0], lam[:, 1], p_ri[:, 0], p_ri[:, 1], q_ri[:, 0], q_ri[:, 1],
         bp[:, 0], ct[0, :, 0], ct[0, :, 1]], axis=1))
    vctr = np.ascontiguousarray(vc[:, :, 0].T)
    vcti = np.ascontiguousarray(vc[:, :, 1].T)
    scal = np.ascontiguousarray(
        np.concatenate([Dv.reshape(1), ls.reshape(1), gs.reshape(1)]).reshape(1, 3))

    consts = {
        "cI64": np.eye(64, dtype=np.float32),
        "cJ128": np.eye(P, dtype=np.float32)[::-1].astype(MLBF),
        "cE8": np.repeat(0.5 * np.eye(16, dtype=np.float32), 8, axis=1).astype(MLBF),
        "cOnes": np.ones((P, 4), np.float32),
        "cOnesB": np.ones((P, 4), np.float32).astype(MLBF),
        "cOnesR": np.ones((1, P), np.float32),
        "cOnesRB": np.ones((1, P), np.float32).astype(MLBF),
    }

    key = ("prog", PHASE, DEBUG, SERIAL_DEC)
    if key not in _NC_CACHE:
        nc_new = build_program()
        mybir.codegen_inst_isa_subclasses(nc_new)
        _NC_CACHE[key] = nc_new
    nc = _NC_CACHE[key]

    in_maps = []
    for r in range(NC):
        m = {
            "u_s": np.ascontiguousarray(R[r * NTOK:(r + 1) * NTOK, :].T),
            "packedT": packedT, "bsT3": bsT3, "lA_T": lA_T, "lB_T": lB_T,
            "sm": sm, "VcT_re": vctr, "VcT_im": vcti, "scal": scal,
        }
        m.update(consts)
        in_maps.append(m)

    res = run_bass_kernel_spmd(nc, in_maps, core_ids=list(range(NC)))
    out = np.empty((NC * NTOK, D), np.float32)
    for r in range(NC):
        out[r * NTOK:(r + 1) * NTOK, :] = res.results[r]["out_s"].T
    if DEBUG:
        kernel._dbg = res.results
    return out.reshape(2, -1, D)


# version 26 (final)
# speedup vs baseline: 1.0978x; 1.0299x over previous
import numpy as np
import ml_dtypes
from contextlib import ExitStack

import concourse.bass as bass
import concourse.mybir as mybir
from concourse.bass_utils import run_bass_kernel_spmd
from concourse import dve_ops as _dve_ops
from concourse.dve_op import DveOp as _DveOp
from concourse.dve_spec import (Spec as _Spec, Src0 as _Src0, Src1 as _Src1,
                                C0 as _C0, C1 as _C1, relu as _relu,
                                lower as _dve_lower)


def _fp4_mag_sgn_ref(in0, in1, s0, s1, imm2):
    x = np.asarray(in0).astype(np.float32)
    g = np.asarray(in1).astype(np.float32)
    r1 = np.maximum(x - s0, 0.0)
    r2 = np.maximum(x + x - s1, 0.0)
    return (((x + r1) + r2) * g).astype(np.float32)


def _register_fp4_op():
    for o in _dve_ops.OPS:
        if o.name == "FP4_MAG_SGN":
            return o
    spec = _Spec(body=((_Src0 + _relu(_Src0 - _C0))
                       + _relu((_Src0 + _Src0) - _C1)) * _Src1)
    sha = {"v3": _dve_lower(spec, ver="v3", name="FP4_MAG_SGN").sha("v3")}
    op = _DveOp("FP4_MAG_SGN", spec, False, sha, reference=_fp4_mag_sgn_ref)
    _dve_ops.OPS.append(op)
    _dve_ops._SUB_OPCODE_FOR_NAME["FP4_MAG_SGN"] = (
        _dve_ops._CUSTOM_DVE_ROW_BASE + len(_dve_ops.OPS) - 1)
    _dve_ops.CUSTOM_DVE_SPECS["FP4_MAG_SGN"] = spec
    _dve_ops.CUSTOM_DVE_REFS["FP4_MAG_SGN"] = _fp4_mag_sgn_ref
    return op


FP4_OP = _register_fp4_op()

F32 = mybir.dt.float32
BF16 = mybir.dt.bfloat16
I32 = mybir.dt.int32
I16 = mybir.dt.int16
OP = mybir.AluOpType
AF = mybir.ActivationFunctionType
MLBF = ml_dtypes.bfloat16

P = 128
NTOK = 1024
D = 4096
NCH = 32
NC = 8
NEU = 10
N_IPC = 16
N_OSL = 4
OSL = 1024
NUNIT = N_IPC * N_OSL
H = 512
DEBUG = False
PHASE = 5
SERIAL_DEC = False


def build_program():
    nc = bass.Bass("TRN2", target_bir_lowering=False, detect_race_conditions=False)

    u_in = nc.dram_tensor("u_s", [D, NTOK], F32, kind="ExternalInput").ap()
    pk_in = nc.dram_tensor("packedT", [2048, D], I32, kind="ExternalInput").ap()
    bst_in = nc.dram_tensor("bsT3", [16, 16 * D], F32, kind="ExternalInput").ap()
    at_in = nc.dram_tensor("lA_T", [P, NCH * 16], F32, kind="ExternalInput").ap()
    bt_in = nc.dram_tensor("lB_T", [16, D], F32, kind="ExternalInput").ap()
    sm_in = nc.dram_tensor("sm", [64, 9], F32, kind="ExternalInput").ap()
    vcr_in = nc.dram_tensor("VcT_re", [64, 64], F32, kind="ExternalInput").ap()
    vci_in = nc.dram_tensor("VcT_im", [64, 64], F32, kind="ExternalInput").ap()
    scal_in = nc.dram_tensor("scal", [1, 3], F32, kind="ExternalInput").ap()
    i64_in = nc.dram_tensor("cI64", [64, 64], F32, kind="ExternalInput").ap()
    j128_in = nc.dram_tensor("cJ128", [P, P], BF16, kind="ExternalInput").ap()
    e8_in = nc.dram_tensor("cE8", [16, P], BF16, kind="ExternalInput").ap()
    ones_in = nc.dram_tensor("cOnes", [P, 4], F32, kind="ExternalInput").ap()
    onesb_in = nc.dram_tensor("cOnesB", [P, 4], BF16, kind="ExternalInput").ap()
    onesr_in = nc.dram_tensor("cOnesR", [1, P], F32, kind="ExternalInput").ap()
    onesrb_in = nc.dram_tensor("cOnesRB", [1, P], BF16, kind="ExternalInput").ap()

    out_t = nc.dram_tensor("out_s", [D, NTOK], F32, kind="ExternalOutput").ap()
    dbg = {}
    if DEBUG:
        for nm, shp in [("K", [1, P]), ("T0", [P, P]), ("BmT", [P, P]),
                        ("CBl", [P, P]), ("ACl", [P, P]), ("Ab", [64, P]),
                        ("x1", [P, NTOK]), ("W0", [P, P]), ("ln", [P, NTOK]),
                        ("smx", [64, 96]), ("smr", [1, 576]), ("g64", [64, 8])]:
            dbg[nm] = nc.dram_tensor("dbg_" + nm, shp, F32, kind="ExternalOutput").ap()

    wt_d = nc.dram_tensor("WT_d", [NCH, P, NCH, P], BF16, kind="Internal").ap()
    kpad_d = nc.dram_tensor("kpad_d", [1, 256], BF16, kind="Internal").ap()

    es = ExitStack()
    sb = lambda n, s, d: es.enter_context(nc.sbuf_tensor(n, s, d))
    ps_ = lambda n, s: es.enter_context(nc.psum_tensor(n, s, F32))
    sem = lambda n: es.enter_context(nc.semaphore(n))

    ux = sb("ux", [P, NCH * NTOK], BF16)
    ln1b = sb("ln1b", [P, 2 * NTOK], BF16)
    sst = [sb("sst0", [P, NTOK], BF16), sb("sst1", [P, NTOK], BF16)]
    m1b = sb("m1b", [P, NTOK], BF16)
    s1b = sb("s1b", [P, NTOK], BF16)
    s2b = sb("s2b", [P, NTOK], F32)
    xsq = sb("xsq", [P, 2 * NTOK], BF16)
    tevac = sb("tevac", [P, NTOK], BF16)
    outb = sb("outb", [P, 2 * NTOK], F32)
    T0b = sb("T0b", [P, P], BF16)
    BmT = sb("BmT", [P, P], BF16)
    CBl = sb("CBl", [P, P], BF16)
    ACl = sb("ACl", [P, P], BF16)
    wtb = sb("wtb", [P, 2 * D], BF16)
    colc = sb("colc", [P, 2], F32)
    midb = sb("midb", [16, NTOK], BF16)
    arowb = sb("arowb", [16, 1], BF16)
    atb = sb("atb", [P, NCH * 16], BF16)
    btb = sb("btb", [16, D], BF16)
    bstu = sb("bstu", [16, 2 * OSL], BF16)
    bstu16 = sb("bstu16", [16, 6 * OSL], BF16)
    e8g = sb("e8g", [16, P], BF16)
    slab = sb("slab", [P, 2 * OSL], I32)
    dc1 = sb("dc1", [P, 2 * OSL], I16)
    dc2 = sb("dc2", [P, 2 * OSL], I16)
    dc4 = sb("dc4", [P, 2 * OSL], I16)
    wlo = sb("wlo", [P, 2 * OSL], BF16)
    whi = sb("whi", [P, 2 * OSL], BF16)
    smt = sb("smt", [64, 9], F32)
    vctr = sb("vctr", [64, 64], F32)
    vcti = sb("vcti", [64, 64], F32)
    i64f = sb("i64f", [64, 64], F32)
    j128b = sb("j128b", [P, P], BF16)
    e8b = sb("e8b", [16, P], BF16)
    onesf = sb("onesf", [P, 4], F32)
    onesbf = sb("onesbf", [P, 4], BF16)
    onesrf = sb("onesrf", [1, P], F32)
    onesrbf = sb("onesrbf", [1, P], BF16)
    scalt = sb("scalt", [1, 4], F32)
    rowt = sb("rowt", [1, 2 * NTOK], F32)
    rowu = sb("rowu", [1, NTOK], F32)
    _pw_off = {}
    _pw_n = [0]
    for _j in range(13):
        _slots = (0, 1, 2, 3, 4, 5) if _j <= 6 else ((0, 1, 2, 3, 4) if _j <= 11 else (0, 1, 5))
        for _s in _slots:
            _pw_off[(_j, _s)] = _pw_n[0]
            _pw_n[0] += 64
    _pw_off[("scr", 0)] = _pw_n[0]
    _pw_n[0] += 64
    pw = sb("pw", [64, _pw_n[0]], F32)
    smx = sb("smx", [64, 24 * 4], F32)
    smr = sb("smr", [1, 576], F32)
    bkre = sb("bkre", [64, P], F32)
    bkim = sb("bkim", [64, P], F32)
    zre = sb("zre", [64, P], F32)
    zim = sb("zim", [64, P], F32)
    t128 = sb("t128", [P, 64], BF16)
    krow = sb("krow", [1, P], BF16)
    z128 = sb("z128", [1, P], BF16)
    g64 = sb("g64", [64, 8], F32)
    sc16 = sb("sc16", [16, 2], F32)

    pA = ps_("pA", [P, NTOK])
    pB = ps_("pB", [P, NTOK])
    pC = ps_("pC", [P, NTOK])
    pD = ps_("pD", [P, NTOK])

    s_sm = sem("s_sm")
    s_u = sem("s_u")
    KRC = {}

    def u_wait(e, c):
        e.wait_ge(s_u, 16 * (c + 1))
    s_pk = sem("s_pk")
    s_wo = sem("s_wo")
    s_wos = sem("s_wos")
    s_sl = sem("s_sl")
    s_sl2 = sem("s_sl2")
    s_bs = sem("s_bs")
    s_misc = sem("s_misc")
    s_wt = sem("s_wt")
    s_out = sem("s_out")
    s_tp = sem("s_tp")
    bV = sem("bV")
    bP = sem("bP")
    bS = sem("bS")
    c_ln = sem("c_ln")
    c_x1 = sem("c_x1")
    c_y = sem("c_y")
    c_s = sem("c_s")
    c_sev = sem("c_sev")
    c_sq = sem("c_sq")
    c_dec = sem("c_dec")
    c_e8 = sem("c_e8")
    c_ffn = sem("c_ffn")
    c_ev = sem("c_ev")
    c_cc = sem("c_cc")
    c_st = sem("c_st")
    c_sub = sem("c_sub")
    c_fh = sem("c_fh")
    SC2 = {}
    SC3 = {}
    s_dbg = sem("s_dbg")

    ops = {"V": [], "P": [], "S": [], "G": [], "Y": []}

    def em(key, f):
        ops[key].append(f)

    seqcnt = {"V": 0, "P": 0, "S": 0}
    seqsem = {"V": bV, "P": bP, "S": bS}

    def step(key, *fns, raw=False):
        waits = [(k, seqcnt[k]) for k in "VPS" if seqcnt[k] > 0]
        base = seqcnt[key]
        n = 1 if raw or not fns else len(fns)
        seqcnt[key] += n

        def f(eng, waits=waits, fns=fns, base=base, raw=raw):
            for k, v in waits:
                eng.wait_ge(seqsem[k], v)
            if raw or not fns:
                for g in fns:
                    g(eng)
                eng.sem_inc(seqsem[key], 1)
                return
            for i, g in enumerate(fns):
                if i > 0:
                    eng.wait_ge(seqsem[key], base + i)
                r = g(eng)
                ok = False
                if r is not None and hasattr(r, "then_inc"):
                    tn = type(getattr(r, "ins", r)).__name__
                    if "EventSemaphore" not in tn and "DMACopy" not in tn:
                        r.then_inc(seqsem[key], 1)
                        ok = True
                if not ok:
                    eng.sem_inc(seqsem[key], 1)
        em(key, f)

    def barrier():
        for k in "VPS":
            step(k)

    _smx_n = [0]

    def acol():
        i = _smx_n[0]
        _smx_n[0] += 1
        assert i < 96
        return smx[:, i:i + 1]

    _smr_n = [0]

    def arow_(w=1):
        i = _smr_n[0]
        _smr_n[0] += w
        assert _smr_n[0] <= 576
        return smr[:, i:i + w]

    def pwt(j, which):
        off = _pw_off[(j, which)]
        return pw[:, off:off + 64]

    UNITS = [(k // N_IPC, k % N_IPC) for k in range(NUNIT)]
    PSC = ["C"] * 16 + ["D" if k % 2 == 0 else "C" for k in range(16, 64)]

    def _uwait(k):
        for j in range(k - 1, -1, -1):
            if PSC[j] == PSC[k]:
                return j + 1
        return 0

    def y_main(y):
        y.dma_start(out=smt[:], in_=sm_in).then_inc(s_sm, 16)
        y.dma_start(out=vctr[:], in_=vcr_in).then_inc(s_sm, 16)
        y.dma_start(out=vcti[:], in_=vci_in).then_inc(s_sm, 16)
        y.dma_start(out=scalt[:, 0:3], in_=scal_in).then_inc(s_sm, 16)
        y.dma_start(out=i64f[:], in_=i64_in).then_inc(s_sm, 16)
        y.dma_start(out=j128b[:], in_=j128_in).then_inc(s_sm, 16)
        y.dma_start(out=e8b[:], in_=e8_in).then_inc(s_sm, 16)
        y.dma_start(out=onesf[:], in_=ones_in).then_inc(s_sm, 16)
        y.dma_start(out=onesbf[:], in_=onesb_in).then_inc(s_sm, 16)
        y.dma_start(out=onesrf[:], in_=onesr_in).then_inc(s_sm, 16)
        y.dma_start(out=onesrbf[:], in_=onesrb_in).then_inc(s_sm, 16)
        if PHASE >= 2:
            slab_load(y, 0, s_sl)
            slab_load(y, 1, s_sl)
            for k in range(16):
                y.wait_ge(c_dec, k + 1)
                w_out(y, k, s_wos)
                if k + 2 <= 17:
                    slab_load(y, k + 2, s_sl)
        y.wait_ge(bV, KRC["v"])
        y.dma_start(out=kpad_d[:, 0:128], in_=z128[:]).then_inc(s_tp, 16)
        y.dma_start(out=kpad_d[:, 128:256], in_=krow[:]).then_inc(s_tp, 16)
        y.wait_ge(s_tp, 32)
        for p in range(64):
            y.dma_start(out=T0b[p:p + 1, :],
                        in_=kpad_d[0:1, 128 - p:256 - p]).then_inc(s_tp, 16)
        for oc in (range(NCH) if PHASE >= 4 else []):
            osl = (oc * P) // OSL
            y.wait_ge(s_wos, 64 * 16)
            if osl >= 1:
                y.wait_ge(s_wo, 64 * 16 * osl)
            wd = wt_d[oc].rearrange("p ic q -> p (ic q)")
            if oc >= 2:
                y.wait_ge(c_fh, oc - 1)
            y.dma_start(
                out=wtb[:, (oc % 2) * D:(oc % 2) * D + D // 2],
                in_=wd[:, 0:D // 2],
            ).then_inc(s_wt, 16)
            if oc >= 2:
                y.wait_ge(c_ffn, oc - 1)
            y.dma_start(
                out=wtb[:, (oc % 2) * D + D // 2:(oc % 2) * D + D],
                in_=wd[:, D // 2:D],
            ).then_inc(s_wt, 16)
        y.wait_ge(s_out, 32 * 16)
    em("Y", y_main)

    def u_load(eng, c, sem_):
        eng.dma_start(
            out=ux[:, c * NTOK:(c + 1) * NTOK],
            in_=u_in[c * P:(c + 1) * P, :],
        ).then_inc(sem_, 16)

    wt4 = wt_d.rearrange("oc (p2 two) ic q -> ic two p2 oc q", two=2)

    def w_out(eng, k, sem_):
        osl, ipc = UNITS[k]
        for par, wsrc in ((0, wlo), (1, whi)):
            t = wsrc[:, (k % 2) * OSL:(k % 2 + 1) * OSL].rearrange(
                "p (ocs q) -> p ocs q", q=P)
            for h_ in range(2):
                eng.dma_start(
                    out=wt4[2 * ipc + h_, par, :, osl * 8:(osl + 1) * 8, :],
                    in_=t[h_ * 64:(h_ + 1) * 64],
                ).then_inc(sem_, 16)

    def slab_load(eng, k, sem_):
        osl, ipc = UNITS[k]
        eng.dma_start(
            out=slab[:, (k % 2) * OSL:(k % 2 + 1) * OSL],
            in_=pk_in[ipc * P:(ipc + 1) * P, osl * OSL:(osl + 1) * OSL],
        ).then_inc(sem_, 16)

    def bstu_load(g, k):
        osl, ipc = UNITS[k]
        out = bstu16[:, k * OSL:(k + 1) * OSL] if k < 6 else \
            bstu[:, (k % 2) * OSL:(k % 2 + 1) * OSL]
        g.dma_start(
            out=out,
            in_=bst_in[:, ipc * D + osl * OSL:ipc * D + (osl + 1) * OSL],
        ).then_inc(s_bs, 16)

    def g_main(g):
        g.dma_start(out=atb[:], in_=at_in).then_inc(s_misc, 16)
        g.dma_start(out=btb[:], in_=bt_in).then_inc(s_misc, 16)
        if PHASE >= 2:
            for k in range(8):
                bstu_load(g, k)
        for c in range(NCH):
            u_load(g, c, s_u)
        if PHASE >= 2:
            for k in range(8, 16):
                g.wait_ge(c_dec, k - 1)
                bstu_load(g, k)
        if PHASE < 2:
            return
        for k in (16, 17):
            g.wait_ge(c_dec, k - 1)
            bstu_load(g, k)
        for k in range(16, NUNIT):
            g.wait_ge(c_dec, k + 1)
            w_out(g, k, s_wo)
            if k + 2 < NUNIT:
                bstu_load(g, k + 2)
                slab_load(g, k + 2, s_sl2)
    em("G", g_main)

    def decode_unit_v(k):
        osl, ipc = UNITS[k]
        sl = slab[:, (k % 2) * OSL:(k % 2 + 1) * OSL]
        pscale = pC if PSC[k] == "C" else pD
        lo = wlo[:, (k % 2) * OSL:(k % 2 + 1) * OSL]
        hi = whi[:, (k % 2) * OSL:(k % 2 + 1) * OSL]
        c16a = dc1[:, 0:2 * OSL]
        t1s = dc2[:, 0:2 * OSL]

        def f(e, k=k):
            if k <= 17:
                e.wait_ge(s_sl, 16 * (k + 1))
            else:
                e.wait_ge(s_sl2, 16 * (k - 17))
            if 2 <= k <= 17:
                e.wait_ge(s_wos, 64 * (k - 1))
            elif k >= 18:
                e.wait_ge(s_wo, 64 * (k - 17))
            e.wait_ge(c_e8, k + 1)
            dm = dc4[:].bitcast(I32)
            e.tensor_scalar(dm, sl, 4097, None, OP.mult)
            dv = dm.bitcast(I16)
            e.tensor_scalar(c16a, dv, 7, None, OP.bitwise_and)
            e.tensor_scalar(t1s, dv, 8, 2, OP.bitwise_and, OP.logical_shift_right)
            e.tensor_scalar(t1s, t1s, -1, 1, OP.mult, OP.add)
            e._custom_dve(FP4_OP, out=dc4[:, 0:2 * OSL].bitcast(BF16),
                          in0=c16a, in1=t1s, s0=4.0, s1=12.0)
            bv = dc4[:, 0:2 * OSL].bitcast(BF16)
            e.tensor_tensor(lo, bv[:, 0::2], pscale[:, 0:OSL], OP.mult)
            e.tensor_tensor(hi, bv[:, 1::2], pscale[:, 0:OSL], OP.mult).then_inc(c_dec, 1)
        return f

    def st1_sq_v(c):
        def f(e, c=c):
            u_wait(e, c)
            if c >= 2:
                e.wait_ge(c_st, c - 1)
            e.tensor_mul(xsq[:, (c % 2) * NTOK:(c % 2 + 1) * NTOK],
                         ux[:, c * NTOK:(c + 1) * NTOK],
                         ux[:, c * NTOK:(c + 1) * NTOK]).then_inc(c_sq, 1)
        return f

    def st1_mm_p(c):
        def f(e, c=c):
            u_wait(e, c)
            e.wait_ge(c_sq, c + 1)
            for h_ in range(2):
                e.matmul(pA[0:1, h_ * H:(h_ + 1) * H], onesbf[:, 0:1],
                         ux[:, c * NTOK + h_ * H:c * NTOK + (h_ + 1) * H],
                         start=(c == 0), stop=(c == NCH - 1))
                ins = e.matmul(pB[0:1, h_ * H:(h_ + 1) * H], onesbf[:, 0:1],
                         xsq[:, (c % 2) * NTOK + h_ * H:(c % 2) * NTOK + (h_ + 1) * H],
                         start=(c == 0), stop=(c == NCH - 1))
            ins.then_inc(c_st, 1)
        return f

    def e8_unit_p(k):
        osl, ipc = UNITS[k]
        pscale = pC if PSC[k] == "C" else pD

        def f(e, k=k, osl=osl, ipc=ipc):
            e.wait_ge(s_bs, 16 * (k + 1))
            w = _uwait(k)
            if w > 0:
                e.wait_ge(c_dec, w)
            bs_ = bstu16[:, k * OSL:(k + 1) * OSL] if k < 6 else \
                bstu[:, (k % 2) * OSL:(k % 2 + 1) * OSL]
            e.matmul(pscale[:, 0:H], e8g[:], bs_[:, 0:H],
                     start=True, stop=True)
            e.matmul(pscale[:, H:OSL], e8g[:], bs_[:, H:OSL],
                     start=True, stop=True).then_inc(c_e8, 1)
        return f


    ALLSM = 11 * 16

    step("S",
         lambda e: e.wait_ge(s_sm, ALLSM),
         lambda e: e.activation(scalt[:, 3:4], scalt[:, 1:2], AF.Exp))
    step("V", lambda e: e.wait_ge(s_sm, ALLSM), lambda e: e.memset(smr[:], 0.0))
    step("P", lambda e: e.wait_ge(s_sm, ALLSM))

    step("P",
         lambda e: e.matmul(pD[0:64, 512:516], onesrf[:, 0:64], scalt[:, 0:4],
                            start=True, stop=True),
         lambda e: e.matmul(pD[0:64, 0:1], vctr[:], smt[:, 6:7],
                            start=True, stop=True),
         lambda e: e.matmul(pD[0:64, 1:2], vcti[:], smt[:, 6:7],
                            start=True, stop=True))
    alph = acol(); Dcol = acol(); Bre = acol(); Bim = acol(); stepc = acol()
    step("V",
         lambda e: e.tensor_scalar(alph, pD[0:64, 515:516], 0.5, None, OP.mult),
         lambda e: e.tensor_copy(stepc, pD[0:64, 515:516]),
         lambda e: e.tensor_copy(Dcol, pD[0:64, 512:513]),
         lambda e: e.tensor_copy(sc16[:, 0:1], pD[0:16, 514:515]),
         lambda e: e.tensor_copy(Bre, pD[0:64, 0:1]),
         lambda e: e.tensor_copy(Bim, pD[0:64, 1:2]))
    step("V", lambda e: e.tensor_scalar(e8g[:], e8b[:], sc16[:, 0:1], None, OP.mult))

    lam_re = smt[:, 0:1]; lam_im = smt[:, 1:2]
    p_re = smt[:, 2:3]; p_im = smt[:, 3:4]
    q_re = smt[:, 4:5]; q_im = smt[:, 5:6]
    ct_re = smt[:, 7:8]; ct_imr = smt[:, 8:9]
    qc_im = acol()
    dr = acol(); di = acol(); mag = acol(); rec = acol()
    d_re = acol(); d_im = acol()
    step("V",
         lambda e: e.tensor_scalar(qc_im, q_im, -1.0, None, OP.mult),
         lambda e: e.tensor_scalar(dr, lam_re, alph, -1.0, OP.mult, OP.mult),
         lambda e: e.tensor_scalar(dr, dr, 1.0, None, OP.add),
         lambda e: e.tensor_scalar(di, lam_im, alph, -1.0, OP.mult, OP.mult),
         lambda e: e.tensor_mul(mag, dr, dr),
         lambda e: e.tensor_mul(rec, di, di),
         lambda e: e.tensor_add(mag, mag, rec),
         lambda e: e.reciprocal(rec, mag),
         lambda e: e.tensor_mul(d_re, dr, rec),
         lambda e: e.tensor_mul(d_im, di, rec),
         lambda e: e.tensor_scalar(d_im, d_im, -1.0, None, OP.mult))

    def cmul_cols(o_re, o_im, a_re, a_im, b_re, b_im, t1, t2):
        return [
            lambda e: e.tensor_mul(t1, a_re, b_re),
            lambda e: e.tensor_mul(t2, a_im, b_im),
            lambda e: e.tensor_sub(o_re, t1, t2),
            lambda e: e.tensor_mul(t1, a_re, b_im),
            lambda e: e.tensor_mul(t2, a_im, b_re),
            lambda e: e.tensor_add(o_im, t1, t2),
        ]

    tc1 = acol(); tc2 = acol()
    u1_re = acol(); u1_im = acol()
    v1_re = acol(); v1_im = acol()
    e_re = acol(); e_im = acol()
    a0_re = acol(); a0_im = acol()
    se_re = acol(); se_im = acol()
    sB_re = acol(); sB_im = acol()
    v1e_re = acol(); v1e_im = acol()
    dB_re = acol(); dB_im = acol()
    step("V", *cmul_cols(u1_re, u1_im, d_re, d_im, p_re, p_im, tc1, tc2))
    step("V", *cmul_cols(v1_re, v1_im, q_re, qc_im, d_re, d_im, tc1, tc2))
    step("V",
         lambda e: e.tensor_scalar(e_re, lam_re, alph, None, OP.mult),
         lambda e: e.tensor_scalar(e_re, e_re, 1.0, None, OP.add),
         lambda e: e.tensor_scalar(e_im, lam_im, alph, None, OP.mult))
    step("V", *cmul_cols(a0_re, a0_im, d_re, d_im, e_re, e_im, tc1, tc2))
    step("V", *cmul_cols(v1e_re, v1e_im, v1_re, v1_im, e_re, e_im, tc1, tc2))
    step("V", *cmul_cols(dB_re, dB_im, d_re, d_im, Bre, Bim, tc1, tc2))
    step("V", *cmul_cols(se_re, se_im, v1_re, v1_im, p_re, p_im, tc1, tc2))
    step("V", *cmul_cols(sB_re, sB_im, v1_re, v1_im, Bre, Bim, tc1, tc2))

    sig = arow_(2)
    v1B = arow_(2)
    step("P",
         lambda e: e.matmul(pD[0:1, 512:513], onesf[0:64, 0:1], se_re, start=True, stop=True),
         lambda e: e.matmul(pD[0:1, 513:514], onesf[0:64, 0:1], se_im, start=True, stop=True),
         lambda e: e.matmul(pD[0:1, 514:515], onesf[0:64, 0:1], sB_re, start=True, stop=True),
         lambda e: e.matmul(pD[0:1, 515:516], onesf[0:64, 0:1], sB_im, start=True, stop=True))
    step("V",
         lambda e: e.tensor_copy(sig, pD[0:1, 512:514]),
         lambda e: e.tensor_copy(v1B, pD[0:1, 514:516]))

    arow2 = arow_(2)
    den = arow_(2)
    beta = arow_(2)
    gam = arow_(2)
    t1r = arow_(2)
    t0r = arow_(1)

    def cmul_row(o, a, b, t):
        return [
            lambda e: e.tensor_mul(t[:, 0:1], a[:, 0:1], b[:, 0:1]),
            lambda e: e.tensor_mul(o[:, 0:1], a[:, 1:2], b[:, 1:2]),
            lambda e: e.tensor_sub(o[:, 0:1], t[:, 0:1], o[:, 0:1]),
            lambda e: e.tensor_mul(t[:, 1:2], a[:, 0:1], b[:, 1:2]),
            lambda e: e.tensor_mul(o[:, 1:2], a[:, 1:2], b[:, 0:1]),
            lambda e: e.tensor_add(o[:, 1:2], t[:, 1:2], o[:, 1:2]),
        ]

    def cinv_scale_row(o, num_s, den_, t):
        return [
            lambda e: e.tensor_mul(t[:, 0:1], den_[:, 0:1], den_[:, 0:1]),
            lambda e: e.tensor_mul(t[:, 1:2], den_[:, 1:2], den_[:, 1:2]),
            lambda e: e.tensor_add(t[:, 0:1], t[:, 0:1], t[:, 1:2]),
            lambda e: e.reciprocal(t[:, 0:1], t[:, 0:1]),
            lambda e: e.tensor_mul(o[:, 0:1], den_[:, 0:1], t[:, 0:1]),
            lambda e: e.tensor_mul(o[:, 1:2], den_[:, 1:2], t[:, 0:1]),
            lambda e: e.tensor_scalar(o[:, 1:2], o[:, 1:2], -1.0, None, OP.mult),
            lambda e: e.tensor_scalar(o[:, 0:1], o[:, 0:1], num_s, None, OP.mult),
            lambda e: e.tensor_scalar(o[:, 1:2], o[:, 1:2], num_s, None, OP.mult),
        ]

    alphr = scalt[:, 1:2]
    alrow = arow_(1)
    step("V",
         lambda e: e.tensor_scalar(alrow, scalt[:, 3:4], 0.5, None, OP.mult))
    step("V",
         lambda e: e.tensor_scalar(den[:, 0:2], sig[:, 0:2], alrow, None, OP.mult),
         lambda e: e.tensor_scalar(den[:, 0:1], den[:, 0:1], 1.0, None, OP.add),
         *cinv_scale_row(beta, alrow, den, t1r))
    step("V",
         *cmul_row(arow2, beta, sig, t1r),
         lambda e: e.tensor_scalar(arow2[:, 0:2], arow2[:, 0:2], -1.0, None, OP.mult),
         lambda e: e.tensor_scalar(arow2[:, 0:1], arow2[:, 0:1], 1.0, None, OP.add),
         lambda e: e.tensor_scalar(gam[:, 0:2], arow2[:, 0:2], alrow, None, OP.mult))
    bv1B = arow_(2)
    step("V", *cmul_row(bv1B, beta, v1B, t1r))

    assert _smr_n[0] == 18
    step("P", lambda e: e.matmul(pD[0:64, 0:6], onesrf[:, 0:64],
                                 smr[:, 8:14], start=True, stop=True))
    gb_c = g64[:, 2:3]; gb_ci = g64[:, 3:4]
    bb_c = g64[:, 4:5]; bb_ci = g64[:, 5:6]
    bv_c = g64[:, 6:7]; bv_ci = g64[:, 7:8]
    step("P", lambda e: e.matmul(pD[0:64, 6:10], onesrf[:, 0:64],
                                 smr[:, 14:18], start=True, stop=True))
    step("V",
         lambda e: e.tensor_copy(g64[:, 4:6], pD[0:64, 0:2]),
         lambda e: e.tensor_copy(g64[:, 2:4], pD[0:64, 2:4]),
         lambda e: e.tensor_copy(g64[:, 6:8], pD[0:64, 8:10]))

    w1_re = acol(); w1_im = acol()
    Bb_re = acol(); Bb_im = acol()
    step("V",
         lambda e: e.tensor_scalar(tc1, q_re, gb_c, None, OP.mult),
         lambda e: e.tensor_scalar(tc2, qc_im, gb_ci, None, OP.mult),
         lambda e: e.tensor_sub(w1_re, tc1, tc2),
         lambda e: e.tensor_scalar(tc1, q_re, gb_ci, None, OP.mult),
         lambda e: e.tensor_scalar(tc2, qc_im, gb_c, None, OP.mult),
         lambda e: e.tensor_add(w1_im, tc1, tc2),
         lambda e: e.tensor_scalar(tc1, v1e_re, bb_c, None, OP.mult),
         lambda e: e.tensor_scalar(tc2, v1e_im, bb_ci, None, OP.mult),
         lambda e: e.tensor_sub(tc1, tc1, tc2),
         lambda e: e.tensor_add(w1_re, w1_re, tc1),
         lambda e: e.tensor_scalar(tc1, v1e_re, bb_ci, None, OP.mult),
         lambda e: e.tensor_scalar(tc2, v1e_im, bb_c, None, OP.mult),
         lambda e: e.tensor_add(tc1, tc1, tc2),
         lambda e: e.tensor_add(w1_im, w1_im, tc1))
    step("V",
         lambda e: e.tensor_scalar(tc1, u1_re, bv_c, None, OP.mult),
         lambda e: e.tensor_scalar(tc2, u1_im, bv_ci, None, OP.mult),
         lambda e: e.tensor_sub(tc1, tc1, tc2),
         lambda e: e.tensor_sub(Bb_re, dB_re, tc1),
         lambda e: e.tensor_scalar(tc1, u1_re, bv_ci, None, OP.mult),
         lambda e: e.tensor_scalar(tc2, u1_im, bv_c, None, OP.mult),
         lambda e: e.tensor_add(tc1, tc1, tc2),
         lambda e: e.tensor_sub(Bb_im, dB_im, tc1),
         lambda e: e.tensor_scalar(Bb_re, Bb_re, stepc, None, OP.mult),
         lambda e: e.tensor_scalar(Bb_im, Bb_im, stepc, None, OP.mult))

    u1r = arow_(64); u1i = arow_(64); w1r = arow_(64); w1i = arow_(64)
    nu1r = arow_(64); nu1i = arow_(64)
    step("P",
         lambda e: e.transpose(pD[0:1, 0:64], u1_re, i64f[:]),
         lambda e: e.transpose(pD[0:1, 64:128], u1_im, i64f[:]),
         lambda e: e.transpose(pD[0:1, 128:192], w1_re, i64f[:]),
         lambda e: e.transpose(pD[0:1, 192:256], w1_im, i64f[:]))
    step("V",
         lambda e: e.tensor_copy(u1r, pD[0:1, 0:64]),
         lambda e: e.tensor_copy(u1i, pD[0:1, 64:128]),
         lambda e: e.tensor_copy(w1r, pD[0:1, 128:192]),
         lambda e: e.tensor_copy(w1i, pD[0:1, 192:256]),
         lambda e: e.tensor_scalar(nu1r, u1r, -1.0, None, OP.mult),
         lambda e: e.tensor_scalar(nu1i, u1i, -1.0, None, OP.mult))
    step("P",
         lambda e: e.matmul(pD[0:64, 512:576], nu1r, w1r, start=True, stop=False),
         lambda e: e.matmul(pD[0:64, 512:576], u1i, w1i, start=False, stop=True),
         lambda e: e.matmul(pD[0:64, 576:640], nu1r, w1i, start=True, stop=False),
         lambda e: e.matmul(pD[0:64, 576:640], nu1i, w1r, start=False, stop=True))
    dgt = pw[:, _pw_off[("scr", 0)]:_pw_off[("scr", 0)] + 64]
    step("V",
         lambda e: e.tensor_scalar(dgt, i64f[:], a0_re, None, OP.mult),
         lambda e: e.tensor_add(pwt(0, 0), dgt, pD[0:64, 512:576]),
         lambda e: e.tensor_scalar(dgt, i64f[:], a0_im, None, OP.mult),
         lambda e: e.tensor_add(pwt(0, 1), dgt, pD[0:64, 576:640]),
         lambda e: e.tensor_scalar(pwt(0, 5), pwt(0, 1), -1.0, None, OP.mult))

    for j in range(12):
        step("P",
             lambda e, j=j: e.transpose(pD[0:64, 0:64], pwt(j, 0), i64f[:]),
             lambda e, j=j: e.transpose(pD[0:64, 64:128], pwt(j, 1), i64f[:]))
        step("V",
             lambda e, j=j: e.tensor_copy(pwt(j, 2), pD[0:64, 0:64]),
             lambda e, j=j: e.tensor_copy(pwt(j, 3), pD[0:64, 64:128]),
             lambda e, j=j: e.tensor_scalar(pwt(j, 4), pD[0:64, 64:128], -1.0, None, OP.mult))
        step("P",
             lambda e, j=j: e.matmul(pD[0:64, 512:576], pwt(j, 2), pwt(j, 0), start=True, stop=False),
             lambda e, j=j: e.matmul(pD[0:64, 512:576], pwt(j, 4), pwt(j, 1), start=False, stop=True),
             lambda e, j=j: e.matmul(pD[0:64, 576:640], pwt(j, 2), pwt(j, 1), start=True, stop=False),
             lambda e, j=j: e.matmul(pD[0:64, 576:640], pwt(j, 3), pwt(j, 0), start=False, stop=True))
        if PHASE >= 2:
            em("P", e8_unit_p(j))
        if j >= 1:
            em("P", st1_mm_p(2 * j - 2))
            em("P", st1_mm_p(2 * j - 1))
        fns = [lambda e, j=j: e.tensor_copy(pwt(j + 1, 0), pD[0:64, 512:576]),
               lambda e, j=j: e.tensor_copy(pwt(j + 1, 1), pD[0:64, 576:640])]
        if j + 1 <= 6 or j + 1 == 12:
            fns.append(lambda e, j=j: e.tensor_scalar(pwt(j + 1, 5), pD[0:64, 576:640], -1.0, None, OP.mult))
        step("V", *fns)
        if PHASE >= 2:
            em("V", decode_unit_v(j))
        em("V", st1_sq_v(2 * j))
        em("V", st1_sq_v(2 * j + 1))

    w_re = acol(); w_im = acol(); nw_im = acol(); ctn_im = acol()
    step("V",
         lambda e: e.tensor_copy(w_re, ct_re),
         lambda e: e.tensor_scalar(ctn_im, ct_imr, -1.0, None, OP.mult),
         lambda e: e.tensor_copy(w_im, ctn_im))
    for it in range(NEU):
        step("P",
             lambda e: e.matmul(pD[0:64, 0:1], pwt(12, 0), w_re, start=True, stop=False),
             lambda e: e.matmul(pD[0:64, 0:1], pwt(12, 5), w_im, start=False, stop=True),
             lambda e: e.matmul(pD[0:64, 1:2], pwt(12, 0), w_im, start=True, stop=False),
             lambda e: e.matmul(pD[0:64, 1:2], pwt(12, 1), w_re, start=False, stop=True))
        step("V",
             lambda e: e.tensor_add(w_re, pD[0:64, 0:1], ct_re),
             lambda e: e.tensor_add(w_im, pD[0:64, 1:2], ctn_im))
    step("V", lambda e: e.tensor_scalar(nw_im, w_im, -1.0, None, OP.mult))

    step("V",
         lambda e: e.tensor_copy(bkre[:, 0:1], Bb_re),
         lambda e: e.tensor_copy(bkim[:, 0:1], Bb_im))
    for j in range(7):
        w_ = 1 << j
        step("P",
             lambda e, j=j, w_=w_: e.matmul(pD[0:64, 0:w_], pwt(j, 2), bkre[:, 0:w_], start=True, stop=False),
             lambda e, j=j, w_=w_: e.matmul(pD[0:64, 0:w_], pwt(j, 4), bkim[:, 0:w_], start=False, stop=True),
             lambda e, j=j, w_=w_: e.matmul(pD[0:64, 64:64 + w_], pwt(j, 2), bkim[:, 0:w_], start=True, stop=False),
             lambda e, j=j, w_=w_: e.matmul(pD[0:64, 64:64 + w_], pwt(j, 3), bkre[:, 0:w_], start=False, stop=True))
        if PHASE >= 2 and j >= 3:
            em("P", e8_unit_p(9 + j))
        if j == 0:
            em("P", st1_mm_p(22))
            em("P", st1_mm_p(23))
        else:
            em("P", st1_mm_p(23 + j))
        step("V",
             lambda e, w_=w_: e.tensor_copy(bkre[:, w_:2 * w_], pD[0:64, 0:w_]),
             lambda e, w_=w_: e.tensor_copy(bkim[:, w_:2 * w_], pD[0:64, 64:64 + w_]))
        if PHASE >= 2 and j >= 3:
            em("V", decode_unit_v(9 + j))
        em("V", st1_sq_v(24 + j))

    em("V", st1_sq_v(31))
    em("P", st1_mm_p(30))
    em("P", st1_mm_p(31))

    step("P",
         lambda e: e.matmul(pD[0:64, 0:1], pwt(0, 0), w_re, start=True, stop=False),
         lambda e: e.matmul(pD[0:64, 0:1], pwt(0, 5), w_im, start=False, stop=True),
         lambda e: e.matmul(pD[0:64, 1:2], pwt(0, 0), w_im, start=True, stop=False),
         lambda e: e.matmul(pD[0:64, 1:2], pwt(0, 1), w_re, start=False, stop=True))
    step("V",
         lambda e: e.tensor_copy(zre[:, 0:1], pD[0:64, 0:1]),
         lambda e: e.tensor_copy(zim[:, 0:1], pD[0:64, 1:2]))
    for j in range(7):
        w_ = 1 << j
        step("P",
             lambda e, j=j, w_=w_: e.matmul(pD[0:64, 0:w_], pwt(j, 0), zre[:, 0:w_], start=True, stop=False),
             lambda e, j=j, w_=w_: e.matmul(pD[0:64, 0:w_], pwt(j, 5), zim[:, 0:w_], start=False, stop=True),
             lambda e, j=j, w_=w_: e.matmul(pD[0:64, 64:64 + w_], pwt(j, 0), zim[:, 0:w_], start=True, stop=False),
             lambda e, j=j, w_=w_: e.matmul(pD[0:64, 64:64 + w_], pwt(j, 1), zre[:, 0:w_], start=False, stop=True))
        step("V",
             lambda e, w_=w_: e.tensor_copy(zre[:, w_:2 * w_], pD[0:64, 0:w_]),
             lambda e, w_=w_: e.tensor_copy(zim[:, w_:2 * w_], pD[0:64, 64:64 + w_]))

    kfr = arow_(128)
    step("P",
         lambda e: e.matmul(pD[0:1, 0:128], w_re, bkre[:], start=True, stop=False),
         lambda e: e.matmul(pD[0:1, 0:128], nw_im, bkim[:], start=False, stop=True))
    step("V",
         lambda e: e.tensor_copy(kfr, pD[0:1, 0:128]),
         lambda e: e.tensor_scalar(kfr[:, 0:1], kfr[:, 0:1], scalt[:, 0:1], None, OP.add),
         lambda e: e.tensor_copy(krow[:], kfr),
         lambda e: e.memset(z128[:], 0.0))
    KRC["v"] = seqcnt["V"]

    def s_toep2(e):
        e.wait_ge(s_tp, 32)
        for p in range(64, P):
            e.dma_start(out=T0b[p:p + 1, :],
                        in_=kpad_d[0:1, 128 - p:256 - p]).then_inc(s_tp, 16)
    step("S", s_toep2, raw=True)

    step("P", lambda e: e.transpose(pD[0:128, 0:64], bkre[:], i64f[:]))
    step("V", lambda e: e.tensor_copy(t128[:], pD[0:128, 0:64]))
    step("P", lambda e: e.matmul(pD[0:128, 512:576], j128b[:], t128[:], start=True, stop=True))
    step("V", lambda e: e.tensor_copy(BmT[:, 0:64], pD[0:128, 512:576]))
    step("P", lambda e: e.transpose(pD[0:128, 0:64], bkim[:], i64f[:]))
    step("V", lambda e: e.tensor_copy(t128[:], pD[0:128, 0:64]))
    step("P", lambda e: e.matmul(pD[0:128, 512:576], j128b[:], t128[:], start=True, stop=True))
    step("V",
         lambda e: e.tensor_copy(BmT[:, 64:128], pD[0:128, 512:576]),
         lambda e: e.tensor_copy(CBl[0:64, :], zre[:]),
         lambda e: e.tensor_scalar(CBl[64:128, :], zim[:], -1.0, None, OP.mult),
         lambda e: e.tensor_copy(ACl[0:64, 0:64], pwt(7, 2)),
         lambda e: e.tensor_copy(ACl[0:64, 64:128], pwt(7, 3)),
         lambda e: e.tensor_scalar(ACl[64:128, 0:64], pwt(7, 3), -1.0, None, OP.mult),
         lambda e: e.tensor_copy(ACl[64:128, 64:128], pwt(7, 2)),
         lambda e: e.memset(sst[0][:], 0.0))

    step("S")
    step("P")

    def rows_math(psum_sum, psum_sq, mrow, srow):
        return [
            lambda e: e.tensor_scalar(mrow, psum_sum, 1.0 / 4096.0, None, OP.mult),
            lambda e: e.tensor_scalar(rowu[:, 0:NTOK], psum_sq, 1.0 / 4096.0, None, OP.mult),
            lambda e: e.tensor_mul(srow, mrow, mrow),
            lambda e: e.tensor_sub(rowu[:, 0:NTOK], rowu[:, 0:NTOK], srow),
            lambda e: e.tensor_scalar(rowu[:, 0:NTOK], rowu[:, 0:NTOK], 1e-5, None, OP.add),
            lambda e: e.reciprocal(srow, rowu[:, 0:NTOK]),
        ]
    step("V",
         lambda e: e.wait_ge(c_st, NCH),
         *rows_math(pA[0:1, :], pB[0:1, :],
                    rowt[:, 0:NTOK], rowt[:, NTOK:2 * NTOK]))
    step("S", lambda e: e.activation(rowt[:, NTOK:2 * NTOK],
                                     rowt[:, NTOK:2 * NTOK], AF.Sqrt))
    step("V",
         lambda e: e.tensor_copy(xsq[0:1, 0:NTOK], rowt[:, 0:NTOK]),
         lambda e: e.tensor_copy(ln1b[0:1, 0:NTOK], rowt[:, NTOK:2 * NTOK]))
    step("P",
         lambda e: e.matmul(pA[:, 0:H], onesrbf[:], xsq[0:1, 0:H], start=True, stop=True),
         lambda e: e.matmul(pA[:, H:NTOK], onesrbf[:], xsq[0:1, H:NTOK], start=True, stop=True),
         lambda e: e.matmul(pB[:, 0:H], onesrbf[:], ln1b[0:1, 0:H], start=True, stop=True),
         lambda e: e.matmul(pB[:, H:NTOK], onesrbf[:], ln1b[0:1, H:NTOK], start=True, stop=True))
    step("S",
         lambda e: e.copy(m1b[:], pA[:, 0:NTOK]),
         lambda e: e.copy(s1b[:], pB[:, 0:NTOK]))
    barrier()

    def st2_sq(e, c):
        e.wait_ge(c_x1, c + 1)
        if c >= 2:
            e.wait_ge(c_st, NCH + c - 1)
        e.activation(xsq[:, (c % 2) * NTOK:(c % 2 + 1) * NTOK],
                     ux[:, c * NTOK:(c + 1) * NTOK], AF.Square).then_inc(c_sq, 1)

    def st2_mm(e, c):
        if c == 0:
            e.wait_ge(c_dec, 32)
        e.wait_ge(c_sq, NCH + c + 1)
        for h_ in range(2):
            e.matmul(pC[0:1, h_ * H:(h_ + 1) * H], onesbf[:, 0:1],
                     ux[:, c * NTOK + h_ * H:c * NTOK + (h_ + 1) * H],
                     start=(c == 0), stop=(c == NCH - 1))
            ins = e.matmul(pD[0:1, h_ * H:(h_ + 1) * H], onesbf[:, 0:1],
                     xsq[:, (c % 2) * NTOK + h_ * H:(c % 2) * NTOK + (h_ + 1) * H],
                     start=(c == 0), stop=(c == NCH - 1))
        ins.then_inc(c_st, 1)

    def v_conv(e):
        for c in range(NCH):
            lt = ln1b[:, (c % 2) * NTOK:(c % 2 + 1) * NTOK]
            uc = ux[:, c * NTOK:(c + 1) * NTOK]
            if c >= 2:
                e.wait_ge(c_s, c - 1)
            e.tensor_sub(lt, uc, m1b[:])
            e.tensor_mul(lt, lt, s1b[:]).then_inc(c_ln, 1)
            e.wait_ge(c_y, c + 1)
            e.tensor_add(uc[:, 0:H], uc[:, 0:H], pA[:, 0:H])
            e.tensor_add(uc[:, H:NTOK], uc[:, H:NTOK], pA[:, H:NTOK]).then_inc(c_x1, 1)
            if PHASE >= 2 and c < 16:
                decode_unit_v(16 + c)(e)
    em("V", v_conv)

    def p_conv(e):
        for c in range(NCH):
            lt = ln1b[:, (c % 2) * NTOK:(c % 2 + 1) * NTOK]
            sprev = sst[c % 2]
            e.wait_ge(c_ln, c + 1)
            e.wait_ge(s_tp, 32 + 128 * 16)
            if c >= 1:
                e.wait_ge(c_x1, c)
                e.wait_ge(c_sev, c)
            for h_ in range(2):
                sl_ = slice(h_ * H, (h_ + 1) * H)
                e.matmul(pA[:, sl_], T0b[:], lt[:, sl_], start=True, stop=False)
                ins = e.matmul(pA[:, sl_], CBl[:], sprev[:, sl_], start=False, stop=True)
            ins.then_inc(c_y, 1)
            for h_ in range(2):
                sl_ = slice(h_ * H, (h_ + 1) * H)
                e.matmul(pB[:, sl_], ACl[:], sprev[:, sl_], start=True, stop=False)
                ins = e.matmul(pB[:, sl_], BmT[:], lt[:, sl_], start=False, stop=True)
            ins.then_inc(c_s, 1)
            if PHASE >= 2 and c < 16:
                e8_unit_p(16 + c)(e)
            if PHASE >= 3 and 18 <= c:
                st2_mm(e, 2 * (c - 18))
                st2_mm(e, 2 * (c - 18) + 1)
    em("P", p_conv)

    def s_conv(e):
        for c in range(NCH):
            e.wait_ge(c_s, c + 1)
            snew = sst[(c + 1) % 2]
            e.copy(snew[:, 0:H], pB[:, 0:H])
            e.copy(snew[:, H:NTOK], pB[:, H:NTOK]).then_inc(c_sev, 1)
            if PHASE >= 3 and 17 <= c:
                st2_sq(e, 2 * (c - 17))
                if 2 * (c - 17) + 1 < 30:
                    st2_sq(e, 2 * (c - 17) + 1)
    em("S", s_conv)

    barrier()

    def s_sq2(e):
        for c in range(30, NCH):
            st2_sq(e, c)
    em("S", s_sq2)

    def p_st2(e):
        for c in range(28, NCH):
            st2_mm(e, c)
    em("P", p_st2)
    step("S")
    step("P")

    step("V",
         lambda e: e.wait_ge(c_st, 2 * NCH),
         *rows_math(pC[0:1, :], pD[0:1, :],
                    rowt[:, 0:NTOK], rowt[:, NTOK:2 * NTOK]))
    step("S", lambda e: e.activation(rowt[:, NTOK:2 * NTOK],
                                     rowt[:, NTOK:2 * NTOK], AF.Sqrt))
    step("S", lambda e: e.activation(rowu[:, 0:NTOK], rowu[:, 0:NTOK], AF.Sqrt))
    step("V", lambda e: e.tensor_mul(rowu[:, 0:NTOK], rowu[:, 0:NTOK], rowt[:, 0:NTOK]))
    step("S", lambda e: e.copy(xsq[0:1, 0:NTOK], rowu[:, 0:NTOK]))
    step("P",
         lambda e: e.matmul(pA[:, 0:H], onesrf[:], rowt[:, NTOK:NTOK + H], start=True, stop=True),
         lambda e: e.matmul(pA[:, H:NTOK], onesrf[:], rowt[:, NTOK + H:2 * NTOK], start=True, stop=True),
         lambda e: e.matmul(pB[:, 0:H], onesrf[:], rowt[:, 0:H], start=True, stop=True),
         lambda e: e.matmul(pB[:, H:NTOK], onesrf[:], rowt[:, H:NTOK], start=True, stop=True))
    m2b = ln1b[:, 0:NTOK]
    step("S",
         lambda e: e.copy(s2b[:], pA[:, 0:NTOK]),
         lambda e: e.copy(m2b, pB[:, 0:NTOK]))
    SC2["v"] = seqcnt["S"]

    def v_sub(e):
        e.wait_ge(bS, SC2["v"])
        for c in range(NCH):
            e.wait_ge(c_st, NCH + c + 1)
            e.tensor_sub(ux[:, c * NTOK:(c + 1) * NTOK],
                         ux[:, c * NTOK:(c + 1) * NTOK], m2b).then_inc(c_sub, 1)
    if PHASE >= 4:
        em("V", v_sub)

    if PHASE < 4:
        def s_store_x1(e):
            e.wait_ge(c_x1, NCH)
            for oc in range(NCH):
                e.activation(outb[:, (oc % 2) * NTOK:(oc % 2 + 1) * NTOK],
                             ux[:, oc * NTOK:(oc + 1) * NTOK], AF.Identity).then_inc(c_ev, 1)
                e.wait_ge(c_ev, oc + 1)
                e.dma_start(out=out_t[oc * P:(oc + 1) * P, :],
                            in_=outb[:, (oc % 2) * NTOK:(oc % 2 + 1) * NTOK]).then_inc(s_out, 16)
                if oc >= 1:
                    e.wait_ge(s_out, 16 * oc)
        em("S", s_store_x1)
        if PHASE in (2, 3):
            def v_dec_tail(e):
                for k in range(NCH, NUNIT):
                    decode_unit_v(k)(e)
            def p_dec_tail(e):
                for k in range(NCH, NUNIT):
                    e8_unit_p(k)(e)
            em("V", v_dec_tail)
            em("P", p_dec_tail)

    import os as _os
    if _os.environ.get("KNOPTEST"):
        def p_mid(e):
            for _ in range(110):
                e.nop()
    elif False:
        pass
    if _os.environ.get("KNOPTEST"):
        pass
    else:
     def p_mid(e):
        for c in range(NCH):
            if PHASE >= 4:
                e.wait_ge(c_sub, c + 1)
            for h_ in range(2):
                e.matmul(pA[0:16, h_ * H:(h_ + 1) * H],
                         atb[:, c * 16:(c + 1) * 16],
                         ux[:, c * NTOK + h_ * H:c * NTOK + (h_ + 1) * H],
                         start=(c == 0), stop=(c == NCH - 1))
            ins = e.matmul(pB[0:16, 0:1], atb[:, c * 16:(c + 1) * 16], onesbf[:, 0:1],
                           start=(c == 0), stop=(c == NCH - 1))
        return ins
    if PHASE == 4:
        def v_dec_tail2(e):
            for k in range(NCH, NUNIT):
                decode_unit_v(k)(e)
        def p_dec_tail2(e):
            for k in range(NCH, NUNIT):
                e8_unit_p(k)(e)
        em("V", v_dec_tail2)
        em("P", p_dec_tail2)
        barrier()
    if PHASE >= 3:
        step("P", p_mid)
        step("V",
             lambda e: e.tensor_copy(midb[:], pA[0:16, 0:NTOK]),
             lambda e: e.tensor_copy(arowb[:], pB[0:16, 0:1]))
        SC3["v"] = seqcnt["V"]
        barrier()

    def p_ffn(e):
        for oc in range(NCH):
            wt = wtb[:, (oc % 2) * D:(oc % 2 + 1) * D]
            if oc == 0:
                e.wait_ge(bV, SC3["v"])
            e.wait_ge(s_wt, 16 * (2 * oc + 1))
            if oc >= 2:
                e.wait_ge(c_ev, oc - 1)
            for c in range(NCH):
                if oc == 0:
                    e.wait_ge(c_sub, c + 1)
                if c == 16:
                    e.wait_ge(s_wt, 16 * (2 * oc + 2))
                ins_c = None
                for h_ in range(2):
                    ins_c = e.matmul((pA if oc % 2 == 0 else pB)[:, h_ * H:(h_ + 1) * H],
                             wt[:, c * P:(c + 1) * P],
                             ux[:, c * NTOK + h_ * H:c * NTOK + (h_ + 1) * H],
                             start=(c == 0), stop=False)
                if c == 15:
                    ins_c.then_inc(c_fh, 1)
            ins = None
            for h_ in range(2):
                e.matmul((pA if oc % 2 == 0 else pB)[:, h_ * H:(h_ + 1) * H],
                         btb[:, oc * P:(oc + 1) * P],
                         midb[:, h_ * H:(h_ + 1) * H],
                         start=False, stop=False)
                ins = e.matmul((pA if oc % 2 == 0 else pB)[:, h_ * H:(h_ + 1) * H],
                         onesrbf[:, 0:P],
                         xsq[0:1, h_ * H:(h_ + 1) * H],
                         start=False, stop=True)
            ins.then_inc(c_ffn, 1)
            if PHASE >= 5:
                if oc < 10:
                    e8_unit_p(32 + 2 * oc)(e)
                    e8_unit_p(33 + 2 * oc)(e)
                elif oc < 22:
                    e8_unit_p(52 + oc - 10)(e)
    if PHASE >= 4:
        em("P", p_ffn)

    def s_ffn(e):
        for oc in range(NCH):
            e.wait_ge(c_ev, oc + 1)
            e.dma_start(out=out_t[oc * P:(oc + 1) * P, :],
                        in_=outb[:, (oc % 2) * NTOK:(oc % 2 + 1) * NTOK]).then_inc(s_out, 16)
    if PHASE >= 4:
        em("S", s_ffn)

    def v_ffn(e):
        for oc in range(NCH):
            pp = pA if oc % 2 == 0 else pB
            e.wait_ge(c_ffn, oc + 1)
            if oc >= 2:
                e.wait_ge(s_out, 16 * (oc - 1))
            ob = outb[:, (oc % 2) * NTOK:(oc % 2 + 1) * NTOK]
            e.tensor_mul(tevac[:], pp[:, 0:NTOK], s2b[:])
            e.tensor_add(ob, tevac[:], ux[:, oc * NTOK:(oc + 1) * NTOK]).then_inc(c_ev, 1)
            if PHASE >= 5:
                if oc < 10:
                    decode_unit_v(32 + 2 * oc)(e)
                    decode_unit_v(33 + 2 * oc)(e)
                elif oc < 22:
                    decode_unit_v(52 + oc - 10)(e)
    if PHASE >= 4:
        em("V", v_ffn)

    if DEBUG:
        def g_dbg(g):
            g.wait_ge(s_out, 32 * 16)
            dumps = [(dbg["K"], krow[:]), (dbg["T0"], T0b[:]), (dbg["BmT"], BmT[:]),
                     (dbg["CBl"], CBl[:]), (dbg["ACl"], ACl[:]),
                     (dbg["Ab"][:, 0:64], pw[:, 0:64]),
                     (dbg["Ab"][:, 64:128], pw[:, 64:128]),
                     (dbg["x1"], ux[:, 0:NTOK]),
                     (dbg["ln"], ln1b[:, NTOK:2 * NTOK]),
                     ] + (
                         [(dbg["W0"], wtb[:, D:D + P])] if PHASE >= 4 else [])
            for o_, i_ in dumps:
                g.dma_start(out=o_, in_=i_).then_inc(s_dbg, 16)
            g.wait_ge(s_dbg, 16 * len(dumps))
        em("G", g_dbg)

    with es, nc.Block() as block:
        @block.vector
        def _(vector):
            for f in ops["V"]:
                f(vector)

        @block.tensor
        def _(tensor):
            for f in ops["P"]:
                f(tensor)

        @block.scalar
        def _(scalar):
            for f in ops["S"]:
                f(scalar)

        @block.gpsimd
        def _(gpsimd):
            for f in ops["G"]:
                f(gpsimd)

        @block.sync
        def _(sync):
            for f in ops["Y"]:
                f(sync)

    return nc


_NC_CACHE = {}


def kernel(**inputs):
    u = np.asarray(inputs["L0_attn_u"], np.float32)
    lam = np.asarray(inputs["L0_attn_Lambda_ri"], np.float32)
    p_ri = np.asarray(inputs["L0_attn_p_ri"], np.float32)
    q_ri = np.asarray(inputs["L0_attn_q_ri"], np.float32)
    vc = np.asarray(inputs["L0_attn_Vc_ri"], np.float32)
    ct = np.asarray(inputs["L0_attn_Ct_ri"], np.float32)
    bp = np.asarray(inputs["L0_attn_Bp"], np.float32)
    Dv = np.asarray(inputs["L0_attn_D"], np.float32)
    ls = np.asarray(inputs["L0_attn_log_step"], np.float32)
    pk = np.asarray(inputs["L0_ffn_packed_weight"], np.int32)
    bs = np.asarray(inputs["L0_ffn_block_scales"], np.float32)
    gs = np.asarray(inputs["L0_ffn_global_scale"], np.float32)
    lA = np.asarray(inputs["L0_ffn_lora_A"], np.float32)
    lB = np.asarray(inputs["L0_ffn_lora_B"], np.float32)

    R = u.reshape(-1, D)
    packedT = np.ascontiguousarray(pk.T)
    bsT3 = np.ascontiguousarray(
        bs.T.reshape(N_IPC, 16, D).transpose(1, 0, 2).reshape(16, 16 * D))
    lA_T = np.ascontiguousarray(lA.T.reshape(NCH, P, 16).transpose(1, 0, 2).reshape(P, NCH * 16))
    lB_T = np.ascontiguousarray(lB.T)
    sm = np.ascontiguousarray(np.stack(
        [lam[:, 0], lam[:, 1], p_ri[:, 0], p_ri[:, 1], q_ri[:, 0], q_ri[:, 1],
         bp[:, 0], ct[0, :, 0], ct[0, :, 1]], axis=1))
    vctr = np.ascontiguousarray(vc[:, :, 0].T)
    vcti = np.ascontiguousarray(vc[:, :, 1].T)
    scal = np.ascontiguousarray(
        np.concatenate([Dv.reshape(1), ls.reshape(1), gs.reshape(1)]).reshape(1, 3))

    consts = {
        "cI64": np.eye(64, dtype=np.float32),
        "cJ128": np.eye(P, dtype=np.float32)[::-1].astype(MLBF),
        "cE8": np.repeat(0.5 * np.eye(16, dtype=np.float32), 8, axis=1).astype(MLBF),
        "cOnes": np.ones((P, 4), np.float32),
        "cOnesB": np.ones((P, 4), np.float32).astype(MLBF),
        "cOnesR": np.ones((1, P), np.float32),
        "cOnesRB": np.ones((1, P), np.float32).astype(MLBF),
    }

    key = ("prog", PHASE, DEBUG, SERIAL_DEC)
    if key not in _NC_CACHE:
        nc_new = build_program()
        mybir.codegen_inst_isa_subclasses(nc_new)
        _NC_CACHE[key] = nc_new
    nc = _NC_CACHE[key]

    in_maps = []
    for r in range(NC):
        m = {
            "u_s": np.ascontiguousarray(R[r * NTOK:(r + 1) * NTOK, :].T),
            "packedT": packedT, "bsT3": bsT3, "lA_T": lA_T, "lB_T": lB_T,
            "sm": sm, "VcT_re": vctr, "VcT_im": vcti, "scal": scal,
        }
        m.update(consts)
        in_maps.append(m)

    res = run_bass_kernel_spmd(nc, in_maps, core_ids=list(range(NC)))
    out = np.empty((NC * NTOK, D), np.float32)
    for r in range(NC):
        out[r * NTOK:(r + 1) * NTOK, :] = res.results[r]["out_s"].T
    if DEBUG:
        kernel._dbg = res.results
    return out.reshape(2, -1, D)


# version 27
# speedup vs baseline: 1.1018x; 1.0036x over previous
import numpy as np
import ml_dtypes
from contextlib import ExitStack

import concourse.bass as bass
import concourse.mybir as mybir
from concourse.bass_utils import run_bass_kernel_spmd
from concourse import dve_ops as _dve_ops
from concourse.dve_op import DveOp as _DveOp
from concourse.dve_spec import (Spec as _Spec, Src0 as _Src0, Src1 as _Src1,
                                C0 as _C0, C1 as _C1, relu as _relu,
                                lower as _dve_lower)


def _fp4_mag_sgn_ref(in0, in1, s0, s1, imm2):
    x = np.asarray(in0).astype(np.float32)
    g = np.asarray(in1).astype(np.float32)
    r1 = np.maximum(x - s0, 0.0)
    r2 = np.maximum(x + x - s1, 0.0)
    return (((x + r1) + r2) * g).astype(np.float32)


def _register_fp4_op():
    for o in _dve_ops.OPS:
        if o.name == "FP4_MAG_SGN":
            return o
    spec = _Spec(body=((_Src0 + _relu(_Src0 - _C0))
                       + _relu((_Src0 + _Src0) - _C1)) * _Src1)
    sha = {"v3": _dve_lower(spec, ver="v3", name="FP4_MAG_SGN").sha("v3")}
    op = _DveOp("FP4_MAG_SGN", spec, False, sha, reference=_fp4_mag_sgn_ref)
    _dve_ops.OPS.append(op)
    _dve_ops._SUB_OPCODE_FOR_NAME["FP4_MAG_SGN"] = (
        _dve_ops._CUSTOM_DVE_ROW_BASE + len(_dve_ops.OPS) - 1)
    _dve_ops.CUSTOM_DVE_SPECS["FP4_MAG_SGN"] = spec
    _dve_ops.CUSTOM_DVE_REFS["FP4_MAG_SGN"] = _fp4_mag_sgn_ref
    return op


FP4_OP = _register_fp4_op()

F32 = mybir.dt.float32
BF16 = mybir.dt.bfloat16
I32 = mybir.dt.int32
I16 = mybir.dt.int16
OP = mybir.AluOpType
AF = mybir.ActivationFunctionType
MLBF = ml_dtypes.bfloat16

P = 128
NTOK = 1024
D = 4096
NCH = 32
NC = 8
NEU = 8
N_IPC = 16
N_OSL = 4
OSL = 1024
NUNIT = N_IPC * N_OSL
H = 512
DEBUG = False
PHASE = 5
SERIAL_DEC = False


def build_program():
    nc = bass.Bass("TRN2", target_bir_lowering=False, detect_race_conditions=False)

    u_in = nc.dram_tensor("u_s", [D, NTOK], F32, kind="ExternalInput").ap()
    pk_in = nc.dram_tensor("packedT", [2048, D], I32, kind="ExternalInput").ap()
    bst_in = nc.dram_tensor("bsT3", [16, 16 * D], F32, kind="ExternalInput").ap()
    at_in = nc.dram_tensor("lA_T", [P, NCH * 16], F32, kind="ExternalInput").ap()
    bt_in = nc.dram_tensor("lB_T", [16, D], F32, kind="ExternalInput").ap()
    sm_in = nc.dram_tensor("sm", [64, 9], F32, kind="ExternalInput").ap()
    vcr_in = nc.dram_tensor("VcT_re", [64, 64], F32, kind="ExternalInput").ap()
    vci_in = nc.dram_tensor("VcT_im", [64, 64], F32, kind="ExternalInput").ap()
    scal_in = nc.dram_tensor("scal", [1, 3], F32, kind="ExternalInput").ap()
    i64_in = nc.dram_tensor("cI64", [64, 64], F32, kind="ExternalInput").ap()
    j128_in = nc.dram_tensor("cJ128", [P, P], BF16, kind="ExternalInput").ap()
    e8_in = nc.dram_tensor("cE8", [16, P], BF16, kind="ExternalInput").ap()
    ones_in = nc.dram_tensor("cOnes", [P, 4], F32, kind="ExternalInput").ap()
    onesb_in = nc.dram_tensor("cOnesB", [P, 4], BF16, kind="ExternalInput").ap()
    onesr_in = nc.dram_tensor("cOnesR", [1, P], F32, kind="ExternalInput").ap()
    onesrb_in = nc.dram_tensor("cOnesRB", [1, P], BF16, kind="ExternalInput").ap()

    out_t = nc.dram_tensor("out_s", [D, NTOK], F32, kind="ExternalOutput").ap()
    dbg = {}
    if DEBUG:
        for nm, shp in [("K", [1, P]), ("T0", [P, P]), ("BmT", [P, P]),
                        ("CBl", [P, P]), ("ACl", [P, P]), ("Ab", [64, P]),
                        ("x1", [P, NTOK]), ("W0", [P, P]), ("ln", [P, NTOK]),
                        ("smx", [64, 96]), ("smr", [1, 576]), ("g64", [64, 8])]:
            dbg[nm] = nc.dram_tensor("dbg_" + nm, shp, F32, kind="ExternalOutput").ap()

    wt_d = nc.dram_tensor("WT_d", [NCH, P, NCH, P], BF16, kind="Internal").ap()
    kpad_d = nc.dram_tensor("kpad_d", [1, 256], BF16, kind="Internal").ap()

    es = ExitStack()
    sb = lambda n, s, d: es.enter_context(nc.sbuf_tensor(n, s, d))
    ps_ = lambda n, s: es.enter_context(nc.psum_tensor(n, s, F32))
    sem = lambda n: es.enter_context(nc.semaphore(n))

    ux = sb("ux", [P, NCH * NTOK], BF16)
    ln1b = sb("ln1b", [P, 2 * NTOK], BF16)
    sst = [sb("sst0", [P, NTOK], BF16), sb("sst1", [P, NTOK], BF16)]
    m1b = sb("m1b", [P, NTOK], BF16)
    s1b = sb("s1b", [P, NTOK], BF16)
    s2b = sb("s2b", [P, NTOK], F32)
    xsq = sb("xsq", [P, 2 * NTOK], BF16)
    tevac = sb("tevac", [P, NTOK], BF16)
    outb = sb("outb", [P, 2 * NTOK], F32)
    T0b = sb("T0b", [P, P], BF16)
    BmT = sb("BmT", [P, P], BF16)
    CBl = sb("CBl", [P, P], BF16)
    ACl = sb("ACl", [P, P], BF16)
    wtb = sb("wtb", [P, 2 * D], BF16)
    colc = sb("colc", [P, 2], F32)
    midb = sb("midb", [16, NTOK], BF16)
    arowb = sb("arowb", [16, 1], BF16)
    atb = sb("atb", [P, NCH * 16], BF16)
    btb = sb("btb", [16, D], BF16)
    bstu = sb("bstu", [16, 2 * OSL], BF16)
    bstu16 = sb("bstu16", [16, 6 * OSL], BF16)
    e8g = sb("e8g", [16, P], BF16)
    slab = sb("slab", [P, 2 * OSL], I32)
    dc1 = sb("dc1", [P, 2 * OSL], I16)
    dc2 = sb("dc2", [P, 2 * OSL], I16)
    dc4 = sb("dc4", [P, 2 * OSL], I16)
    wlo = sb("wlo", [P, 2 * OSL], BF16)
    whi = sb("whi", [P, 2 * OSL], BF16)
    smt = sb("smt", [64, 9], F32)
    vctr = sb("vctr", [64, 64], F32)
    vcti = sb("vcti", [64, 64], F32)
    i64f = sb("i64f", [64, 64], F32)
    j128b = sb("j128b", [P, P], BF16)
    e8b = sb("e8b", [16, P], BF16)
    onesf = sb("onesf", [P, 4], F32)
    onesbf = sb("onesbf", [P, 4], BF16)
    onesrf = sb("onesrf", [1, P], F32)
    onesrbf = sb("onesrbf", [1, P], BF16)
    scalt = sb("scalt", [1, 4], F32)
    rowt = sb("rowt", [1, 2 * NTOK], F32)
    rowu = sb("rowu", [1, NTOK], F32)
    _pw_off = {}
    _pw_n = [0]
    for _j in range(13):
        _slots = (0, 1, 2, 3, 4, 5) if _j <= 6 else ((0, 1, 2, 3, 4) if _j <= 11 else (0, 1, 5))
        for _s in _slots:
            _pw_off[(_j, _s)] = _pw_n[0]
            _pw_n[0] += 64
    _pw_off[("scr", 0)] = _pw_n[0]
    _pw_n[0] += 64
    pw = sb("pw", [64, _pw_n[0]], F32)
    smx = sb("smx", [64, 24 * 4], F32)
    smr = sb("smr", [1, 576], F32)
    bkre = sb("bkre", [64, P], F32)
    bkim = sb("bkim", [64, P], F32)
    zre = sb("zre", [64, P], F32)
    zim = sb("zim", [64, P], F32)
    t128 = sb("t128", [P, 64], BF16)
    krow = sb("krow", [1, P], BF16)
    z128 = sb("z128", [1, P], BF16)
    g64 = sb("g64", [64, 8], F32)
    sc16 = sb("sc16", [16, 2], F32)

    pA = ps_("pA", [P, NTOK])
    pB = ps_("pB", [P, NTOK])
    pC = ps_("pC", [P, NTOK])
    pD = ps_("pD", [P, NTOK])

    s_sm = sem("s_sm")
    s_u = sem("s_u")
    KRC = {}

    def u_wait(e, c):
        e.wait_ge(s_u, 16 * (c + 1))
    s_pk = sem("s_pk")
    s_wo = sem("s_wo")
    s_wos = sem("s_wos")
    s_sl = sem("s_sl")
    s_sl2 = sem("s_sl2")
    s_bs = sem("s_bs")
    s_misc = sem("s_misc")
    s_wt = sem("s_wt")
    s_out = sem("s_out")
    s_tp = sem("s_tp")
    bV = sem("bV")
    bP = sem("bP")
    bS = sem("bS")
    c_ln = sem("c_ln")
    c_x1 = sem("c_x1")
    c_y = sem("c_y")
    c_s = sem("c_s")
    c_sev = sem("c_sev")
    c_sq = sem("c_sq")
    c_dec = sem("c_dec")
    c_e8 = sem("c_e8")
    c_ffn = sem("c_ffn")
    c_ev = sem("c_ev")
    c_cc = sem("c_cc")
    c_st = sem("c_st")
    c_sub = sem("c_sub")
    c_fh = sem("c_fh")
    SC2 = {}
    SC3 = {}
    s_dbg = sem("s_dbg")

    ops = {"V": [], "P": [], "S": [], "G": [], "Y": []}

    def em(key, f):
        ops[key].append(f)

    seqcnt = {"V": 0, "P": 0, "S": 0}
    seqsem = {"V": bV, "P": bP, "S": bS}

    def step(key, *fns, raw=False):
        waits = [(k, seqcnt[k]) for k in "VPS" if seqcnt[k] > 0]
        base = seqcnt[key]
        n = 1 if raw or not fns else len(fns)
        seqcnt[key] += n

        def f(eng, waits=waits, fns=fns, base=base, raw=raw):
            for k, v in waits:
                eng.wait_ge(seqsem[k], v)
            if raw or not fns:
                for g in fns:
                    g(eng)
                eng.sem_inc(seqsem[key], 1)
                return
            for i, g in enumerate(fns):
                if i > 0:
                    eng.wait_ge(seqsem[key], base + i)
                r = g(eng)
                ok = False
                if r is not None and hasattr(r, "then_inc"):
                    tn = type(getattr(r, "ins", r)).__name__
                    if "EventSemaphore" not in tn and "DMACopy" not in tn:
                        r.then_inc(seqsem[key], 1)
                        ok = True
                if not ok:
                    eng.sem_inc(seqsem[key], 1)
        em(key, f)

    def barrier():
        for k in "VPS":
            step(k)

    _smx_n = [0]

    def acol():
        i = _smx_n[0]
        _smx_n[0] += 1
        assert i < 96
        return smx[:, i:i + 1]

    _smr_n = [0]

    def arow_(w=1):
        i = _smr_n[0]
        _smr_n[0] += w
        assert _smr_n[0] <= 576
        return smr[:, i:i + w]

    def pwt(j, which):
        off = _pw_off[(j, which)]
        return pw[:, off:off + 64]

    UNITS = [(k // N_IPC, k % N_IPC) for k in range(NUNIT)]
    PSC = ["C"] * 16 + ["D" if k % 2 == 0 else "C" for k in range(16, 64)]

    def _uwait(k):
        for j in range(k - 1, -1, -1):
            if PSC[j] == PSC[k]:
                return j + 1
        return 0

    def y_main(y):
        y.dma_start(out=smt[:], in_=sm_in).then_inc(s_sm, 16)
        y.dma_start(out=vctr[:], in_=vcr_in).then_inc(s_sm, 16)
        y.dma_start(out=vcti[:], in_=vci_in).then_inc(s_sm, 16)
        y.dma_start(out=scalt[:, 0:3], in_=scal_in).then_inc(s_sm, 16)
        y.dma_start(out=i64f[:], in_=i64_in).then_inc(s_sm, 16)
        y.dma_start(out=j128b[:], in_=j128_in).then_inc(s_sm, 16)
        y.dma_start(out=e8b[:], in_=e8_in).then_inc(s_sm, 16)
        y.dma_start(out=onesf[:], in_=ones_in).then_inc(s_sm, 16)
        y.dma_start(out=onesbf[:], in_=onesb_in).then_inc(s_sm, 16)
        y.dma_start(out=onesrf[:], in_=onesr_in).then_inc(s_sm, 16)
        y.dma_start(out=onesrbf[:], in_=onesrb_in).then_inc(s_sm, 16)
        if PHASE >= 2:
            slab_load(y, 0, s_sl)
            slab_load(y, 1, s_sl)
            for k in range(16):
                y.wait_ge(c_dec, k + 1)
                w_out(y, k, s_wos)
                if k + 2 <= 17:
                    slab_load(y, k + 2, s_sl)
        y.wait_ge(bV, KRC["v"])
        y.dma_start(out=kpad_d[:, 0:128], in_=z128[:]).then_inc(s_tp, 16)
        y.dma_start(out=kpad_d[:, 128:256], in_=krow[:]).then_inc(s_tp, 16)
        y.wait_ge(s_tp, 32)
        for p in range(64):
            y.dma_start(out=T0b[p:p + 1, :],
                        in_=kpad_d[0:1, 128 - p:256 - p]).then_inc(s_tp, 16)
        for oc in (range(NCH) if PHASE >= 4 else []):
            osl = (oc * P) // OSL
            y.wait_ge(s_wos, 64 * 16)
            if osl >= 1:
                y.wait_ge(s_wo, 64 * 16 * osl)
            wd = wt_d[oc].rearrange("p ic q -> p (ic q)")
            if oc >= 2:
                y.wait_ge(c_fh, oc - 1)
            y.dma_start(
                out=wtb[:, (oc % 2) * D:(oc % 2) * D + D // 2],
                in_=wd[:, 0:D // 2],
            ).then_inc(s_wt, 16)
            if oc >= 2:
                y.wait_ge(c_ffn, oc - 1)
            y.dma_start(
                out=wtb[:, (oc % 2) * D + D // 2:(oc % 2) * D + D],
                in_=wd[:, D // 2:D],
            ).then_inc(s_wt, 16)
        y.wait_ge(s_out, 32 * 16)
    em("Y", y_main)

    def u_load(eng, c, sem_):
        eng.dma_start(
            out=ux[:, c * NTOK:(c + 1) * NTOK],
            in_=u_in[c * P:(c + 1) * P, :],
        ).then_inc(sem_, 16)

    wt4 = wt_d.rearrange("oc (p2 two) ic q -> ic two p2 oc q", two=2)

    def w_out(eng, k, sem_):
        osl, ipc = UNITS[k]
        for par, wsrc in ((0, wlo), (1, whi)):
            t = wsrc[:, (k % 2) * OSL:(k % 2 + 1) * OSL].rearrange(
                "p (ocs q) -> p ocs q", q=P)
            for h_ in range(2):
                eng.dma_start(
                    out=wt4[2 * ipc + h_, par, :, osl * 8:(osl + 1) * 8, :],
                    in_=t[h_ * 64:(h_ + 1) * 64],
                ).then_inc(sem_, 16)

    def slab_load(eng, k, sem_):
        osl, ipc = UNITS[k]
        eng.dma_start(
            out=slab[:, (k % 2) * OSL:(k % 2 + 1) * OSL],
            in_=pk_in[ipc * P:(ipc + 1) * P, osl * OSL:(osl + 1) * OSL],
        ).then_inc(sem_, 16)

    def bstu_load(g, k):
        osl, ipc = UNITS[k]
        out = bstu16[:, k * OSL:(k + 1) * OSL] if k < 6 else \
            bstu[:, (k % 2) * OSL:(k % 2 + 1) * OSL]
        g.dma_start(
            out=out,
            in_=bst_in[:, ipc * D + osl * OSL:ipc * D + (osl + 1) * OSL],
        ).then_inc(s_bs, 16)

    def g_main(g):
        g.dma_start(out=atb[:], in_=at_in).then_inc(s_misc, 16)
        g.dma_start(out=btb[:], in_=bt_in).then_inc(s_misc, 16)
        if PHASE >= 2:
            for k in range(8):
                bstu_load(g, k)
        for c in range(NCH):
            u_load(g, c, s_u)
        if PHASE >= 2:
            for k in range(8, 16):
                g.wait_ge(c_dec, k - 1)
                bstu_load(g, k)
        if PHASE < 2:
            return
        for k in (16, 17):
            g.wait_ge(c_dec, k - 1)
            bstu_load(g, k)
        for k in range(16, NUNIT):
            g.wait_ge(c_dec, k + 1)
            w_out(g, k, s_wo)
            if k + 2 < NUNIT:
                bstu_load(g, k + 2)
                slab_load(g, k + 2, s_sl2)
    em("G", g_main)

    def decode_unit_v(k):
        osl, ipc = UNITS[k]
        sl = slab[:, (k % 2) * OSL:(k % 2 + 1) * OSL]
        pscale = pC if PSC[k] == "C" else pD
        lo = wlo[:, (k % 2) * OSL:(k % 2 + 1) * OSL]
        hi = whi[:, (k % 2) * OSL:(k % 2 + 1) * OSL]
        c16a = dc1[:, 0:2 * OSL]
        t1s = dc2[:, 0:2 * OSL]

        def f(e, k=k):
            if k <= 17:
                e.wait_ge(s_sl, 16 * (k + 1))
            else:
                e.wait_ge(s_sl2, 16 * (k - 17))
            if 2 <= k <= 17:
                e.wait_ge(s_wos, 64 * (k - 1))
            elif k >= 18:
                e.wait_ge(s_wo, 64 * (k - 17))
            e.wait_ge(c_e8, k + 1)
            dm = dc4[:].bitcast(I32)
            e.tensor_scalar(dm, sl, 4097, None, OP.mult)
            dv = dm.bitcast(I16)
            e.tensor_scalar(c16a, dv, 7, None, OP.bitwise_and)
            e.tensor_scalar(t1s, dv, 8, 2, OP.bitwise_and, OP.logical_shift_right)
            e.tensor_scalar(t1s, t1s, -1, 1, OP.mult, OP.add)
            e._custom_dve(FP4_OP, out=dc4[:, 0:2 * OSL].bitcast(BF16),
                          in0=c16a, in1=t1s, s0=4.0, s1=12.0)
            bv = dc4[:, 0:2 * OSL].bitcast(BF16)
            e.tensor_tensor(lo, bv[:, 0::2], pscale[:, 0:OSL], OP.mult)
            e.tensor_tensor(hi, bv[:, 1::2], pscale[:, 0:OSL], OP.mult).then_inc(c_dec, 1)
        return f

    def st1_sq_v(c):
        def f(e, c=c):
            u_wait(e, c)
            if c >= 2:
                e.wait_ge(c_st, c - 1)
            e.tensor_mul(xsq[:, (c % 2) * NTOK:(c % 2 + 1) * NTOK],
                         ux[:, c * NTOK:(c + 1) * NTOK],
                         ux[:, c * NTOK:(c + 1) * NTOK]).then_inc(c_sq, 1)
        return f

    def st1_mm_p(c):
        def f(e, c=c):
            u_wait(e, c)
            e.wait_ge(c_sq, c + 1)
            for h_ in range(2):
                e.matmul(pA[0:1, h_ * H:(h_ + 1) * H], onesbf[:, 0:1],
                         ux[:, c * NTOK + h_ * H:c * NTOK + (h_ + 1) * H],
                         start=(c == 0), stop=(c == NCH - 1))
                ins = e.matmul(pB[0:1, h_ * H:(h_ + 1) * H], onesbf[:, 0:1],
                         xsq[:, (c % 2) * NTOK + h_ * H:(c % 2) * NTOK + (h_ + 1) * H],
                         start=(c == 0), stop=(c == NCH - 1))
            ins.then_inc(c_st, 1)
        return f

    def e8_unit_p(k):
        osl, ipc = UNITS[k]
        pscale = pC if PSC[k] == "C" else pD

        def f(e, k=k, osl=osl, ipc=ipc):
            e.wait_ge(s_bs, 16 * (k + 1))
            w = _uwait(k)
            if w > 0:
                e.wait_ge(c_dec, w)
            bs_ = bstu16[:, k * OSL:(k + 1) * OSL] if k < 6 else \
                bstu[:, (k % 2) * OSL:(k % 2 + 1) * OSL]
            e.matmul(pscale[:, 0:H], e8g[:], bs_[:, 0:H],
                     start=True, stop=True)
            e.matmul(pscale[:, H:OSL], e8g[:], bs_[:, H:OSL],
                     start=True, stop=True).then_inc(c_e8, 1)
        return f


    ALLSM = 11 * 16

    step("S",
         lambda e: e.wait_ge(s_sm, ALLSM),
         lambda e: e.activation(scalt[:, 3:4], scalt[:, 1:2], AF.Exp))
    step("V", lambda e: e.wait_ge(s_sm, ALLSM), lambda e: e.memset(smr[:], 0.0))
    step("P", lambda e: e.wait_ge(s_sm, ALLSM))

    step("P",
         lambda e: e.matmul(pD[0:64, 512:516], onesrf[:, 0:64], scalt[:, 0:4],
                            start=True, stop=True),
         lambda e: e.matmul(pD[0:64, 0:1], vctr[:], smt[:, 6:7],
                            start=True, stop=True),
         lambda e: e.matmul(pD[0:64, 1:2], vcti[:], smt[:, 6:7],
                            start=True, stop=True))
    alph = acol(); Dcol = acol(); Bre = acol(); Bim = acol(); stepc = acol()
    step("V",
         lambda e: e.tensor_scalar(alph, pD[0:64, 515:516], 0.5, None, OP.mult),
         lambda e: e.tensor_copy(stepc, pD[0:64, 515:516]),
         lambda e: e.tensor_copy(Dcol, pD[0:64, 512:513]),
         lambda e: e.tensor_copy(sc16[:, 0:1], pD[0:16, 514:515]),
         lambda e: e.tensor_copy(Bre, pD[0:64, 0:1]),
         lambda e: e.tensor_copy(Bim, pD[0:64, 1:2]))
    step("V", lambda e: e.tensor_scalar(e8g[:], e8b[:], sc16[:, 0:1], None, OP.mult))

    lam_re = smt[:, 0:1]; lam_im = smt[:, 1:2]
    p_re = smt[:, 2:3]; p_im = smt[:, 3:4]
    q_re = smt[:, 4:5]; q_im = smt[:, 5:6]
    ct_re = smt[:, 7:8]; ct_imr = smt[:, 8:9]
    qc_im = acol()
    dr = acol(); di = acol(); mag = acol(); rec = acol()
    d_re = acol(); d_im = acol()
    step("V",
         lambda e: e.tensor_scalar(qc_im, q_im, -1.0, None, OP.mult),
         lambda e: e.tensor_scalar(dr, lam_re, alph, -1.0, OP.mult, OP.mult),
         lambda e: e.tensor_scalar(dr, dr, 1.0, None, OP.add),
         lambda e: e.tensor_scalar(di, lam_im, alph, -1.0, OP.mult, OP.mult),
         lambda e: e.tensor_mul(mag, dr, dr),
         lambda e: e.tensor_mul(rec, di, di),
         lambda e: e.tensor_add(mag, mag, rec),
         lambda e: e.reciprocal(rec, mag),
         lambda e: e.tensor_mul(d_re, dr, rec),
         lambda e: e.tensor_mul(d_im, di, rec),
         lambda e: e.tensor_scalar(d_im, d_im, -1.0, None, OP.mult))

    def cmul_cols(o_re, o_im, a_re, a_im, b_re, b_im, t1, t2):
        return [
            lambda e: e.tensor_mul(t1, a_re, b_re),
            lambda e: e.tensor_mul(t2, a_im, b_im),
            lambda e: e.tensor_sub(o_re, t1, t2),
            lambda e: e.tensor_mul(t1, a_re, b_im),
            lambda e: e.tensor_mul(t2, a_im, b_re),
            lambda e: e.tensor_add(o_im, t1, t2),
        ]

    tc1 = acol(); tc2 = acol()
    u1_re = acol(); u1_im = acol()
    v1_re = acol(); v1_im = acol()
    e_re = acol(); e_im = acol()
    a0_re = acol(); a0_im = acol()
    se_re = acol(); se_im = acol()
    sB_re = acol(); sB_im = acol()
    v1e_re = acol(); v1e_im = acol()
    dB_re = acol(); dB_im = acol()
    step("V", *cmul_cols(u1_re, u1_im, d_re, d_im, p_re, p_im, tc1, tc2))
    step("V", *cmul_cols(v1_re, v1_im, q_re, qc_im, d_re, d_im, tc1, tc2))
    step("V",
         lambda e: e.tensor_scalar(e_re, lam_re, alph, None, OP.mult),
         lambda e: e.tensor_scalar(e_re, e_re, 1.0, None, OP.add),
         lambda e: e.tensor_scalar(e_im, lam_im, alph, None, OP.mult))
    step("V", *cmul_cols(a0_re, a0_im, d_re, d_im, e_re, e_im, tc1, tc2))
    step("V", *cmul_cols(v1e_re, v1e_im, v1_re, v1_im, e_re, e_im, tc1, tc2))
    step("V", *cmul_cols(dB_re, dB_im, d_re, d_im, Bre, Bim, tc1, tc2))
    step("V", *cmul_cols(se_re, se_im, v1_re, v1_im, p_re, p_im, tc1, tc2))
    step("V", *cmul_cols(sB_re, sB_im, v1_re, v1_im, Bre, Bim, tc1, tc2))

    sig = arow_(2)
    v1B = arow_(2)
    step("P",
         lambda e: e.matmul(pD[0:1, 512:513], onesf[0:64, 0:1], se_re, start=True, stop=True),
         lambda e: e.matmul(pD[0:1, 513:514], onesf[0:64, 0:1], se_im, start=True, stop=True),
         lambda e: e.matmul(pD[0:1, 514:515], onesf[0:64, 0:1], sB_re, start=True, stop=True),
         lambda e: e.matmul(pD[0:1, 515:516], onesf[0:64, 0:1], sB_im, start=True, stop=True))
    step("V",
         lambda e: e.tensor_copy(sig, pD[0:1, 512:514]),
         lambda e: e.tensor_copy(v1B, pD[0:1, 514:516]))

    arow2 = arow_(2)
    den = arow_(2)
    beta = arow_(2)
    gam = arow_(2)
    t1r = arow_(2)
    t0r = arow_(1)

    def cmul_row(o, a, b, t):
        return [
            lambda e: e.tensor_mul(t[:, 0:1], a[:, 0:1], b[:, 0:1]),
            lambda e: e.tensor_mul(o[:, 0:1], a[:, 1:2], b[:, 1:2]),
            lambda e: e.tensor_sub(o[:, 0:1], t[:, 0:1], o[:, 0:1]),
            lambda e: e.tensor_mul(t[:, 1:2], a[:, 0:1], b[:, 1:2]),
            lambda e: e.tensor_mul(o[:, 1:2], a[:, 1:2], b[:, 0:1]),
            lambda e: e.tensor_add(o[:, 1:2], t[:, 1:2], o[:, 1:2]),
        ]

    def cinv_scale_row(o, num_s, den_, t):
        return [
            lambda e: e.tensor_mul(t[:, 0:1], den_[:, 0:1], den_[:, 0:1]),
            lambda e: e.tensor_mul(t[:, 1:2], den_[:, 1:2], den_[:, 1:2]),
            lambda e: e.tensor_add(t[:, 0:1], t[:, 0:1], t[:, 1:2]),
            lambda e: e.reciprocal(t[:, 0:1], t[:, 0:1]),
            lambda e: e.tensor_mul(o[:, 0:1], den_[:, 0:1], t[:, 0:1]),
            lambda e: e.tensor_mul(o[:, 1:2], den_[:, 1:2], t[:, 0:1]),
            lambda e: e.tensor_scalar(o[:, 1:2], o[:, 1:2], -1.0, None, OP.mult),
            lambda e: e.tensor_scalar(o[:, 0:1], o[:, 0:1], num_s, None, OP.mult),
            lambda e: e.tensor_scalar(o[:, 1:2], o[:, 1:2], num_s, None, OP.mult),
        ]

    alphr = scalt[:, 1:2]
    alrow = arow_(1)
    step("V",
         lambda e: e.tensor_scalar(alrow, scalt[:, 3:4], 0.5, None, OP.mult))
    step("V",
         lambda e: e.tensor_scalar(den[:, 0:2], sig[:, 0:2], alrow, None, OP.mult),
         lambda e: e.tensor_scalar(den[:, 0:1], den[:, 0:1], 1.0, None, OP.add),
         *cinv_scale_row(beta, alrow, den, t1r))
    step("V",
         *cmul_row(arow2, beta, sig, t1r),
         lambda e: e.tensor_scalar(arow2[:, 0:2], arow2[:, 0:2], -1.0, None, OP.mult),
         lambda e: e.tensor_scalar(arow2[:, 0:1], arow2[:, 0:1], 1.0, None, OP.add),
         lambda e: e.tensor_scalar(gam[:, 0:2], arow2[:, 0:2], alrow, None, OP.mult))
    bv1B = arow_(2)
    step("V", *cmul_row(bv1B, beta, v1B, t1r))

    assert _smr_n[0] == 18
    step("P", lambda e: e.matmul(pD[0:64, 0:6], onesrf[:, 0:64],
                                 smr[:, 8:14], start=True, stop=True))
    gb_c = g64[:, 2:3]; gb_ci = g64[:, 3:4]
    bb_c = g64[:, 4:5]; bb_ci = g64[:, 5:6]
    bv_c = g64[:, 6:7]; bv_ci = g64[:, 7:8]
    step("P", lambda e: e.matmul(pD[0:64, 6:10], onesrf[:, 0:64],
                                 smr[:, 14:18], start=True, stop=True))
    step("V",
         lambda e: e.tensor_copy(g64[:, 4:6], pD[0:64, 0:2]),
         lambda e: e.tensor_copy(g64[:, 2:4], pD[0:64, 2:4]),
         lambda e: e.tensor_copy(g64[:, 6:8], pD[0:64, 8:10]))

    w1_re = acol(); w1_im = acol()
    Bb_re = acol(); Bb_im = acol()
    step("V",
         lambda e: e.tensor_scalar(tc1, q_re, gb_c, None, OP.mult),
         lambda e: e.tensor_scalar(tc2, qc_im, gb_ci, None, OP.mult),
         lambda e: e.tensor_sub(w1_re, tc1, tc2),
         lambda e: e.tensor_scalar(tc1, q_re, gb_ci, None, OP.mult),
         lambda e: e.tensor_scalar(tc2, qc_im, gb_c, None, OP.mult),
         lambda e: e.tensor_add(w1_im, tc1, tc2),
         lambda e: e.tensor_scalar(tc1, v1e_re, bb_c, None, OP.mult),
         lambda e: e.tensor_scalar(tc2, v1e_im, bb_ci, None, OP.mult),
         lambda e: e.tensor_sub(tc1, tc1, tc2),
         lambda e: e.tensor_add(w1_re, w1_re, tc1),
         lambda e: e.tensor_scalar(tc1, v1e_re, bb_ci, None, OP.mult),
         lambda e: e.tensor_scalar(tc2, v1e_im, bb_c, None, OP.mult),
         lambda e: e.tensor_add(tc1, tc1, tc2),
         lambda e: e.tensor_add(w1_im, w1_im, tc1))
    step("V",
         lambda e: e.tensor_scalar(tc1, u1_re, bv_c, None, OP.mult),
         lambda e: e.tensor_scalar(tc2, u1_im, bv_ci, None, OP.mult),
         lambda e: e.tensor_sub(tc1, tc1, tc2),
         lambda e: e.tensor_sub(Bb_re, dB_re, tc1),
         lambda e: e.tensor_scalar(tc1, u1_re, bv_ci, None, OP.mult),
         lambda e: e.tensor_scalar(tc2, u1_im, bv_c, None, OP.mult),
         lambda e: e.tensor_add(tc1, tc1, tc2),
         lambda e: e.tensor_sub(Bb_im, dB_im, tc1),
         lambda e: e.tensor_scalar(Bb_re, Bb_re, stepc, None, OP.mult),
         lambda e: e.tensor_scalar(Bb_im, Bb_im, stepc, None, OP.mult))

    u1r = arow_(64); u1i = arow_(64); w1r = arow_(64); w1i = arow_(64)
    nu1r = arow_(64); nu1i = arow_(64)
    step("P",
         lambda e: e.transpose(pD[0:1, 0:64], u1_re, i64f[:]),
         lambda e: e.transpose(pD[0:1, 64:128], u1_im, i64f[:]),
         lambda e: e.transpose(pD[0:1, 128:192], w1_re, i64f[:]),
         lambda e: e.transpose(pD[0:1, 192:256], w1_im, i64f[:]))
    step("V",
         lambda e: e.tensor_copy(u1r, pD[0:1, 0:64]),
         lambda e: e.tensor_copy(u1i, pD[0:1, 64:128]),
         lambda e: e.tensor_copy(w1r, pD[0:1, 128:192]),
         lambda e: e.tensor_copy(w1i, pD[0:1, 192:256]),
         lambda e: e.tensor_scalar(nu1r, u1r, -1.0, None, OP.mult),
         lambda e: e.tensor_scalar(nu1i, u1i, -1.0, None, OP.mult))
    step("P",
         lambda e: e.matmul(pD[0:64, 512:576], nu1r, w1r, start=True, stop=False),
         lambda e: e.matmul(pD[0:64, 512:576], u1i, w1i, start=False, stop=True),
         lambda e: e.matmul(pD[0:64, 576:640], nu1r, w1i, start=True, stop=False),
         lambda e: e.matmul(pD[0:64, 576:640], nu1i, w1r, start=False, stop=True))
    dgt = pw[:, _pw_off[("scr", 0)]:_pw_off[("scr", 0)] + 64]
    step("V",
         lambda e: e.tensor_scalar(dgt, i64f[:], a0_re, None, OP.mult),
         lambda e: e.tensor_add(pwt(0, 0), dgt, pD[0:64, 512:576]),
         lambda e: e.tensor_scalar(dgt, i64f[:], a0_im, None, OP.mult),
         lambda e: e.tensor_add(pwt(0, 1), dgt, pD[0:64, 576:640]),
         lambda e: e.tensor_scalar(pwt(0, 5), pwt(0, 1), -1.0, None, OP.mult))

    for j in range(12):
        step("P",
             lambda e, j=j: e.transpose(pD[0:64, 0:64], pwt(j, 0), i64f[:]),
             lambda e, j=j: e.transpose(pD[0:64, 64:128], pwt(j, 1), i64f[:]))
        step("V",
             lambda e, j=j: e.tensor_copy(pwt(j, 2), pD[0:64, 0:64]),
             lambda e, j=j: e.tensor_copy(pwt(j, 3), pD[0:64, 64:128]),
             lambda e, j=j: e.tensor_scalar(pwt(j, 4), pD[0:64, 64:128], -1.0, None, OP.mult))
        step("P",
             lambda e, j=j: e.matmul(pD[0:64, 512:576], pwt(j, 2), pwt(j, 0), start=True, stop=False),
             lambda e, j=j: e.matmul(pD[0:64, 512:576], pwt(j, 4), pwt(j, 1), start=False, stop=True),
             lambda e, j=j: e.matmul(pD[0:64, 576:640], pwt(j, 2), pwt(j, 1), start=True, stop=False),
             lambda e, j=j: e.matmul(pD[0:64, 576:640], pwt(j, 3), pwt(j, 0), start=False, stop=True))
        if PHASE >= 2:
            em("P", e8_unit_p(j))
        if j >= 1:
            em("P", st1_mm_p(2 * j - 2))
            em("P", st1_mm_p(2 * j - 1))
        fns = [lambda e, j=j: e.tensor_copy(pwt(j + 1, 0), pD[0:64, 512:576]),
               lambda e, j=j: e.tensor_copy(pwt(j + 1, 1), pD[0:64, 576:640])]
        if j + 1 <= 6 or j + 1 == 12:
            fns.append(lambda e, j=j: e.tensor_scalar(pwt(j + 1, 5), pD[0:64, 576:640], -1.0, None, OP.mult))
        step("V", *fns)
        if PHASE >= 2:
            em("V", decode_unit_v(j))
        em("V", st1_sq_v(2 * j))
        em("V", st1_sq_v(2 * j + 1))

    w_re = acol(); w_im = acol(); nw_im = acol(); ctn_im = acol()
    step("V",
         lambda e: e.tensor_copy(w_re, ct_re),
         lambda e: e.tensor_scalar(ctn_im, ct_imr, -1.0, None, OP.mult),
         lambda e: e.tensor_copy(w_im, ctn_im))
    for it in range(NEU):
        step("P",
             lambda e: e.matmul(pD[0:64, 0:1], pwt(12, 0), w_re, start=True, stop=False),
             lambda e: e.matmul(pD[0:64, 0:1], pwt(12, 5), w_im, start=False, stop=True),
             lambda e: e.matmul(pD[0:64, 1:2], pwt(12, 0), w_im, start=True, stop=False),
             lambda e: e.matmul(pD[0:64, 1:2], pwt(12, 1), w_re, start=False, stop=True))
        step("V",
             lambda e: e.tensor_add(w_re, pD[0:64, 0:1], ct_re),
             lambda e: e.tensor_add(w_im, pD[0:64, 1:2], ctn_im))
    step("V", lambda e: e.tensor_scalar(nw_im, w_im, -1.0, None, OP.mult))

    step("V",
         lambda e: e.tensor_copy(bkre[:, 0:1], Bb_re),
         lambda e: e.tensor_copy(bkim[:, 0:1], Bb_im))
    for j in range(7):
        w_ = 1 << j
        step("P",
             lambda e, j=j, w_=w_: e.matmul(pD[0:64, 0:w_], pwt(j, 2), bkre[:, 0:w_], start=True, stop=False),
             lambda e, j=j, w_=w_: e.matmul(pD[0:64, 0:w_], pwt(j, 4), bkim[:, 0:w_], start=False, stop=True),
             lambda e, j=j, w_=w_: e.matmul(pD[0:64, 64:64 + w_], pwt(j, 2), bkim[:, 0:w_], start=True, stop=False),
             lambda e, j=j, w_=w_: e.matmul(pD[0:64, 64:64 + w_], pwt(j, 3), bkre[:, 0:w_], start=False, stop=True))
        if PHASE >= 2 and j >= 3:
            em("P", e8_unit_p(9 + j))
        if j == 0:
            em("P", st1_mm_p(22))
            em("P", st1_mm_p(23))
        else:
            em("P", st1_mm_p(23 + j))
        step("V",
             lambda e, w_=w_: e.tensor_copy(bkre[:, w_:2 * w_], pD[0:64, 0:w_]),
             lambda e, w_=w_: e.tensor_copy(bkim[:, w_:2 * w_], pD[0:64, 64:64 + w_]))
        if PHASE >= 2 and j >= 3:
            em("V", decode_unit_v(9 + j))
        em("V", st1_sq_v(24 + j))

    em("V", st1_sq_v(31))
    em("P", st1_mm_p(30))
    em("P", st1_mm_p(31))

    step("P",
         lambda e: e.matmul(pD[0:64, 0:1], pwt(0, 0), w_re, start=True, stop=False),
         lambda e: e.matmul(pD[0:64, 0:1], pwt(0, 5), w_im, start=False, stop=True),
         lambda e: e.matmul(pD[0:64, 1:2], pwt(0, 0), w_im, start=True, stop=False),
         lambda e: e.matmul(pD[0:64, 1:2], pwt(0, 1), w_re, start=False, stop=True))
    step("V",
         lambda e: e.tensor_copy(zre[:, 0:1], pD[0:64, 0:1]),
         lambda e: e.tensor_copy(zim[:, 0:1], pD[0:64, 1:2]))
    for j in range(7):
        w_ = 1 << j
        step("P",
             lambda e, j=j, w_=w_: e.matmul(pD[0:64, 0:w_], pwt(j, 0), zre[:, 0:w_], start=True, stop=False),
             lambda e, j=j, w_=w_: e.matmul(pD[0:64, 0:w_], pwt(j, 5), zim[:, 0:w_], start=False, stop=True),
             lambda e, j=j, w_=w_: e.matmul(pD[0:64, 64:64 + w_], pwt(j, 0), zim[:, 0:w_], start=True, stop=False),
             lambda e, j=j, w_=w_: e.matmul(pD[0:64, 64:64 + w_], pwt(j, 1), zre[:, 0:w_], start=False, stop=True))
        step("V",
             lambda e, w_=w_: e.tensor_copy(zre[:, w_:2 * w_], pD[0:64, 0:w_]),
             lambda e, w_=w_: e.tensor_copy(zim[:, w_:2 * w_], pD[0:64, 64:64 + w_]))

    kfr = arow_(128)
    step("P",
         lambda e: e.matmul(pD[0:1, 0:128], w_re, bkre[:], start=True, stop=False),
         lambda e: e.matmul(pD[0:1, 0:128], nw_im, bkim[:], start=False, stop=True))
    step("V",
         lambda e: e.tensor_copy(kfr, pD[0:1, 0:128]),
         lambda e: e.tensor_scalar(kfr[:, 0:1], kfr[:, 0:1], scalt[:, 0:1], None, OP.add),
         lambda e: e.tensor_copy(krow[:], kfr),
         lambda e: e.memset(z128[:], 0.0))
    KRC["v"] = seqcnt["V"]

    def s_toep2(e):
        e.wait_ge(s_tp, 32)
        for p in range(64, P):
            e.dma_start(out=T0b[p:p + 1, :],
                        in_=kpad_d[0:1, 128 - p:256 - p]).then_inc(s_tp, 16)
    step("S", s_toep2, raw=True)

    step("P", lambda e: e.transpose(pD[0:128, 0:64], bkre[:], i64f[:]))
    step("V", lambda e: e.tensor_copy(t128[:], pD[0:128, 0:64]))
    step("P", lambda e: e.matmul(pD[0:128, 512:576], j128b[:], t128[:], start=True, stop=True))
    step("V", lambda e: e.tensor_copy(BmT[:, 0:64], pD[0:128, 512:576]))
    step("P", lambda e: e.transpose(pD[0:128, 0:64], bkim[:], i64f[:]))
    step("V", lambda e: e.tensor_copy(t128[:], pD[0:128, 0:64]))
    step("P", lambda e: e.matmul(pD[0:128, 512:576], j128b[:], t128[:], start=True, stop=True))
    step("V",
         lambda e: e.tensor_copy(BmT[:, 64:128], pD[0:128, 512:576]),
         lambda e: e.tensor_copy(CBl[0:64, :], zre[:]),
         lambda e: e.tensor_scalar(CBl[64:128, :], zim[:], -1.0, None, OP.mult),
         lambda e: e.tensor_copy(ACl[0:64, 0:64], pwt(7, 2)),
         lambda e: e.tensor_copy(ACl[0:64, 64:128], pwt(7, 3)),
         lambda e: e.tensor_scalar(ACl[64:128, 0:64], pwt(7, 3), -1.0, None, OP.mult),
         lambda e: e.tensor_copy(ACl[64:128, 64:128], pwt(7, 2)),
         lambda e: e.memset(sst[0][:], 0.0))

    step("S")
    step("P")

    def rows_math(psum_sum, psum_sq, mrow, srow):
        return [
            lambda e: e.tensor_scalar(mrow, psum_sum, 1.0 / 4096.0, None, OP.mult),
            lambda e: e.tensor_scalar(rowu[:, 0:NTOK], psum_sq, 1.0 / 4096.0, None, OP.mult),
            lambda e: e.tensor_mul(srow, mrow, mrow),
            lambda e: e.tensor_sub(rowu[:, 0:NTOK], rowu[:, 0:NTOK], srow),
            lambda e: e.tensor_scalar(rowu[:, 0:NTOK], rowu[:, 0:NTOK], 1e-5, None, OP.add),
            lambda e: e.reciprocal(srow, rowu[:, 0:NTOK]),
        ]
    step("V",
         lambda e: e.wait_ge(c_st, NCH),
         *rows_math(pA[0:1, :], pB[0:1, :],
                    rowt[:, 0:NTOK], rowt[:, NTOK:2 * NTOK]))
    step("S", lambda e: e.activation(rowt[:, NTOK:2 * NTOK],
                                     rowt[:, NTOK:2 * NTOK], AF.Sqrt))
    step("V",
         lambda e: e.tensor_copy(xsq[0:1, 0:NTOK], rowt[:, 0:NTOK]),
         lambda e: e.tensor_copy(ln1b[0:1, 0:NTOK], rowt[:, NTOK:2 * NTOK]))
    step("P",
         lambda e: e.matmul(pA[:, 0:H], onesrbf[:], xsq[0:1, 0:H], start=True, stop=True),
         lambda e: e.matmul(pA[:, H:NTOK], onesrbf[:], xsq[0:1, H:NTOK], start=True, stop=True),
         lambda e: e.matmul(pB[:, 0:H], onesrbf[:], ln1b[0:1, 0:H], start=True, stop=True),
         lambda e: e.matmul(pB[:, H:NTOK], onesrbf[:], ln1b[0:1, H:NTOK], start=True, stop=True))
    step("S",
         lambda e: e.copy(m1b[:], pA[:, 0:NTOK]),
         lambda e: e.copy(s1b[:], pB[:, 0:NTOK]))
    barrier()

    def st2_sq(e, c):
        e.wait_ge(c_x1, c + 1)
        if c >= 2:
            e.wait_ge(c_st, NCH + c - 1)
        e.activation(xsq[:, (c % 2) * NTOK:(c % 2 + 1) * NTOK],
                     ux[:, c * NTOK:(c + 1) * NTOK], AF.Square).then_inc(c_sq, 1)

    def st2_mm(e, c):
        if c == 0:
            e.wait_ge(c_dec, 32)
        e.wait_ge(c_sq, NCH + c + 1)
        for h_ in range(2):
            e.matmul(pC[0:1, h_ * H:(h_ + 1) * H], onesbf[:, 0:1],
                     ux[:, c * NTOK + h_ * H:c * NTOK + (h_ + 1) * H],
                     start=(c == 0), stop=(c == NCH - 1))
            ins = e.matmul(pD[0:1, h_ * H:(h_ + 1) * H], onesbf[:, 0:1],
                     xsq[:, (c % 2) * NTOK + h_ * H:(c % 2) * NTOK + (h_ + 1) * H],
                     start=(c == 0), stop=(c == NCH - 1))
        ins.then_inc(c_st, 1)

    def v_conv(e):
        for c in range(NCH):
            lt = ln1b[:, (c % 2) * NTOK:(c % 2 + 1) * NTOK]
            uc = ux[:, c * NTOK:(c + 1) * NTOK]
            if c >= 2:
                e.wait_ge(c_s, c - 1)
            e.tensor_sub(lt, uc, m1b[:])
            e.tensor_mul(lt, lt, s1b[:]).then_inc(c_ln, 1)
            e.wait_ge(c_y, c + 1)
            e.tensor_add(uc[:, 0:H], uc[:, 0:H], pA[:, 0:H])
            e.tensor_add(uc[:, H:NTOK], uc[:, H:NTOK], pA[:, H:NTOK]).then_inc(c_x1, 1)
            if PHASE >= 2 and c < 16:
                decode_unit_v(16 + c)(e)
    em("V", v_conv)

    def p_conv(e):
        for c in range(NCH):
            lt = ln1b[:, (c % 2) * NTOK:(c % 2 + 1) * NTOK]
            sprev = sst[c % 2]
            e.wait_ge(c_ln, c + 1)
            e.wait_ge(s_tp, 32 + 128 * 16)
            if c >= 1:
                e.wait_ge(c_x1, c)
                e.wait_ge(c_sev, c)
            for h_ in range(2):
                sl_ = slice(h_ * H, (h_ + 1) * H)
                e.matmul(pA[:, sl_], T0b[:], lt[:, sl_], start=True, stop=False)
                ins = e.matmul(pA[:, sl_], CBl[:], sprev[:, sl_], start=False, stop=True)
            ins.then_inc(c_y, 1)
            for h_ in range(2):
                sl_ = slice(h_ * H, (h_ + 1) * H)
                e.matmul(pB[:, sl_], ACl[:], sprev[:, sl_], start=True, stop=False)
                ins = e.matmul(pB[:, sl_], BmT[:], lt[:, sl_], start=False, stop=True)
            ins.then_inc(c_s, 1)
            if PHASE >= 2 and c < 16:
                e8_unit_p(16 + c)(e)
            if PHASE >= 3 and 18 <= c:
                st2_mm(e, 2 * (c - 18))
                st2_mm(e, 2 * (c - 18) + 1)
    em("P", p_conv)

    def s_conv(e):
        for c in range(NCH):
            e.wait_ge(c_s, c + 1)
            snew = sst[(c + 1) % 2]
            e.copy(snew[:, 0:H], pB[:, 0:H])
            e.copy(snew[:, H:NTOK], pB[:, H:NTOK]).then_inc(c_sev, 1)
            if PHASE >= 3 and 17 <= c:
                st2_sq(e, 2 * (c - 17))
                if 2 * (c - 17) + 1 < 30:
                    st2_sq(e, 2 * (c - 17) + 1)
    em("S", s_conv)

    barrier()

    def s_sq2(e):
        for c in range(30, NCH):
            st2_sq(e, c)
    em("S", s_sq2)

    def p_st2(e):
        for c in range(28, NCH):
            st2_mm(e, c)
    em("P", p_st2)
    step("S")
    step("P")

    step("V",
         lambda e: e.wait_ge(c_st, 2 * NCH),
         *rows_math(pC[0:1, :], pD[0:1, :],
                    rowt[:, 0:NTOK], rowt[:, NTOK:2 * NTOK]))
    step("S", lambda e: e.activation(rowt[:, NTOK:2 * NTOK],
                                     rowt[:, NTOK:2 * NTOK], AF.Sqrt))
    step("S", lambda e: e.activation(rowu[:, 0:NTOK], rowu[:, 0:NTOK], AF.Sqrt))
    step("V", lambda e: e.tensor_mul(rowu[:, 0:NTOK], rowu[:, 0:NTOK], rowt[:, 0:NTOK]))
    step("S", lambda e: e.copy(xsq[0:1, 0:NTOK], rowu[:, 0:NTOK]))
    step("P",
         lambda e: e.matmul(pA[:, 0:H], onesrf[:], rowt[:, NTOK:NTOK + H], start=True, stop=True),
         lambda e: e.matmul(pA[:, H:NTOK], onesrf[:], rowt[:, NTOK + H:2 * NTOK], start=True, stop=True),
         lambda e: e.matmul(pB[:, 0:H], onesrf[:], rowt[:, 0:H], start=True, stop=True),
         lambda e: e.matmul(pB[:, H:NTOK], onesrf[:], rowt[:, H:NTOK], start=True, stop=True))
    m2b = ln1b[:, 0:NTOK]
    step("S",
         lambda e: e.copy(s2b[:], pA[:, 0:NTOK]),
         lambda e: e.copy(m2b, pB[:, 0:NTOK]))
    SC2["v"] = seqcnt["S"]

    def v_sub(e):
        e.wait_ge(bS, SC2["v"])
        for c in range(NCH):
            e.wait_ge(c_st, NCH + c + 1)
            e.tensor_sub(ux[:, c * NTOK:(c + 1) * NTOK],
                         ux[:, c * NTOK:(c + 1) * NTOK], m2b).then_inc(c_sub, 1)
    if PHASE >= 4:
        em("V", v_sub)

    if PHASE < 4:
        def s_store_x1(e):
            e.wait_ge(c_x1, NCH)
            for oc in range(NCH):
                e.activation(outb[:, (oc % 2) * NTOK:(oc % 2 + 1) * NTOK],
                             ux[:, oc * NTOK:(oc + 1) * NTOK], AF.Identity).then_inc(c_ev, 1)
                e.wait_ge(c_ev, oc + 1)
                e.dma_start(out=out_t[oc * P:(oc + 1) * P, :],
                            in_=outb[:, (oc % 2) * NTOK:(oc % 2 + 1) * NTOK]).then_inc(s_out, 16)
                if oc >= 1:
                    e.wait_ge(s_out, 16 * oc)
        em("S", s_store_x1)
        if PHASE in (2, 3):
            def v_dec_tail(e):
                for k in range(NCH, NUNIT):
                    decode_unit_v(k)(e)
            def p_dec_tail(e):
                for k in range(NCH, NUNIT):
                    e8_unit_p(k)(e)
            em("V", v_dec_tail)
            em("P", p_dec_tail)

    import os as _os
    if _os.environ.get("KNOPTEST"):
        def p_mid(e):
            for _ in range(110):
                e.nop()
    elif False:
        pass
    if _os.environ.get("KNOPTEST"):
        pass
    else:
     def p_mid(e):
        for c in range(NCH):
            if PHASE >= 4:
                e.wait_ge(c_sub, c + 1)
            for h_ in range(2):
                e.matmul(pA[0:16, h_ * H:(h_ + 1) * H],
                         atb[:, c * 16:(c + 1) * 16],
                         ux[:, c * NTOK + h_ * H:c * NTOK + (h_ + 1) * H],
                         start=(c == 0), stop=(c == NCH - 1))
            ins = e.matmul(pB[0:16, 0:1], atb[:, c * 16:(c + 1) * 16], onesbf[:, 0:1],
                           start=(c == 0), stop=(c == NCH - 1))
        return ins
    if PHASE == 4:
        def v_dec_tail2(e):
            for k in range(NCH, NUNIT):
                decode_unit_v(k)(e)
        def p_dec_tail2(e):
            for k in range(NCH, NUNIT):
                e8_unit_p(k)(e)
        em("V", v_dec_tail2)
        em("P", p_dec_tail2)
        barrier()
    if PHASE >= 3:
        step("P", p_mid)
        step("V",
             lambda e: e.tensor_copy(midb[:], pA[0:16, 0:NTOK]),
             lambda e: e.tensor_copy(arowb[:], pB[0:16, 0:1]))
        SC3["v"] = seqcnt["V"]
        barrier()

    def p_ffn(e):
        for oc in range(NCH):
            wt = wtb[:, (oc % 2) * D:(oc % 2 + 1) * D]
            if oc == 0:
                e.wait_ge(bV, SC3["v"])
            e.wait_ge(s_wt, 16 * (2 * oc + 1))
            if oc >= 2:
                e.wait_ge(c_ev, oc - 1)
            for c in range(NCH):
                if oc == 0:
                    e.wait_ge(c_sub, c + 1)
                if c == 16:
                    e.wait_ge(s_wt, 16 * (2 * oc + 2))
                ins_c = None
                for h_ in range(2):
                    ins_c = e.matmul((pA if oc % 2 == 0 else pB)[:, h_ * H:(h_ + 1) * H],
                             wt[:, c * P:(c + 1) * P],
                             ux[:, c * NTOK + h_ * H:c * NTOK + (h_ + 1) * H],
                             start=(c == 0), stop=False)
                if c == 15:
                    ins_c.then_inc(c_fh, 1)
            ins = None
            for h_ in range(2):
                e.matmul((pA if oc % 2 == 0 else pB)[:, h_ * H:(h_ + 1) * H],
                         btb[:, oc * P:(oc + 1) * P],
                         midb[:, h_ * H:(h_ + 1) * H],
                         start=False, stop=False)
                ins = e.matmul((pA if oc % 2 == 0 else pB)[:, h_ * H:(h_ + 1) * H],
                         onesrbf[:, 0:P],
                         xsq[0:1, h_ * H:(h_ + 1) * H],
                         start=False, stop=True)
            ins.then_inc(c_ffn, 1)
            if PHASE >= 5:
                if oc < 10:
                    e8_unit_p(32 + 2 * oc)(e)
                    e8_unit_p(33 + 2 * oc)(e)
                elif oc < 22:
                    e8_unit_p(52 + oc - 10)(e)
    if PHASE >= 4:
        em("P", p_ffn)

    def s_ffn(e):
        for oc in range(NCH):
            e.wait_ge(c_ev, oc + 1)
            e.dma_start(out=out_t[oc * P:(oc + 1) * P, :],
                        in_=outb[:, (oc % 2) * NTOK:(oc % 2 + 1) * NTOK]).then_inc(s_out, 16)
    if PHASE >= 4:
        em("S", s_ffn)

    def v_ffn(e):
        for oc in range(NCH):
            pp = pA if oc % 2 == 0 else pB
            e.wait_ge(c_ffn, oc + 1)
            if oc >= 2:
                e.wait_ge(s_out, 16 * (oc - 1))
            ob = outb[:, (oc % 2) * NTOK:(oc % 2 + 1) * NTOK]
            e.tensor_mul(tevac[:], pp[:, 0:NTOK], s2b[:])
            e.tensor_add(ob, tevac[:], ux[:, oc * NTOK:(oc + 1) * NTOK]).then_inc(c_ev, 1)
            if PHASE >= 5:
                if oc < 10:
                    decode_unit_v(32 + 2 * oc)(e)
                    decode_unit_v(33 + 2 * oc)(e)
                elif oc < 22:
                    decode_unit_v(52 + oc - 10)(e)
    if PHASE >= 4:
        em("V", v_ffn)

    if DEBUG:
        def g_dbg(g):
            g.wait_ge(s_out, 32 * 16)
            dumps = [(dbg["K"], krow[:]), (dbg["T0"], T0b[:]), (dbg["BmT"], BmT[:]),
                     (dbg["CBl"], CBl[:]), (dbg["ACl"], ACl[:]),
                     (dbg["Ab"][:, 0:64], pw[:, 0:64]),
                     (dbg["Ab"][:, 64:128], pw[:, 64:128]),
                     (dbg["x1"], ux[:, 0:NTOK]),
                     (dbg["ln"], ln1b[:, NTOK:2 * NTOK]),
                     ] + (
                         [(dbg["W0"], wtb[:, D:D + P])] if PHASE >= 4 else [])
            for o_, i_ in dumps:
                g.dma_start(out=o_, in_=i_).then_inc(s_dbg, 16)
            g.wait_ge(s_dbg, 16 * len(dumps))
        em("G", g_dbg)

    with es, nc.Block() as block:
        @block.vector
        def _(vector):
            for f in ops["V"]:
                f(vector)

        @block.tensor
        def _(tensor):
            for f in ops["P"]:
                f(tensor)

        @block.scalar
        def _(scalar):
            for f in ops["S"]:
                f(scalar)

        @block.gpsimd
        def _(gpsimd):
            for f in ops["G"]:
                f(gpsimd)

        @block.sync
        def _(sync):
            for f in ops["Y"]:
                f(sync)

    return nc


_NC_CACHE = {}


def kernel(**inputs):
    u = np.asarray(inputs["L0_attn_u"], np.float32)
    lam = np.asarray(inputs["L0_attn_Lambda_ri"], np.float32)
    p_ri = np.asarray(inputs["L0_attn_p_ri"], np.float32)
    q_ri = np.asarray(inputs["L0_attn_q_ri"], np.float32)
    vc = np.asarray(inputs["L0_attn_Vc_ri"], np.float32)
    ct = np.asarray(inputs["L0_attn_Ct_ri"], np.float32)
    bp = np.asarray(inputs["L0_attn_Bp"], np.float32)
    Dv = np.asarray(inputs["L0_attn_D"], np.float32)
    ls = np.asarray(inputs["L0_attn_log_step"], np.float32)
    pk = np.asarray(inputs["L0_ffn_packed_weight"], np.int32)
    bs = np.asarray(inputs["L0_ffn_block_scales"], np.float32)
    gs = np.asarray(inputs["L0_ffn_global_scale"], np.float32)
    lA = np.asarray(inputs["L0_ffn_lora_A"], np.float32)
    lB = np.asarray(inputs["L0_ffn_lora_B"], np.float32)

    R = u.reshape(-1, D)
    packedT = np.ascontiguousarray(pk.T)
    bsT3 = np.ascontiguousarray(
        bs.T.reshape(N_IPC, 16, D).transpose(1, 0, 2).reshape(16, 16 * D))
    lA_T = np.ascontiguousarray(lA.T.reshape(NCH, P, 16).transpose(1, 0, 2).reshape(P, NCH * 16))
    lB_T = np.ascontiguousarray(lB.T)
    sm = np.ascontiguousarray(np.stack(
        [lam[:, 0], lam[:, 1], p_ri[:, 0], p_ri[:, 1], q_ri[:, 0], q_ri[:, 1],
         bp[:, 0], ct[0, :, 0], ct[0, :, 1]], axis=1))
    vctr = np.ascontiguousarray(vc[:, :, 0].T)
    vcti = np.ascontiguousarray(vc[:, :, 1].T)
    scal = np.ascontiguousarray(
        np.concatenate([Dv.reshape(1), ls.reshape(1), gs.reshape(1)]).reshape(1, 3))

    consts = {
        "cI64": np.eye(64, dtype=np.float32),
        "cJ128": np.eye(P, dtype=np.float32)[::-1].astype(MLBF),
        "cE8": np.repeat(0.5 * np.eye(16, dtype=np.float32), 8, axis=1).astype(MLBF),
        "cOnes": np.ones((P, 4), np.float32),
        "cOnesB": np.ones((P, 4), np.float32).astype(MLBF),
        "cOnesR": np.ones((1, P), np.float32),
        "cOnesRB": np.ones((1, P), np.float32).astype(MLBF),
    }

    key = ("prog", PHASE, DEBUG, SERIAL_DEC)
    if key not in _NC_CACHE:
        nc_new = build_program()
        mybir.codegen_inst_isa_subclasses(nc_new)
        _NC_CACHE[key] = nc_new
    nc = _NC_CACHE[key]

    in_maps = []
    for r in range(NC):
        m = {
            "u_s": np.ascontiguousarray(R[r * NTOK:(r + 1) * NTOK, :].T),
            "packedT": packedT, "bsT3": bsT3, "lA_T": lA_T, "lB_T": lB_T,
            "sm": sm, "VcT_re": vctr, "VcT_im": vcti, "scal": scal,
        }
        m.update(consts)
        in_maps.append(m)

    res = run_bass_kernel_spmd(nc, in_maps, core_ids=list(range(NC)))
    out = np.empty((NC * NTOK, D), np.float32)
    for r in range(NC):
        out[r * NTOK:(r + 1) * NTOK, :] = res.results[r]["out_s"].T
    if DEBUG:
        kernel._dbg = res.results
    return out.reshape(2, -1, D)
